# version 08
# baseline (speedup 1.0000x reference)
import numpy as np
import ml_dtypes
from contextlib import ExitStack

import concourse.bass as bass
import concourse.mybir as mybir
from concourse.bass_utils import run_bass_kernel_spmd

F32 = mybir.dt.float32
BF16 = mybir.dt.bfloat16
I32 = mybir.dt.int32
AF = mybir.ActivationFunctionType
ALU = mybir.AluOpType
AX = mybir.AxisListType

HID = 4096
NH = 32
NKV = 8
BATCH = 4
SEQ = 2048
TOK = 1024
NT = 8
NCH = 32
EPS = 1e-5
QB = 127.0
CAPACITY = 3072.0
ALPHA = 1.0 / 16.0
SM_SCALE = 0.08838834764831845
SHIFT_CAP = 80.0

DEBUG_DUMPS = False
ENGS = ("pe", "act", "dve", "pool", "sp")
DMA_CH = ("dW0", "dW1", "dX0", "dX1", "dK0", "dK1", "dM", "dS", "cc0", "cc1", "cc2")


class Bld:
    def __init__(self, nc, ex):
        self.nc = nc
        self.sem = {ch: ex.enter_context(nc.semaphore("sem_" + ch))
                    for ch in ENGS + DMA_CH}
        self.cnt = {ch: 0 for ch in ENGS + DMA_CH}
        self.waited = {e: {} for e in ENGS}
        self.stream = {e: [] for e in ENGS}
        self.last_inc = {e: True for e in ENGS}

    def _wait(self, eng, tok):
        ch, v = tok
        if v <= 0 or self.waited[eng].get(ch, 0) >= v:
            return
        self.waited[eng][ch] = v
        sem = self.sem[ch]
        self.stream[eng].append(lambda h, sem=sem, v=v: h.wait_ge(sem, v))

    def op(self, eng, fn, deps=(), inc=True):
        for t in deps:
            if t is not None:
                self._wait(eng, t)
        tok = None
        if inc:
            self.cnt[eng] += 1
            tok = (eng, self.cnt[eng])
            sem = self.sem[eng]
            self.stream[eng].append(lambda h, fn=fn, sem=sem: fn(h).then_inc(sem, 1))
        else:
            self.stream[eng].append(lambda h, fn=fn: fn(h))
        self.last_inc[eng] = bool(inc)
        return tok

    def dma(self, eng, ch, out, in_, deps=()):
        for t in deps:
            if t is not None:
                self._wait(eng, t)
        self.cnt[ch] += 16
        tok = (ch, self.cnt[ch])
        sem = self.sem[ch]
        self.stream[eng].append(
            lambda h, out=out, in_=in_, sem=sem:
                h.dma_start(out=out, in_=in_).then_inc(sem, 16))
        self.last_inc[eng] = False
        return tok

    def coll(self, kind, alu, groups, ins, outs, deps=(), ch="cc0"):
        for t in deps:
            if t is not None:
                self._wait("pool", t)
        self.cnt[ch] += 1
        tok = (ch, self.cnt[ch])
        sem = self.sem[ch]
        self.stream["pool"].append(
            lambda h, sem=sem: h.collective_compute(
                kind, alu, replica_groups=groups, ins=ins, outs=outs
            ).then_inc(sem, 1))
        self.last_inc["pool"] = False
        return tok

    def barrier(self, engines=("pe", "act", "dve", "pool")):
        for e in engines:
            if not self.last_inc[e] and self.stream[e]:
                self.cnt[e] += 1
                sem = self.sem[e]
                self.stream[e].append(
                    lambda h, sem=sem: h.nop().then_inc(sem, 1))
                self.last_inc[e] = True
        toks = [(e, self.cnt[e]) for e in engines]
        for e in engines:
            for t in toks:
                if t[0] != e:
                    self._wait(e, t)


def build_nc():
    nc = bass.Bass("TRN2", target_bir_lowering=False,
                   detect_race_conditions=False)

    def din(name, shape, dt=F32):
        return nc.dram_tensor(name, shape, dt, kind="ExternalInput").ap()

    x_own = din("x_own", [TOK, HID])
    wq = din("wq", [NH, 128, HID])
    wk = din("wk", [NKV, 128, HID])
    wv = din("wv", [NKV, 128, HID])
    wo = din("wo", [NH, 128, HID])
    wm = din("wm", [NCH, 128, HID])
    a_all = din("a_all", [HID, 128])
    b_all = din("b_all", [128, HID])
    gw1t = din("gw1t", [HID, 32])
    gb1c = din("gb1c", [32, 1])
    g32 = din("g32", [128, 32])
    bt32 = din("bt32", [128, 32])
    gw2t = din("gw2t", [32, 8])
    gb2c = din("gb2c", [8, 1])
    lnw = din("lnw", [128, HID])
    lnb = din("lnb", [128, HID])
    cosb = din("cosb", [128, TOK])
    sinb = din("sinb", [128, TOK])
    identbf = din("identbf", [128, 128], BF16)
    identf = din("identf", [128, 128])
    onesf = din("onesf", [128, 128])
    onesbf = din("onesbf", [128, 1], BF16)
    selS = din("selS", [8, 128], BF16)

    y_own = nc.dram_tensor("y", [TOK, HID], F32, kind="ExternalOutput").ap()

    x2d = nc.dram_tensor("x2d", [TOK, HID], F32)
    at3d = nc.dram_tensor("at3d", [TOK, HID], BF16)
    ccg_in = nc.dram_tensor("ccg_in", [2 * NKV * 128, TOK], BF16)
    ccg_out = nc.dram_tensor("ccg_out", [2 * 2 * NKV * 128, TOK], BF16)
    cnt_in = nc.dram_tensor("cnt_in", [1, 16], F32)
    if DEBUG_DUMPS:
        dbg_qn = nc.dram_tensor("dbg_qn", [128, NCH * TOK], BF16,
                                kind="ExternalOutput")
        dbg_q3 = nc.dram_tensor("dbg_q3", [128, NCH * TOK], BF16,
                                kind="ExternalOutput")
        dbg_b3 = nc.dram_tensor("dbg_b3", [128, NCH * TOK], BF16,
                                kind="ExternalOutput")
        dbg_ccg = nc.dram_tensor("dbg_ccg", [2 * NKV * 128, TOK], BF16,
                                 kind="ExternalOutput")
        dbg_at3 = nc.dram_tensor("dbg_at3", [TOK, HID], BF16,
                                 kind="ExternalOutput")
        dbg_x2 = nc.dram_tensor("dbg_x2", [TOK, HID], F32,
                                kind="ExternalOutput")
        dbg_sc = nc.dram_tensor("dbg_sc", [128, 64], F32,
                                kind="ExternalOutput")
    cnt_out = nc.dram_tensor("cnt_out", [1, 16], F32)
    kn_in = nc.dram_tensor("kn_in", [1, NKV], F32)
    kn_out = nc.dram_tensor("kn_out", [1, NKV], F32)

    ex = ExitStack()
    with ex:
        b = Bld(nc, ex)
        E = ex.enter_context

        def ACT(fn, deps=(), inc=True):
            return b.op("act", fn, deps, inc)

        def DVE(fn, deps=(), inc=True):
            return b.op("dve", fn, deps, inc)

        def PE(fn, deps=(), inc=False):
            return b.op("pe", fn, deps, inc)

        s_identbf = E(nc.sbuf_tensor("s_identbf", [128, 128], BF16))
        s_identf = E(nc.sbuf_tensor("s_identf", [128, 128], F32))
        s_onesf = E(nc.sbuf_tensor("s_onesf", [128, 128], F32))
        s_onesbf = E(nc.sbuf_tensor("s_onesbf", [128, 1], BF16))
        s_selS = E(nc.sbuf_tensor("s_selS", [8, 128], BF16))
        s_g32 = E(nc.sbuf_tensor("s_g32", [128, 32], F32))
        s_bt32 = E(nc.sbuf_tensor("s_bt32", [128, 32], F32))
        s_gb1 = E(nc.sbuf_tensor("s_gb1", [32, 1], F32))
        s_gb2 = E(nc.sbuf_tensor("s_gb2", [8, 1], F32))
        s_gw2bf = E(nc.sbuf_tensor("s_gw2bf", [32, 8], BF16))
        s_scales = E(nc.sbuf_tensor("s_scales", [128, 12], F32))
        s_negM = E(nc.sbuf_tensor("s_negM", [128, 1], F32))
        s_stat = E(nc.sbuf_tensor("s_stat", [128, 48], F32))
        s_row = E(nc.sbuf_tensor("s_row", [1, 64], F32))
        s_qn2 = E(nc.sbuf_tensor("s_qn2", [1, NH], F32))
        s_kn2 = E(nc.sbuf_tensor("s_kn2", [1, NKV], F32))
        s_wcol = E(nc.sbuf_tensor("s_wcol", [128, 160], F32))
        s_cnt = E(nc.sbuf_tensor("s_cnt", [128, 16], F32))
        s_cntrow = E(nc.sbuf_tensor("s_cntrow", [1, 16], F32))
        s_alb = E(nc.sbuf_tensor("s_alb", [128, 16], F32))
        s_c1 = E(nc.sbuf_tensor("s_c1", [128, NT, 8], F32))
        s_c2 = E(nc.sbuf_tensor("s_c2", [128, NT, 8], F32))
        s_tsc = E(nc.sbuf_tensor("s_tsc", [128, TOK], BF16))
        s_cstg = E(nc.sbuf_tensor("s_cstg", [128, 128], F32))

        regA_buf = E(nc.sbuf_tensor("regA", [128, NCH * TOK], BF16))
        regC_buf = E(nc.sbuf_tensor("regC", [128, NCH * TOK], BF16))

        strm = ExitStack()
        SB = {}

        _sbn = [0]

        def open_stream_base(st):
            _sbn[0] += 1
            n = _sbn[0]
            SB["blk"] = [st.enter_context(
                nc.sbuf_tensor(f"s_wblk{i}_{n}", [128, 2048], F32))
                for i in range(2)]
            SB["bf"] = [st.enter_context(
                nc.sbuf_tensor(f"s_wbf{i}_{n}", [128, 2048], BF16))
                for i in range(2)]
            SB["junk"] = st.enter_context(
                nc.sbuf_tensor(f"s_wjunk_{n}", [128, 2048], F32))

        def open_stream_qi(st):
            n = _sbn[0]
            SB["qi"] = [st.enter_context(
                nc.sbuf_tensor(f"s_wqi{i}_{n}", [128, 2048], I32))
                for i in range(2)]

        def open_stream_bufs(st):
            open_stream_base(st)
            open_stream_qi(st)

        open_stream_base(strm)

        pscl = ExitStack()
        p_s1 = pscl.enter_context(nc.psum_tensor("p_s1", [1, 512], F32))
        pb128 = pscl.enter_context(nc.psum_tensor("pb128", [128, 512], F32))

        class PbfRing:

            def __init__(self, ap):
                self.ap = ap
                self.rd = [None, None]
                self.i = 0

            def next(self):
                s = self.i % 8
                self.i += 1
                bk = s % 2
                c0 = bk * 1024 + (s // 2) * 128
                return self.ap[:, c0:c0 + 128], bk

        def _mkset(base):
            names = ("sum", "sq", "r", "b", "am", "s", "inv", "t0", "t1",
                     "den", "m1", "m2", "d", "e2", "rg", "g1", "g2")
            d = {nm: s_stat[:, base + i:base + i + 1]
                 for i, nm in enumerate(names)}
            d["last"] = None
            return d

        SCP = [_mkset(0), _mkset(17)]
        sc0 = SCP[0]
        c_sum = sc0["sum"]
        c_sq = sc0["sq"]
        c_r = sc0["r"]
        c_b = sc0["b"]
        c_am = sc0["am"]
        c_s = sc0["s"]
        c_inv = sc0["inv"]
        c_t0 = sc0["t0"]
        c_t1 = sc0["t1"]
        c_den = sc0["den"]
        c_m1 = sc0["m1"]
        c_m2 = sc0["m2"]
        c_d = sc0["d"]
        c_e2 = sc0["e2"]
        c_rg = sc0["rg"]
        c_g1 = sc0["g1"]
        c_g2 = sc0["g2"]

        tc = []
        for dst, src in ((s_identbf, identbf), (s_identf, identf),
                         (s_onesf, onesf), (s_onesbf, onesbf), (s_selS, selS),
                         (s_g32, g32), (s_bt32, bt32), (s_gb1, gb1c),
                         (s_gb2, gb2c)):
            tc.append(b.dma("pool", "dM", dst[:], src))
        t_const = tc[-1]
        t_cnt0 = DVE(lambda h: h.memset(s_cnt[:], 0.0))
        c_eps = s_stat[:, 40:41]
        t_eps = DVE(lambda h: h.memset(c_eps, EPS))
        r_tiny = s_row[0:1, 62:63]
        t_tiny = DVE(lambda h: h.memset(r_tiny, 1e-30))

        def ln_stats(src_ap, n, deps, copy_dst, sq_dst, sc=None):
            if sc is None:
                sc = SCP[0]
            csum, csq, ct0, ct1, cr, cb = (sc["sum"], sc["sq"], sc["t0"],
                                           sc["t1"], sc["r"], sc["b"])
            deps = list(deps) + [sc["last"]]
            dst = copy_dst
            t2 = ACT(lambda h, dst=dst: h.activation(out=dst, in_=src_ap,
                                                     func=AF.Copy,
                                                     accum_out=csum), deps)
            t1 = ACT(lambda h, sq_dst=sq_dst: h.activation(
                out=sq_dst, in_=src_ap, func=AF.Square, accum_out=csq),
                     deps)
            t3 = DVE(lambda h: h.tensor_scalar(out=ct0, in0=csum,
                                               scalar1=1.0 / n, scalar2=None,
                                               op0=ALU.mult), [t2])
            t4 = DVE(lambda h: h.tensor_tensor(out=ct1, in0=ct0, in1=ct0,
                                               op=ALU.mult), [t3])
            t5 = DVE(lambda h: h.scalar_tensor_tensor(out=ct1, in0=csq,
                                                      scalar=1.0 / n, in1=ct1,
                                                      op0=ALU.mult,
                                                      op1=ALU.subtract),
                     [t1, t4])
            t6 = ACT(lambda h: h.activation(out=ct1, in_=ct1, func=AF.Ln,
                                            bias=c_eps, scale=1.0),
                     [t5, t_eps])
            t7 = ACT(lambda h: h.activation(out=cr, in_=ct1, func=AF.Exp,
                                            bias=0.0, scale=-0.5), [t6])
            t8 = DVE(lambda h: h.tensor_tensor(out=cb, in0=ct0, in1=cr,
                                               op=ALU.mult), [t7, t3])
            t9 = DVE(lambda h: h.tensor_scalar(out=cb, in0=cb, scalar1=-1.0,
                                               scalar2=None, op0=ALU.mult),
                     [t8])
            return t7, t9, t2

        def emit_act_quant(src_col, out_col, y1ap, deps_in, n_chunks=4,
                           sc=None):
            if sc is None:
                sc = SCP[0]
            cs_, cinv_ = sc["s"], sc["inv"]
            A = y1ap[:, 0:1024]
            Bi = y1ap[:, 1024:2048].bitcast(I32)
            C = y1ap[:, 2048:3072]
            Es = y1ap[:, 3072:4096]
            tA = tQ = None
            for k in range(n_chunks):
                tA = DVE(lambda h, k=k: h.tensor_scalar(
                    out=A, in0=src_col(k), scalar1=cs_, scalar2=None,
                    op0=ALU.mult), list(deps_in) + [tQ])
                tB = DVE(lambda h: h.tensor_copy(out=Bi, in_=A), [tA])
                tC = DVE(lambda h: h.tensor_tensor(out=C, in0=A, in1=Bi,
                                                   op=ALU.subtract), [tB])
                tD = DVE(lambda h: h.tensor_scalar(out=A, in0=C, scalar1=0.5,
                                                   scalar2=None,
                                                   op0=ALU.is_ge), [tC])
                tE = DVE(lambda h: h.scalar_tensor_tensor(
                    out=Es, in0=C, scalar=-0.5, in1=A, op0=ALU.is_le,
                    op1=ALU.subtract), [tD])
                tF = DVE(lambda h: h.tensor_tensor(out=C, in0=Bi, in1=Es,
                                                   op=ALU.subtract), [tE])
                tQ = DVE(lambda h, k=k: h.tensor_scalar(
                    out=out_col(k), in0=C, scalar1=cinv_, scalar2=None,
                    op0=ALU.mult), [tF])
            sc["last"] = tQ
            return tA, tQ

        class HalfStream:
            def __init__(self):
                self.i = 0
                self.t_q1 = [None, None]
                self.t_q2 = [None, None]
                self.t_mm = [None, None]
                self.t_abs = [None, None]

            def next(self, src_slice, mode, scol=0, wcol=0, extra_deps=(),
                     scol_c2=(8,)):
                par = self.i % 2
                self.i += 1
                blk, bfb, jnk = SB["blk"][par], SB["bf"][par], SB["junk"]
                qib = SB["qi"][par] if "qi" in SB else None
                tl = b.dma("sp", "dW%d" % par, blk[:], src_slice,
                           deps=[self.t_q1[par], self.t_abs[par]] +
                                list(extra_deps))
                if mode == "abs":
                    ta = ACT(lambda h, blk=blk, jnk=jnk, wcol=wcol:
                             h.activation(
                        out=jnk[:], in_=blk[:], func=AF.Abs,
                        accum_out=s_wcol[:, wcol:wcol + 1]), [tl])
                    self.t_abs[par] = ta
                    return par, ta
                if mode == "quant":
                    c2col = scol_c2[0]
                    qif = qib[:].bitcast(F32)
                    tq1 = DVE(lambda h, blk=blk, qif=qif, scol=scol:
                              h.tensor_scalar(
                        out=qif, in0=blk[:],
                        scalar1=s_scales[:, scol:scol + 1], scalar2=0.5,
                        op0=ALU.mult, op1=ALU.is_ge), [tl, self.t_q2[par]])
                    tq2 = DVE(lambda h, bfb=bfb, qif=qif, blk=blk,
                              c2col=c2col: h.scalar_tensor_tensor(
                        out=bfb[:], in0=blk[:],
                        scalar=s_scales[:, c2col:c2col + 1], in1=qif,
                        op0=ALU.is_le, op1=ALU.subtract),
                        [tq1, self.t_mm[par]])
                    self.t_q1[par] = tq2
                    self.t_q2[par] = tq2
                    return par, tq2
                tq = DVE(lambda h, blk=blk, bfb=bfb: h.tensor_copy(
                    out=bfb[:], in_=blk[:]),
                    [tl, self.t_mm[par]])
                self.t_q1[par] = tq
                self.t_q2[par] = tq
                return par, tq

            def bf(self, par):
                return SB["bf"][par]

            def consumed(self, par, tok):
                self.t_mm[par] = tok

        ws = HalfStream()

        scale_queue = []
        wc_base = {"wk": 0, "wv": 16, "wq": 32, "wo": 96}
        for nm_, src_, name_ in ((NKV, wk, "wk"), (NKV, wv, "wv"),
                                 (NH, wq, "wq"), (NH, wo, "wo")):
            for m in range(nm_):
                for hf in range(2):
                    scale_queue.append(
                        (name_, src_, m, hf, wc_base[name_] + 2 * m + hf))
        sq_pos = [0]
        sq_abs = {"wk": [], "wv": [], "wq": [], "wo": []}
        t_pb_use = [None]

        def emit_scale_halves(k):
            for _ in range(k):
                if sq_pos[0] >= len(scale_queue):
                    return
                name, src, m, hf, wcol = scale_queue[sq_pos[0]]
                sq_pos[0] += 1
                par, ta = ws.next(src[m, :, hf * 2048:(hf + 1) * 2048], "abs",
                                  wcol=wcol)
                sq_abs[name].append(ta)

        def finish_scale(name, nm, scol_s, scol_d, extra, slot):
            nelem = nm * 128 * HID
            cb = wc_base[name]
            tr = DVE(lambda h, nm=nm, cb=cb: h.tensor_reduce(
                out=c_t0, in_=s_wcol[:, cb:cb + 2 * nm], axis=AX.X,
                op=ALU.add), [sq_abs[name][-1]])
            tm1 = PE(lambda h, slot=slot: h.matmul(
                p_s1[0:1, slot:slot + 1], s_onesf[:, 0:1], c_t0,
                start=True, stop=True), [tr, t_pb_use[0], t_const], inc=True)
            r0 = 32 + slot * 4
            tm2 = DVE(lambda h, nelem=nelem, r0=r0, slot=slot: h.tensor_scalar(
                out=s_row[0:1, r0:r0 + 1], in0=p_s1[0:1, slot:slot + 1],
                scalar1=1.0 / nelem, scalar2=1e-5, op0=ALU.mult, op1=ALU.max),
                [tm1])
            tm3 = DVE(lambda h, r0=r0: h.reciprocal(
                out=s_row[0:1, r0 + 1:r0 + 2], in_=s_row[0:1, r0:r0 + 1]),
                [tm2])
            tm4 = DVE(lambda h, r0=r0, extra=extra: h.tensor_scalar(
                out=s_row[0:1, r0 + 2:r0 + 3], in0=s_row[0:1, r0:r0 + 1],
                scalar1=-extra, scalar2=None, op0=ALU.mult), [tm2])
            tm5 = DVE(lambda h, r0=r0: h.tensor_scalar(
                out=s_row[0:1, r0 + 3:r0 + 4], in0=s_row[0:1, r0:r0 + 1],
                scalar1=-0.5, scalar2=None, op0=ALU.mult), [tm2])
            tb1 = PE(lambda h, scol_d=scol_d, r0=r0: h.matmul(
                pb128[:, scol_d:scol_d + 1], s_onesf[0:1, :],
                s_row[0:1, r0 + 2:r0 + 3], start=True, stop=True),
                [tm4], inc=True)
            tb2 = PE(lambda h, scol_s=scol_s, r0=r0: h.matmul(
                pb128[:, scol_s:scol_s + 1], s_onesf[0:1, :],
                s_row[0:1, r0 + 1:r0 + 2], start=True, stop=True),
                [tm3], inc=True)
            tb3 = PE(lambda h, slot=slot, r0=r0: h.matmul(
                pb128[:, 8 + slot:9 + slot], s_onesf[0:1, :],
                s_row[0:1, r0 + 3:r0 + 4], start=True, stop=True),
                [tm5], inc=True)
            lo, hi = min(scol_s, scol_d), max(scol_s, scol_d)
            te_ = DVE(lambda h, lo=lo, hi=hi: h.tensor_copy(
                out=s_scales[:, lo:hi + 1], in_=pb128[:, lo:hi + 1]),
                [tb1, tb2, tb3])
            te = DVE(lambda h, slot=slot: h.tensor_copy(
                out=s_scales[:, 8 + slot:9 + slot],
                in_=pb128[:, 8 + slot:9 + slot]), [tb3, te_])
            t_pb_use[0] = te
            return te

        qn3 = regA_buf[:].rearrange("p (c t) -> p c t", t=TOK)

        scope1 = ExitStack()
        pbf1s = ExitStack()
        ring1 = PbfRing(pbf1s.enter_context(
            nc.psum_tensor("pbf1", [128, 2048], BF16))[:])
        s_x = [scope1.enter_context(nc.sbuf_tensor(f"s_x{i}", [128, HID], F32))
               for i in range(2)]
        s_y1 = regC_buf[:, 0:8192].bitcast(F32)
        s_lnw = regC_buf[:, 8192:16384].bitcast(F32)
        s_lnb = regC_buf[:, 16384:24576].bitcast(F32)
        s_qtm = regC_buf[:, 24576:28672]

        t_lnw = b.dma("pool", "dM", s_lnw, lnw)
        t_lnb = b.dma("pool", "dM", s_lnb, lnb)

        t_x_use = [None, None]
        t_pbf = [None] * 8
        tok_qn = None
        prev_tq1 = None
        prev_tq2 = None
        prev_tz = None
        for t in range(NT):
            par = t % 2
            xb = s_x[par]
            tl = b.dma("pool", "dX%d" % par, xb[:],
                       x_own[t * 128:(t + 1) * 128, :], deps=[t_x_use[par]])
            sc = SCP[par]
            scr, scb, scam, scs, scinv = (sc["r"], sc["b"], sc["am"],
                                          sc["s"], sc["inv"])
            r1, b1, _ = ln_stats(xb[:], HID, [tl, prev_tq2],
                                 copy_dst=s_y1, sq_dst=s_y1, sc=sc)
            ty = ACT(lambda h, xb=xb, scr=scr, scb=scb: h.activation(
                out=s_y1, in_=xb[:], func=AF.Identity, bias=scb, scale=scr),
                [r1, b1])
            r2, b2, _ = ln_stats(s_y1, HID, [ty],
                                 copy_dst=xb[:], sq_dst=xb[:], sc=sc)
            tz = ACT(lambda h, xb=xb, scr=scr, scb=scb: h.activation(
                out=xb[:], in_=s_y1, func=AF.Identity, bias=scb, scale=scr),
                [r2, b2, prev_tq1])
            tz2 = DVE(lambda h, xb=xb: h.tensor_tensor(
                out=s_y1, in0=xb[:], in1=s_lnw, op=ALU.mult),
                [tz, t_lnw])
            tz3 = DVE(lambda h, xb=xb: h.tensor_tensor(
                out=xb[:], in0=s_y1, in1=s_lnb, op=ALU.add),
                [tz2, t_lnb])
            tam = DVE(lambda h, xb=xb, scam=scam: h.tensor_reduce(
                out=scam, in_=xb[:], axis=AX.X, op=ALU.max,
                apply_absolute_value=True), [tz3])
            tmc = DVE(lambda h, scam=scam: h.tensor_scalar(
                out=scam, in0=scam, scalar1=1e-5, scalar2=None,
                op0=ALU.max), [tam])
            trc = DVE(lambda h, scs=scs, scam=scam: h.reciprocal(
                out=scs, in_=scam), [tmc])
            ts_ = DVE(lambda h, scs=scs: h.tensor_scalar(
                out=scs, in0=scs, scalar1=QB,
                scalar2=None, op0=ALU.mult), [trc])
            tin = DVE(lambda h, scinv=scinv, scam=scam: h.tensor_scalar(
                out=scinv, in0=scam, scalar1=1.0 / QB, scalar2=None,
                op0=ALU.mult), [tmc])
            tq1, tq2 = emit_act_quant(
                lambda k, xb=xb: xb[:, k * 1024:(k + 1) * 1024],
                lambda k: s_qtm[:, k * 1024:(k + 1) * 1024],
                s_y1, [ts_, tin, tz3, tam, tok_qn], sc=sc)
            prev_tq1 = tq1
            t_x_use[par] = tq1
            prev_tq2 = tq2
            prev_tz = tz
            for c in range(NCH):
                sl, bk = ring1.next()
                tp = PE(lambda h, c=c, sl=sl: h.transpose(
                    sl, s_qtm[:, c * 128:(c + 1) * 128], s_identbf[:]),
                    [tq2, ring1.rd[bk], t_const], inc=True)
                tev = DVE(lambda h, c=c, t=t, sl=sl: h.tensor_copy(
                    out=qn3[:, c, t * 128:(t + 1) * 128], in_=sl), [tp])
                ring1.rd[bk] = tev
                tok_qn = tev
            emit_scale_halves(6)
        scope1.close()
        pbf1s.close()

        if DEBUG_DUMPS:
            td_ = b.dma("pool", "dM", dbg_qn.ap(), regA_buf[:],
                        deps=[tok_qn, prev_tq1])
            b._wait("dve", td_)
            b._wait("pe", td_)
        tok_swk = finish_scale("wk", NKV, 2, 3, 1.0, 0)
        tok_swv = finish_scale("wv", NKV, 4, 5, 1.0, 1)

        Q3 = regC_buf[:].rearrange("p (m t) -> p m t", t=TOK)
        strmqi = ExitStack()
        open_stream_qi(strmqi)
        scope2 = ExitStack()
        s_cos = scope2.enter_context(nc.sbuf_tensor("s_cos", [128, TOK], F32))
        s_sin = scope2.enter_context(nc.sbuf_tensor("s_sin", [128, TOK], F32))
        s_kpre = scope2.enter_context(nc.sbuf_tensor("s_kpre", [128, TOK], F32))
        s_krot = scope2.enter_context(nc.sbuf_tensor("s_krot", [128, TOK], F32))
        s_kt1 = scope2.enter_context(nc.sbuf_tensor("s_kt1", [128, TOK], F32))
        s_stg = [scope2.enter_context(
            nc.sbuf_tensor(f"s_stg{i}", [128, TOK], BF16)) for i in range(2)]
        s_sq = s_kt1[:].bitcast(BF16)[:, 0:TOK]
        t_cos = b.dma("pool", "dM", s_cos[:], cosb,
                      deps=[prev_tq1, prev_tq2, prev_tz])
        t_sin = b.dma("pool", "dM", s_sin[:], sinb)
        pj = ExitStack()
        pproj4 = [pj.enter_context(nc.psum_tensor(f"pproj{i}", [128, 512], F32))
                  for i in range(4)]
        pnorm = pj.enter_context(nc.psum_tensor("pnorm", [1, 1024], F32))

        t_pp = [None, None, None, None]
        t_stg_use = [None, None]
        t_rope_rd = [None]
        t_stage = []
        t_kn = []
        t_pnorm = [None]
        t_sq_use = [None]

        blk_ctr = [0]

        def proj_block(src, m, scol_q, dcol, rope, kind, scale_tok,
                       c2col_q=8):
            pr = blk_ctr[0] % 2
            blk_ctr[0] += 1
            pproj = pproj4[2 * pr:2 * pr + 2]
            mm_last = [None, None]
            for hf in range(2):
                par, trdy = ws.next(src[m, :, hf * 2048:(hf + 1) * 2048],
                                    "quant", scol=scol_q,
                                    extra_deps=[scale_tok],
                                    scol_c2=(c2col_q,))
                tokm = None
                for j in range(16):
                    c = hf * 16 + j
                    for tg in range(2):
                        dep = [trdy, tok_qn, scale_tok]
                        if c == 0:
                            dep.append(t_pp[2 * pr + tg])
                        tokm = PE(lambda h, wb_=ws.bf(par), j=j, c=c, tg=tg,
                                  pp_=pproj[tg]: h.matmul(
                            pp_[:],
                            wb_[:, j * 128:(j + 1) * 128],
                            qn3[:, c, tg * 512:(tg + 1) * 512],
                            start=(c == 0), stop=(c == 31)),
                            dep, inc=(tg == 1 and (c == 15 or c == 31)))
                mm_last[hf] = tokm
                ws.consumed(par, tokm)
            stg_par = m % 2
            deps_e = [mm_last[0], mm_last[1]]
            if rope:
                te0 = ACT(lambda h, dcol=dcol, pp_=pproj[0]: h.activation(
                    out=s_kpre[:, 0:512], in_=pp_[:], func=AF.Copy,
                    scale=s_scales[:, dcol:dcol + 1]),
                    deps_e + [t_rope_rd[0]])
                te1 = ACT(lambda h, dcol=dcol, pp_=pproj[1]: h.activation(
                    out=s_kpre[:, 512:1024], in_=pp_[:], func=AF.Copy,
                    scale=s_scales[:, dcol:dcol + 1]), deps_e)
                GPS = lambda fn, deps: b.op("pool", fn, deps, True)
                tn = GPS(lambda h: h.tensor_scalar(
                    out=s_krot[0:64, :], in0=s_kpre[64:128, :], scalar1=-1.0,
                    scalar2=None, op0=ALU.mult), [te0, te1])
                tcp = GPS(lambda h: h.tensor_copy(out=s_krot[64:128, :],
                                                  in_=s_kpre[0:64, :]),
                          [te0, te1])
                t1 = GPS(lambda h: h.tensor_tensor(out=s_kt1[:], in0=s_kpre[:],
                                                   in1=s_cos[:], op=ALU.mult),
                         [te0, te1, t_cos, t_sq_use[0]])
                t2 = GPS(lambda h: h.tensor_tensor(out=s_kpre[:],
                                                   in0=s_krot[:], in1=s_sin[:],
                                                   op=ALU.mult),
                         [tn, tcp, t_sin, t1])
                if kind == "q":
                    t3 = GPS(lambda h, m=m: h.tensor_tensor(
                        out=Q3[:, m, :], in0=s_kt1[:], in1=s_kpre[:],
                        op=ALU.add), [t1, t2])
                else:
                    t3 = GPS(lambda h, stg_par=stg_par: h.tensor_tensor(
                        out=s_stg[stg_par][:], in0=s_kt1[:], in1=s_kpre[:],
                        op=ALU.add), [t1, t2, t_stg_use[stg_par]])
                t_rope_rd[0] = t3
                t_pp[2 * pr], t_pp[2 * pr + 1] = te0, te1
            else:
                te0 = ACT(lambda h, stg_par=stg_par, dcol=dcol, pp_=pproj[0]:
                          h.activation(
                    out=s_stg[stg_par][:, 0:512], in_=pp_[:],
                    func=AF.Copy, scale=s_scales[:, dcol:dcol + 1]),
                    deps_e + [t_stg_use[stg_par]])
                te1 = ACT(lambda h, stg_par=stg_par, dcol=dcol, pp_=pproj[1]:
                          h.activation(
                    out=s_stg[stg_par][:, 512:1024], in_=pp_[:],
                    func=AF.Copy, scale=s_scales[:, dcol:dcol + 1]), deps_e)
                t3 = te1
                t_pp[2 * pr], t_pp[2 * pr + 1] = te0, te1
            tsq = None
            if kind == "q":
                tsq = ACT(lambda h, m=m: h.activation(
                    out=s_sq, in_=Q3[:, m, :], func=AF.Square),
                    [t3, t_sq_use[0]])
            elif kind == "k":
                tsq = ACT(lambda h, stg_par=stg_par: h.activation(
                    out=s_sq, in_=s_stg[stg_par][:], func=AF.Square),
                    [t3, t_sq_use[0]])
            if tsq is not None:
                tn1 = PE(lambda h: h.matmul(pnorm[0:1, 0:512],
                                            s_onesbf[:, 0:1], s_sq[:, 0:512],
                                            start=True, stop=True),
                         [tsq, t_pnorm[0], t_const], inc=True)
                tn2 = PE(lambda h: h.matmul(pnorm[0:1, 512:1024],
                                            s_onesbf[:, 0:1],
                                            s_sq[:, 512:1024],
                                            start=True, stop=True),
                         [tsq], inc=True)
                t_sq_use[0] = tn2
                dst = s_qn2 if kind == "q" else s_kn2
                tnr = DVE(lambda h, dst=dst, m=m: h.tensor_reduce(
                    out=dst[0:1, m:m + 1], in_=pnorm[0:1, :], axis=AX.X,
                    op=ALU.max), [tn1, tn2])
                t_pnorm[0] = tnr
                if kind == "k":
                    t_kn.append(tnr)
            if kind in ("k", "v"):
                row0 = (0 if kind == "k" else NKV * 128) + m * 128
                td = b.dma("pool", "dK0", ccg_in.ap()[row0:row0 + 128, :],
                           s_stg[stg_par][:],
                           deps=[t3] + ([tsq] if kind == "k" else []))
                t_stg_use[stg_par] = td
                t_stage.append(td)
            return t3

        for m in range(NKV):
            proj_block(wk, m, 2, 3, True, "k", tok_swk, c2col_q=8)
            emit_scale_halves(3)
        for m in range(NKV):
            proj_block(wv, m, 4, 5, False, "v", tok_swv, c2col_q=9)
            emit_scale_halves(3)
        tok_swq = finish_scale("wq", NH, 0, 1, SM_SCALE, 2)

        t_kn_dma = b.dma("pool", "dK0", kn_in.ap(), s_kn2[0:1, :], deps=t_kn)
        tok_ag = b.coll("AllGather", ALU.bypass,
                        [[0, 1], [2, 3], [4, 5], [6, 7]],
                        [ccg_in.ap().opt()], [ccg_out.ap().opt()],
                        deps=t_stage, ch="cc0")
        tok_knr = b.coll("AllReduce", ALU.max,
                         [[0, 1], [2, 3], [4, 5], [6, 7]],
                         [kn_in.ap().opt()], [kn_out.ap().opt()],
                         deps=[t_kn_dma], ch="cc1")
        t_kn_back = b.dma("pool", "dK0", s_kn2[0:1, :], kn_out.ap(),
                          deps=[tok_knr])

        for m in range(NH):
            proj_block(wq, m, 0, 1, True, "q", tok_swq, c2col_q=10)
            emit_scale_halves(2)
        tok_swo = finish_scale("wo", NH, 6, 7, 1.0, 3)

        tq2m = DVE(lambda h: h.tensor_reduce(out=s_row[0:1, 0:1],
                                             in_=s_qn2[0:1, :], axis=AX.X,
                                             op=ALU.max), [t_pnorm[0]])
        tk2m = DVE(lambda h: h.tensor_reduce(out=s_row[0:1, 1:2],
                                             in_=s_kn2[0:1, :], axis=AX.X,
                                             op=ALU.max), [t_kn_back])
        tl1 = ACT(lambda h: h.activation(out=s_row[0:1, 2:3],
                                         in_=s_row[0:1, 0:1], func=AF.Ln,
                                         bias=r_tiny, scale=1.0),
                  [tq2m, t_tiny])
        tl2 = ACT(lambda h: h.activation(out=s_row[0:1, 3:4],
                                         in_=s_row[0:1, 1:2], func=AF.Ln,
                                         bias=r_tiny, scale=1.0),
                  [tk2m, t_tiny])
        tl3 = DVE(lambda h: h.tensor_tensor(out=s_row[0:1, 4:5],
                                            in0=s_row[0:1, 2:3],
                                            in1=s_row[0:1, 3:4], op=ALU.add),
                  [tl1, tl2])
        tl4 = ACT(lambda h: h.activation(out=s_row[0:1, 5:6],
                                         in_=s_row[0:1, 4:5], func=AF.Exp,
                                         bias=0.0, scale=0.5), [tl3])
        tl5 = DVE(lambda h: h.tensor_scalar(out=s_row[0:1, 6:7],
                                            in0=s_row[0:1, 5:6],
                                            scalar1=SHIFT_CAP, scalar2=-1.0,
                                            op0=ALU.min, op1=ALU.mult), [tl4])
        tlb = PE(lambda h: h.matmul(pb128[:, 2:3], s_onesf[0:1, :],
                                    s_row[0:1, 6:7], start=True, stop=True),
                 [tl5, t_pb_use[0]], inc=True)
        tok_negM = DVE(lambda h: h.tensor_copy(out=s_negM[:],
                                               in_=pb128[:, 2:3]), [tlb])
        t_pb_use[0] = tok_negM

        if DEBUG_DUMPS:
            td_ = b.dma("pool", "dM", dbg_q3.ap(), regC_buf[:],
                        deps=[tok_negM, t_rope_rd[0]])
            b._wait("dve", td_)
        pj.close()
        pscl.close()
        scope2.close()
        strmqi.close()
        strm.close()
        b.barrier(("pe", "act", "dve"))
        sd = ExitStack()
        psc = [sd.enter_context(nc.psum_tensor(f"psc{i}", [128, 1024], F32))
               for i in range(2)]
        pout = [sd.enter_context(nc.psum_tensor(f"pout{i}", [128, 512], F32))
                for i in range(3)]
        s_kst = [sd.enter_context(nc.sbuf_tensor(f"s_kst{i}", [128, 2048],
                                                 BF16)) for i in range(2)]
        s_vstg = [sd.enter_context(nc.sbuf_tensor(f"s_vstg{i}", [128, 2048],
                                                  BF16)) for i in range(2)]
        s_vtm = [sd.enter_context(nc.sbuf_tensor(f"s_vtm{i}", [128, 16 * 129],
                                                 BF16)) for i in range(2)]
        s_atn = [sd.enter_context(nc.sbuf_tensor(f"s_atn{i}", [128, 1024],
                                                 BF16)) for i in range(4)]

        t_vtm_use = [DVE(lambda h: h.memset(s_vtm[0][:], 1.0)),
                     DVE(lambda h: h.memset(s_vtm[1][:], 1.0))]
        t_kst_use = [None, None]
        t_vstg_use = [None, None]
        t_psc_use = [None, None]
        t_atn_use = [None, None, None, None]
        t_pout_use = [None, None, None]
        t_pbf = [None] * 8

        B3 = regA_buf[:].rearrange("p (m t) -> p m t", t=TOK)

        for g in range(NKV):
            par = g % 2
            tk0 = b.dma("pool", "dK%d" % par, s_kst[par][:, 0:1024],
                        ccg_out.ap()[g * 128:(g + 1) * 128, :],
                        deps=[tok_ag, t_kst_use[par]] +
                             ([tok_swo] if g < 2 else []))
            tk1 = b.dma("pool", "dK%d" % par, s_kst[par][:, 1024:2048],
                        ccg_out.ap()[2048 + g * 128:2048 + (g + 1) * 128, :],
                        deps=[tok_ag])
            tv0_ = b.dma("pool", "dK%d" % par, s_vstg[par][:, 0:1024],
                         ccg_out.ap()[1024 + g * 128:1024 + (g + 1) * 128, :],
                         deps=[tok_ag, t_vstg_use[par]])
            tv1_ = b.dma("pool", "dK%d" % par, s_vstg[par][:, 1024:2048],
                         ccg_out.ap()[3072 + g * 128:3072 + (g + 1) * 128, :],
                         deps=[tok_ag])
            t_ld_all = tv1_
            tvl = None
            for kt in range(16):
                bk = kt % 2
                sl = pout[bk][:].bitcast(BF16)[:, 0:128]
                tp = PE(lambda h, par=par, kt=kt, sl=sl: h.transpose(
                    sl, s_vstg[par][:, kt * 128:(kt + 1) * 128], s_identbf[:]),
                    [t_ld_all, t_pout_use[bk], t_vtm_use[par]], inc=True)
                tev = DVE(lambda h, par=par, kt=kt, sl=sl: h.tensor_copy(
                    out=s_vtm[par][:, kt * 129:kt * 129 + 128], in_=sl), [tp])
                t_pout_use[bk] = tev
                tvl = tev
            t_vstg_use[par] = tvl
            tok_vtm = tvl

            for hq in range(4):
                hh = 4 * g + hq
                av_toks = [None] * NT
                exp_toks = [None] * 16

                def emit_av(j, hh=hh, par=par, av_toks=av_toks,
                            exp_toks=exp_toks):
                    ab = j % 4
                    tav = None
                    for qt in range(NT):
                        bk, slot = qt // 3, qt % 3
                        dep = [exp_toks[j], tok_vtm]
                        if j == 0:
                            dep.append(t_pout_use[bk])
                        tav = PE(lambda h, ab=ab, par=par, j=j, bk=bk,
                                 slot=slot, qt=qt: h.matmul(
                            pout[bk][:, slot * 129:slot * 129 + 129],
                            s_atn[ab][:, qt * 128:(qt + 1) * 128],
                            s_vtm[par][:, j * 129:j * 129 + 129],
                            start=(j == 0 and slot == 0), stop=(j == 15),
                            skip_group_check=True),
                            dep, inc=(j == 15 or qt == NT - 1))
                        if j == 15:
                            av_toks[qt] = tav
                    t_atn_use[ab] = tav

                for kt in range(16):
                    pp = kt % 2
                    ab = kt % 4
                    ts0 = PE(lambda h, par=par, kt=kt, hh=hh, pp=pp: h.matmul(
                        psc[pp][:, 0:512],
                        s_kst[par][:, kt * 128:(kt + 1) * 128],
                        Q3[:, hh, 0:512], start=True, stop=True),
                        [t_ld_all, t_psc_use[pp], tok_negM], inc=True)
                    ts1 = PE(lambda h, par=par, kt=kt, hh=hh, pp=pp: h.matmul(
                        psc[pp][:, 512:1024],
                        s_kst[par][:, kt * 128:(kt + 1) * 128],
                        Q3[:, hh, 512:1024], start=True, stop=True),
                        [], inc=True)
                    tex = ACT(lambda h, pp=pp, ab=ab: h.activation(
                        out=s_atn[ab][:], in_=psc[pp][:], func=AF.Exp,
                        bias=s_negM[:], scale=1.0),
                        [ts0, ts1, t_atn_use[ab]])
                    exp_toks[kt] = tex
                    t_psc_use[pp] = tex
                    if kt >= 2:
                        emit_av(kt - 2)
                emit_av(14)
                emit_av(15)
                t_kst_use[par] = av_toks[-1]
                t_vtm_use[par] = av_toks[-1]
                for qt in range(NT):
                    bk, slot = qt // 3, qt % 3
                    tcd_ = DVE(lambda h, bk=bk, slot=slot: h.tensor_copy(
                        out=c_t0,
                        in_=pout[bk][:, slot * 129 + 128:slot * 129 + 129]),
                        [av_toks[qt]])
                    trd = DVE(lambda h: h.reciprocal(out=c_den, in_=c_t0),
                              [tcd_])
                    tnm = DVE(lambda h, bk=bk, slot=slot, hh=hh, qt=qt:
                              h.tensor_scalar(
                        out=B3[:, hh, qt * 128:(qt + 1) * 128],
                        in0=pout[bk][:, slot * 129:slot * 129 + 128],
                        scalar1=c_den, scalar2=None, op0=ALU.mult), [trd])
                    t_pout_use[bk] = tnm

        tok_sdpa = [t for t in t_pout_use if t is not None]
        if DEBUG_DUMPS:
            td_ = b.dma("pool", "dM", dbg_b3.ap(), regA_buf[:],
                        deps=tok_sdpa)
            b._wait("dve", td_)
            b._wait("pool", td_)
        sd.close()
        b.barrier(("pe", "act", "dve"))
        p5 = ExitStack()
        pproj5 = [p5.enter_context(nc.psum_tensor(f"pp5{i}", [128, 512], F32))
                  for i in range(4)]
        ring5 = PbfRing(p5.enter_context(
            nc.psum_tensor("pbf5", [128, 2048], BF16))[:])
        open_stream_bufs(p5)
        s_y1p5 = p5.enter_context(nc.sbuf_tensor("s_y1b", [128, HID], F32))
        s_sh5v = p5.enter_context(nc.sbuf_tensor("s_sh5v", [128, HID], BF16))
        s_qtm5 = s_sh5v[:]
        s_aofm5 = s_sh5v[:, 0:TOK]
        s_at3s = s_sh5v[:, 2048:3072].rearrange("p (t f) -> p t f", f=128)

        Qo3 = regC_buf[:].rearrange("p (c t) -> p c t", t=TOK)

        tok_qo = None
        t_pbf = [None] * 8
        t_prev_q = None
        for qt in range(NT):
            tam = DVE(lambda h, qt=qt: h.tensor_reduce(
                out=c_am, in_=B3[:, :, qt * 128:(qt + 1) * 128], axis=AX.XY,
                op=ALU.max, apply_absolute_value=True), [t_prev_q])
            tmc = DVE(lambda h: h.tensor_scalar(out=c_am, in0=c_am,
                                                scalar1=1e-5, scalar2=None,
                                                op0=ALU.max), [tam])
            trc = DVE(lambda h: h.reciprocal(out=c_s, in_=c_am), [tmc])
            ts_ = DVE(lambda h: h.tensor_scalar(out=c_s, in0=c_s, scalar1=QB,
                                                scalar2=None, op0=ALU.mult),
                      [trc])
            tin = DVE(lambda h: h.tensor_scalar(out=c_inv, in0=c_am,
                                                scalar1=1.0 / QB, scalar2=None,
                                                op0=ALU.mult), [tmc])
            tq1, tq2 = emit_act_quant(
                lambda k, qt=qt: B3[:, 8 * k:8 * (k + 1),
                                    qt * 128:(qt + 1) * 128],
                lambda k: s_qtm5[:, k * 1024:(k + 1) * 1024].rearrange(
                    "p (c f) -> p c f", c=8),
                s_y1p5[:], [ts_, tin, tok_qo])
            t_prev_q = tq2
            for c in range(NCH):
                sl, bk = ring5.next()
                tp = PE(lambda h, c=c, sl=sl: h.transpose(
                    sl, s_qtm5[:, c * 128:(c + 1) * 128], s_identbf[:]),
                    [tq2, ring5.rd[bk]], inc=True)
                tev = DVE(lambda h, c=c, qt=qt, sl=sl: h.tensor_copy(
                    out=Qo3[:, c, qt * 128:(qt + 1) * 128], in_=sl), [tp])
                ring5.rd[bk] = tev
                tok_qo = tev

        t_pp5 = [None, None, None, None]
        t_aofm_use = None
        t_at3s_use = None
        t_at3_dma = []
        for m in range(NH):
            pr = m % 2
            pp = pproj5[2 * pr:2 * pr + 2]
            mm_last = [None, None]
            for hf in range(2):
                xd = [tok_swo] + (tok_sdpa if m == 0 else [])
                par, trdy = ws.next(wo[m, :, hf * 2048:(hf + 1) * 2048],
                                    "quant", scol=6, extra_deps=xd,
                                    scol_c2=(11,))
                tokm = None
                for j in range(16):
                    c = hf * 16 + j
                    for tg in range(2):
                        dep = [trdy, tok_qo, tok_swo]
                        if c == 0:
                            dep.append(t_pp5[2 * pr + tg])
                        tokm = PE(lambda h, wb_=ws.bf(par), j=j, c=c, tg=tg,
                                  pp_=pp[tg]: h.matmul(
                            pp_[:],
                            wb_[:, j * 128:(j + 1) * 128],
                            Qo3[:, c, tg * 512:(tg + 1) * 512],
                            start=(c == 0), stop=(c == 31)),
                            dep, inc=(tg == 1 and (c == 15 or c == 31)))
                mm_last[hf] = tokm
                ws.consumed(par, tokm)
            te0 = ACT(lambda h, pp_=pp[0]: h.activation(
                out=s_aofm5[:, 0:512], in_=pp_[:], func=AF.Copy,
                scale=s_scales[:, 7:8]),
                [mm_last[0], mm_last[1], t_aofm_use])
            te1 = ACT(lambda h, pp_=pp[1]: h.activation(
                out=s_aofm5[:, 512:1024], in_=pp_[:], func=AF.Copy,
                scale=s_scales[:, 7:8]), [mm_last[0], mm_last[1]])
            t_pp5[2 * pr], t_pp5[2 * pr + 1] = te0, te1
            tevl = None
            for qt in range(NT):
                sl, bk = ring5.next()
                tp = PE(lambda h, qt=qt, sl=sl: h.transpose(
                    sl, s_aofm5[:, qt * 128:(qt + 1) * 128], s_identbf[:]),
                    [te0, te1, ring5.rd[bk]], inc=True)
                tev = DVE(lambda h, qt=qt, m=m, sl=sl: h.tensor_copy(
                    out=s_at3s[:, qt, :], in_=sl), [tp, t_at3s_use])
                ring5.rd[bk] = tev
                tevl = tev
            t_aofm_use = tevl
            tdm = b.dma("pool", "dK1",
                        at3d.ap().rearrange("(t p) (m f) -> p t m f", p=128,
                                            f=128)[:, :, m, :],
                        s_at3s,
                        deps=[tevl])
            t_at3s_use = tdm
            t_at3_dma.append(tdm)
        tok_at3 = t_at3_dma[-1]

        p5.close()
        b.barrier(("pe", "act", "dve", "pool"))
        p6 = ExitStack()
        ph1 = p6.enter_context(nc.psum_tensor("ph1", [32, 512], F32))
        pg1 = p6.enter_context(nc.psum_tensor("pg1", [8, 512], F32))
        pA = p6.enter_context(nc.psum_tensor("pA", [128, 512], F32))
        pf32 = p6.enter_context(nc.psum_tensor("pf32", [128, 512], F32))
        pg2 = pf32[:, 384:392]
        pwb = [p6.enter_context(nc.psum_tensor(f"pwb{i}", [128, 512], F32))
               for i in range(2)]
        ring6 = PbfRing(p6.enter_context(
            nc.psum_tensor("pbf6", [128, 2048], BF16))[:])
        p_s16 = pwb[0]

        X3 = regC_buf[:].rearrange("p (c t) -> p c t", t=TOK)

        lora = ExitStack()
        s_gw1bf = lora.enter_context(
            nc.sbuf_tensor("s_gw1bf", [128, NCH, 32], BF16))
        s_abf = lora.enter_context(
            nc.sbuf_tensor("s_abf", [128, NCH, 128], BF16))
        s_bbf = lora.enter_context(nc.sbuf_tensor("s_bbf", [128, HID], BF16))
        s_x6v = [regA_buf[:, 0:8192].bitcast(F32),
               regA_buf[:, 8192:16384].bitcast(F32)]
        s_y16v = regA_buf[:, 16384:24576].bitcast(F32)
        s_z6v = regA_buf[:, 24576:32768].bitcast(F32)
        s_qtm6v = p6.enter_context(nc.sbuf_tensor("s_qtm6", [128, HID], BF16))
        s_at3l = [p6.enter_context(
            nc.sbuf_tensor(f"s_at3l{i}", [128, HID], BF16)) for i in range(2)]
        s_hfm = p6.enter_context(nc.sbuf_tensor("s_hfm", [32, 128], F32))
        s_ht = p6.enter_context(nc.sbuf_tensor("s_ht", [128, 32], F32))
        s_hsq = p6.enter_context(nc.sbuf_tensor("s_hsq", [128, 32], F32))
        s_ht2 = p6.enter_context(nc.sbuf_tensor("s_ht2", [128, 32], F32))
        s_h2fm = p6.enter_context(nc.sbuf_tensor("s_h2fm", [32, 128], BF16))
        s_gfm = p6.enter_context(nc.sbuf_tensor("s_gfm", [8, 128], F32))
        s_gt = p6.enter_context(nc.sbuf_tensor("s_gt", [128, 8], F32))
        s_eq1 = p6.enter_context(nc.sbuf_tensor("s_eq1", [128, 8], F32))
        s_eq2 = p6.enter_context(nc.sbuf_tensor("s_eq2", [128, 8], F32))
        s_msk = p6.enter_context(nc.sbuf_tensor("s_msk", [128, 8], F32))
        s_wt = p6.enter_context(nc.sbuf_tensor("s_wt", [128, 8], F32))
        s_wt2 = p6.enter_context(nc.sbuf_tensor("s_wt2", [128, 8], F32))
        s_wfm = p6.enter_context(nc.sbuf_tensor("s_wfm", [8, TOK], BF16))
        s_wb = p6.enter_context(nc.sbuf_tensor("s_wb", [128, TOK], F32))
        s_tmp = p6.enter_context(nc.sbuf_tensor("s_tmp", [128, TOK], F32))

        t_gw = []
        for c in range(NCH):
            tg_ = b.dma("pool", "dM", s_cstg[:, 0:32],
                        gw1t[c * 128:(c + 1) * 128, :],
                        deps=[t_gw[-1][1] if t_gw else tok_at3])
            tcst = DVE(lambda h, c=c: h.tensor_copy(out=s_gw1bf[:, c, :],
                                                    in_=s_cstg[:, 0:32]),
                       [tg_])
            t_gw.append((tg_, tcst))
        t_g2 = b.dma("pool", "dM", s_hfm[0:32, 0:8], gw2t)
        t_g2c = DVE(lambda h: h.tensor_copy(out=s_gw2bf[:],
                                            in_=s_hfm[0:32, 0:8]), [t_g2])
        t_ab = []
        for c in range(NCH):
            ta_ = b.dma("pool", "dM", s_cstg[:],
                        a_all[c * 128:(c + 1) * 128, :],
                        deps=[t_ab[-1][1] if t_ab else t_gw[-1][1]])
            tac = DVE(lambda h, c=c: h.tensor_copy(out=s_abf[:, c, :],
                                                   in_=s_cstg[:]), [ta_])
            t_ab.append((ta_, tac))
        t_bl = b.dma("pool", "dM", s_x6v[0], b_all)
        t_blc = DVE(lambda h: h.tensor_copy(out=s_bbf[:], in_=s_x6v[0]),
                    [t_bl])
        t_x_use = [t_blc, None]

        tok_gstash = []
        t_ph1 = None
        t_pg1 = None
        t_pg2 = None
        t_pA = None
        t_pf32 = None
        t_cnt_prev = t_cnt0
        t_pbf = [None] * 8
        tok_x3 = None
        t_tmp_use = None
        t_spill = []

        prev_tx1 = None
        prev_tsp = None
        prev_tx3 = None
        t_a3_use = [None, None]
        for t in range(NT):
            par = t % 2
            xb = s_x6v[par]
            a3 = s_at3l[par]
            tl = b.dma("pool", "dX%d" % par, xb,
                       x_own[t * 128:(t + 1) * 128, :], deps=[t_x_use[par]])
            tla = b.dma("pool", "dK%d" % par, a3[:],
                        at3d.ap()[t * 128:(t + 1) * 128, :],
                        deps=[t_a3_use[par], tok_at3])
            r3, b3, tcp = ln_stats(a3[:], HID, [tla, prev_tx1, tok_x3],
                                   copy_dst=s_y16v, sq_dst=s_qtm6v[:])
            t_a3_use[par] = r3
            tx1 = DVE(lambda h, xb=xb: h.scalar_tensor_tensor(
                out=s_z6v, in0=s_y16v, scalar=c_r, in1=xb,
                op0=ALU.mult, op1=ALU.add),
                [r3, b3, tcp, tl, prev_tsp, prev_tx3])
            tx2 = DVE(lambda h: h.tensor_scalar(out=s_z6v, in0=s_z6v,
                                                scalar1=c_b, scalar2=None,
                                                op0=ALU.add), [tx1])
            t_x_use[par] = tx1
            prev_tx1 = tx1
            tsp = b.dma("pool", "dM", x2d.ap()[t * 128:(t + 1) * 128, :],
                        s_z6v, deps=[tx2])
            t_spill.append(tsp)
            prev_tsp = tsp
            r4, b4, _ = ln_stats(s_z6v, HID, [tx2],
                                 copy_dst=s_y16v, sq_dst=s_y16v)
            tx3 = ACT(lambda h: h.activation(out=s_qtm6v[:], in_=s_z6v,
                                             func=AF.Identity, bias=c_b,
                                             scale=c_r), [r4, b4, tok_x3])
            prev_tx3 = tx3
            for c in range(NCH):
                sl, bk = ring6.next()
                tp = PE(lambda h, c=c, sl=sl: h.transpose(
                    sl, s_qtm6v[:, c * 128:(c + 1) * 128], s_identbf[:]),
                    [tx3, ring6.rd[bk]], inc=True)
                tev = DVE(lambda h, c=c, t=t, sl=sl: h.tensor_copy(
                    out=X3[:, c, t * 128:(t + 1) * 128], in_=sl), [tp])
                ring6.rd[bk] = tev
                tok_x3 = tev

            tokm = None
            for c in range(NCH):
                dep = [t_gw[c][1]]
                if c == 0:
                    dep += [tok_x3, t_ph1]
                tokm = PE(lambda h, c=c, t=t: h.matmul(
                    ph1[:, 0:128], s_gw1bf[:, c, :],
                    X3[:, c, t * 128:(t + 1) * 128],
                    start=(c == 0), stop=(c == NCH - 1)),
                    dep, inc=(c == NCH - 1))
            th0 = ACT(lambda h: h.activation(out=s_hfm[:], in_=ph1[:, 0:128],
                                             func=AF.Identity, bias=s_gb1[:],
                                             scale=1.0), [tokm, t_ph1])
            t_ph1 = th0
            tht = PE(lambda h: h.transpose(pf32[0:128, 0:32], s_hfm[:],
                                           s_identf[0:32, 0:32]),
                     [th0, t_pf32, t_pg2], inc=True)
            the = DVE(lambda h: h.tensor_copy(out=s_ht[:],
                                              in_=pf32[0:128, 0:32]), [tht])
            t_pf32 = the
            ts1 = DVE(lambda h: h.tensor_reduce(out=c_t0, in_=s_ht[:],
                                                axis=AX.X, op=ALU.add), [the])
            ts2 = DVE(lambda h: h.tensor_tensor(out=s_hsq[:], in0=s_ht[:],
                                                in1=s_ht[:], op=ALU.mult),
                      [the])
            ts3 = DVE(lambda h: h.tensor_reduce(out=c_t1, in_=s_hsq[:],
                                                axis=AX.X, op=ALU.add), [ts2])
            ts4 = DVE(lambda h: h.tensor_scalar(out=c_t0, in0=c_t0,
                                                scalar1=1.0 / 32, scalar2=None,
                                                op0=ALU.mult), [ts1])
            ts5 = DVE(lambda h: h.tensor_tensor(out=c_sq, in0=c_t0, in1=c_t0,
                                                op=ALU.mult), [ts4])
            ts6 = DVE(lambda h: h.scalar_tensor_tensor(out=c_t1, in0=c_t1,
                                                       scalar=1.0 / 32,
                                                       in1=c_sq, op0=ALU.mult,
                                                       op1=ALU.subtract),
                      [ts3, ts5])
            ts7 = ACT(lambda h: h.activation(out=c_t1, in_=c_t1, func=AF.Ln,
                                             bias=c_eps, scale=1.0),
                      [ts6, t_eps])
            ts8 = ACT(lambda h: h.activation(out=c_r, in_=c_t1, func=AF.Exp,
                                             bias=0.0, scale=-0.5), [ts7])
            ts9 = DVE(lambda h: h.tensor_tensor(out=c_b, in0=c_t0, in1=c_r,
                                                op=ALU.mult), [ts8, ts4])
            tsa = DVE(lambda h: h.tensor_scalar(out=c_b, in0=c_b, scalar1=-1.0,
                                                scalar2=None, op0=ALU.mult),
                      [ts9])
            tsb = ACT(lambda h: h.activation(out=s_ht2[:], in_=s_ht[:],
                                             func=AF.Identity, bias=c_b,
                                             scale=c_r), [ts8, tsa, the])
            tsc_ = DVE(lambda h: h.tensor_tensor(out=s_ht2[:], in0=s_ht2[:],
                                                 in1=s_g32[:], op=ALU.mult),
                       [tsb, t_const])
            tsd = DVE(lambda h: h.tensor_tensor(out=s_ht2[:], in0=s_ht2[:],
                                                in1=s_bt32[:], op=ALU.add),
                      [tsc_])
            tse = DVE(lambda h: h.tensor_relu(out=s_ht2[:], in_=s_ht2[:]),
                      [tsd])
            tsf = PE(lambda h: h.transpose(pf32[0:32, 0:128], s_ht2[:],
                                           s_identf[:]),
                     [tse, t_pf32], inc=True)
            tsg = DVE(lambda h: h.tensor_copy(out=s_h2fm[:],
                                              in_=pf32[0:32, 0:128]), [tsf])
            t_pf32 = tsg
            tsh = PE(lambda h: h.matmul(pg1[:, 0:128], s_gw2bf[:], s_h2fm[:],
                                        start=True, stop=True),
                     [tsg, t_g2c, t_pg1], inc=True)
            tsi = ACT(lambda h: h.activation(out=s_gfm[:], in_=pg1[:, 0:128],
                                             func=AF.Identity, bias=s_gb2[:],
                                             scale=1.0), [tsh])
            t_pg1 = tsi
            tsj = PE(lambda h: h.transpose(pg2, s_gfm[:],
                                           s_identf[0:8, 0:8]),
                     [tsi, t_pg2], inc=True)
            tsk = DVE(lambda h: h.tensor_copy(out=s_gt[:], in_=pg2), [tsj])
            t_pg2 = tsk
            tt1 = DVE(lambda h: h.tensor_reduce(out=c_m1, in_=s_gt[:],
                                                axis=AX.X, op=ALU.max), [tsk])
            tt2 = DVE(lambda h: h.tensor_scalar(out=s_eq1[:], in0=s_gt[:],
                                                scalar1=c_m1, scalar2=None,
                                                op0=ALU.is_equal), [tt1])
            tt3 = DVE(lambda h: h.scalar_tensor_tensor(out=s_msk[:],
                                                       in0=s_eq1[:],
                                                       scalar=-1e30,
                                                       in1=s_gt[:],
                                                       op0=ALU.mult,
                                                       op1=ALU.add), [tt2])
            tt4 = DVE(lambda h: h.tensor_reduce(out=c_m2, in_=s_msk[:],
                                                axis=AX.X, op=ALU.max), [tt3])
            tt5 = DVE(lambda h: h.tensor_scalar(out=s_eq2[:], in0=s_msk[:],
                                                scalar1=c_m2, scalar2=None,
                                                op0=ALU.is_equal), [tt4])
            tt6 = DVE(lambda h: h.tensor_tensor(out=c_d, in0=c_m2, in1=c_m1,
                                                op=ALU.subtract), [tt4, tt1])
            tt7 = ACT(lambda h: h.activation(out=c_e2, in_=c_d, func=AF.Exp,
                                             bias=0.0, scale=1.0), [tt6])
            tt8 = DVE(lambda h: h.tensor_scalar(out=c_rg, in0=c_e2,
                                                scalar1=1.0, scalar2=None,
                                                op0=ALU.add), [tt7])
            tt9 = DVE(lambda h: h.reciprocal(out=c_rg, in_=c_rg), [tt8])
            tta = DVE(lambda h: h.tensor_scalar(out=c_g1, in0=c_rg,
                                                scalar1=ALPHA, scalar2=None,
                                                op0=ALU.mult), [tt9])
            ttb = DVE(lambda h: h.tensor_tensor(out=c_g2, in0=c_e2, in1=c_rg,
                                                op=ALU.mult), [tt7, tt9])
            ttc = DVE(lambda h: h.tensor_scalar(out=c_g2, in0=c_g2,
                                                scalar1=ALPHA, scalar2=None,
                                                op0=ALU.mult), [ttb])
            ttd = DVE(lambda h, t=t: h.tensor_scalar(out=s_c1[:, t, :],
                                                     in0=s_eq1[:],
                                                     scalar1=c_g1,
                                                     scalar2=None,
                                                     op0=ALU.mult),
                      [tta, tt2])
            tte = DVE(lambda h, t=t: h.tensor_scalar(out=s_c2[:, t, :],
                                                     in0=s_eq2[:],
                                                     scalar1=c_g2,
                                                     scalar2=None,
                                                     op0=ALU.mult),
                      [ttc, tt5])
            ttf = DVE(lambda h: h.tensor_tensor(out=s_cnt[:, 0:8],
                                                in0=s_cnt[:, 0:8],
                                                in1=s_eq1[:], op=ALU.add),
                      [tt2, t_cnt_prev])
            ttg = DVE(lambda h: h.tensor_tensor(out=s_cnt[:, 8:16],
                                                in0=s_cnt[:, 8:16],
                                                in1=s_eq2[:], op=ALU.add),
                      [tt5, ttf])
            t_cnt_prev = ttg
            tok_gstash.append((ttd, tte))

            tokA = None
            for c in range(NCH):
                dep = [t_ab[c][1]]
                if c == 0:
                    dep += [tok_x3, t_pA]
                tokA = PE(lambda h, c=c, t=t: h.matmul(
                    pA[:, 0:128], s_abf[:, c, :], X3[:, c, t * 128:(t + 1) * 128],
                    start=(c == 0), stop=(c == NCH - 1)),
                    dep, inc=(c == NCH - 1))
            tAe = DVE(lambda h, t=t: h.tensor_copy(
                out=s_tmp[:, t * 128:(t + 1) * 128], in_=pA[:, 0:128]), [tokA])
            t_pA = tAe
            t_tmp_use = tAe

        tcs = PE(lambda h: h.matmul(p_s16[0:1, 0:16], s_onesf[:, 0:1],
                                    s_cnt[:], start=True, stop=True),
                 [t_cnt_prev], inc=True)
        tce = DVE(lambda h: h.tensor_copy(out=s_cntrow[:],
                                          in_=p_s16[0:1, 0:16]), [tcs])
        tcd = b.dma("pool", "dM", cnt_in.ap(), s_cntrow[:], deps=[tce])
        tok_cr = b.coll("AllReduce", ALU.add, [[0, 1, 2, 3, 4, 5, 6, 7]],
                        [cnt_in.ap().opt()], [cnt_out.ap().opt()],
                        deps=[tcd], ch="cc2")
        tcb = b.dma("pool", "dM", s_cntrow[:], cnt_out.ap(), deps=[tok_cr])
        tal = DVE(lambda h: h.tensor_scalar(out=s_cntrow[:], in0=s_cntrow[:],
                                            scalar1=CAPACITY, scalar2=None,
                                            op0=ALU.is_le), [tcb])
        tab = PE(lambda h: h.matmul(pA[:, 0:16], s_onesf[0:1, :],
                                    s_cntrow[:], start=True, stop=True),
                 [tal, t_pA], inc=True)
        tae = DVE(lambda h: h.tensor_copy(out=s_alb[:], in_=pA[:, 0:16]),
                  [tab])
        t_wfm = None
        for t in range(NT):
            tw1 = DVE(lambda h, t=t: h.tensor_tensor(
                out=s_wt[:], in0=s_c1[:, t, :], in1=s_alb[:, 0:8],
                op=ALU.mult), [tae, tok_gstash[t][0]])
            tw2 = DVE(lambda h, t=t: h.tensor_tensor(
                out=s_wt2[:], in0=s_c2[:, t, :], in1=s_alb[:, 8:16],
                op=ALU.mult), [tae, tok_gstash[t][1]])
            tw3 = DVE(lambda h: h.tensor_tensor(out=s_wt[:], in0=s_wt[:],
                                                in1=s_wt2[:], op=ALU.add),
                      [tw1, tw2])
            tw4 = PE(lambda h: h.transpose(pf32[0:8, 0:128], s_wt[:],
                                           s_identf[:]),
                     [tw3, t_pf32, t_pg2], inc=True)
            tw5 = DVE(lambda h, t=t: h.tensor_copy(
                out=s_wfm[:, t * 128:(t + 1) * 128],
                in_=pf32[0:8, 0:128]), [tw4])
            t_pf32 = tw5
            t_wfm = tw5
        twb0 = PE(lambda h: h.matmul(pwb[0][:], s_selS[:], s_wfm[:, 0:512],
                                     start=True, stop=True), [t_wfm],
                  inc=True)
        twb1 = PE(lambda h: h.matmul(pwb[1][:], s_selS[:], s_wfm[:, 512:1024],
                                     start=True, stop=True), [t_wfm],
                  inc=True)
        twe0 = DVE(lambda h: h.tensor_copy(out=s_wb[:, 0:512], in_=pwb[0][:]),
                   [twb0])
        twe1 = DVE(lambda h: h.tensor_copy(out=s_wb[:, 512:1024],
                                           in_=pwb[1][:]), [twb1])
        tok_tsc = DVE(lambda h: h.tensor_tensor(out=s_tsc[:], in0=s_tmp[:],
                                                in1=s_wb[:], op=ALU.mult),
                      [twe0, twe1, t_tmp_use])

        p6.close()
        b.barrier(("pe", "act", "dve"))
        p7 = ExitStack()
        pproj7 = [p7.enter_context(nc.psum_tensor(f"pp7{i}", [128, 512], F32))
                  for i in range(4)]
        ring7 = PbfRing(p7.enter_context(
            nc.psum_tensor("pbf7", [128, 2048], BF16))[:])
        open_stream_bufs(p7)
        s_aofm7v = p7.enter_context(nc.sbuf_tensor("s_aofm7", [128, TOK], BF16))

        F3 = regA_buf[:].rearrange("p (t f) -> p t f", f=HID)

        t_pp7 = [None, None, None, None]
        t_aofm_use = None
        t_pbf = [None] * 8
        tok_f3 = None
        for m in range(NCH):
            pr = m % 2
            pp = pproj7[2 * pr:2 * pr + 2]
            for hf in range(2):
                xd = [tok_tsc, t_spill[-1]] if m == 0 else []
                par, trdy = ws.next(wm[m, :, hf * 2048:(hf + 1) * 2048],
                                    "cast", extra_deps=xd)
                tokm = None
                for j in range(16):
                    c = hf * 16 + j
                    for tg in range(2):
                        dep = [trdy, tok_x3]
                        if c == 0:
                            dep.append(t_pp7[2 * pr + tg])
                        tokm = PE(lambda h, wb_=ws.bf(par), j=j, c=c, tg=tg,
                                  pp_=pp[tg]: h.matmul(
                            pp_[:],
                            wb_[:, j * 128:(j + 1) * 128],
                            X3[:, c, tg * 512:(tg + 1) * 512],
                            start=(c == 0), stop=False),
                            dep, inc=(j == 15 and tg == 1))
                ws.consumed(par, tokm)
            tlb_ = [None, None]
            for tg in range(2):
                tlb_[tg] = PE(lambda h, m=m, tg=tg, pp_=pp[tg]: h.matmul(
                    pp_[:], s_bbf[:, m * 128:(m + 1) * 128],
                    s_tsc[:, tg * 512:(tg + 1) * 512],
                    start=False, stop=True),
                    [tok_tsc, t_blc], inc=True)
            te0 = ACT(lambda h, pp_=pp[0]: h.activation(
                out=s_aofm7v[:, 0:512], in_=pp_[:], func=AF.Copy),
                [tlb_[0], t_aofm_use])
            te1 = ACT(lambda h, pp_=pp[1]: h.activation(
                out=s_aofm7v[:, 512:1024], in_=pp_[:], func=AF.Copy),
                [tlb_[1]])
            t_pp7[2 * pr], t_pp7[2 * pr + 1] = te0, te1
            tevl = None
            for qt in range(NT):
                sl, bk = ring7.next()
                tp = PE(lambda h, qt=qt, sl=sl: h.transpose(
                    sl, s_aofm7v[:, qt * 128:(qt + 1) * 128], s_identbf[:]),
                    [te0, te1, ring7.rd[bk]], inc=True)
                tev = DVE(lambda h, qt=qt, m=m, sl=sl: h.tensor_copy(
                    out=F3[:, qt, m * 128:(m + 1) * 128], in_=sl),
                    [tp, t_spill[-1]])
                ring7.rd[bk] = tev
                tevl = tev
            t_aofm_use = tevl
        tok_f3 = t_aofm_use

        p7.close()
        lora.close()
        b.barrier(("pe", "act", "dve", "pool"))
        p8 = ExitStack()
        s_x8v = [p8.enter_context(nc.sbuf_tensor(f"s_x8{i}", [128, HID], F32))
               for i in range(2)]
        s_y18v = p8.enter_context(nc.sbuf_tensor("s_y18", [128, HID], F32))
        s_z8v = p8.enter_context(nc.sbuf_tensor("s_z8", [128, HID], F32))
        s_jnk8 = p8.enter_context(nc.sbuf_tensor("s_jnk8", [128, HID], BF16))
        t_x_use = [None, None]
        t_store = []
        tok_dM_all = ("dM", b.cnt["dM"])
        prev_tx1 = None
        prev_tst = None
        for t in range(NT):
            par = t % 2
            xb = s_x8v[par]
            tl = b.dma("pool", "dX%d" % par, xb[:],
                       x2d.ap()[t * 128:(t + 1) * 128, :],
                       deps=[t_x_use[par], tok_dM_all])
            r5, b5, tcp = ln_stats(F3[:, t, :], HID, [tok_f3, prev_tx1],
                                   copy_dst=s_y18v[:], sq_dst=s_jnk8[:])
            tx1 = DVE(lambda h, xb=xb: h.scalar_tensor_tensor(
                out=s_z8v[:], in0=s_y18v[:], scalar=c_r, in1=xb[:],
                op0=ALU.mult, op1=ALU.add), [r5, b5, tcp, tl, prev_tst])
            tx2 = DVE(lambda h: h.tensor_scalar(out=s_z8v[:], in0=s_z8v[:],
                                                scalar1=c_b, scalar2=None,
                                                op0=ALU.add), [tx1])
            t_x_use[par] = tx1
            prev_tx1 = tx1
            tst = b.dma("pool", "dS", y_own[t * 128:(t + 1) * 128, :],
                        s_z8v[:], deps=[tx2])
            t_store.append(tst)
            prev_tst = tst
        if DEBUG_DUMPS:
            b.dma("pool", "dM", dbg_ccg.ap(), ccg_in.ap(), deps=t_stage)
            b.dma("pool", "dM", dbg_at3.ap(), at3d.ap(), deps=[tok_at3])
            b.dma("pool", "dM", dbg_x2.ap(), x2d.ap(), deps=[t_spill[-1]])
            tsc_ = b.dma("pool", "dM", dbg_sc.ap()[:, 0:12], s_scales[:],
                         deps=[t_pb_use[0]])
            b.dma("pool", "dM", dbg_sc.ap()[0:1, 12:12 + NH], s_qn2[0:1, :],
                  deps=[t_pnorm[0]])
            b.dma("pool", "dM", dbg_sc.ap()[0:1, 12 + NH:12 + NH + 1],
                  s_negM[0:1, :], deps=[tok_negM])
            t_store.append(b.dma("pool", "dS", dbg_sc.ap()[1:2, 12:20],
                                 s_kn2[0:1, :], deps=[t_kn_back]))
        b.op("sp", lambda h: h.nop(), deps=[t_store[-1]], inc=False)

        with nc.Block() as block:
            @block.tensor
            def _(h):
                for f in b.stream["pe"]:
                    f(h)

            @block.scalar
            def _(h):
                for f in b.stream["act"]:
                    f(h)

            @block.vector
            def _(h):
                for f in b.stream["dve"]:
                    f(h)

            @block.gpsimd
            def _(h):
                for f in b.stream["pool"]:
                    f(h)

            @block.sync
            def _(h):
                for f in b.stream["sp"]:
                    f(h)

        p8.close()

    return nc


def _prep_inputs(inputs):
    f32 = np.float32
    x = np.ascontiguousarray(np.asarray(inputs["L0_attn_hidden_states"], f32))
    Wq = np.asarray(inputs["L0_attn_Wq"], f32)
    Wk = np.asarray(inputs["L0_attn_Wk"], f32)
    Wv = np.asarray(inputs["L0_attn_Wv"], f32)
    Wo = np.asarray(inputs["L0_attn_Wo"], f32)
    Wm = np.asarray(inputs["L0_ffn_weight"], f32)
    lA = np.asarray(inputs["L0_ffn_lora_A"], f32)
    lB = np.asarray(inputs["L0_ffn_lora_B"], f32)
    gw1 = np.asarray(inputs["L0_ffn_gw1"], f32)
    gb1 = np.asarray(inputs["L0_ffn_gb1"], f32)
    lng = np.asarray(inputs["L0_ffn_ln_gamma"], f32)
    lnbta = np.asarray(inputs["L0_ffn_ln_beta"], f32)
    gw2 = np.asarray(inputs["L0_ffn_gw2"], f32)
    gb2 = np.asarray(inputs["L0_ffn_gb2"], f32)
    ln_w = np.asarray(inputs["L0_attn_ln_w"], f32)
    ln_b = np.asarray(inputs["L0_attn_ln_b"], f32)

    def blk(W):
        nm = W.shape[0] // 128
        return np.ascontiguousarray(
            W.reshape(nm, 128, NCH, 128).transpose(0, 3, 2, 1)
        ).reshape(nm, 128, HID)

    shared = {
        "wq": blk(Wq), "wk": blk(Wk), "wv": blk(Wv), "wo": blk(Wo),
        "wm": blk(Wm),
        "a_all": np.ascontiguousarray(lA.transpose(1, 0, 2).reshape(HID, 128)),
        "b_all": np.ascontiguousarray(lB.reshape(128, HID)),
        "gw1t": np.ascontiguousarray(gw1.T),
        "gb1c": np.ascontiguousarray(gb1.reshape(32, 1)),
        "g32": np.ascontiguousarray(np.broadcast_to(lng, (128, 32))),
        "bt32": np.ascontiguousarray(np.broadcast_to(lnbta, (128, 32))),
        "gw2t": np.ascontiguousarray(gw2.T),
        "gb2c": np.ascontiguousarray(gb2.reshape(8, 1)),
        "lnw": np.ascontiguousarray(np.broadcast_to(ln_w, (128, HID))),
        "lnb": np.ascontiguousarray(np.broadcast_to(ln_b, (128, HID))),
        "identbf": np.eye(128, dtype=ml_dtypes.bfloat16),
        "identf": np.eye(128, dtype=f32),
        "onesf": np.ones((128, 128), dtype=f32),
        "onesbf": np.ones((128, 1), dtype=ml_dtypes.bfloat16),
    }
    S = np.zeros((8, 128), dtype=ml_dtypes.bfloat16)
    for p in range(128):
        S[p // 16, p] = 1.0
    shared["selS"] = S

    inv = (1.0 / (np.float32(10000.0) **
                  (np.arange(0, 128, 2, dtype=f32) / f32(128.0)))).astype(f32)
    maps = []
    for core in range(8):
        bb, hh = core // 2, core % 2
        pos = np.arange(hh * TOK, (hh + 1) * TOK, dtype=f32)
        freqs = pos[None, :] * inv[:, None]
        emb = np.concatenate([freqs, freqs], axis=0)
        m = dict(shared)
        m["cosb"] = np.cos(emb).astype(f32)
        m["sinb"] = np.sin(emb).astype(f32)
        m["x_own"] = np.ascontiguousarray(x[bb, hh * TOK:(hh + 1) * TOK, :])
        maps.append(m)
    return maps


def kernel(**inputs):
    maps = _prep_inputs(inputs)
    nc = build_nc()
    res = run_bass_kernel_spmd(nc, maps, core_ids=list(range(8)))
    out = np.empty((BATCH, SEQ, HID), dtype=np.float32)
    for core in range(8):
        bb, hh = core // 2, core % 2
        out[bb, hh * TOK:(hh + 1) * TOK, :] = np.asarray(res.results[core]["y"])
    return out


# version 09
# speedup vs baseline: 1.2217x; 1.2217x over previous
import numpy as np
import ml_dtypes
from contextlib import ExitStack

import concourse.bass as bass
import concourse.mybir as mybir
from concourse.bass_utils import run_bass_kernel_spmd

F32 = mybir.dt.float32
BF16 = mybir.dt.bfloat16
I32 = mybir.dt.int32
AF = mybir.ActivationFunctionType
ALU = mybir.AluOpType
AX = mybir.AxisListType

HID = 4096
NH = 32
NKV = 8
BATCH = 4
SEQ = 2048
TOK = 1024
NT = 8
NCH = 32
EPS = 1e-5
QB = 127.0
CAPACITY = 3072.0
ALPHA = 1.0 / 16.0
SM_SCALE = 0.08838834764831845
SHIFT_CAP = 80.0

DEBUG_DUMPS = False
ENGS = ("pe", "act", "dve", "pool", "sp")
DMA_CH = ("dW0", "dW1", "dX0", "dX1", "dK0", "dK1", "dM", "dS", "cc0", "cc1", "cc2")


class Bld:
    def __init__(self, nc, ex):
        self.nc = nc
        self.sem = {ch: ex.enter_context(nc.semaphore("sem_" + ch))
                    for ch in ENGS + DMA_CH}
        self.cnt = {ch: 0 for ch in ENGS + DMA_CH}
        self.waited = {e: {} for e in ENGS}
        self.stream = {e: [] for e in ENGS}
        self.last_inc = {e: True for e in ENGS}

    def _wait(self, eng, tok):
        ch, v = tok
        if v <= 0 or self.waited[eng].get(ch, 0) >= v:
            return
        self.waited[eng][ch] = v
        sem = self.sem[ch]
        self.stream[eng].append(lambda h, sem=sem, v=v: h.wait_ge(sem, v))

    def op(self, eng, fn, deps=(), inc=True):
        for t in deps:
            if t is not None:
                self._wait(eng, t)
        tok = None
        if inc:
            self.cnt[eng] += 1
            tok = (eng, self.cnt[eng])
            sem = self.sem[eng]
            self.stream[eng].append(lambda h, fn=fn, sem=sem: fn(h).then_inc(sem, 1))
        else:
            self.stream[eng].append(lambda h, fn=fn: fn(h))
        self.last_inc[eng] = bool(inc)
        return tok

    def dma(self, eng, ch, out, in_, deps=()):
        for t in deps:
            if t is not None:
                self._wait(eng, t)
        self.cnt[ch] += 16
        tok = (ch, self.cnt[ch])
        sem = self.sem[ch]
        self.stream[eng].append(
            lambda h, out=out, in_=in_, sem=sem:
                h.dma_start(out=out, in_=in_).then_inc(sem, 16))
        self.last_inc[eng] = False
        return tok

    def coll(self, kind, alu, groups, ins, outs, deps=(), ch="cc0"):
        for t in deps:
            if t is not None:
                self._wait("pool", t)
        self.cnt[ch] += 1
        tok = (ch, self.cnt[ch])
        sem = self.sem[ch]
        self.stream["pool"].append(
            lambda h, sem=sem: h.collective_compute(
                kind, alu, replica_groups=groups, ins=ins, outs=outs
            ).then_inc(sem, 1))
        self.last_inc["pool"] = False
        return tok

    def barrier(self, engines=("pe", "act", "dve", "pool")):
        for e in engines:
            if not self.last_inc[e] and self.stream[e]:
                self.cnt[e] += 1
                sem = self.sem[e]
                self.stream[e].append(
                    lambda h, sem=sem: h.nop().then_inc(sem, 1))
                self.last_inc[e] = True
        toks = [(e, self.cnt[e]) for e in engines]
        for e in engines:
            for t in toks:
                if t[0] != e:
                    self._wait(e, t)


def build_nc():
    nc = bass.Bass("TRN2", target_bir_lowering=False,
                   detect_race_conditions=False)

    def din(name, shape, dt=F32):
        return nc.dram_tensor(name, shape, dt, kind="ExternalInput").ap()

    x_own = din("x_own", [TOK, HID])
    wq = din("wq", [NH, 128, HID])
    wk = din("wk", [NKV, 128, HID])
    wv = din("wv", [NKV, 128, HID])
    wo = din("wo", [NH, 128, HID])
    wm = din("wm", [NCH, 128, HID])
    a_all = din("a_all", [HID, 128])
    b_all = din("b_all", [128, HID])
    gw1t = din("gw1t", [HID, 32])
    gb1c = din("gb1c", [32, 1])
    g32 = din("g32", [128, 32])
    bt32 = din("bt32", [128, 32])
    gw2t = din("gw2t", [32, 8])
    gb2c = din("gb2c", [8, 1])
    lnw = din("lnw", [128, HID])
    lnb = din("lnb", [128, HID])
    cosb = din("cosb", [128, TOK])
    sinb = din("sinb", [128, TOK])
    identbf = din("identbf", [128, 128], BF16)
    identf = din("identf", [128, 128])
    onesf = din("onesf", [128, 128])
    onesbf = din("onesbf", [128, 1], BF16)
    selS = din("selS", [8, 128], BF16)

    y_own = nc.dram_tensor("y", [TOK, HID], F32, kind="ExternalOutput").ap()

    x2d = nc.dram_tensor("x2d", [TOK, HID], F32)
    at3d = nc.dram_tensor("at3d", [TOK, HID], BF16)
    ccg_in = nc.dram_tensor("ccg_in", [2 * NKV * 128, TOK], BF16)
    ccg_out = nc.dram_tensor("ccg_out", [2 * 2 * NKV * 128, TOK], BF16)
    cnt_in = nc.dram_tensor("cnt_in", [1, 16], F32)
    if DEBUG_DUMPS:
        dbg_qn = nc.dram_tensor("dbg_qn", [128, NCH * TOK], BF16,
                                kind="ExternalOutput")
        dbg_q3 = nc.dram_tensor("dbg_q3", [128, NCH * TOK], BF16,
                                kind="ExternalOutput")
        dbg_b3 = nc.dram_tensor("dbg_b3", [128, NCH * TOK], BF16,
                                kind="ExternalOutput")
        dbg_ccg = nc.dram_tensor("dbg_ccg", [2 * NKV * 128, TOK], BF16,
                                 kind="ExternalOutput")
        dbg_at3 = nc.dram_tensor("dbg_at3", [TOK, HID], BF16,
                                 kind="ExternalOutput")
        dbg_x2 = nc.dram_tensor("dbg_x2", [TOK, HID], F32,
                                kind="ExternalOutput")
        dbg_sc = nc.dram_tensor("dbg_sc", [128, 64], F32,
                                kind="ExternalOutput")
    cnt_out = nc.dram_tensor("cnt_out", [1, 16], F32)
    kn_in = nc.dram_tensor("kn_in", [1, NKV], F32)
    kn_out = nc.dram_tensor("kn_out", [1, NKV], F32)

    ex = ExitStack()
    with ex:
        b = Bld(nc, ex)
        E = ex.enter_context

        def ACT(fn, deps=(), inc=True):
            return b.op("act", fn, deps, inc)

        def DVE(fn, deps=(), inc=True):
            return b.op("dve", fn, deps, inc)

        def PE(fn, deps=(), inc=False):
            return b.op("pe", fn, deps, inc)

        s_identbf = E(nc.sbuf_tensor("s_identbf", [128, 128], BF16))
        s_identf = E(nc.sbuf_tensor("s_identf", [128, 128], F32))
        s_onesf = E(nc.sbuf_tensor("s_onesf", [128, 128], F32))
        s_onesbf = E(nc.sbuf_tensor("s_onesbf", [128, 1], BF16))
        s_selS = E(nc.sbuf_tensor("s_selS", [8, 128], BF16))
        s_g32 = E(nc.sbuf_tensor("s_g32", [128, 32], F32))
        s_bt32 = E(nc.sbuf_tensor("s_bt32", [128, 32], F32))
        s_gb1 = E(nc.sbuf_tensor("s_gb1", [32, 1], F32))
        s_gb2 = E(nc.sbuf_tensor("s_gb2", [8, 1], F32))
        s_gw2bf = E(nc.sbuf_tensor("s_gw2bf", [32, 8], BF16))
        s_scales = E(nc.sbuf_tensor("s_scales", [128, 12], F32))
        s_negM = E(nc.sbuf_tensor("s_negM", [128, 1], F32))
        s_stat = E(nc.sbuf_tensor("s_stat", [128, 48], F32))
        s_row = E(nc.sbuf_tensor("s_row", [1, 64], F32))
        s_qn2 = E(nc.sbuf_tensor("s_qn2", [1, NH], F32))
        s_kn2 = E(nc.sbuf_tensor("s_kn2", [1, NKV], F32))
        s_wcol = E(nc.sbuf_tensor("s_wcol", [128, 160], F32))
        s_cnt = E(nc.sbuf_tensor("s_cnt", [128, 16], F32))
        s_cntrow = E(nc.sbuf_tensor("s_cntrow", [1, 16], F32))
        s_alb = E(nc.sbuf_tensor("s_alb", [128, 16], F32))
        s_c1 = E(nc.sbuf_tensor("s_c1", [128, NT, 8], F32))
        s_c2 = E(nc.sbuf_tensor("s_c2", [128, NT, 8], F32))
        s_tsc = E(nc.sbuf_tensor("s_tsc", [128, TOK], BF16))
        s_cstg = E(nc.sbuf_tensor("s_cstg", [128, 128], F32))

        regA_buf = E(nc.sbuf_tensor("regA", [128, NCH * TOK], BF16))
        regC_buf = E(nc.sbuf_tensor("regC", [128, NCH * TOK], BF16))

        strm = ExitStack()
        SB = {}

        _sbn = [0]

        def open_stream_base(st):
            _sbn[0] += 1
            n = _sbn[0]
            SB["blk"] = [st.enter_context(
                nc.sbuf_tensor(f"s_wblk{i}_{n}", [128, 2048], F32))
                for i in range(2)]
            SB["bf"] = [st.enter_context(
                nc.sbuf_tensor(f"s_wbf{i}_{n}", [128, 2048], BF16))
                for i in range(2)]
            SB["junk"] = st.enter_context(
                nc.sbuf_tensor(f"s_wjunk_{n}", [128, 2048], F32))

        def open_stream_qi(st):
            n = _sbn[0]
            SB["qi"] = [st.enter_context(
                nc.sbuf_tensor(f"s_wqi{i}_{n}", [128, 2048], I32))
                for i in range(2)]

        def open_stream_bufs(st):
            open_stream_base(st)
            open_stream_qi(st)

        open_stream_base(strm)

        pscl = ExitStack()
        p_s1 = pscl.enter_context(nc.psum_tensor("p_s1", [1, 512], F32))
        pb128 = pscl.enter_context(nc.psum_tensor("pb128", [128, 512], F32))

        class PbfRing:

            def __init__(self, ap):
                self.ap = ap
                self.rd = [None, None]
                self.i = 0

            def next(self):
                s = self.i % 8
                self.i += 1
                bk = s % 2
                c0 = bk * 1024 + (s // 2) * 128
                return self.ap[:, c0:c0 + 128], bk

        def _mkset(base):
            names = ("sum", "sq", "r", "b", "am", "s", "inv", "t0", "t1",
                     "den", "m1", "m2", "d", "e2", "rg", "g1", "g2")
            d = {nm: s_stat[:, base + i:base + i + 1]
                 for i, nm in enumerate(names)}
            d["last"] = None
            return d

        SCP = [_mkset(0), _mkset(17)]
        sc0 = SCP[0]
        c_sum = sc0["sum"]
        c_sq = sc0["sq"]
        c_r = sc0["r"]
        c_b = sc0["b"]
        c_am = sc0["am"]
        c_s = sc0["s"]
        c_inv = sc0["inv"]
        c_t0 = sc0["t0"]
        c_t1 = sc0["t1"]
        c_den = sc0["den"]
        c_m1 = sc0["m1"]
        c_m2 = sc0["m2"]
        c_d = sc0["d"]
        c_e2 = sc0["e2"]
        c_rg = sc0["rg"]
        c_g1 = sc0["g1"]
        c_g2 = sc0["g2"]

        tc = []
        for dst, src in ((s_identbf, identbf), (s_identf, identf),
                         (s_onesf, onesf), (s_onesbf, onesbf), (s_selS, selS),
                         (s_g32, g32), (s_bt32, bt32), (s_gb1, gb1c),
                         (s_gb2, gb2c)):
            tc.append(b.dma("pool", "dM", dst[:], src))
        t_const = tc[-1]
        t_cnt0 = DVE(lambda h: h.memset(s_cnt[:], 0.0))
        c_eps = s_stat[:, 40:41]
        t_eps = DVE(lambda h: h.memset(c_eps, EPS))
        r_tiny = s_row[0:1, 62:63]
        t_tiny = DVE(lambda h: h.memset(r_tiny, 1e-30))

        def ln_stats(src_ap, n, deps, copy_dst, sq_dst, sc=None):
            if sc is None:
                sc = SCP[0]
            csum, csq, ct0, ct1, cr, cb = (sc["sum"], sc["sq"], sc["t0"],
                                           sc["t1"], sc["r"], sc["b"])
            deps = list(deps) + [sc["last"]]
            dst = copy_dst
            t2 = ACT(lambda h, dst=dst: h.activation(out=dst, in_=src_ap,
                                                     func=AF.Copy,
                                                     accum_out=csum), deps)
            t1 = ACT(lambda h, sq_dst=sq_dst: h.activation(
                out=sq_dst, in_=src_ap, func=AF.Square, accum_out=csq),
                     deps)
            t3 = DVE(lambda h: h.tensor_scalar(out=ct0, in0=csum,
                                               scalar1=1.0 / n, scalar2=None,
                                               op0=ALU.mult), [t2])
            t4 = DVE(lambda h: h.tensor_tensor(out=ct1, in0=ct0, in1=ct0,
                                               op=ALU.mult), [t3])
            t5 = DVE(lambda h: h.scalar_tensor_tensor(out=ct1, in0=csq,
                                                      scalar=1.0 / n, in1=ct1,
                                                      op0=ALU.mult,
                                                      op1=ALU.subtract),
                     [t1, t4])
            t6 = ACT(lambda h: h.activation(out=ct1, in_=ct1, func=AF.Ln,
                                            bias=c_eps, scale=1.0),
                     [t5, t_eps])
            t7 = ACT(lambda h: h.activation(out=cr, in_=ct1, func=AF.Exp,
                                            bias=0.0, scale=-0.5), [t6])
            t8 = DVE(lambda h: h.tensor_tensor(out=cb, in0=ct0, in1=cr,
                                               op=ALU.mult), [t7, t3])
            t9 = DVE(lambda h: h.tensor_scalar(out=cb, in0=cb, scalar1=-1.0,
                                               scalar2=None, op0=ALU.mult),
                     [t8])
            return t7, t9, t2

        def emit_act_quant(src_col, out_col, y1ap, deps_in, n_chunks=4,
                           sc=None):
            if sc is None:
                sc = SCP[0]
            cs_, cinv_ = sc["s"], sc["inv"]
            A = y1ap[:, 0:1024]
            Bi = y1ap[:, 1024:2048].bitcast(I32)
            C = y1ap[:, 2048:3072]
            Es = y1ap[:, 3072:4096]
            tA = tQ = None
            for k in range(n_chunks):
                tA = DVE(lambda h, k=k: h.tensor_scalar(
                    out=A, in0=src_col(k), scalar1=cs_, scalar2=None,
                    op0=ALU.mult), list(deps_in) + [tQ])
                tB = DVE(lambda h: h.tensor_copy(out=Bi, in_=A), [tA])
                tC = DVE(lambda h: h.tensor_tensor(out=C, in0=A, in1=Bi,
                                                   op=ALU.subtract), [tB])
                tD = DVE(lambda h: h.tensor_scalar(out=A, in0=C, scalar1=0.5,
                                                   scalar2=None,
                                                   op0=ALU.is_ge), [tC])
                tE = DVE(lambda h: h.scalar_tensor_tensor(
                    out=Es, in0=C, scalar=-0.5, in1=A, op0=ALU.is_le,
                    op1=ALU.subtract), [tD])
                tF = DVE(lambda h: h.tensor_tensor(out=C, in0=Bi, in1=Es,
                                                   op=ALU.subtract), [tE])
                tQ = DVE(lambda h, k=k: h.tensor_scalar(
                    out=out_col(k), in0=C, scalar1=cinv_, scalar2=None,
                    op0=ALU.mult), [tF])
            sc["last"] = tQ
            return tA, tQ

        class HalfStream:
            def __init__(self):
                self.i = 0
                self.t_q1 = [None, None]
                self.t_q2 = [None, None]
                self.t_mm = [None, None]
                self.t_abs = [None, None]

            def next(self, src_slice, mode, scol=0, wcol=0, extra_deps=(),
                     scol_c2=(8,), par=None):
                if par is None:
                    par = self.i % 2
                    self.i += 1
                blk, bfb, jnk = SB["blk"][par], SB["bf"][par], SB["junk"]
                qib = SB["qi"][par] if "qi" in SB else None
                tl = b.dma("sp", "dW%d" % par, blk[:], src_slice,
                           deps=[self.t_q1[par], self.t_abs[par]] +
                                list(extra_deps))
                if mode == "abs":
                    ta = ACT(lambda h, blk=blk, jnk=jnk, wcol=wcol:
                             h.activation(
                        out=jnk[:], in_=blk[:], func=AF.Abs,
                        accum_out=s_wcol[:, wcol:wcol + 1]), [tl])
                    self.t_abs[par] = ta
                    return par, ta
                if mode == "quant":
                    c2col = scol_c2[0]
                    qif = qib[:].bitcast(F32)
                    tq1 = DVE(lambda h, blk=blk, qif=qif, scol=scol:
                              h.tensor_scalar(
                        out=qif, in0=blk[:],
                        scalar1=s_scales[:, scol:scol + 1], scalar2=0.5,
                        op0=ALU.mult, op1=ALU.is_ge), [tl, self.t_q2[par]])
                    tq2 = DVE(lambda h, bfb=bfb, qif=qif, blk=blk,
                              c2col=c2col: h.scalar_tensor_tensor(
                        out=bfb[:], in0=blk[:],
                        scalar=s_scales[:, c2col:c2col + 1], in1=qif,
                        op0=ALU.is_le, op1=ALU.subtract),
                        [tq1, self.t_mm[par]])
                    self.t_q1[par] = tq2
                    self.t_q2[par] = tq2
                    return par, tq2
                tq = DVE(lambda h, blk=blk, bfb=bfb: h.tensor_copy(
                    out=bfb[:], in_=blk[:]),
                    [tl, self.t_mm[par]])
                self.t_q1[par] = tq
                self.t_q2[par] = tq
                return par, tq

            def bf(self, par):
                return SB["bf"][par]

            def consumed(self, par, tok):
                self.t_mm[par] = tok

        ws = HalfStream()

        scale_queue = []
        wc_base = {"wk": 0, "wv": 16, "wq": 32, "wo": 96}
        for nm_, src_, name_ in ((NKV, wk, "wk"), (NKV, wv, "wv"),
                                 (NH, wq, "wq"), (NH, wo, "wo")):
            for m in range(nm_):
                for hf in range(2):
                    scale_queue.append(
                        (name_, src_, m, hf, wc_base[name_] + 2 * m + hf))
        sq_pos = [0]
        sq_abs = {"wk": [], "wv": [], "wq": [], "wo": []}
        t_pb_use = [None]

        def emit_scale_halves(k):
            for _ in range(k):
                if sq_pos[0] >= len(scale_queue):
                    return
                name, src, m, hf, wcol = scale_queue[sq_pos[0]]
                sq_pos[0] += 1
                par, ta = ws.next(src[m, :, hf * 2048:(hf + 1) * 2048], "abs",
                                  wcol=wcol)
                sq_abs[name].append(ta)

        def finish_scale(name, nm, scol_s, scol_d, extra, slot):
            nelem = nm * 128 * HID
            cb = wc_base[name]
            tr = DVE(lambda h, nm=nm, cb=cb: h.tensor_reduce(
                out=c_t0, in_=s_wcol[:, cb:cb + 2 * nm], axis=AX.X,
                op=ALU.add), [sq_abs[name][-1]])
            tm1 = PE(lambda h, slot=slot: h.matmul(
                p_s1[0:1, slot:slot + 1], s_onesf[:, 0:1], c_t0,
                start=True, stop=True), [tr, t_pb_use[0], t_const], inc=True)
            r0 = 32 + slot * 4
            tm2 = DVE(lambda h, nelem=nelem, r0=r0, slot=slot: h.tensor_scalar(
                out=s_row[0:1, r0:r0 + 1], in0=p_s1[0:1, slot:slot + 1],
                scalar1=1.0 / nelem, scalar2=1e-5, op0=ALU.mult, op1=ALU.max),
                [tm1])
            tm3 = DVE(lambda h, r0=r0: h.reciprocal(
                out=s_row[0:1, r0 + 1:r0 + 2], in_=s_row[0:1, r0:r0 + 1]),
                [tm2])
            tm4 = DVE(lambda h, r0=r0, extra=extra: h.tensor_scalar(
                out=s_row[0:1, r0 + 2:r0 + 3], in0=s_row[0:1, r0:r0 + 1],
                scalar1=-extra, scalar2=None, op0=ALU.mult), [tm2])
            tm5 = DVE(lambda h, r0=r0: h.tensor_scalar(
                out=s_row[0:1, r0 + 3:r0 + 4], in0=s_row[0:1, r0:r0 + 1],
                scalar1=-0.5, scalar2=None, op0=ALU.mult), [tm2])
            tb1 = PE(lambda h, scol_d=scol_d, r0=r0: h.matmul(
                pb128[:, scol_d:scol_d + 1], s_onesf[0:1, :],
                s_row[0:1, r0 + 2:r0 + 3], start=True, stop=True),
                [tm4], inc=True)
            tb2 = PE(lambda h, scol_s=scol_s, r0=r0: h.matmul(
                pb128[:, scol_s:scol_s + 1], s_onesf[0:1, :],
                s_row[0:1, r0 + 1:r0 + 2], start=True, stop=True),
                [tm3], inc=True)
            tb3 = PE(lambda h, slot=slot, r0=r0: h.matmul(
                pb128[:, 8 + slot:9 + slot], s_onesf[0:1, :],
                s_row[0:1, r0 + 3:r0 + 4], start=True, stop=True),
                [tm5], inc=True)
            lo, hi = min(scol_s, scol_d), max(scol_s, scol_d)
            te_ = DVE(lambda h, lo=lo, hi=hi: h.tensor_copy(
                out=s_scales[:, lo:hi + 1], in_=pb128[:, lo:hi + 1]),
                [tb1, tb2, tb3])
            te = DVE(lambda h, slot=slot: h.tensor_copy(
                out=s_scales[:, 8 + slot:9 + slot],
                in_=pb128[:, 8 + slot:9 + slot]), [tb3, te_])
            t_pb_use[0] = te
            return te

        qn3 = regA_buf[:].rearrange("p (c t) -> p c t", t=TOK)

        scope1 = ExitStack()
        pbf1s = ExitStack()
        ring1 = PbfRing(pbf1s.enter_context(
            nc.psum_tensor("pbf1", [128, 2048], BF16))[:])
        s_x = [scope1.enter_context(nc.sbuf_tensor(f"s_x{i}", [128, HID], F32))
               for i in range(2)]
        s_y1 = regC_buf[:, 0:8192].bitcast(F32)
        s_lnw = regC_buf[:, 8192:16384].bitcast(F32)
        s_lnb = regC_buf[:, 16384:24576].bitcast(F32)
        s_qtm = regC_buf[:, 24576:28672]

        t_lnw = b.dma("pool", "dM", s_lnw, lnw)
        t_lnb = b.dma("pool", "dM", s_lnb, lnb)

        t_x_use = [None, None]
        t_pbf = [None] * 8
        tok_qn = None
        prev_tq1 = None
        prev_tq2 = None
        prev_tz = None
        for t in range(NT):
            par = t % 2
            xb = s_x[par]
            tl = b.dma("pool", "dX%d" % par, xb[:],
                       x_own[t * 128:(t + 1) * 128, :], deps=[t_x_use[par]])
            sc = SCP[par]
            scr, scb, scam, scs, scinv = (sc["r"], sc["b"], sc["am"],
                                          sc["s"], sc["inv"])
            r1, b1, _ = ln_stats(xb[:], HID, [tl, prev_tq2],
                                 copy_dst=s_y1, sq_dst=s_y1, sc=sc)
            ty = ACT(lambda h, xb=xb, scr=scr, scb=scb: h.activation(
                out=s_y1, in_=xb[:], func=AF.Identity, bias=scb, scale=scr),
                [r1, b1])
            r2, b2, _ = ln_stats(s_y1, HID, [ty],
                                 copy_dst=xb[:], sq_dst=xb[:], sc=sc)
            tz = ACT(lambda h, xb=xb, scr=scr, scb=scb: h.activation(
                out=xb[:], in_=s_y1, func=AF.Identity, bias=scb, scale=scr),
                [r2, b2, prev_tq1])
            tz2 = DVE(lambda h, xb=xb: h.tensor_tensor(
                out=s_y1, in0=xb[:], in1=s_lnw, op=ALU.mult),
                [tz, t_lnw])
            tz3 = DVE(lambda h, xb=xb: h.tensor_tensor(
                out=xb[:], in0=s_y1, in1=s_lnb, op=ALU.add),
                [tz2, t_lnb])
            tam = DVE(lambda h, xb=xb, scam=scam: h.tensor_reduce(
                out=scam, in_=xb[:], axis=AX.X, op=ALU.max,
                apply_absolute_value=True), [tz3])
            tmc = DVE(lambda h, scam=scam: h.tensor_scalar(
                out=scam, in0=scam, scalar1=1e-5, scalar2=None,
                op0=ALU.max), [tam])
            trc = DVE(lambda h, scs=scs, scam=scam: h.reciprocal(
                out=scs, in_=scam), [tmc])
            ts_ = DVE(lambda h, scs=scs: h.tensor_scalar(
                out=scs, in0=scs, scalar1=QB,
                scalar2=None, op0=ALU.mult), [trc])
            tin = DVE(lambda h, scinv=scinv, scam=scam: h.tensor_scalar(
                out=scinv, in0=scam, scalar1=1.0 / QB, scalar2=None,
                op0=ALU.mult), [tmc])
            tq1, tq2 = emit_act_quant(
                lambda k, xb=xb: xb[:, k * 1024:(k + 1) * 1024],
                lambda k: s_qtm[:, k * 1024:(k + 1) * 1024],
                s_y1, [ts_, tin, tz3, tam, tok_qn], sc=sc)
            prev_tq1 = tq1
            t_x_use[par] = tq1
            prev_tq2 = tq2
            prev_tz = tz
            for c in range(NCH):
                sl, bk = ring1.next()
                tp = PE(lambda h, c=c, sl=sl: h.transpose(
                    sl, s_qtm[:, c * 128:(c + 1) * 128], s_identbf[:]),
                    [tq2, ring1.rd[bk], t_const], inc=True)
                tev = DVE(lambda h, c=c, t=t, sl=sl: h.tensor_copy(
                    out=qn3[:, c, t * 128:(t + 1) * 128], in_=sl), [tp])
                ring1.rd[bk] = tev
                tok_qn = tev
            emit_scale_halves(6)
        scope1.close()
        pbf1s.close()

        if DEBUG_DUMPS:
            td_ = b.dma("pool", "dM", dbg_qn.ap(), regA_buf[:],
                        deps=[tok_qn, prev_tq1])
            b._wait("dve", td_)
            b._wait("pe", td_)
        tok_swk = finish_scale("wk", NKV, 2, 3, 1.0, 0)
        tok_swv = finish_scale("wv", NKV, 4, 5, 1.0, 1)

        Q3 = regC_buf[:].rearrange("p (m t) -> p m t", t=TOK)
        strmqi = ExitStack()
        open_stream_qi(strmqi)
        scope2 = ExitStack()
        s_cos = scope2.enter_context(nc.sbuf_tensor("s_cos", [128, TOK], F32))
        s_sin = scope2.enter_context(nc.sbuf_tensor("s_sin", [128, TOK], F32))
        s_kpre = scope2.enter_context(nc.sbuf_tensor("s_kpre", [128, TOK], F32))
        s_krot = scope2.enter_context(nc.sbuf_tensor("s_krot", [128, TOK], F32))
        s_kt1 = scope2.enter_context(nc.sbuf_tensor("s_kt1", [128, TOK], F32))
        s_stg = [scope2.enter_context(
            nc.sbuf_tensor(f"s_stg{i}", [128, TOK], BF16)) for i in range(2)]
        s_sq = s_kt1[:].bitcast(BF16)[:, 0:TOK]
        t_cos = b.dma("pool", "dM", s_cos[:], cosb,
                      deps=[prev_tq1, prev_tq2, prev_tz])
        t_sin = b.dma("pool", "dM", s_sin[:], sinb)
        pj = ExitStack()
        pproj4 = [pj.enter_context(nc.psum_tensor(f"pproj{i}", [128, 512], F32))
                  for i in range(4)]
        pnorm = pj.enter_context(nc.psum_tensor("pnorm", [1, 1024], F32))

        t_pp = [None, None, None, None]
        t_stg_use = [None, None]
        t_rope_rd = [None]
        t_stage = []
        t_kn = []
        t_pnorm = [None]
        t_sq_use = [None]

        blk_ctr = [0]
        cur_pr = [0]

        def q_half(src, m, hf, scol_q, scale_tok, c2col_q):
            return ws.next(src[m, :, hf * 2048:(hf + 1) * 2048], "quant",
                           scol=scol_q, extra_deps=[scale_tok],
                           scol_c2=(c2col_q,), par=hf)

        def mm_half(hf, qinfo, scale_tok, new_block):
            if new_block:
                cur_pr[0] = blk_ctr[0] % 2
                blk_ctr[0] += 1
            pr = cur_pr[0]
            pproj = pproj4[2 * pr:2 * pr + 2]
            par, trdy = qinfo
            tokm = None
            for j in range(16):
                c = hf * 16 + j
                for tg in range(2):
                    dep = [trdy, tok_qn, scale_tok]
                    if c == 0:
                        dep.append(t_pp[2 * pr + tg])
                    tokm = PE(lambda h, wb_=ws.bf(par), j=j, c=c, tg=tg,
                              pp_=pproj[tg]: h.matmul(
                        pp_[:],
                        wb_[:, j * 128:(j + 1) * 128],
                        qn3[:, c, tg * 512:(tg + 1) * 512],
                        start=(c == 0), stop=(c == 31)),
                        dep, inc=(tg == 1 and (c == 15 or c == 31)))
            ws.consumed(par, tokm)
            return tokm

        def proj_evac(m, dcol, rope, kind, mm_last):
            pr = cur_pr[0]
            pproj = pproj4[2 * pr:2 * pr + 2]
            stg_par = m % 2
            deps_e = [mm_last[0], mm_last[1]]
            if rope:
                te0 = ACT(lambda h, dcol=dcol, pp_=pproj[0]: h.activation(
                    out=s_kpre[:, 0:512], in_=pp_[:], func=AF.Copy,
                    scale=s_scales[:, dcol:dcol + 1]),
                    deps_e + [t_rope_rd[0]])
                te1 = ACT(lambda h, dcol=dcol, pp_=pproj[1]: h.activation(
                    out=s_kpre[:, 512:1024], in_=pp_[:], func=AF.Copy,
                    scale=s_scales[:, dcol:dcol + 1]), deps_e)
                GPS = DVE
                tn = GPS(lambda h: h.tensor_scalar(
                    out=s_krot[0:64, :], in0=s_kpre[64:128, :], scalar1=-1.0,
                    scalar2=None, op0=ALU.mult), [te0, te1])
                tcp = GPS(lambda h: h.tensor_copy(out=s_krot[64:128, :],
                                                  in_=s_kpre[0:64, :]),
                          [te0, te1])
                t1 = GPS(lambda h: h.tensor_tensor(out=s_kt1[:], in0=s_kpre[:],
                                                   in1=s_cos[:], op=ALU.mult),
                         [te0, te1, t_cos, t_sq_use[0]])
                t2 = GPS(lambda h: h.tensor_tensor(out=s_kpre[:],
                                                   in0=s_krot[:], in1=s_sin[:],
                                                   op=ALU.mult),
                         [tn, tcp, t_sin, t1])
                if kind == "q":
                    t3 = GPS(lambda h, m=m: h.tensor_tensor(
                        out=Q3[:, m, :], in0=s_kt1[:], in1=s_kpre[:],
                        op=ALU.add), [t1, t2])
                else:
                    t3 = GPS(lambda h, stg_par=stg_par: h.tensor_tensor(
                        out=s_stg[stg_par][:], in0=s_kt1[:], in1=s_kpre[:],
                        op=ALU.add), [t1, t2, t_stg_use[stg_par]])
                t_rope_rd[0] = t3
                t_pp[2 * pr], t_pp[2 * pr + 1] = te0, te1
            else:
                te0 = ACT(lambda h, stg_par=stg_par, dcol=dcol, pp_=pproj[0]:
                          h.activation(
                    out=s_stg[stg_par][:, 0:512], in_=pp_[:],
                    func=AF.Copy, scale=s_scales[:, dcol:dcol + 1]),
                    deps_e + [t_stg_use[stg_par]])
                te1 = ACT(lambda h, stg_par=stg_par, dcol=dcol, pp_=pproj[1]:
                          h.activation(
                    out=s_stg[stg_par][:, 512:1024], in_=pp_[:],
                    func=AF.Copy, scale=s_scales[:, dcol:dcol + 1]), deps_e)
                t3 = te1
                t_pp[2 * pr], t_pp[2 * pr + 1] = te0, te1
            tsq = None
            if kind == "q":
                tsq = ACT(lambda h, m=m: h.activation(
                    out=s_sq, in_=Q3[:, m, :], func=AF.Square),
                    [t3, t_sq_use[0]])
            elif kind == "k":
                tsq = ACT(lambda h, stg_par=stg_par: h.activation(
                    out=s_sq, in_=s_stg[stg_par][:], func=AF.Square),
                    [t3, t_sq_use[0]])
            if tsq is not None:
                tn1 = PE(lambda h: h.matmul(pnorm[0:1, 0:512],
                                            s_onesbf[:, 0:1], s_sq[:, 0:512],
                                            start=True, stop=True),
                         [tsq, t_pnorm[0], t_const], inc=True)
                tn2 = PE(lambda h: h.matmul(pnorm[0:1, 512:1024],
                                            s_onesbf[:, 0:1],
                                            s_sq[:, 512:1024],
                                            start=True, stop=True),
                         [tsq], inc=True)
                t_sq_use[0] = tn2
                dst = s_qn2 if kind == "q" else s_kn2
                tnr = DVE(lambda h, dst=dst, m=m: h.tensor_reduce(
                    out=dst[0:1, m:m + 1], in_=pnorm[0:1, :], axis=AX.X,
                    op=ALU.max), [tn1, tn2])
                t_pnorm[0] = tnr
                if kind == "k":
                    t_kn.append(tnr)
            if kind in ("k", "v"):
                row0 = (0 if kind == "k" else NKV * 128) + m * 128
                td = b.dma("pool", "dK0", ccg_in.ap()[row0:row0 + 128, :],
                           s_stg[stg_par][:],
                           deps=[t3] + ([tsq] if kind == "k" else []))
                t_stg_use[stg_par] = td
                t_stage.append(td)
            return t3

        def run_pipeline(blocks, nsh):
            n = len(blocks)
            (bsrc, m, scol, dcol, rope, kind, stok, c2c) = blocks[0]
            qA = q_half(bsrc, m, 0, scol, stok, c2c)
            qB = q_half(bsrc, m, 1, scol, stok, c2c)
            for i in range(n):
                (bsrc, m, scol, dcol, rope, kind, stok, c2c) = blocks[i]
                mm0 = mm_half(0, qA, stok, True)
                if i + 1 < n:
                    (b2, m2, s2, d2, r2, k2, st2, cc2) = blocks[i + 1]
                    qA = q_half(b2, m2, 0, s2, st2, cc2)
                mm1 = mm_half(1, qB, stok, False)
                proj_evac(m, dcol, rope, kind, [mm0, mm1])
                if i + 1 < n:
                    qB = q_half(b2, m2, 1, s2, st2, cc2)
                emit_scale_halves(nsh)

        blocks = ([(wk, m, 2, 3, True, "k", tok_swk, 8)
                   for m in range(NKV)] +
                  [(wv, m, 4, 5, False, "v", tok_swv, 9)
                   for m in range(NKV)])
        run_pipeline(blocks, 3)
        tok_swq = finish_scale("wq", NH, 0, 1, SM_SCALE, 2)

        t_kn_dma = b.dma("pool", "dK0", kn_in.ap(), s_kn2[0:1, :], deps=t_kn)
        tok_ag = b.coll("AllGather", ALU.bypass,
                        [[0, 1], [2, 3], [4, 5], [6, 7]],
                        [ccg_in.ap().opt()], [ccg_out.ap().opt()],
                        deps=t_stage, ch="cc0")
        tok_knr = b.coll("AllReduce", ALU.max,
                         [[0, 1], [2, 3], [4, 5], [6, 7]],
                         [kn_in.ap().opt()], [kn_out.ap().opt()],
                         deps=[t_kn_dma], ch="cc1")
        t_kn_back = b.dma("pool", "dK0", s_kn2[0:1, :], kn_out.ap(),
                          deps=[tok_knr])

        run_pipeline([(wq, m, 0, 1, True, "q", tok_swq, 10)
                      for m in range(NH)], 2)
        tok_swo = finish_scale("wo", NH, 6, 7, 1.0, 3)

        tq2m = DVE(lambda h: h.tensor_reduce(out=s_row[0:1, 0:1],
                                             in_=s_qn2[0:1, :], axis=AX.X,
                                             op=ALU.max), [t_pnorm[0]])
        tk2m = DVE(lambda h: h.tensor_reduce(out=s_row[0:1, 1:2],
                                             in_=s_kn2[0:1, :], axis=AX.X,
                                             op=ALU.max), [t_kn_back])
        tl1 = ACT(lambda h: h.activation(out=s_row[0:1, 2:3],
                                         in_=s_row[0:1, 0:1], func=AF.Ln,
                                         bias=r_tiny, scale=1.0),
                  [tq2m, t_tiny])
        tl2 = ACT(lambda h: h.activation(out=s_row[0:1, 3:4],
                                         in_=s_row[0:1, 1:2], func=AF.Ln,
                                         bias=r_tiny, scale=1.0),
                  [tk2m, t_tiny])
        tl3 = DVE(lambda h: h.tensor_tensor(out=s_row[0:1, 4:5],
                                            in0=s_row[0:1, 2:3],
                                            in1=s_row[0:1, 3:4], op=ALU.add),
                  [tl1, tl2])
        tl4 = ACT(lambda h: h.activation(out=s_row[0:1, 5:6],
                                         in_=s_row[0:1, 4:5], func=AF.Exp,
                                         bias=0.0, scale=0.5), [tl3])
        tl5 = DVE(lambda h: h.tensor_scalar(out=s_row[0:1, 6:7],
                                            in0=s_row[0:1, 5:6],
                                            scalar1=SHIFT_CAP, scalar2=-1.0,
                                            op0=ALU.min, op1=ALU.mult), [tl4])
        tlb = PE(lambda h: h.matmul(pb128[:, 2:3], s_onesf[0:1, :],
                                    s_row[0:1, 6:7], start=True, stop=True),
                 [tl5, t_pb_use[0]], inc=True)
        tok_negM = DVE(lambda h: h.tensor_copy(out=s_negM[:],
                                               in_=pb128[:, 2:3]), [tlb])
        t_pb_use[0] = tok_negM

        if DEBUG_DUMPS:
            td_ = b.dma("pool", "dM", dbg_q3.ap(), regC_buf[:],
                        deps=[tok_negM, t_rope_rd[0]])
            b._wait("dve", td_)
        pj.close()
        pscl.close()
        scope2.close()
        strmqi.close()
        strm.close()
        b.barrier(("pe", "act", "dve"))
        sd = ExitStack()
        psc = [sd.enter_context(nc.psum_tensor(f"psc{i}", [128, 1024], F32))
               for i in range(2)]
        pout = [sd.enter_context(nc.psum_tensor(f"pout{i}", [128, 512], F32))
                for i in range(3)]
        s_kst = [sd.enter_context(nc.sbuf_tensor(f"s_kst{i}", [128, 2048],
                                                 BF16)) for i in range(2)]
        s_vstg = [sd.enter_context(nc.sbuf_tensor(f"s_vstg{i}", [128, 2048],
                                                  BF16)) for i in range(2)]
        s_vtm = [sd.enter_context(nc.sbuf_tensor(f"s_vtm{i}", [128, 16 * 129],
                                                 BF16)) for i in range(2)]
        s_atn = [sd.enter_context(nc.sbuf_tensor(f"s_atn{i}", [128, 1024],
                                                 BF16)) for i in range(4)]

        t_vtm_use = [DVE(lambda h: h.memset(s_vtm[0][:], 1.0)),
                     DVE(lambda h: h.memset(s_vtm[1][:], 1.0))]
        t_kst_use = [None, None]
        t_vstg_use = [None, None]
        t_psc_use = [None, None]
        t_atn_use = [None, None, None, None]
        t_pout_use = [None, None, None]
        t_pbf = [None] * 8

        B3 = regA_buf[:].rearrange("p (m t) -> p m t", t=TOK)

        for g in range(NKV):
            par = g % 2
            tk0 = b.dma("pool", "dK%d" % par, s_kst[par][:, 0:1024],
                        ccg_out.ap()[g * 128:(g + 1) * 128, :],
                        deps=[tok_ag, t_kst_use[par]] +
                             ([tok_swo] if g < 2 else []))
            tk1 = b.dma("pool", "dK%d" % par, s_kst[par][:, 1024:2048],
                        ccg_out.ap()[2048 + g * 128:2048 + (g + 1) * 128, :],
                        deps=[tok_ag])
            tv0_ = b.dma("pool", "dK%d" % par, s_vstg[par][:, 0:1024],
                         ccg_out.ap()[1024 + g * 128:1024 + (g + 1) * 128, :],
                         deps=[tok_ag, t_vstg_use[par]])
            tv1_ = b.dma("pool", "dK%d" % par, s_vstg[par][:, 1024:2048],
                         ccg_out.ap()[3072 + g * 128:3072 + (g + 1) * 128, :],
                         deps=[tok_ag])
            t_ld_all = tv1_
            tvl = None
            for kt in range(16):
                bk = kt % 2
                sl = pout[bk][:].bitcast(BF16)[:, 0:128]
                tp = PE(lambda h, par=par, kt=kt, sl=sl: h.transpose(
                    sl, s_vstg[par][:, kt * 128:(kt + 1) * 128], s_identbf[:]),
                    [t_ld_all, t_pout_use[bk], t_vtm_use[par]], inc=True)
                tev = DVE(lambda h, par=par, kt=kt, sl=sl: h.tensor_copy(
                    out=s_vtm[par][:, kt * 129:kt * 129 + 128], in_=sl), [tp])
                t_pout_use[bk] = tev
                tvl = tev
            t_vstg_use[par] = tvl
            tok_vtm = tvl

            for hq in range(4):
                hh = 4 * g + hq
                av_toks = [None] * NT
                exp_toks = [None] * 16

                def emit_av(j, hh=hh, par=par, av_toks=av_toks,
                            exp_toks=exp_toks):
                    ab = j % 4
                    tav = None
                    for qt in range(NT):
                        bk, slot = qt // 3, qt % 3
                        dep = [exp_toks[j], tok_vtm]
                        if j == 0:
                            dep.append(t_pout_use[bk])
                        tav = PE(lambda h, ab=ab, par=par, j=j, bk=bk,
                                 slot=slot, qt=qt: h.matmul(
                            pout[bk][:, slot * 129:slot * 129 + 129],
                            s_atn[ab][:, qt * 128:(qt + 1) * 128],
                            s_vtm[par][:, j * 129:j * 129 + 129],
                            start=(j == 0 and slot == 0), stop=(j == 15),
                            skip_group_check=True),
                            dep, inc=(j == 15 or qt == NT - 1))
                        if j == 15:
                            av_toks[qt] = tav
                    t_atn_use[ab] = tav

                for kt in range(16):
                    pp = kt % 2
                    ab = kt % 4
                    ts0 = PE(lambda h, par=par, kt=kt, hh=hh, pp=pp: h.matmul(
                        psc[pp][:, 0:512],
                        s_kst[par][:, kt * 128:(kt + 1) * 128],
                        Q3[:, hh, 0:512], start=True, stop=True),
                        [t_ld_all, t_psc_use[pp], tok_negM], inc=True)
                    ts1 = PE(lambda h, par=par, kt=kt, hh=hh, pp=pp: h.matmul(
                        psc[pp][:, 512:1024],
                        s_kst[par][:, kt * 128:(kt + 1) * 128],
                        Q3[:, hh, 512:1024], start=True, stop=True),
                        [], inc=True)
                    tex = ACT(lambda h, pp=pp, ab=ab: h.activation(
                        out=s_atn[ab][:], in_=psc[pp][:], func=AF.Exp,
                        bias=s_negM[:], scale=1.0),
                        [ts0, ts1, t_atn_use[ab]])
                    exp_toks[kt] = tex
                    t_psc_use[pp] = tex
                    if kt >= 2:
                        emit_av(kt - 2)
                emit_av(14)
                emit_av(15)
                t_kst_use[par] = av_toks[-1]
                t_vtm_use[par] = av_toks[-1]
                for qt in range(NT):
                    bk, slot = qt // 3, qt % 3
                    tcd_ = DVE(lambda h, bk=bk, slot=slot: h.tensor_copy(
                        out=c_t0,
                        in_=pout[bk][:, slot * 129 + 128:slot * 129 + 129]),
                        [av_toks[qt]])
                    trd = DVE(lambda h: h.reciprocal(out=c_den, in_=c_t0),
                              [tcd_])
                    tnm = DVE(lambda h, bk=bk, slot=slot, hh=hh, qt=qt:
                              h.tensor_scalar(
                        out=B3[:, hh, qt * 128:(qt + 1) * 128],
                        in0=pout[bk][:, slot * 129:slot * 129 + 128],
                        scalar1=c_den, scalar2=None, op0=ALU.mult), [trd])
                    t_pout_use[bk] = tnm

        tok_sdpa = [t for t in t_pout_use if t is not None]
        if DEBUG_DUMPS:
            td_ = b.dma("pool", "dM", dbg_b3.ap(), regA_buf[:],
                        deps=tok_sdpa)
            b._wait("dve", td_)
            b._wait("pool", td_)
        sd.close()
        b.barrier(("pe", "act", "dve"))
        p5 = ExitStack()
        pproj5 = [p5.enter_context(nc.psum_tensor(f"pp5{i}", [128, 512], F32))
                  for i in range(4)]
        ring5 = PbfRing(p5.enter_context(
            nc.psum_tensor("pbf5", [128, 2048], BF16))[:])
        open_stream_bufs(p5)
        s_y1p5 = p5.enter_context(nc.sbuf_tensor("s_y1b", [128, HID], F32))
        s_sh5v = p5.enter_context(nc.sbuf_tensor("s_sh5v", [128, HID], BF16))
        s_qtm5 = s_sh5v[:]
        s_aofm5 = s_sh5v[:, 0:TOK]
        s_at3s = s_sh5v[:, 2048:3072].rearrange("p (t f) -> p t f", f=128)

        Qo3 = regC_buf[:].rearrange("p (c t) -> p c t", t=TOK)

        tok_qo = None
        t_pbf = [None] * 8
        t_prev_q = None
        for qt in range(NT):
            tam = DVE(lambda h, qt=qt: h.tensor_reduce(
                out=c_am, in_=B3[:, :, qt * 128:(qt + 1) * 128], axis=AX.XY,
                op=ALU.max, apply_absolute_value=True), [t_prev_q])
            tmc = DVE(lambda h: h.tensor_scalar(out=c_am, in0=c_am,
                                                scalar1=1e-5, scalar2=None,
                                                op0=ALU.max), [tam])
            trc = DVE(lambda h: h.reciprocal(out=c_s, in_=c_am), [tmc])
            ts_ = DVE(lambda h: h.tensor_scalar(out=c_s, in0=c_s, scalar1=QB,
                                                scalar2=None, op0=ALU.mult),
                      [trc])
            tin = DVE(lambda h: h.tensor_scalar(out=c_inv, in0=c_am,
                                                scalar1=1.0 / QB, scalar2=None,
                                                op0=ALU.mult), [tmc])
            tq1, tq2 = emit_act_quant(
                lambda k, qt=qt: B3[:, 8 * k:8 * (k + 1),
                                    qt * 128:(qt + 1) * 128],
                lambda k: s_qtm5[:, k * 1024:(k + 1) * 1024].rearrange(
                    "p (c f) -> p c f", c=8),
                s_y1p5[:], [ts_, tin, tok_qo])
            t_prev_q = tq2
            for c in range(NCH):
                sl, bk = ring5.next()
                tp = PE(lambda h, c=c, sl=sl: h.transpose(
                    sl, s_qtm5[:, c * 128:(c + 1) * 128], s_identbf[:]),
                    [tq2, ring5.rd[bk]], inc=True)
                tev = DVE(lambda h, c=c, qt=qt, sl=sl: h.tensor_copy(
                    out=Qo3[:, c, qt * 128:(qt + 1) * 128], in_=sl), [tp])
                ring5.rd[bk] = tev
                tok_qo = tev

        t_pp5 = [None, None, None, None]
        t_aofm_use = None
        t_at3s_use = None
        t_at3_dma = []
        for m in range(NH):
            pr = m % 2
            pp = pproj5[2 * pr:2 * pr + 2]
            mm_last = [None, None]
            for hf in range(2):
                xd = [tok_swo] + (tok_sdpa if m == 0 else [])
                par, trdy = ws.next(wo[m, :, hf * 2048:(hf + 1) * 2048],
                                    "quant", scol=6, extra_deps=xd,
                                    scol_c2=(11,))
                tokm = None
                for j in range(16):
                    c = hf * 16 + j
                    for tg in range(2):
                        dep = [trdy, tok_qo, tok_swo]
                        if c == 0:
                            dep.append(t_pp5[2 * pr + tg])
                        tokm = PE(lambda h, wb_=ws.bf(par), j=j, c=c, tg=tg,
                                  pp_=pp[tg]: h.matmul(
                            pp_[:],
                            wb_[:, j * 128:(j + 1) * 128],
                            Qo3[:, c, tg * 512:(tg + 1) * 512],
                            start=(c == 0), stop=(c == 31)),
                            dep, inc=(tg == 1 and (c == 15 or c == 31)))
                mm_last[hf] = tokm
                ws.consumed(par, tokm)
            te0 = ACT(lambda h, pp_=pp[0]: h.activation(
                out=s_aofm5[:, 0:512], in_=pp_[:], func=AF.Copy,
                scale=s_scales[:, 7:8]),
                [mm_last[0], mm_last[1], t_aofm_use])
            te1 = ACT(lambda h, pp_=pp[1]: h.activation(
                out=s_aofm5[:, 512:1024], in_=pp_[:], func=AF.Copy,
                scale=s_scales[:, 7:8]), [mm_last[0], mm_last[1]])
            t_pp5[2 * pr], t_pp5[2 * pr + 1] = te0, te1
            tevl = None
            for qt in range(NT):
                sl, bk = ring5.next()
                tp = PE(lambda h, qt=qt, sl=sl: h.transpose(
                    sl, s_aofm5[:, qt * 128:(qt + 1) * 128], s_identbf[:]),
                    [te0, te1, ring5.rd[bk]], inc=True)
                tev = DVE(lambda h, qt=qt, m=m, sl=sl: h.tensor_copy(
                    out=s_at3s[:, qt, :], in_=sl), [tp, t_at3s_use])
                ring5.rd[bk] = tev
                tevl = tev
            t_aofm_use = tevl
            tdm = b.dma("pool", "dK1",
                        at3d.ap().rearrange("(t p) (m f) -> p t m f", p=128,
                                            f=128)[:, :, m, :],
                        s_at3s,
                        deps=[tevl])
            t_at3s_use = tdm
            t_at3_dma.append(tdm)
        tok_at3 = t_at3_dma[-1]

        p5.close()
        b.barrier(("pe", "act", "dve", "pool"))
        p6 = ExitStack()
        ph1 = p6.enter_context(nc.psum_tensor("ph1", [32, 512], F32))
        pg1 = p6.enter_context(nc.psum_tensor("pg1", [8, 512], F32))
        pA = p6.enter_context(nc.psum_tensor("pA", [128, 512], F32))
        pf32 = p6.enter_context(nc.psum_tensor("pf32", [128, 512], F32))
        pg2 = pf32[:, 384:392]
        pwb = [p6.enter_context(nc.psum_tensor(f"pwb{i}", [128, 512], F32))
               for i in range(2)]
        ring6 = PbfRing(p6.enter_context(
            nc.psum_tensor("pbf6", [128, 2048], BF16))[:])
        p_s16 = pwb[0]

        X3 = regC_buf[:].rearrange("p (c t) -> p c t", t=TOK)

        lora = ExitStack()
        s_gw1bf = lora.enter_context(
            nc.sbuf_tensor("s_gw1bf", [128, NCH, 32], BF16))
        s_abf = lora.enter_context(
            nc.sbuf_tensor("s_abf", [128, NCH, 128], BF16))
        s_bbf = lora.enter_context(nc.sbuf_tensor("s_bbf", [128, HID], BF16))
        s_x6v = [regA_buf[:, 0:8192].bitcast(F32),
               regA_buf[:, 8192:16384].bitcast(F32)]
        s_y16v = regA_buf[:, 16384:24576].bitcast(F32)
        s_z6v = regA_buf[:, 24576:32768].bitcast(F32)
        s_qtm6v = p6.enter_context(nc.sbuf_tensor("s_qtm6", [128, HID], BF16))
        s_at3l = [p6.enter_context(
            nc.sbuf_tensor(f"s_at3l{i}", [128, HID], BF16)) for i in range(2)]
        s_hfm = p6.enter_context(nc.sbuf_tensor("s_hfm", [32, 128], F32))
        s_ht = p6.enter_context(nc.sbuf_tensor("s_ht", [128, 32], F32))
        s_hsq = p6.enter_context(nc.sbuf_tensor("s_hsq", [128, 32], F32))
        s_ht2 = p6.enter_context(nc.sbuf_tensor("s_ht2", [128, 32], F32))
        s_h2fm = p6.enter_context(nc.sbuf_tensor("s_h2fm", [32, 128], BF16))
        s_gfm = p6.enter_context(nc.sbuf_tensor("s_gfm", [8, 128], F32))
        s_gt = p6.enter_context(nc.sbuf_tensor("s_gt", [128, 8], F32))
        s_eq1 = p6.enter_context(nc.sbuf_tensor("s_eq1", [128, 8], F32))
        s_eq2 = p6.enter_context(nc.sbuf_tensor("s_eq2", [128, 8], F32))
        s_msk = p6.enter_context(nc.sbuf_tensor("s_msk", [128, 8], F32))
        s_wt = p6.enter_context(nc.sbuf_tensor("s_wt", [128, 8], F32))
        s_wt2 = p6.enter_context(nc.sbuf_tensor("s_wt2", [128, 8], F32))
        s_wfm = p6.enter_context(nc.sbuf_tensor("s_wfm", [8, TOK], BF16))
        s_wb = p6.enter_context(nc.sbuf_tensor("s_wb", [128, TOK], F32))
        s_tmp = p6.enter_context(nc.sbuf_tensor("s_tmp", [128, TOK], F32))

        t_gw = []
        for c in range(NCH):
            tg_ = b.dma("pool", "dM", s_cstg[:, 0:32],
                        gw1t[c * 128:(c + 1) * 128, :],
                        deps=[t_gw[-1][1] if t_gw else tok_at3])
            tcst = DVE(lambda h, c=c: h.tensor_copy(out=s_gw1bf[:, c, :],
                                                    in_=s_cstg[:, 0:32]),
                       [tg_])
            t_gw.append((tg_, tcst))
        t_g2 = b.dma("pool", "dM", s_hfm[0:32, 0:8], gw2t)
        t_g2c = DVE(lambda h: h.tensor_copy(out=s_gw2bf[:],
                                            in_=s_hfm[0:32, 0:8]), [t_g2])
        t_ab = []
        for c in range(NCH):
            ta_ = b.dma("pool", "dM", s_cstg[:],
                        a_all[c * 128:(c + 1) * 128, :],
                        deps=[t_ab[-1][1] if t_ab else t_gw[-1][1]])
            tac = DVE(lambda h, c=c: h.tensor_copy(out=s_abf[:, c, :],
                                                   in_=s_cstg[:]), [ta_])
            t_ab.append((ta_, tac))
        t_bl = b.dma("pool", "dM", s_x6v[0], b_all)
        t_blc = DVE(lambda h: h.tensor_copy(out=s_bbf[:], in_=s_x6v[0]),
                    [t_bl])
        t_x_use = [t_blc, None]

        tok_gstash = []
        t_ph1 = None
        t_pg1 = None
        t_pg2 = None
        t_pA = None
        t_pf32 = None
        t_cnt_prev = t_cnt0
        t_pbf = [None] * 8
        tok_x3 = None
        t_tmp_use = None
        t_spill = []

        prev_tx1 = None
        prev_tsp = None
        prev_tx3 = None
        t_a3_use = [None, None]
        for t in range(NT):
            par = t % 2
            xb = s_x6v[par]
            a3 = s_at3l[par]
            tl = b.dma("pool", "dX%d" % par, xb,
                       x_own[t * 128:(t + 1) * 128, :], deps=[t_x_use[par]])
            tla = b.dma("pool", "dK%d" % par, a3[:],
                        at3d.ap()[t * 128:(t + 1) * 128, :],
                        deps=[t_a3_use[par], tok_at3])
            r3, b3, tcp = ln_stats(a3[:], HID, [tla, prev_tx1, tok_x3],
                                   copy_dst=s_y16v, sq_dst=s_qtm6v[:])
            t_a3_use[par] = r3
            tx1 = DVE(lambda h, xb=xb: h.scalar_tensor_tensor(
                out=s_z6v, in0=s_y16v, scalar=c_r, in1=xb,
                op0=ALU.mult, op1=ALU.add),
                [r3, b3, tcp, tl, prev_tsp, prev_tx3])
            tx2 = DVE(lambda h: h.tensor_scalar(out=s_z6v, in0=s_z6v,
                                                scalar1=c_b, scalar2=None,
                                                op0=ALU.add), [tx1])
            t_x_use[par] = tx1
            prev_tx1 = tx1
            tsp = b.dma("pool", "dM", x2d.ap()[t * 128:(t + 1) * 128, :],
                        s_z6v, deps=[tx2])
            t_spill.append(tsp)
            prev_tsp = tsp
            r4, b4, _ = ln_stats(s_z6v, HID, [tx2],
                                 copy_dst=s_y16v, sq_dst=s_y16v)
            tx3 = ACT(lambda h: h.activation(out=s_qtm6v[:], in_=s_z6v,
                                             func=AF.Identity, bias=c_b,
                                             scale=c_r), [r4, b4, tok_x3])
            prev_tx3 = tx3
            for c in range(NCH):
                sl, bk = ring6.next()
                tp = PE(lambda h, c=c, sl=sl: h.transpose(
                    sl, s_qtm6v[:, c * 128:(c + 1) * 128], s_identbf[:]),
                    [tx3, ring6.rd[bk]], inc=True)
                tev = DVE(lambda h, c=c, t=t, sl=sl: h.tensor_copy(
                    out=X3[:, c, t * 128:(t + 1) * 128], in_=sl), [tp])
                ring6.rd[bk] = tev
                tok_x3 = tev

            tokm = None
            for c in range(NCH):
                dep = [t_gw[c][1]]
                if c == 0:
                    dep += [tok_x3, t_ph1]
                tokm = PE(lambda h, c=c, t=t: h.matmul(
                    ph1[:, 0:128], s_gw1bf[:, c, :],
                    X3[:, c, t * 128:(t + 1) * 128],
                    start=(c == 0), stop=(c == NCH - 1)),
                    dep, inc=(c == NCH - 1))
            th0 = ACT(lambda h: h.activation(out=s_hfm[:], in_=ph1[:, 0:128],
                                             func=AF.Identity, bias=s_gb1[:],
                                             scale=1.0), [tokm, t_ph1])
            t_ph1 = th0
            tht = PE(lambda h: h.transpose(pf32[0:128, 0:32], s_hfm[:],
                                           s_identf[0:32, 0:32]),
                     [th0, t_pf32, t_pg2], inc=True)
            the = DVE(lambda h: h.tensor_copy(out=s_ht[:],
                                              in_=pf32[0:128, 0:32]), [tht])
            t_pf32 = the
            ts1 = DVE(lambda h: h.tensor_reduce(out=c_t0, in_=s_ht[:],
                                                axis=AX.X, op=ALU.add), [the])
            ts2 = DVE(lambda h: h.tensor_tensor(out=s_hsq[:], in0=s_ht[:],
                                                in1=s_ht[:], op=ALU.mult),
                      [the])
            ts3 = DVE(lambda h: h.tensor_reduce(out=c_t1, in_=s_hsq[:],
                                                axis=AX.X, op=ALU.add), [ts2])
            ts4 = DVE(lambda h: h.tensor_scalar(out=c_t0, in0=c_t0,
                                                scalar1=1.0 / 32, scalar2=None,
                                                op0=ALU.mult), [ts1])
            ts5 = DVE(lambda h: h.tensor_tensor(out=c_sq, in0=c_t0, in1=c_t0,
                                                op=ALU.mult), [ts4])
            ts6 = DVE(lambda h: h.scalar_tensor_tensor(out=c_t1, in0=c_t1,
                                                       scalar=1.0 / 32,
                                                       in1=c_sq, op0=ALU.mult,
                                                       op1=ALU.subtract),
                      [ts3, ts5])
            ts7 = ACT(lambda h: h.activation(out=c_t1, in_=c_t1, func=AF.Ln,
                                             bias=c_eps, scale=1.0),
                      [ts6, t_eps])
            ts8 = ACT(lambda h: h.activation(out=c_r, in_=c_t1, func=AF.Exp,
                                             bias=0.0, scale=-0.5), [ts7])
            ts9 = DVE(lambda h: h.tensor_tensor(out=c_b, in0=c_t0, in1=c_r,
                                                op=ALU.mult), [ts8, ts4])
            tsa = DVE(lambda h: h.tensor_scalar(out=c_b, in0=c_b, scalar1=-1.0,
                                                scalar2=None, op0=ALU.mult),
                      [ts9])
            tsb = ACT(lambda h: h.activation(out=s_ht2[:], in_=s_ht[:],
                                             func=AF.Identity, bias=c_b,
                                             scale=c_r), [ts8, tsa, the])
            tsc_ = DVE(lambda h: h.tensor_tensor(out=s_ht2[:], in0=s_ht2[:],
                                                 in1=s_g32[:], op=ALU.mult),
                       [tsb, t_const])
            tsd = DVE(lambda h: h.tensor_tensor(out=s_ht2[:], in0=s_ht2[:],
                                                in1=s_bt32[:], op=ALU.add),
                      [tsc_])
            tse = DVE(lambda h: h.tensor_relu(out=s_ht2[:], in_=s_ht2[:]),
                      [tsd])
            tsf = PE(lambda h: h.transpose(pf32[0:32, 0:128], s_ht2[:],
                                           s_identf[:]),
                     [tse, t_pf32], inc=True)
            tsg = DVE(lambda h: h.tensor_copy(out=s_h2fm[:],
                                              in_=pf32[0:32, 0:128]), [tsf])
            t_pf32 = tsg
            tsh = PE(lambda h: h.matmul(pg1[:, 0:128], s_gw2bf[:], s_h2fm[:],
                                        start=True, stop=True),
                     [tsg, t_g2c, t_pg1], inc=True)
            tsi = ACT(lambda h: h.activation(out=s_gfm[:], in_=pg1[:, 0:128],
                                             func=AF.Identity, bias=s_gb2[:],
                                             scale=1.0), [tsh])
            t_pg1 = tsi
            tsj = PE(lambda h: h.transpose(pg2, s_gfm[:],
                                           s_identf[0:8, 0:8]),
                     [tsi, t_pg2], inc=True)
            tsk = DVE(lambda h: h.tensor_copy(out=s_gt[:], in_=pg2), [tsj])
            t_pg2 = tsk
            tt1 = DVE(lambda h: h.tensor_reduce(out=c_m1, in_=s_gt[:],
                                                axis=AX.X, op=ALU.max), [tsk])
            tt2 = DVE(lambda h: h.tensor_scalar(out=s_eq1[:], in0=s_gt[:],
                                                scalar1=c_m1, scalar2=None,
                                                op0=ALU.is_equal), [tt1])
            tt3 = DVE(lambda h: h.scalar_tensor_tensor(out=s_msk[:],
                                                       in0=s_eq1[:],
                                                       scalar=-1e30,
                                                       in1=s_gt[:],
                                                       op0=ALU.mult,
                                                       op1=ALU.add), [tt2])
            tt4 = DVE(lambda h: h.tensor_reduce(out=c_m2, in_=s_msk[:],
                                                axis=AX.X, op=ALU.max), [tt3])
            tt5 = DVE(lambda h: h.tensor_scalar(out=s_eq2[:], in0=s_msk[:],
                                                scalar1=c_m2, scalar2=None,
                                                op0=ALU.is_equal), [tt4])
            tt6 = DVE(lambda h: h.tensor_tensor(out=c_d, in0=c_m2, in1=c_m1,
                                                op=ALU.subtract), [tt4, tt1])
            tt7 = ACT(lambda h: h.activation(out=c_e2, in_=c_d, func=AF.Exp,
                                             bias=0.0, scale=1.0), [tt6])
            tt8 = DVE(lambda h: h.tensor_scalar(out=c_rg, in0=c_e2,
                                                scalar1=1.0, scalar2=None,
                                                op0=ALU.add), [tt7])
            tt9 = DVE(lambda h: h.reciprocal(out=c_rg, in_=c_rg), [tt8])
            tta = DVE(lambda h: h.tensor_scalar(out=c_g1, in0=c_rg,
                                                scalar1=ALPHA, scalar2=None,
                                                op0=ALU.mult), [tt9])
            ttb = DVE(lambda h: h.tensor_tensor(out=c_g2, in0=c_e2, in1=c_rg,
                                                op=ALU.mult), [tt7, tt9])
            ttc = DVE(lambda h: h.tensor_scalar(out=c_g2, in0=c_g2,
                                                scalar1=ALPHA, scalar2=None,
                                                op0=ALU.mult), [ttb])
            ttd = DVE(lambda h, t=t: h.tensor_scalar(out=s_c1[:, t, :],
                                                     in0=s_eq1[:],
                                                     scalar1=c_g1,
                                                     scalar2=None,
                                                     op0=ALU.mult),
                      [tta, tt2])
            tte = DVE(lambda h, t=t: h.tensor_scalar(out=s_c2[:, t, :],
                                                     in0=s_eq2[:],
                                                     scalar1=c_g2,
                                                     scalar2=None,
                                                     op0=ALU.mult),
                      [ttc, tt5])
            ttf = DVE(lambda h: h.tensor_tensor(out=s_cnt[:, 0:8],
                                                in0=s_cnt[:, 0:8],
                                                in1=s_eq1[:], op=ALU.add),
                      [tt2, t_cnt_prev])
            ttg = DVE(lambda h: h.tensor_tensor(out=s_cnt[:, 8:16],
                                                in0=s_cnt[:, 8:16],
                                                in1=s_eq2[:], op=ALU.add),
                      [tt5, ttf])
            t_cnt_prev = ttg
            tok_gstash.append((ttd, tte))

            tokA = None
            for c in range(NCH):
                dep = [t_ab[c][1]]
                if c == 0:
                    dep += [tok_x3, t_pA]
                tokA = PE(lambda h, c=c, t=t: h.matmul(
                    pA[:, 0:128], s_abf[:, c, :], X3[:, c, t * 128:(t + 1) * 128],
                    start=(c == 0), stop=(c == NCH - 1)),
                    dep, inc=(c == NCH - 1))
            tAe = DVE(lambda h, t=t: h.tensor_copy(
                out=s_tmp[:, t * 128:(t + 1) * 128], in_=pA[:, 0:128]), [tokA])
            t_pA = tAe
            t_tmp_use = tAe

        tcs = PE(lambda h: h.matmul(p_s16[0:1, 0:16], s_onesf[:, 0:1],
                                    s_cnt[:], start=True, stop=True),
                 [t_cnt_prev], inc=True)
        tce = DVE(lambda h: h.tensor_copy(out=s_cntrow[:],
                                          in_=p_s16[0:1, 0:16]), [tcs])
        tcd = b.dma("pool", "dM", cnt_in.ap(), s_cntrow[:], deps=[tce])
        tok_cr = b.coll("AllReduce", ALU.add, [[0, 1, 2, 3, 4, 5, 6, 7]],
                        [cnt_in.ap().opt()], [cnt_out.ap().opt()],
                        deps=[tcd], ch="cc2")
        tcb = b.dma("pool", "dM", s_cntrow[:], cnt_out.ap(), deps=[tok_cr])
        tal = DVE(lambda h: h.tensor_scalar(out=s_cntrow[:], in0=s_cntrow[:],
                                            scalar1=CAPACITY, scalar2=None,
                                            op0=ALU.is_le), [tcb])
        tab = PE(lambda h: h.matmul(pA[:, 0:16], s_onesf[0:1, :],
                                    s_cntrow[:], start=True, stop=True),
                 [tal, t_pA], inc=True)
        tae = DVE(lambda h: h.tensor_copy(out=s_alb[:], in_=pA[:, 0:16]),
                  [tab])
        t_wfm = None
        for t in range(NT):
            tw1 = DVE(lambda h, t=t: h.tensor_tensor(
                out=s_wt[:], in0=s_c1[:, t, :], in1=s_alb[:, 0:8],
                op=ALU.mult), [tae, tok_gstash[t][0]])
            tw2 = DVE(lambda h, t=t: h.tensor_tensor(
                out=s_wt2[:], in0=s_c2[:, t, :], in1=s_alb[:, 8:16],
                op=ALU.mult), [tae, tok_gstash[t][1]])
            tw3 = DVE(lambda h: h.tensor_tensor(out=s_wt[:], in0=s_wt[:],
                                                in1=s_wt2[:], op=ALU.add),
                      [tw1, tw2])
            tw4 = PE(lambda h: h.transpose(pf32[0:8, 0:128], s_wt[:],
                                           s_identf[:]),
                     [tw3, t_pf32, t_pg2], inc=True)
            tw5 = DVE(lambda h, t=t: h.tensor_copy(
                out=s_wfm[:, t * 128:(t + 1) * 128],
                in_=pf32[0:8, 0:128]), [tw4])
            t_pf32 = tw5
            t_wfm = tw5
        twb0 = PE(lambda h: h.matmul(pwb[0][:], s_selS[:], s_wfm[:, 0:512],
                                     start=True, stop=True), [t_wfm],
                  inc=True)
        twb1 = PE(lambda h: h.matmul(pwb[1][:], s_selS[:], s_wfm[:, 512:1024],
                                     start=True, stop=True), [t_wfm],
                  inc=True)
        twe0 = DVE(lambda h: h.tensor_copy(out=s_wb[:, 0:512], in_=pwb[0][:]),
                   [twb0])
        twe1 = DVE(lambda h: h.tensor_copy(out=s_wb[:, 512:1024],
                                           in_=pwb[1][:]), [twb1])
        tok_tsc = DVE(lambda h: h.tensor_tensor(out=s_tsc[:], in0=s_tmp[:],
                                                in1=s_wb[:], op=ALU.mult),
                      [twe0, twe1, t_tmp_use])

        p6.close()
        b.barrier(("pe", "act", "dve"))
        p7 = ExitStack()
        pproj7 = [p7.enter_context(nc.psum_tensor(f"pp7{i}", [128, 512], F32))
                  for i in range(4)]
        ring7 = PbfRing(p7.enter_context(
            nc.psum_tensor("pbf7", [128, 2048], BF16))[:])
        open_stream_bufs(p7)
        s_aofm7v = p7.enter_context(nc.sbuf_tensor("s_aofm7", [128, TOK], BF16))

        F3 = regA_buf[:].rearrange("p (t f) -> p t f", f=HID)

        t_pp7 = [None, None, None, None]
        t_aofm_use = None
        t_pbf = [None] * 8
        tok_f3 = None
        for m in range(NCH):
            pr = m % 2
            pp = pproj7[2 * pr:2 * pr + 2]
            for hf in range(2):
                xd = [tok_tsc, t_spill[-1]] if m == 0 else []
                par, trdy = ws.next(wm[m, :, hf * 2048:(hf + 1) * 2048],
                                    "cast", extra_deps=xd)
                tokm = None
                for j in range(16):
                    c = hf * 16 + j
                    for tg in range(2):
                        dep = [trdy, tok_x3]
                        if c == 0:
                            dep.append(t_pp7[2 * pr + tg])
                        tokm = PE(lambda h, wb_=ws.bf(par), j=j, c=c, tg=tg,
                                  pp_=pp[tg]: h.matmul(
                            pp_[:],
                            wb_[:, j * 128:(j + 1) * 128],
                            X3[:, c, tg * 512:(tg + 1) * 512],
                            start=(c == 0), stop=False),
                            dep, inc=(j == 15 and tg == 1))
                ws.consumed(par, tokm)
            tlb_ = [None, None]
            for tg in range(2):
                tlb_[tg] = PE(lambda h, m=m, tg=tg, pp_=pp[tg]: h.matmul(
                    pp_[:], s_bbf[:, m * 128:(m + 1) * 128],
                    s_tsc[:, tg * 512:(tg + 1) * 512],
                    start=False, stop=True),
                    [tok_tsc, t_blc], inc=True)
            te0 = ACT(lambda h, pp_=pp[0]: h.activation(
                out=s_aofm7v[:, 0:512], in_=pp_[:], func=AF.Copy),
                [tlb_[0], t_aofm_use])
            te1 = ACT(lambda h, pp_=pp[1]: h.activation(
                out=s_aofm7v[:, 512:1024], in_=pp_[:], func=AF.Copy),
                [tlb_[1]])
            t_pp7[2 * pr], t_pp7[2 * pr + 1] = te0, te1
            tevl = None
            for qt in range(NT):
                sl, bk = ring7.next()
                tp = PE(lambda h, qt=qt, sl=sl: h.transpose(
                    sl, s_aofm7v[:, qt * 128:(qt + 1) * 128], s_identbf[:]),
                    [te0, te1, ring7.rd[bk]], inc=True)
                tev = DVE(lambda h, qt=qt, m=m, sl=sl: h.tensor_copy(
                    out=F3[:, qt, m * 128:(m + 1) * 128], in_=sl),
                    [tp, t_spill[-1]])
                ring7.rd[bk] = tev
                tevl = tev
            t_aofm_use = tevl
        tok_f3 = t_aofm_use

        p7.close()
        lora.close()
        b.barrier(("pe", "act", "dve", "pool"))
        p8 = ExitStack()
        s_x8v = [p8.enter_context(nc.sbuf_tensor(f"s_x8{i}", [128, HID], F32))
               for i in range(2)]
        s_y18v = p8.enter_context(nc.sbuf_tensor("s_y18", [128, HID], F32))
        s_z8v = p8.enter_context(nc.sbuf_tensor("s_z8", [128, HID], F32))
        s_jnk8 = p8.enter_context(nc.sbuf_tensor("s_jnk8", [128, HID], BF16))
        t_x_use = [None, None]
        t_store = []
        tok_dM_all = ("dM", b.cnt["dM"])
        prev_tx1 = None
        prev_tst = None
        for t in range(NT):
            par = t % 2
            xb = s_x8v[par]
            tl = b.dma("pool", "dX%d" % par, xb[:],
                       x2d.ap()[t * 128:(t + 1) * 128, :],
                       deps=[t_x_use[par], tok_dM_all])
            r5, b5, tcp = ln_stats(F3[:, t, :], HID, [tok_f3, prev_tx1],
                                   copy_dst=s_y18v[:], sq_dst=s_jnk8[:])
            tx1 = DVE(lambda h, xb=xb: h.scalar_tensor_tensor(
                out=s_z8v[:], in0=s_y18v[:], scalar=c_r, in1=xb[:],
                op0=ALU.mult, op1=ALU.add), [r5, b5, tcp, tl, prev_tst])
            tx2 = DVE(lambda h: h.tensor_scalar(out=s_z8v[:], in0=s_z8v[:],
                                                scalar1=c_b, scalar2=None,
                                                op0=ALU.add), [tx1])
            t_x_use[par] = tx1
            prev_tx1 = tx1
            tst = b.dma("pool", "dS", y_own[t * 128:(t + 1) * 128, :],
                        s_z8v[:], deps=[tx2])
            t_store.append(tst)
            prev_tst = tst
        if DEBUG_DUMPS:
            b.dma("pool", "dM", dbg_ccg.ap(), ccg_in.ap(), deps=t_stage)
            b.dma("pool", "dM", dbg_at3.ap(), at3d.ap(), deps=[tok_at3])
            b.dma("pool", "dM", dbg_x2.ap(), x2d.ap(), deps=[t_spill[-1]])
            tsc_ = b.dma("pool", "dM", dbg_sc.ap()[:, 0:12], s_scales[:],
                         deps=[t_pb_use[0]])
            b.dma("pool", "dM", dbg_sc.ap()[0:1, 12:12 + NH], s_qn2[0:1, :],
                  deps=[t_pnorm[0]])
            b.dma("pool", "dM", dbg_sc.ap()[0:1, 12 + NH:12 + NH + 1],
                  s_negM[0:1, :], deps=[tok_negM])
            t_store.append(b.dma("pool", "dS", dbg_sc.ap()[1:2, 12:20],
                                 s_kn2[0:1, :], deps=[t_kn_back]))
        b.op("sp", lambda h: h.nop(), deps=[t_store[-1]], inc=False)

        with nc.Block() as block:
            @block.tensor
            def _(h):
                for f in b.stream["pe"]:
                    f(h)

            @block.scalar
            def _(h):
                for f in b.stream["act"]:
                    f(h)

            @block.vector
            def _(h):
                for f in b.stream["dve"]:
                    f(h)

            @block.gpsimd
            def _(h):
                for f in b.stream["pool"]:
                    f(h)

            @block.sync
            def _(h):
                for f in b.stream["sp"]:
                    f(h)

        p8.close()

    return nc


def _prep_inputs(inputs):
    f32 = np.float32
    x = np.ascontiguousarray(np.asarray(inputs["L0_attn_hidden_states"], f32))
    Wq = np.asarray(inputs["L0_attn_Wq"], f32)
    Wk = np.asarray(inputs["L0_attn_Wk"], f32)
    Wv = np.asarray(inputs["L0_attn_Wv"], f32)
    Wo = np.asarray(inputs["L0_attn_Wo"], f32)
    Wm = np.asarray(inputs["L0_ffn_weight"], f32)
    lA = np.asarray(inputs["L0_ffn_lora_A"], f32)
    lB = np.asarray(inputs["L0_ffn_lora_B"], f32)
    gw1 = np.asarray(inputs["L0_ffn_gw1"], f32)
    gb1 = np.asarray(inputs["L0_ffn_gb1"], f32)
    lng = np.asarray(inputs["L0_ffn_ln_gamma"], f32)
    lnbta = np.asarray(inputs["L0_ffn_ln_beta"], f32)
    gw2 = np.asarray(inputs["L0_ffn_gw2"], f32)
    gb2 = np.asarray(inputs["L0_ffn_gb2"], f32)
    ln_w = np.asarray(inputs["L0_attn_ln_w"], f32)
    ln_b = np.asarray(inputs["L0_attn_ln_b"], f32)

    def blk(W):
        nm = W.shape[0] // 128
        return np.ascontiguousarray(
            W.reshape(nm, 128, NCH, 128).transpose(0, 3, 2, 1)
        ).reshape(nm, 128, HID)

    shared = {
        "wq": blk(Wq), "wk": blk(Wk), "wv": blk(Wv), "wo": blk(Wo),
        "wm": blk(Wm),
        "a_all": np.ascontiguousarray(lA.transpose(1, 0, 2).reshape(HID, 128)),
        "b_all": np.ascontiguousarray(lB.reshape(128, HID)),
        "gw1t": np.ascontiguousarray(gw1.T),
        "gb1c": np.ascontiguousarray(gb1.reshape(32, 1)),
        "g32": np.ascontiguousarray(np.broadcast_to(lng, (128, 32))),
        "bt32": np.ascontiguousarray(np.broadcast_to(lnbta, (128, 32))),
        "gw2t": np.ascontiguousarray(gw2.T),
        "gb2c": np.ascontiguousarray(gb2.reshape(8, 1)),
        "lnw": np.ascontiguousarray(np.broadcast_to(ln_w, (128, HID))),
        "lnb": np.ascontiguousarray(np.broadcast_to(ln_b, (128, HID))),
        "identbf": np.eye(128, dtype=ml_dtypes.bfloat16),
        "identf": np.eye(128, dtype=f32),
        "onesf": np.ones((128, 128), dtype=f32),
        "onesbf": np.ones((128, 1), dtype=ml_dtypes.bfloat16),
    }
    S = np.zeros((8, 128), dtype=ml_dtypes.bfloat16)
    for p in range(128):
        S[p // 16, p] = 1.0
    shared["selS"] = S

    inv = (1.0 / (np.float32(10000.0) **
                  (np.arange(0, 128, 2, dtype=f32) / f32(128.0)))).astype(f32)
    maps = []
    for core in range(8):
        bb, hh = core // 2, core % 2
        pos = np.arange(hh * TOK, (hh + 1) * TOK, dtype=f32)
        freqs = pos[None, :] * inv[:, None]
        emb = np.concatenate([freqs, freqs], axis=0)
        m = dict(shared)
        m["cosb"] = np.cos(emb).astype(f32)
        m["sinb"] = np.sin(emb).astype(f32)
        m["x_own"] = np.ascontiguousarray(x[bb, hh * TOK:(hh + 1) * TOK, :])
        maps.append(m)
    return maps


def kernel(**inputs):
    maps = _prep_inputs(inputs)
    nc = build_nc()
    res = run_bass_kernel_spmd(nc, maps, core_ids=list(range(8)))
    out = np.empty((BATCH, SEQ, HID), dtype=np.float32)
    for core in range(8):
        bb, hh = core // 2, core % 2
        out[bb, hh * TOK:(hh + 1) * TOK, :] = np.asarray(res.results[core]["y"])
    return out


# version 10
# speedup vs baseline: 1.2642x; 1.0348x over previous
import numpy as np
import ml_dtypes
from contextlib import ExitStack

import concourse.bass as bass
import concourse.mybir as mybir
from concourse.bass_utils import run_bass_kernel_spmd

F32 = mybir.dt.float32
BF16 = mybir.dt.bfloat16
I32 = mybir.dt.int32
AF = mybir.ActivationFunctionType
ALU = mybir.AluOpType
AX = mybir.AxisListType

HID = 4096
NH = 32
NKV = 8
BATCH = 4
SEQ = 2048
TOK = 1024
NT = 8
NCH = 32
EPS = 1e-5
QB = 127.0
CAPACITY = 3072.0
ALPHA = 1.0 / 16.0
SM_SCALE = 0.08838834764831845
SHIFT_CAP = 80.0

DEBUG_DUMPS = False
ENGS = ("pe", "act", "dve", "pool", "sp")
DMA_CH = ("dW0", "dW1", "dX0", "dX1", "dK0", "dK1", "dM", "dS", "cc0", "cc1", "cc2")


class Bld:
    def __init__(self, nc, ex):
        self.nc = nc
        self.sem = {ch: ex.enter_context(nc.semaphore("sem_" + ch))
                    for ch in ENGS + DMA_CH}
        self.cnt = {ch: 0 for ch in ENGS + DMA_CH}
        self.waited = {e: {} for e in ENGS}
        self.stream = {e: [] for e in ENGS}
        self.last_inc = {e: True for e in ENGS}

    def _wait(self, eng, tok):
        ch, v = tok
        if v <= 0 or self.waited[eng].get(ch, 0) >= v:
            return
        self.waited[eng][ch] = v
        sem = self.sem[ch]
        self.stream[eng].append(lambda h, sem=sem, v=v: h.wait_ge(sem, v))

    def op(self, eng, fn, deps=(), inc=True):
        for t in deps:
            if t is not None:
                self._wait(eng, t)
        tok = None
        if inc:
            self.cnt[eng] += 1
            tok = (eng, self.cnt[eng])
            sem = self.sem[eng]
            self.stream[eng].append(lambda h, fn=fn, sem=sem: fn(h).then_inc(sem, 1))
        else:
            self.stream[eng].append(lambda h, fn=fn: fn(h))
        self.last_inc[eng] = bool(inc)
        return tok

    def dma(self, eng, ch, out, in_, deps=()):
        for t in deps:
            if t is not None:
                self._wait(eng, t)
        self.cnt[ch] += 16
        tok = (ch, self.cnt[ch])
        sem = self.sem[ch]
        self.stream[eng].append(
            lambda h, out=out, in_=in_, sem=sem:
                h.dma_start(out=out, in_=in_).then_inc(sem, 16))
        self.last_inc[eng] = False
        return tok

    def coll(self, kind, alu, groups, ins, outs, deps=(), ch="cc0"):
        for t in deps:
            if t is not None:
                self._wait("pool", t)
        self.cnt[ch] += 1
        tok = (ch, self.cnt[ch])
        sem = self.sem[ch]
        self.stream["pool"].append(
            lambda h, sem=sem: h.collective_compute(
                kind, alu, replica_groups=groups, ins=ins, outs=outs
            ).then_inc(sem, 1))
        self.last_inc["pool"] = False
        return tok

    def barrier(self, engines=("pe", "act", "dve", "pool")):
        for e in engines:
            if not self.last_inc[e] and self.stream[e]:
                self.cnt[e] += 1
                sem = self.sem[e]
                self.stream[e].append(
                    lambda h, sem=sem: h.nop().then_inc(sem, 1))
                self.last_inc[e] = True
        toks = [(e, self.cnt[e]) for e in engines]
        for e in engines:
            for t in toks:
                if t[0] != e:
                    self._wait(e, t)


def build_nc():
    nc = bass.Bass("TRN2", target_bir_lowering=False,
                   detect_race_conditions=False)

    def din(name, shape, dt=F32):
        return nc.dram_tensor(name, shape, dt, kind="ExternalInput").ap()

    x_own = din("x_own", [TOK, HID])
    wq = din("wq", [NH, 128, HID])
    wk = din("wk", [NKV, 128, HID])
    wv = din("wv", [NKV, 128, HID])
    wo = din("wo", [NH, 128, HID])
    wm = din("wm", [NCH, 128, HID])
    a_all = din("a_all", [HID, 128])
    b_all = din("b_all", [128, HID])
    gw1t = din("gw1t", [HID, 32])
    gb1c = din("gb1c", [32, 1])
    g32 = din("g32", [128, 32])
    bt32 = din("bt32", [128, 32])
    gw2t = din("gw2t", [32, 8])
    gb2c = din("gb2c", [8, 1])
    lnw = din("lnw", [128, HID])
    lnb = din("lnb", [128, HID])
    cosb = din("cosb", [128, TOK])
    sinb = din("sinb", [128, TOK])
    identbf = din("identbf", [128, 128], BF16)
    identf = din("identf", [128, 128])
    onesf = din("onesf", [128, 128])
    onesbf = din("onesbf", [128, 1], BF16)
    selS = din("selS", [8, 128], BF16)

    y_own = nc.dram_tensor("y", [TOK, HID], F32, kind="ExternalOutput").ap()

    x2d = nc.dram_tensor("x2d", [TOK, HID], F32)
    at3d = nc.dram_tensor("at3d", [TOK, HID], BF16)
    ccg_in = nc.dram_tensor("ccg_in", [2 * NKV * 128, TOK], BF16)
    ccg_out = nc.dram_tensor("ccg_out", [2 * 2 * NKV * 128, TOK], BF16)
    cnt_in = nc.dram_tensor("cnt_in", [1, 16], F32)
    if DEBUG_DUMPS:
        dbg_qn = nc.dram_tensor("dbg_qn", [128, NCH * TOK], BF16,
                                kind="ExternalOutput")
        dbg_q3 = nc.dram_tensor("dbg_q3", [128, NCH * TOK], BF16,
                                kind="ExternalOutput")
        dbg_b3 = nc.dram_tensor("dbg_b3", [128, NCH * TOK], BF16,
                                kind="ExternalOutput")
        dbg_ccg = nc.dram_tensor("dbg_ccg", [2 * NKV * 128, TOK], BF16,
                                 kind="ExternalOutput")
        dbg_at3 = nc.dram_tensor("dbg_at3", [TOK, HID], BF16,
                                 kind="ExternalOutput")
        dbg_x2 = nc.dram_tensor("dbg_x2", [TOK, HID], F32,
                                kind="ExternalOutput")
        dbg_sc = nc.dram_tensor("dbg_sc", [128, 64], F32,
                                kind="ExternalOutput")
    cnt_out = nc.dram_tensor("cnt_out", [1, 16], F32)
    kn_in = nc.dram_tensor("kn_in", [1, NKV], F32)
    kn_out = nc.dram_tensor("kn_out", [1, NKV], F32)

    ex = ExitStack()
    with ex:
        b = Bld(nc, ex)
        E = ex.enter_context

        def ACT(fn, deps=(), inc=True):
            return b.op("act", fn, deps, inc)

        def DVE(fn, deps=(), inc=True):
            return b.op("dve", fn, deps, inc)

        def PE(fn, deps=(), inc=False):
            return b.op("pe", fn, deps, inc)

        s_identbf = E(nc.sbuf_tensor("s_identbf", [128, 128], BF16))
        s_identf = E(nc.sbuf_tensor("s_identf", [128, 128], F32))
        s_onesf = E(nc.sbuf_tensor("s_onesf", [128, 128], F32))
        s_onesbf = E(nc.sbuf_tensor("s_onesbf", [128, 1], BF16))
        s_selS = E(nc.sbuf_tensor("s_selS", [8, 128], BF16))
        s_g32 = E(nc.sbuf_tensor("s_g32", [128, 32], F32))
        s_bt32 = E(nc.sbuf_tensor("s_bt32", [128, 32], F32))
        s_gb1 = E(nc.sbuf_tensor("s_gb1", [32, 1], F32))
        s_gb2 = E(nc.sbuf_tensor("s_gb2", [8, 1], F32))
        s_gw2bf = E(nc.sbuf_tensor("s_gw2bf", [32, 8], BF16))
        s_scales = E(nc.sbuf_tensor("s_scales", [128, 12], F32))
        s_negM = E(nc.sbuf_tensor("s_negM", [128, 1], F32))
        s_stat = E(nc.sbuf_tensor("s_stat", [128, 48], F32))
        s_row = E(nc.sbuf_tensor("s_row", [1, 64], F32))
        s_qn2 = E(nc.sbuf_tensor("s_qn2", [1, NH], F32))
        s_kn2 = E(nc.sbuf_tensor("s_kn2", [1, NKV], F32))
        s_wcol = E(nc.sbuf_tensor("s_wcol", [128, 160], F32))
        s_cnt = E(nc.sbuf_tensor("s_cnt", [128, 16], F32))
        s_cntrow = E(nc.sbuf_tensor("s_cntrow", [1, 16], F32))
        s_alb = E(nc.sbuf_tensor("s_alb", [128, 16], F32))
        s_c1 = E(nc.sbuf_tensor("s_c1", [128, NT, 8], F32))
        s_c2 = E(nc.sbuf_tensor("s_c2", [128, NT, 8], F32))
        s_tsc = E(nc.sbuf_tensor("s_tsc", [128, TOK], BF16))
        s_cstg = E(nc.sbuf_tensor("s_cstg", [128, 128], F32))

        regA_buf = E(nc.sbuf_tensor("regA", [128, NCH * TOK], BF16))
        regC_buf = E(nc.sbuf_tensor("regC", [128, NCH * TOK], BF16))

        strm = ExitStack()
        SB = {}

        _sbn = [0]

        def open_stream_base(st):
            _sbn[0] += 1
            n = _sbn[0]
            SB["blk"] = [st.enter_context(
                nc.sbuf_tensor(f"s_wblk{i}_{n}", [128, 2048], F32))
                for i in range(2)]
            SB["bf"] = [st.enter_context(
                nc.sbuf_tensor(f"s_wbf{i}_{n}", [128, 2048], BF16))
                for i in range(2)]
            SB["junk"] = st.enter_context(
                nc.sbuf_tensor(f"s_wjunk_{n}", [128, 2048], F32))

        def open_stream_qi(st):
            n = _sbn[0]
            SB["qi"] = [st.enter_context(
                nc.sbuf_tensor(f"s_wqi{i}_{n}", [128, 2048], I32))
                for i in range(2)]

        def open_stream_bufs(st):
            open_stream_base(st)
            open_stream_qi(st)

        open_stream_base(strm)

        pscl = ExitStack()
        p_s1 = pscl.enter_context(nc.psum_tensor("p_s1", [1, 512], F32))
        pb128 = pscl.enter_context(nc.psum_tensor("pb128", [128, 512], F32))

        class PbfRing:

            def __init__(self, ap):
                self.ap = ap
                self.rd = [None, None]
                self.i = 0

            def next(self):
                s = self.i % 8
                self.i += 1
                bk = s % 2
                c0 = bk * 1024 + (s // 2) * 128
                return self.ap[:, c0:c0 + 128], bk

        def _mkset(base):
            names = ("sum", "sq", "r", "b", "am", "s", "inv", "t0", "t1",
                     "den", "m1", "m2", "d", "e2", "rg", "g1", "g2")
            d = {nm: s_stat[:, base + i:base + i + 1]
                 for i, nm in enumerate(names)}
            d["last"] = None
            return d

        SCP = [_mkset(0), _mkset(17)]
        sc0 = SCP[0]
        c_sum = sc0["sum"]
        c_sq = sc0["sq"]
        c_r = sc0["r"]
        c_b = sc0["b"]
        c_am = sc0["am"]
        c_s = sc0["s"]
        c_inv = sc0["inv"]
        c_t0 = sc0["t0"]
        c_t1 = sc0["t1"]
        c_den = sc0["den"]
        c_m1 = sc0["m1"]
        c_m2 = sc0["m2"]
        c_d = sc0["d"]
        c_e2 = sc0["e2"]
        c_rg = sc0["rg"]
        c_g1 = sc0["g1"]
        c_g2 = sc0["g2"]

        tc = []
        for dst, src in ((s_identbf, identbf), (s_identf, identf),
                         (s_onesf, onesf), (s_onesbf, onesbf), (s_selS, selS),
                         (s_g32, g32), (s_bt32, bt32), (s_gb1, gb1c),
                         (s_gb2, gb2c)):
            tc.append(b.dma("pool", "dM", dst[:], src))
        t_const = tc[-1]
        t_cnt0 = DVE(lambda h: h.memset(s_cnt[:], 0.0))
        c_eps = s_stat[:, 40:41]
        t_eps = DVE(lambda h: h.memset(c_eps, EPS))
        r_tiny = s_row[0:1, 62:63]
        t_tiny = DVE(lambda h: h.memset(r_tiny, 1e-30))

        def ln_stats(src_ap, n, deps, copy_dst, sq_dst, sc=None):
            if sc is None:
                sc = SCP[0]
            csum, csq, ct0, ct1, cr, cb = (sc["sum"], sc["sq"], sc["t0"],
                                           sc["t1"], sc["r"], sc["b"])
            deps = list(deps) + [sc["last"]]
            dst = copy_dst
            t2 = ACT(lambda h, dst=dst: h.activation(out=dst, in_=src_ap,
                                                     func=AF.Copy,
                                                     accum_out=csum), deps)
            t1 = ACT(lambda h, sq_dst=sq_dst: h.activation(
                out=sq_dst, in_=src_ap, func=AF.Square, accum_out=csq),
                     deps)
            t3 = DVE(lambda h: h.tensor_scalar(out=ct0, in0=csum,
                                               scalar1=1.0 / n, scalar2=None,
                                               op0=ALU.mult), [t2])
            t4 = DVE(lambda h: h.tensor_tensor(out=ct1, in0=ct0, in1=ct0,
                                               op=ALU.mult), [t3])
            t5 = DVE(lambda h: h.scalar_tensor_tensor(out=ct1, in0=csq,
                                                      scalar=1.0 / n, in1=ct1,
                                                      op0=ALU.mult,
                                                      op1=ALU.subtract),
                     [t1, t4])
            t6 = ACT(lambda h: h.activation(out=ct1, in_=ct1, func=AF.Ln,
                                            bias=c_eps, scale=1.0),
                     [t5, t_eps])
            t7 = ACT(lambda h: h.activation(out=cr, in_=ct1, func=AF.Exp,
                                            bias=0.0, scale=-0.5), [t6])
            t8 = DVE(lambda h: h.tensor_tensor(out=cb, in0=ct0, in1=cr,
                                               op=ALU.mult), [t7, t3])
            t9 = DVE(lambda h: h.tensor_scalar(out=cb, in0=cb, scalar1=-1.0,
                                               scalar2=None, op0=ALU.mult),
                     [t8])
            return t7, t9, t2

        def emit_act_quant(src_col, out_col, y1ap, deps_in, n_chunks=4,
                           sc=None):
            if sc is None:
                sc = SCP[0]
            cs_, cinv_ = sc["s"], sc["inv"]
            A = y1ap[:, 0:1024]
            Bi = y1ap[:, 1024:2048].bitcast(I32)
            C = y1ap[:, 2048:3072]
            Es = y1ap[:, 3072:4096]
            tA = tQ = None
            for k in range(n_chunks):
                tA = DVE(lambda h, k=k: h.tensor_scalar(
                    out=A, in0=src_col(k), scalar1=cs_, scalar2=None,
                    op0=ALU.mult), list(deps_in) + [tQ])
                tB = DVE(lambda h: h.tensor_copy(out=Bi, in_=A), [tA])
                tC = DVE(lambda h: h.tensor_tensor(out=C, in0=A, in1=Bi,
                                                   op=ALU.subtract), [tB])
                tD = DVE(lambda h: h.tensor_scalar(out=A, in0=C, scalar1=0.5,
                                                   scalar2=None,
                                                   op0=ALU.is_ge), [tC])
                tE = DVE(lambda h: h.scalar_tensor_tensor(
                    out=Es, in0=C, scalar=-0.5, in1=A, op0=ALU.is_le,
                    op1=ALU.subtract), [tD])
                tF = DVE(lambda h: h.tensor_tensor(out=C, in0=Bi, in1=Es,
                                                   op=ALU.subtract), [tE])
                tQ = DVE(lambda h, k=k: h.tensor_scalar(
                    out=out_col(k), in0=C, scalar1=cinv_, scalar2=None,
                    op0=ALU.mult), [tF])
            sc["last"] = tQ
            return tA, tQ

        class HalfStream:
            def __init__(self):
                self.i = 0
                self.t_q1 = [None, None]
                self.t_q2 = [None, None]
                self.t_mm = [None, None]
                self.t_abs = [None, None]

            def next(self, src_slice, mode, scol=0, wcol=0, extra_deps=(),
                     scol_c2=(8,), par=None):
                if par is None:
                    par = self.i % 2
                    self.i += 1
                blk, bfb, jnk = SB["blk"][par], SB["bf"][par], SB["junk"]
                qib = SB["qi"][par] if "qi" in SB else None
                tl = b.dma("sp", "dW%d" % par, blk[:], src_slice,
                           deps=[self.t_q1[par], self.t_abs[par]] +
                                list(extra_deps))
                if mode == "abs":
                    ta = ACT(lambda h, blk=blk, jnk=jnk, wcol=wcol:
                             h.activation(
                        out=jnk[:], in_=blk[:], func=AF.Abs,
                        accum_out=s_wcol[:, wcol:wcol + 1]), [tl])
                    self.t_abs[par] = ta
                    return par, ta
                if mode == "quant":
                    c2col = scol_c2[0]
                    qif = qib[:].bitcast(F32)
                    tq1 = DVE(lambda h, blk=blk, qif=qif, scol=scol:
                              h.tensor_scalar(
                        out=qif, in0=blk[:],
                        scalar1=s_scales[:, scol:scol + 1], scalar2=0.5,
                        op0=ALU.mult, op1=ALU.is_ge), [tl, self.t_q2[par]])
                    tq2 = DVE(lambda h, bfb=bfb, qif=qif, blk=blk,
                              c2col=c2col: h.scalar_tensor_tensor(
                        out=bfb[:], in0=blk[:],
                        scalar=s_scales[:, c2col:c2col + 1], in1=qif,
                        op0=ALU.is_le, op1=ALU.subtract),
                        [tq1, self.t_mm[par]])
                    self.t_q1[par] = tq2
                    self.t_q2[par] = tq2
                    return par, tq2
                tq = DVE(lambda h, blk=blk, bfb=bfb: h.tensor_copy(
                    out=bfb[:], in_=blk[:]),
                    [tl, self.t_mm[par]])
                self.t_q1[par] = tq
                self.t_q2[par] = tq
                return par, tq

            def bf(self, par):
                return SB["bf"][par]

            def consumed(self, par, tok):
                self.t_mm[par] = tok

        ws = HalfStream()

        scale_queue = []
        wc_base = {"wk": 0, "wv": 16, "wq": 32, "wo": 96}
        for nm_, src_, name_ in ((NKV, wk, "wk"), (NKV, wv, "wv"),
                                 (NH, wq, "wq"), (NH, wo, "wo")):
            for m in range(nm_):
                for hf in range(2):
                    scale_queue.append(
                        (name_, src_, m, hf, wc_base[name_] + 2 * m + hf))
        sq_pos = [0]
        sq_abs = {"wk": [], "wv": [], "wq": [], "wo": []}
        t_pb_use = [None]

        def emit_scale_halves(k):
            for _ in range(k):
                if sq_pos[0] >= len(scale_queue):
                    return
                name, src, m, hf, wcol = scale_queue[sq_pos[0]]
                sq_pos[0] += 1
                par, ta = ws.next(src[m, :, hf * 2048:(hf + 1) * 2048], "abs",
                                  wcol=wcol)
                sq_abs[name].append(ta)

        def finish_scale(name, nm, scol_s, scol_d, extra, slot):
            nelem = nm * 128 * HID
            cb = wc_base[name]
            tr = DVE(lambda h, nm=nm, cb=cb: h.tensor_reduce(
                out=c_t0, in_=s_wcol[:, cb:cb + 2 * nm], axis=AX.X,
                op=ALU.add), [sq_abs[name][-1]])
            tm1 = PE(lambda h, slot=slot: h.matmul(
                p_s1[0:1, slot:slot + 1], s_onesf[:, 0:1], c_t0,
                start=True, stop=True), [tr, t_pb_use[0], t_const], inc=True)
            r0 = 32 + slot * 4
            tm2 = DVE(lambda h, nelem=nelem, r0=r0, slot=slot: h.tensor_scalar(
                out=s_row[0:1, r0:r0 + 1], in0=p_s1[0:1, slot:slot + 1],
                scalar1=1.0 / nelem, scalar2=1e-5, op0=ALU.mult, op1=ALU.max),
                [tm1])
            tm3 = DVE(lambda h, r0=r0: h.reciprocal(
                out=s_row[0:1, r0 + 1:r0 + 2], in_=s_row[0:1, r0:r0 + 1]),
                [tm2])
            tm4 = DVE(lambda h, r0=r0, extra=extra: h.tensor_scalar(
                out=s_row[0:1, r0 + 2:r0 + 3], in0=s_row[0:1, r0:r0 + 1],
                scalar1=-extra, scalar2=None, op0=ALU.mult), [tm2])
            tm5 = DVE(lambda h, r0=r0: h.tensor_scalar(
                out=s_row[0:1, r0 + 3:r0 + 4], in0=s_row[0:1, r0:r0 + 1],
                scalar1=-0.5, scalar2=None, op0=ALU.mult), [tm2])
            tb1 = PE(lambda h, scol_d=scol_d, r0=r0: h.matmul(
                pb128[:, scol_d:scol_d + 1], s_onesf[0:1, :],
                s_row[0:1, r0 + 2:r0 + 3], start=True, stop=True),
                [tm4], inc=True)
            tb2 = PE(lambda h, scol_s=scol_s, r0=r0: h.matmul(
                pb128[:, scol_s:scol_s + 1], s_onesf[0:1, :],
                s_row[0:1, r0 + 1:r0 + 2], start=True, stop=True),
                [tm3], inc=True)
            tb3 = PE(lambda h, slot=slot, r0=r0: h.matmul(
                pb128[:, 8 + slot:9 + slot], s_onesf[0:1, :],
                s_row[0:1, r0 + 3:r0 + 4], start=True, stop=True),
                [tm5], inc=True)
            lo, hi = min(scol_s, scol_d), max(scol_s, scol_d)
            te_ = DVE(lambda h, lo=lo, hi=hi: h.tensor_copy(
                out=s_scales[:, lo:hi + 1], in_=pb128[:, lo:hi + 1]),
                [tb1, tb2, tb3])
            te = DVE(lambda h, slot=slot: h.tensor_copy(
                out=s_scales[:, 8 + slot:9 + slot],
                in_=pb128[:, 8 + slot:9 + slot]), [tb3, te_])
            t_pb_use[0] = te
            return te

        qn3 = regA_buf[:].rearrange("p (c t) -> p c t", t=TOK)

        scope1 = ExitStack()
        pbf1s = ExitStack()
        ring1 = PbfRing(pbf1s.enter_context(
            nc.psum_tensor("pbf1", [128, 2048], BF16))[:])
        s_x = [scope1.enter_context(nc.sbuf_tensor(f"s_x{i}", [128, HID], F32))
               for i in range(2)]
        s_y1 = regC_buf[:, 0:8192].bitcast(F32)
        s_lnw = regC_buf[:, 8192:16384].bitcast(F32)
        s_lnb = regC_buf[:, 16384:24576].bitcast(F32)
        s_qtm = regC_buf[:, 24576:28672]

        t_lnw = b.dma("pool", "dM", s_lnw, lnw)
        t_lnb = b.dma("pool", "dM", s_lnb, lnb)

        t_x_use = [None, None]
        t_pbf = [None] * 8
        tok_qn = None
        prev_tq1 = None
        prev_tq2 = None
        prev_tz = None
        for t in range(NT):
            par = t % 2
            xb = s_x[par]
            tl = b.dma("pool", "dX%d" % par, xb[:],
                       x_own[t * 128:(t + 1) * 128, :], deps=[t_x_use[par]])
            sc = SCP[par]
            scr, scb, scam, scs, scinv = (sc["r"], sc["b"], sc["am"],
                                          sc["s"], sc["inv"])
            r1, b1, _ = ln_stats(xb[:], HID, [tl, prev_tq2],
                                 copy_dst=s_y1, sq_dst=s_y1, sc=sc)
            ty = ACT(lambda h, xb=xb, scr=scr, scb=scb: h.activation(
                out=s_y1, in_=xb[:], func=AF.Identity, bias=scb, scale=scr),
                [r1, b1])
            r2, b2, _ = ln_stats(s_y1, HID, [ty],
                                 copy_dst=xb[:], sq_dst=xb[:], sc=sc)
            tz = ACT(lambda h, xb=xb, scr=scr, scb=scb: h.activation(
                out=xb[:], in_=s_y1, func=AF.Identity, bias=scb, scale=scr),
                [r2, b2, prev_tq1])
            tz2 = DVE(lambda h, xb=xb: h.tensor_tensor(
                out=s_y1, in0=xb[:], in1=s_lnw, op=ALU.mult),
                [tz, t_lnw])
            tz3 = DVE(lambda h, xb=xb: h.tensor_tensor(
                out=xb[:], in0=s_y1, in1=s_lnb, op=ALU.add),
                [tz2, t_lnb])
            tam = DVE(lambda h, xb=xb, scam=scam: h.tensor_reduce(
                out=scam, in_=xb[:], axis=AX.X, op=ALU.max,
                apply_absolute_value=True), [tz3])
            tmc = DVE(lambda h, scam=scam: h.tensor_scalar(
                out=scam, in0=scam, scalar1=1e-5, scalar2=None,
                op0=ALU.max), [tam])
            trc = DVE(lambda h, scs=scs, scam=scam: h.reciprocal(
                out=scs, in_=scam), [tmc])
            ts_ = DVE(lambda h, scs=scs: h.tensor_scalar(
                out=scs, in0=scs, scalar1=QB,
                scalar2=None, op0=ALU.mult), [trc])
            tin = DVE(lambda h, scinv=scinv, scam=scam: h.tensor_scalar(
                out=scinv, in0=scam, scalar1=1.0 / QB, scalar2=None,
                op0=ALU.mult), [tmc])
            tq1, tq2 = emit_act_quant(
                lambda k, xb=xb: xb[:, k * 1024:(k + 1) * 1024],
                lambda k: s_qtm[:, k * 1024:(k + 1) * 1024],
                s_y1, [ts_, tin, tz3, tam, tok_qn], sc=sc)
            prev_tq1 = tq1
            t_x_use[par] = tq1
            prev_tq2 = tq2
            prev_tz = tz
            for c in range(NCH):
                sl, bk = ring1.next()
                tp = PE(lambda h, c=c, sl=sl: h.transpose(
                    sl, s_qtm[:, c * 128:(c + 1) * 128], s_identbf[:]),
                    [tq2, ring1.rd[bk], t_const], inc=True)
                tev = DVE(lambda h, c=c, t=t, sl=sl: h.tensor_copy(
                    out=qn3[:, c, t * 128:(t + 1) * 128], in_=sl), [tp])
                ring1.rd[bk] = tev
                tok_qn = tev
            emit_scale_halves(6)
        scope1.close()
        pbf1s.close()

        if DEBUG_DUMPS:
            td_ = b.dma("pool", "dM", dbg_qn.ap(), regA_buf[:],
                        deps=[tok_qn, prev_tq1])
            b._wait("dve", td_)
            b._wait("pe", td_)
        tok_swk = finish_scale("wk", NKV, 2, 3, 1.0, 0)
        tok_swv = finish_scale("wv", NKV, 4, 5, 1.0, 1)

        Q3 = regC_buf[:].rearrange("p (m t) -> p m t", t=TOK)
        strmqi = ExitStack()
        open_stream_qi(strmqi)
        scope2 = ExitStack()
        s_cos = scope2.enter_context(nc.sbuf_tensor("s_cos", [128, TOK], F32))
        s_sin = scope2.enter_context(nc.sbuf_tensor("s_sin", [128, TOK], F32))
        s_kpre = scope2.enter_context(nc.sbuf_tensor("s_kpre", [128, TOK], F32))
        s_krot = scope2.enter_context(nc.sbuf_tensor("s_krot", [128, TOK], F32))
        s_kt1 = scope2.enter_context(nc.sbuf_tensor("s_kt1", [128, TOK], F32))
        s_stg = [scope2.enter_context(
            nc.sbuf_tensor(f"s_stg{i}", [128, TOK], BF16)) for i in range(2)]
        s_sq = s_kt1[:].bitcast(BF16)[:, 0:TOK]
        t_cos = b.dma("pool", "dM", s_cos[:], cosb,
                      deps=[prev_tq1, prev_tq2, prev_tz])
        t_sin = b.dma("pool", "dM", s_sin[:], sinb)
        pj = ExitStack()
        pproj4 = [pj.enter_context(nc.psum_tensor(f"pproj{i}", [128, 512], F32))
                  for i in range(4)]
        pnorm = pj.enter_context(nc.psum_tensor("pnorm", [1, 1024], F32))

        t_pp = [None, None, None, None]
        t_stg_use = [None, None]
        t_rope_rd = [None]
        t_stage = []
        t_kn = []
        t_pnorm = [None]
        t_sq_use = [None]

        blk_ctr = [0]
        cur_pr = [0]

        def q_half(src, m, hf, scol_q, scale_tok, c2col_q):
            return ws.next(src[m, :, hf * 2048:(hf + 1) * 2048], "quant",
                           scol=scol_q, extra_deps=[scale_tok],
                           scol_c2=(c2col_q,), par=hf)

        def mm_half(hf, qinfo, scale_tok, new_block):
            if new_block:
                cur_pr[0] = blk_ctr[0] % 2
                blk_ctr[0] += 1
            pr = cur_pr[0]
            pproj = pproj4[2 * pr:2 * pr + 2]
            par, trdy = qinfo
            tokm = None
            for j in range(16):
                c = hf * 16 + j
                for tg in range(2):
                    dep = [trdy, tok_qn, scale_tok]
                    if c == 0:
                        dep.append(t_pp[2 * pr + tg])
                    tokm = PE(lambda h, wb_=ws.bf(par), j=j, c=c, tg=tg,
                              pp_=pproj[tg]: h.matmul(
                        pp_[:],
                        wb_[:, j * 128:(j + 1) * 128],
                        qn3[:, c, tg * 512:(tg + 1) * 512],
                        start=(c == 0), stop=(c == 31)),
                        dep, inc=(tg == 1 and (c == 15 or c == 31)))
            ws.consumed(par, tokm)
            return tokm

        pend_norm = [None]

        def flush_norm():
            if pend_norm[0] is None:
                return
            tsq, kind, m = pend_norm[0]
            pend_norm[0] = None
            tn1 = PE(lambda h: h.matmul(pnorm[0:1, 0:512],
                                        s_onesbf[:, 0:1], s_sq[:, 0:512],
                                        start=True, stop=True),
                     [tsq, t_pnorm[0], t_const], inc=True)
            tn2 = PE(lambda h: h.matmul(pnorm[0:1, 512:1024],
                                        s_onesbf[:, 0:1],
                                        s_sq[:, 512:1024],
                                        start=True, stop=True),
                     [tsq], inc=True)
            t_sq_use[0] = tn2
            dst = s_qn2 if kind == "q" else s_kn2
            tnr = DVE(lambda h, dst=dst, m=m: h.tensor_reduce(
                out=dst[0:1, m:m + 1], in_=pnorm[0:1, :], axis=AX.X,
                op=ALU.max), [tn1, tn2])
            t_pnorm[0] = tnr
            if kind == "k":
                t_kn.append(tnr)

        def proj_evac(m, dcol, rope, kind, mm_last):
            flush_norm()
            pr = cur_pr[0]
            pproj = pproj4[2 * pr:2 * pr + 2]
            stg_par = m % 2
            deps_e = [mm_last[0], mm_last[1]]
            if rope:
                te0 = ACT(lambda h, dcol=dcol, pp_=pproj[0]: h.activation(
                    out=s_kpre[:, 0:512], in_=pp_[:], func=AF.Copy,
                    scale=s_scales[:, dcol:dcol + 1]),
                    deps_e + [t_rope_rd[0]])
                te1 = ACT(lambda h, dcol=dcol, pp_=pproj[1]: h.activation(
                    out=s_kpre[:, 512:1024], in_=pp_[:], func=AF.Copy,
                    scale=s_scales[:, dcol:dcol + 1]), deps_e)
                GPS = DVE
                tn = GPS(lambda h: h.tensor_scalar(
                    out=s_krot[0:64, :], in0=s_kpre[64:128, :], scalar1=-1.0,
                    scalar2=None, op0=ALU.mult), [te0, te1])
                tcp = GPS(lambda h: h.tensor_copy(out=s_krot[64:128, :],
                                                  in_=s_kpre[0:64, :]),
                          [te0, te1])
                t1 = GPS(lambda h: h.tensor_tensor(out=s_kt1[:], in0=s_kpre[:],
                                                   in1=s_cos[:], op=ALU.mult),
                         [te0, te1, t_cos, t_sq_use[0]])
                t2 = GPS(lambda h: h.tensor_tensor(out=s_kpre[:],
                                                   in0=s_krot[:], in1=s_sin[:],
                                                   op=ALU.mult),
                         [tn, tcp, t_sin, t1])
                if kind == "q":
                    t3 = GPS(lambda h, m=m: h.tensor_tensor(
                        out=Q3[:, m, :], in0=s_kt1[:], in1=s_kpre[:],
                        op=ALU.add), [t1, t2])
                else:
                    t3 = GPS(lambda h, stg_par=stg_par: h.tensor_tensor(
                        out=s_stg[stg_par][:], in0=s_kt1[:], in1=s_kpre[:],
                        op=ALU.add), [t1, t2, t_stg_use[stg_par]])
                t_rope_rd[0] = t3
                t_pp[2 * pr], t_pp[2 * pr + 1] = te0, te1
            else:
                te0 = ACT(lambda h, stg_par=stg_par, dcol=dcol, pp_=pproj[0]:
                          h.activation(
                    out=s_stg[stg_par][:, 0:512], in_=pp_[:],
                    func=AF.Copy, scale=s_scales[:, dcol:dcol + 1]),
                    deps_e + [t_stg_use[stg_par]])
                te1 = ACT(lambda h, stg_par=stg_par, dcol=dcol, pp_=pproj[1]:
                          h.activation(
                    out=s_stg[stg_par][:, 512:1024], in_=pp_[:],
                    func=AF.Copy, scale=s_scales[:, dcol:dcol + 1]), deps_e)
                t3 = te1
                t_pp[2 * pr], t_pp[2 * pr + 1] = te0, te1
            tsq = None
            if kind == "q":
                tsq = ACT(lambda h, m=m: h.activation(
                    out=s_sq, in_=Q3[:, m, :], func=AF.Square),
                    [t3, t_sq_use[0]])
            elif kind == "k":
                tsq = ACT(lambda h, stg_par=stg_par: h.activation(
                    out=s_sq, in_=s_stg[stg_par][:], func=AF.Square),
                    [t3, t_sq_use[0]])
            if tsq is not None:
                pend_norm[0] = (tsq, kind, m)
            if kind in ("k", "v"):
                row0 = (0 if kind == "k" else NKV * 128) + m * 128
                td = b.dma("pool", "dK0", ccg_in.ap()[row0:row0 + 128, :],
                           s_stg[stg_par][:],
                           deps=[t3] + ([tsq] if kind == "k" else []))
                t_stg_use[stg_par] = td
                t_stage.append(td)
            return t3

        def run_pipeline(blocks, nsh):
            n = len(blocks)
            (bsrc, m, scol, dcol, rope, kind, stok, c2c) = blocks[0]
            qA = q_half(bsrc, m, 0, scol, stok, c2c)
            qB = q_half(bsrc, m, 1, scol, stok, c2c)
            for i in range(n):
                (bsrc, m, scol, dcol, rope, kind, stok, c2c) = blocks[i]
                mm0 = mm_half(0, qA, stok, True)
                if i + 1 < n:
                    (b2, m2, s2, d2, r2, k2, st2, cc2) = blocks[i + 1]
                    qA = q_half(b2, m2, 0, s2, st2, cc2)
                mm1 = mm_half(1, qB, stok, False)
                proj_evac(m, dcol, rope, kind, [mm0, mm1])
                if i + 1 < n:
                    qB = q_half(b2, m2, 1, s2, st2, cc2)
                emit_scale_halves(nsh)

        blocks = ([(wk, m, 2, 3, True, "k", tok_swk, 8)
                   for m in range(NKV)] +
                  [(wv, m, 4, 5, False, "v", tok_swv, 9)
                   for m in range(NKV)])
        run_pipeline(blocks, 3)
        flush_norm()
        tok_swq = finish_scale("wq", NH, 0, 1, SM_SCALE, 2)

        t_kn_dma = b.dma("pool", "dK0", kn_in.ap(), s_kn2[0:1, :], deps=t_kn)
        tok_ag = b.coll("AllGather", ALU.bypass,
                        [[0, 1], [2, 3], [4, 5], [6, 7]],
                        [ccg_in.ap().opt()], [ccg_out.ap().opt()],
                        deps=t_stage, ch="cc0")
        tok_knr = b.coll("AllReduce", ALU.max,
                         [[0, 1], [2, 3], [4, 5], [6, 7]],
                         [kn_in.ap().opt()], [kn_out.ap().opt()],
                         deps=[t_kn_dma], ch="cc1")
        t_kn_back = b.dma("pool", "dK0", s_kn2[0:1, :], kn_out.ap(),
                          deps=[tok_knr])

        run_pipeline([(wq, m, 0, 1, True, "q", tok_swq, 10)
                      for m in range(NH)], 2)
        flush_norm()
        tok_swo = finish_scale("wo", NH, 6, 7, 1.0, 3)

        tq2m = DVE(lambda h: h.tensor_reduce(out=s_row[0:1, 0:1],
                                             in_=s_qn2[0:1, :], axis=AX.X,
                                             op=ALU.max), [t_pnorm[0]])
        tk2m = DVE(lambda h: h.tensor_reduce(out=s_row[0:1, 1:2],
                                             in_=s_kn2[0:1, :], axis=AX.X,
                                             op=ALU.max), [t_kn_back])
        tl1 = ACT(lambda h: h.activation(out=s_row[0:1, 2:3],
                                         in_=s_row[0:1, 0:1], func=AF.Ln,
                                         bias=r_tiny, scale=1.0),
                  [tq2m, t_tiny])
        tl2 = ACT(lambda h: h.activation(out=s_row[0:1, 3:4],
                                         in_=s_row[0:1, 1:2], func=AF.Ln,
                                         bias=r_tiny, scale=1.0),
                  [tk2m, t_tiny])
        tl3 = DVE(lambda h: h.tensor_tensor(out=s_row[0:1, 4:5],
                                            in0=s_row[0:1, 2:3],
                                            in1=s_row[0:1, 3:4], op=ALU.add),
                  [tl1, tl2])
        tl4 = ACT(lambda h: h.activation(out=s_row[0:1, 5:6],
                                         in_=s_row[0:1, 4:5], func=AF.Exp,
                                         bias=0.0, scale=0.5), [tl3])
        tl5 = DVE(lambda h: h.tensor_scalar(out=s_row[0:1, 6:7],
                                            in0=s_row[0:1, 5:6],
                                            scalar1=SHIFT_CAP, scalar2=-1.0,
                                            op0=ALU.min, op1=ALU.mult), [tl4])
        tlb = PE(lambda h: h.matmul(pb128[:, 2:3], s_onesf[0:1, :],
                                    s_row[0:1, 6:7], start=True, stop=True),
                 [tl5, t_pb_use[0]], inc=True)
        tok_negM = DVE(lambda h: h.tensor_copy(out=s_negM[:],
                                               in_=pb128[:, 2:3]), [tlb])
        t_pb_use[0] = tok_negM

        if DEBUG_DUMPS:
            td_ = b.dma("pool", "dM", dbg_q3.ap(), regC_buf[:],
                        deps=[tok_negM, t_rope_rd[0]])
            b._wait("dve", td_)
        pj.close()
        pscl.close()
        scope2.close()
        strmqi.close()
        strm.close()
        b.barrier(("pe", "act", "dve"))
        sd = ExitStack()
        psc = [sd.enter_context(nc.psum_tensor(f"psc{i}", [128, 1024], F32))
               for i in range(2)]
        pout = [sd.enter_context(nc.psum_tensor(f"pout{i}", [128, 512], F32))
                for i in range(3)]
        s_kst = [sd.enter_context(nc.sbuf_tensor(f"s_kst{i}", [128, 2048],
                                                 BF16)) for i in range(2)]
        s_vstg = [sd.enter_context(nc.sbuf_tensor(f"s_vstg{i}", [128, 2048],
                                                  BF16)) for i in range(2)]
        s_vtm = [sd.enter_context(nc.sbuf_tensor(f"s_vtm{i}", [128, 16 * 129],
                                                 BF16)) for i in range(2)]
        s_atn = [sd.enter_context(nc.sbuf_tensor(f"s_atn{i}", [128, 1024],
                                                 BF16)) for i in range(4)]

        t_vtm_use = [DVE(lambda h: h.memset(s_vtm[0][:], 1.0)),
                     DVE(lambda h: h.memset(s_vtm[1][:], 1.0))]
        t_kst_use = [None, None]
        t_vstg_use = [None, None]
        t_psc_use = [None, None]
        t_atn_use = [None, None, None, None]
        t_pout_use = [None, None, None]
        t_pbf = [None] * 8

        B3 = regA_buf[:].rearrange("p (m t) -> p m t", t=TOK)

        for g in range(NKV):
            par = g % 2
            tk0 = b.dma("pool", "dK%d" % par, s_kst[par][:, 0:1024],
                        ccg_out.ap()[g * 128:(g + 1) * 128, :],
                        deps=[tok_ag, t_kst_use[par]] +
                             ([tok_swo] if g < 2 else []))
            tk1 = b.dma("pool", "dK%d" % par, s_kst[par][:, 1024:2048],
                        ccg_out.ap()[2048 + g * 128:2048 + (g + 1) * 128, :],
                        deps=[tok_ag])
            tv0_ = b.dma("pool", "dK%d" % par, s_vstg[par][:, 0:1024],
                         ccg_out.ap()[1024 + g * 128:1024 + (g + 1) * 128, :],
                         deps=[tok_ag, t_vstg_use[par]])
            tv1_ = b.dma("pool", "dK%d" % par, s_vstg[par][:, 1024:2048],
                         ccg_out.ap()[3072 + g * 128:3072 + (g + 1) * 128, :],
                         deps=[tok_ag])
            t_ld_all = tv1_
            tvl = None
            for kt in range(16):
                bk = kt % 2
                sl = pout[bk][:].bitcast(BF16)[:, 0:128]
                tp = PE(lambda h, par=par, kt=kt, sl=sl: h.transpose(
                    sl, s_vstg[par][:, kt * 128:(kt + 1) * 128], s_identbf[:]),
                    [t_ld_all, t_pout_use[bk], t_vtm_use[par]], inc=True)
                tev = DVE(lambda h, par=par, kt=kt, sl=sl: h.tensor_copy(
                    out=s_vtm[par][:, kt * 129:kt * 129 + 128], in_=sl), [tp])
                t_pout_use[bk] = tev
                tvl = tev
            t_vstg_use[par] = tvl
            tok_vtm = tvl

            for hq in range(4):
                hh = 4 * g + hq
                av_toks = [None] * NT
                exp_toks = [None] * 16

                def emit_av(j, hh=hh, par=par, av_toks=av_toks,
                            exp_toks=exp_toks):
                    ab = j % 4
                    tav = None
                    for qt in range(NT):
                        bk, slot = qt // 3, qt % 3
                        dep = [exp_toks[j], tok_vtm]
                        if j == 0:
                            dep.append(t_pout_use[bk])
                        tav = PE(lambda h, ab=ab, par=par, j=j, bk=bk,
                                 slot=slot, qt=qt: h.matmul(
                            pout[bk][:, slot * 129:slot * 129 + 129],
                            s_atn[ab][:, qt * 128:(qt + 1) * 128],
                            s_vtm[par][:, j * 129:j * 129 + 129],
                            start=(j == 0 and slot == 0), stop=(j == 15),
                            skip_group_check=True),
                            dep, inc=(j == 15 or qt == NT - 1))
                        if j == 15:
                            av_toks[qt] = tav
                    t_atn_use[ab] = tav

                for kt in range(16):
                    pp = kt % 2
                    ab = kt % 4
                    ts0 = PE(lambda h, par=par, kt=kt, hh=hh, pp=pp: h.matmul(
                        psc[pp][:, 0:512],
                        s_kst[par][:, kt * 128:(kt + 1) * 128],
                        Q3[:, hh, 0:512], start=True, stop=True),
                        [t_ld_all, t_psc_use[pp], tok_negM], inc=True)
                    ts1 = PE(lambda h, par=par, kt=kt, hh=hh, pp=pp: h.matmul(
                        psc[pp][:, 512:1024],
                        s_kst[par][:, kt * 128:(kt + 1) * 128],
                        Q3[:, hh, 512:1024], start=True, stop=True),
                        [], inc=True)
                    tex = ACT(lambda h, pp=pp, ab=ab: h.activation(
                        out=s_atn[ab][:], in_=psc[pp][:], func=AF.Exp,
                        bias=s_negM[:], scale=1.0),
                        [ts0, ts1, t_atn_use[ab]])
                    exp_toks[kt] = tex
                    t_psc_use[pp] = tex
                    if kt >= 2:
                        emit_av(kt - 2)
                emit_av(14)
                emit_av(15)
                t_kst_use[par] = av_toks[-1]
                t_vtm_use[par] = av_toks[-1]
                for qt in range(NT):
                    bk, slot = qt // 3, qt % 3
                    tcd_ = DVE(lambda h, bk=bk, slot=slot: h.tensor_copy(
                        out=c_t0,
                        in_=pout[bk][:, slot * 129 + 128:slot * 129 + 129]),
                        [av_toks[qt]])
                    trd = DVE(lambda h: h.reciprocal(out=c_den, in_=c_t0),
                              [tcd_])
                    tnm = DVE(lambda h, bk=bk, slot=slot, hh=hh, qt=qt:
                              h.tensor_scalar(
                        out=B3[:, hh, qt * 128:(qt + 1) * 128],
                        in0=pout[bk][:, slot * 129:slot * 129 + 128],
                        scalar1=c_den, scalar2=None, op0=ALU.mult), [trd])
                    t_pout_use[bk] = tnm

        tok_sdpa = [t for t in t_pout_use if t is not None]
        if DEBUG_DUMPS:
            td_ = b.dma("pool", "dM", dbg_b3.ap(), regA_buf[:],
                        deps=tok_sdpa)
            b._wait("dve", td_)
            b._wait("pool", td_)
        sd.close()
        b.barrier(("pe", "act", "dve"))
        p5 = ExitStack()
        pproj5 = [p5.enter_context(nc.psum_tensor(f"pp5{i}", [128, 512], F32))
                  for i in range(4)]
        ring5 = PbfRing(p5.enter_context(
            nc.psum_tensor("pbf5", [128, 2048], BF16))[:])
        open_stream_bufs(p5)
        s_y1p5 = p5.enter_context(nc.sbuf_tensor("s_y1b", [128, HID], F32))
        s_sh5v = p5.enter_context(nc.sbuf_tensor("s_sh5v", [128, HID], BF16))
        s_qtm5 = s_sh5v[:]
        s_aofm5 = s_sh5v[:, 0:TOK]
        s_at3s = s_sh5v[:, 2048:3072].rearrange("p (t f) -> p t f", f=128)

        Qo3 = regC_buf[:].rearrange("p (c t) -> p c t", t=TOK)

        tok_qo = None
        t_prev_chain = [None]
        p5_sc_toks = {}

        def p5_scale(qt):
            sc = SCP[qt % 2]
            tam = DVE(lambda h, qt=qt, sc=sc: h.tensor_reduce(
                out=sc["am"], in_=B3[:, :, qt * 128:(qt + 1) * 128],
                axis=AX.XY, op=ALU.max, apply_absolute_value=True),
                [sc["last"]])
            tmc = DVE(lambda h, sc=sc: h.tensor_scalar(
                out=sc["am"], in0=sc["am"], scalar1=1e-5, scalar2=None,
                op0=ALU.max), [tam])
            trc = DVE(lambda h, sc=sc: h.reciprocal(
                out=sc["s"], in_=sc["am"]), [tmc])
            ts_ = DVE(lambda h, sc=sc: h.tensor_scalar(
                out=sc["s"], in0=sc["s"], scalar1=QB, scalar2=None,
                op0=ALU.mult), [trc])
            tin = DVE(lambda h, sc=sc: h.tensor_scalar(
                out=sc["inv"], in0=sc["am"], scalar1=1.0 / QB, scalar2=None,
                op0=ALU.mult), [tmc])
            p5_sc_toks[qt] = [ts_, tin]

        p5_scale(0)
        for qt in range(NT):
            sc = SCP[qt % 2]
            prev_evac = tok_qo
            if qt + 1 < NT:
                p5_scale(qt + 1)
            for ck in range(4):
                tq1, tq2 = emit_act_quant(
                    lambda k, qt=qt, ck=ck: B3[:, 8 * ck:8 * (ck + 1),
                                               qt * 128:(qt + 1) * 128],
                    lambda k, ck=ck: s_qtm5[:, ck * 1024:
                                            (ck + 1) * 1024].rearrange(
                        "p (c f) -> p c f", c=8),
                    s_y1p5[:],
                    p5_sc_toks[qt] + [prev_evac, t_prev_chain[0]],
                    n_chunks=1, sc=sc)
                t_prev_chain[0] = tq2
                for c in range(8 * ck, 8 * ck + 8):
                    sl, bk = ring5.next()
                    tp = PE(lambda h, c=c, sl=sl: h.transpose(
                        sl, s_qtm5[:, c * 128:(c + 1) * 128], s_identbf[:]),
                        [tq2, ring5.rd[bk]], inc=True)
                    tev = DVE(lambda h, c=c, qt=qt, sl=sl: h.tensor_copy(
                        out=Qo3[:, c, qt * 128:(qt + 1) * 128], in_=sl), [tp])
                    ring5.rd[bk] = tev
                    tok_qo = tev

        t_pp5 = [None, None, None, None]
        t_aofm_use = None
        t_at3s_use = None
        t_at3_dma = []
        for m in range(NH):
            pr = m % 2
            pp = pproj5[2 * pr:2 * pr + 2]
            mm_last = [None, None]
            for hf in range(2):
                xd = [tok_swo] + (tok_sdpa if m == 0 else [])
                par, trdy = ws.next(wo[m, :, hf * 2048:(hf + 1) * 2048],
                                    "quant", scol=6, extra_deps=xd,
                                    scol_c2=(11,))
                tokm = None
                for j in range(16):
                    c = hf * 16 + j
                    for tg in range(2):
                        dep = [trdy, tok_qo, tok_swo]
                        if c == 0:
                            dep.append(t_pp5[2 * pr + tg])
                        tokm = PE(lambda h, wb_=ws.bf(par), j=j, c=c, tg=tg,
                                  pp_=pp[tg]: h.matmul(
                            pp_[:],
                            wb_[:, j * 128:(j + 1) * 128],
                            Qo3[:, c, tg * 512:(tg + 1) * 512],
                            start=(c == 0), stop=(c == 31)),
                            dep, inc=(tg == 1 and (c == 15 or c == 31)))
                mm_last[hf] = tokm
                ws.consumed(par, tokm)
            te0 = ACT(lambda h, pp_=pp[0]: h.activation(
                out=s_aofm5[:, 0:512], in_=pp_[:], func=AF.Copy,
                scale=s_scales[:, 7:8]),
                [mm_last[0], mm_last[1], t_aofm_use])
            te1 = ACT(lambda h, pp_=pp[1]: h.activation(
                out=s_aofm5[:, 512:1024], in_=pp_[:], func=AF.Copy,
                scale=s_scales[:, 7:8]), [mm_last[0], mm_last[1]])
            t_pp5[2 * pr], t_pp5[2 * pr + 1] = te0, te1
            tevl = None
            for qt in range(NT):
                sl, bk = ring5.next()
                tp = PE(lambda h, qt=qt, sl=sl: h.transpose(
                    sl, s_aofm5[:, qt * 128:(qt + 1) * 128], s_identbf[:]),
                    [te0, te1, ring5.rd[bk]], inc=True)
                tev = DVE(lambda h, qt=qt, m=m, sl=sl: h.tensor_copy(
                    out=s_at3s[:, qt, :], in_=sl), [tp, t_at3s_use])
                ring5.rd[bk] = tev
                tevl = tev
            t_aofm_use = tevl
            tdm = b.dma("pool", "dK1",
                        at3d.ap().rearrange("(t p) (m f) -> p t m f", p=128,
                                            f=128)[:, :, m, :],
                        s_at3s,
                        deps=[tevl])
            t_at3s_use = tdm
            t_at3_dma.append(tdm)
        tok_at3 = t_at3_dma[-1]

        p5.close()
        b.barrier(("pe", "act", "dve", "pool"))
        p6 = ExitStack()
        ph1 = p6.enter_context(nc.psum_tensor("ph1", [32, 512], F32))
        pg1 = p6.enter_context(nc.psum_tensor("pg1", [8, 512], F32))
        pA = p6.enter_context(nc.psum_tensor("pA", [128, 512], F32))
        pf32 = p6.enter_context(nc.psum_tensor("pf32", [128, 512], F32))
        pg2 = pf32[:, 384:392]
        pwb = [p6.enter_context(nc.psum_tensor(f"pwb{i}", [128, 512], F32))
               for i in range(2)]
        ring6 = PbfRing(p6.enter_context(
            nc.psum_tensor("pbf6", [128, 2048], BF16))[:])
        p_s16 = pwb[0]

        X3 = regC_buf[:].rearrange("p (c t) -> p c t", t=TOK)

        lora = ExitStack()
        s_gw1bf = lora.enter_context(
            nc.sbuf_tensor("s_gw1bf", [128, NCH, 32], BF16))
        s_abf = lora.enter_context(
            nc.sbuf_tensor("s_abf", [128, NCH, 128], BF16))
        s_bbf = lora.enter_context(nc.sbuf_tensor("s_bbf", [128, HID], BF16))
        s_x6v = [regA_buf[:, 0:8192].bitcast(F32),
               regA_buf[:, 8192:16384].bitcast(F32)]
        s_y16v = regA_buf[:, 16384:24576].bitcast(F32)
        s_z6v = regA_buf[:, 24576:32768].bitcast(F32)
        s_qtm6v = p6.enter_context(nc.sbuf_tensor("s_qtm6", [128, HID], BF16))
        s_at3l = [p6.enter_context(
            nc.sbuf_tensor(f"s_at3l{i}", [128, HID], BF16)) for i in range(2)]
        s_hfm = p6.enter_context(nc.sbuf_tensor("s_hfm", [32, 128], F32))
        s_ht = p6.enter_context(nc.sbuf_tensor("s_ht", [128, 32], F32))
        s_hsq = p6.enter_context(nc.sbuf_tensor("s_hsq", [128, 32], F32))
        s_ht2 = p6.enter_context(nc.sbuf_tensor("s_ht2", [128, 32], F32))
        s_h2fm = p6.enter_context(nc.sbuf_tensor("s_h2fm", [32, 128], BF16))
        s_gfm = p6.enter_context(nc.sbuf_tensor("s_gfm", [8, 128], F32))
        s_gt = p6.enter_context(nc.sbuf_tensor("s_gt", [128, 8], F32))
        s_eq1 = p6.enter_context(nc.sbuf_tensor("s_eq1", [128, 8], F32))
        s_eq2 = p6.enter_context(nc.sbuf_tensor("s_eq2", [128, 8], F32))
        s_msk = p6.enter_context(nc.sbuf_tensor("s_msk", [128, 8], F32))
        s_wt = p6.enter_context(nc.sbuf_tensor("s_wt", [128, 8], F32))
        s_wt2 = p6.enter_context(nc.sbuf_tensor("s_wt2", [128, 8], F32))
        s_wfm = p6.enter_context(nc.sbuf_tensor("s_wfm", [8, TOK], BF16))
        s_wb = p6.enter_context(nc.sbuf_tensor("s_wb", [128, TOK], F32))
        s_tmp = p6.enter_context(nc.sbuf_tensor("s_tmp", [128, TOK], F32))

        t_gw = []
        for c in range(NCH):
            tg_ = b.dma("pool", "dM", s_cstg[:, 0:32],
                        gw1t[c * 128:(c + 1) * 128, :],
                        deps=[t_gw[-1][1] if t_gw else tok_at3])
            tcst = DVE(lambda h, c=c: h.tensor_copy(out=s_gw1bf[:, c, :],
                                                    in_=s_cstg[:, 0:32]),
                       [tg_])
            t_gw.append((tg_, tcst))
        t_g2 = b.dma("pool", "dM", s_hfm[0:32, 0:8], gw2t)
        t_g2c = DVE(lambda h: h.tensor_copy(out=s_gw2bf[:],
                                            in_=s_hfm[0:32, 0:8]), [t_g2])
        t_ab = []
        for c in range(NCH):
            ta_ = b.dma("pool", "dM", s_cstg[:],
                        a_all[c * 128:(c + 1) * 128, :],
                        deps=[t_ab[-1][1] if t_ab else t_gw[-1][1]])
            tac = DVE(lambda h, c=c: h.tensor_copy(out=s_abf[:, c, :],
                                                   in_=s_cstg[:]), [ta_])
            t_ab.append((ta_, tac))
        t_bl = b.dma("pool", "dM", s_x6v[0], b_all)
        t_blc = DVE(lambda h: h.tensor_copy(out=s_bbf[:], in_=s_x6v[0]),
                    [t_bl])
        t_x_use = [t_blc, None]

        tok_gstash = []
        t_ph1 = None
        t_pg1 = None
        t_pg2 = None
        t_pA = None
        t_pf32 = None
        t_cnt_prev = t_cnt0
        t_pbf = [None] * 8
        tok_x3 = None
        t_tmp_use = None
        t_spill = []

        prev_tx1 = None
        prev_tsp = None
        prev_tx3 = None
        t_a3_use = [None, None]
        for t in range(NT):
            par = t % 2
            xb = s_x6v[par]
            a3 = s_at3l[par]
            tl = b.dma("pool", "dX%d" % par, xb,
                       x_own[t * 128:(t + 1) * 128, :], deps=[t_x_use[par]])
            tla = b.dma("pool", "dK%d" % par, a3[:],
                        at3d.ap()[t * 128:(t + 1) * 128, :],
                        deps=[t_a3_use[par], tok_at3])
            r3, b3, tcp = ln_stats(a3[:], HID, [tla, prev_tx1, tok_x3],
                                   copy_dst=s_y16v, sq_dst=s_qtm6v[:])
            t_a3_use[par] = r3
            tx1 = DVE(lambda h, xb=xb: h.scalar_tensor_tensor(
                out=s_z6v, in0=s_y16v, scalar=c_r, in1=xb,
                op0=ALU.mult, op1=ALU.add),
                [r3, b3, tcp, tl, prev_tsp, prev_tx3])
            tx2 = DVE(lambda h: h.tensor_scalar(out=s_z6v, in0=s_z6v,
                                                scalar1=c_b, scalar2=None,
                                                op0=ALU.add), [tx1])
            t_x_use[par] = tx1
            prev_tx1 = tx1
            tsp = b.dma("pool", "dM", x2d.ap()[t * 128:(t + 1) * 128, :],
                        s_z6v, deps=[tx2])
            t_spill.append(tsp)
            prev_tsp = tsp
            r4, b4, _ = ln_stats(s_z6v, HID, [tx2],
                                 copy_dst=s_y16v, sq_dst=s_y16v)
            tx3 = ACT(lambda h: h.activation(out=s_qtm6v[:], in_=s_z6v,
                                             func=AF.Identity, bias=c_b,
                                             scale=c_r), [r4, b4, tok_x3])
            prev_tx3 = tx3
            for c in range(NCH):
                sl, bk = ring6.next()
                tp = PE(lambda h, c=c, sl=sl: h.transpose(
                    sl, s_qtm6v[:, c * 128:(c + 1) * 128], s_identbf[:]),
                    [tx3, ring6.rd[bk]], inc=True)
                tev = DVE(lambda h, c=c, t=t, sl=sl: h.tensor_copy(
                    out=X3[:, c, t * 128:(t + 1) * 128], in_=sl), [tp])
                ring6.rd[bk] = tev
                tok_x3 = tev

            tokm = None
            for c in range(NCH):
                dep = [t_gw[c][1]]
                if c == 0:
                    dep += [tok_x3, t_ph1]
                tokm = PE(lambda h, c=c, t=t: h.matmul(
                    ph1[:, 0:128], s_gw1bf[:, c, :],
                    X3[:, c, t * 128:(t + 1) * 128],
                    start=(c == 0), stop=(c == NCH - 1)),
                    dep, inc=(c == NCH - 1))
            th0 = ACT(lambda h: h.activation(out=s_hfm[:], in_=ph1[:, 0:128],
                                             func=AF.Identity, bias=s_gb1[:],
                                             scale=1.0), [tokm, t_ph1])
            t_ph1 = th0
            tht = PE(lambda h: h.transpose(pf32[0:128, 0:32], s_hfm[:],
                                           s_identf[0:32, 0:32]),
                     [th0, t_pf32, t_pg2], inc=True)
            the = DVE(lambda h: h.tensor_copy(out=s_ht[:],
                                              in_=pf32[0:128, 0:32]), [tht])
            t_pf32 = the
            ts1 = DVE(lambda h: h.tensor_reduce(out=c_t0, in_=s_ht[:],
                                                axis=AX.X, op=ALU.add), [the])
            ts2 = DVE(lambda h: h.tensor_tensor(out=s_hsq[:], in0=s_ht[:],
                                                in1=s_ht[:], op=ALU.mult),
                      [the])
            ts3 = DVE(lambda h: h.tensor_reduce(out=c_t1, in_=s_hsq[:],
                                                axis=AX.X, op=ALU.add), [ts2])
            ts4 = DVE(lambda h: h.tensor_scalar(out=c_t0, in0=c_t0,
                                                scalar1=1.0 / 32, scalar2=None,
                                                op0=ALU.mult), [ts1])
            ts5 = DVE(lambda h: h.tensor_tensor(out=c_sq, in0=c_t0, in1=c_t0,
                                                op=ALU.mult), [ts4])
            ts6 = DVE(lambda h: h.scalar_tensor_tensor(out=c_t1, in0=c_t1,
                                                       scalar=1.0 / 32,
                                                       in1=c_sq, op0=ALU.mult,
                                                       op1=ALU.subtract),
                      [ts3, ts5])
            ts7 = ACT(lambda h: h.activation(out=c_t1, in_=c_t1, func=AF.Ln,
                                             bias=c_eps, scale=1.0),
                      [ts6, t_eps])
            ts8 = ACT(lambda h: h.activation(out=c_r, in_=c_t1, func=AF.Exp,
                                             bias=0.0, scale=-0.5), [ts7])
            ts9 = DVE(lambda h: h.tensor_tensor(out=c_b, in0=c_t0, in1=c_r,
                                                op=ALU.mult), [ts8, ts4])
            tsa = DVE(lambda h: h.tensor_scalar(out=c_b, in0=c_b, scalar1=-1.0,
                                                scalar2=None, op0=ALU.mult),
                      [ts9])
            tsb = ACT(lambda h: h.activation(out=s_ht2[:], in_=s_ht[:],
                                             func=AF.Identity, bias=c_b,
                                             scale=c_r), [ts8, tsa, the])
            tsc_ = DVE(lambda h: h.tensor_tensor(out=s_ht2[:], in0=s_ht2[:],
                                                 in1=s_g32[:], op=ALU.mult),
                       [tsb, t_const])
            tsd = DVE(lambda h: h.tensor_tensor(out=s_ht2[:], in0=s_ht2[:],
                                                in1=s_bt32[:], op=ALU.add),
                      [tsc_])
            tse = DVE(lambda h: h.tensor_relu(out=s_ht2[:], in_=s_ht2[:]),
                      [tsd])
            tsf = PE(lambda h: h.transpose(pf32[0:32, 0:128], s_ht2[:],
                                           s_identf[:]),
                     [tse, t_pf32], inc=True)
            tsg = DVE(lambda h: h.tensor_copy(out=s_h2fm[:],
                                              in_=pf32[0:32, 0:128]), [tsf])
            t_pf32 = tsg
            tsh = PE(lambda h: h.matmul(pg1[:, 0:128], s_gw2bf[:], s_h2fm[:],
                                        start=True, stop=True),
                     [tsg, t_g2c, t_pg1], inc=True)
            tsi = ACT(lambda h: h.activation(out=s_gfm[:], in_=pg1[:, 0:128],
                                             func=AF.Identity, bias=s_gb2[:],
                                             scale=1.0), [tsh])
            t_pg1 = tsi
            tsj = PE(lambda h: h.transpose(pg2, s_gfm[:],
                                           s_identf[0:8, 0:8]),
                     [tsi, t_pg2], inc=True)
            tsk = DVE(lambda h: h.tensor_copy(out=s_gt[:], in_=pg2), [tsj])
            t_pg2 = tsk
            tt1 = DVE(lambda h: h.tensor_reduce(out=c_m1, in_=s_gt[:],
                                                axis=AX.X, op=ALU.max), [tsk])
            tt2 = DVE(lambda h: h.tensor_scalar(out=s_eq1[:], in0=s_gt[:],
                                                scalar1=c_m1, scalar2=None,
                                                op0=ALU.is_equal), [tt1])
            tt3 = DVE(lambda h: h.scalar_tensor_tensor(out=s_msk[:],
                                                       in0=s_eq1[:],
                                                       scalar=-1e30,
                                                       in1=s_gt[:],
                                                       op0=ALU.mult,
                                                       op1=ALU.add), [tt2])
            tt4 = DVE(lambda h: h.tensor_reduce(out=c_m2, in_=s_msk[:],
                                                axis=AX.X, op=ALU.max), [tt3])
            tt5 = DVE(lambda h: h.tensor_scalar(out=s_eq2[:], in0=s_msk[:],
                                                scalar1=c_m2, scalar2=None,
                                                op0=ALU.is_equal), [tt4])
            tt6 = DVE(lambda h: h.tensor_tensor(out=c_d, in0=c_m2, in1=c_m1,
                                                op=ALU.subtract), [tt4, tt1])
            tt7 = ACT(lambda h: h.activation(out=c_e2, in_=c_d, func=AF.Exp,
                                             bias=0.0, scale=1.0), [tt6])
            tt8 = DVE(lambda h: h.tensor_scalar(out=c_rg, in0=c_e2,
                                                scalar1=1.0, scalar2=None,
                                                op0=ALU.add), [tt7])
            tt9 = DVE(lambda h: h.reciprocal(out=c_rg, in_=c_rg), [tt8])
            tta = DVE(lambda h: h.tensor_scalar(out=c_g1, in0=c_rg,
                                                scalar1=ALPHA, scalar2=None,
                                                op0=ALU.mult), [tt9])
            ttb = DVE(lambda h: h.tensor_tensor(out=c_g2, in0=c_e2, in1=c_rg,
                                                op=ALU.mult), [tt7, tt9])
            ttc = DVE(lambda h: h.tensor_scalar(out=c_g2, in0=c_g2,
                                                scalar1=ALPHA, scalar2=None,
                                                op0=ALU.mult), [ttb])
            ttd = DVE(lambda h, t=t: h.tensor_scalar(out=s_c1[:, t, :],
                                                     in0=s_eq1[:],
                                                     scalar1=c_g1,
                                                     scalar2=None,
                                                     op0=ALU.mult),
                      [tta, tt2])
            tte = DVE(lambda h, t=t: h.tensor_scalar(out=s_c2[:, t, :],
                                                     in0=s_eq2[:],
                                                     scalar1=c_g2,
                                                     scalar2=None,
                                                     op0=ALU.mult),
                      [ttc, tt5])
            ttf = DVE(lambda h: h.tensor_tensor(out=s_cnt[:, 0:8],
                                                in0=s_cnt[:, 0:8],
                                                in1=s_eq1[:], op=ALU.add),
                      [tt2, t_cnt_prev])
            ttg = DVE(lambda h: h.tensor_tensor(out=s_cnt[:, 8:16],
                                                in0=s_cnt[:, 8:16],
                                                in1=s_eq2[:], op=ALU.add),
                      [tt5, ttf])
            t_cnt_prev = ttg
            tok_gstash.append((ttd, tte))

            tokA = None
            for c in range(NCH):
                dep = [t_ab[c][1]]
                if c == 0:
                    dep += [tok_x3, t_pA]
                tokA = PE(lambda h, c=c, t=t: h.matmul(
                    pA[:, 0:128], s_abf[:, c, :], X3[:, c, t * 128:(t + 1) * 128],
                    start=(c == 0), stop=(c == NCH - 1)),
                    dep, inc=(c == NCH - 1))
            tAe = DVE(lambda h, t=t: h.tensor_copy(
                out=s_tmp[:, t * 128:(t + 1) * 128], in_=pA[:, 0:128]), [tokA])
            t_pA = tAe
            t_tmp_use = tAe

        tcs = PE(lambda h: h.matmul(p_s16[0:1, 0:16], s_onesf[:, 0:1],
                                    s_cnt[:], start=True, stop=True),
                 [t_cnt_prev], inc=True)
        tce = DVE(lambda h: h.tensor_copy(out=s_cntrow[:],
                                          in_=p_s16[0:1, 0:16]), [tcs])
        tcd = b.dma("pool", "dM", cnt_in.ap(), s_cntrow[:], deps=[tce])
        tok_cr = b.coll("AllReduce", ALU.add, [[0, 1, 2, 3, 4, 5, 6, 7]],
                        [cnt_in.ap().opt()], [cnt_out.ap().opt()],
                        deps=[tcd], ch="cc2")
        tcb = b.dma("pool", "dM", s_cntrow[:], cnt_out.ap(), deps=[tok_cr])
        tal = DVE(lambda h: h.tensor_scalar(out=s_cntrow[:], in0=s_cntrow[:],
                                            scalar1=CAPACITY, scalar2=None,
                                            op0=ALU.is_le), [tcb])
        tab = PE(lambda h: h.matmul(pA[:, 0:16], s_onesf[0:1, :],
                                    s_cntrow[:], start=True, stop=True),
                 [tal, t_pA], inc=True)
        tae = DVE(lambda h: h.tensor_copy(out=s_alb[:], in_=pA[:, 0:16]),
                  [tab])
        t_wfm = None
        for t in range(NT):
            tw1 = DVE(lambda h, t=t: h.tensor_tensor(
                out=s_wt[:], in0=s_c1[:, t, :], in1=s_alb[:, 0:8],
                op=ALU.mult), [tae, tok_gstash[t][0]])
            tw2 = DVE(lambda h, t=t: h.tensor_tensor(
                out=s_wt2[:], in0=s_c2[:, t, :], in1=s_alb[:, 8:16],
                op=ALU.mult), [tae, tok_gstash[t][1]])
            tw3 = DVE(lambda h: h.tensor_tensor(out=s_wt[:], in0=s_wt[:],
                                                in1=s_wt2[:], op=ALU.add),
                      [tw1, tw2])
            tw4 = PE(lambda h: h.transpose(pf32[0:8, 0:128], s_wt[:],
                                           s_identf[:]),
                     [tw3, t_pf32, t_pg2], inc=True)
            tw5 = DVE(lambda h, t=t: h.tensor_copy(
                out=s_wfm[:, t * 128:(t + 1) * 128],
                in_=pf32[0:8, 0:128]), [tw4])
            t_pf32 = tw5
            t_wfm = tw5
        twb0 = PE(lambda h: h.matmul(pwb[0][:], s_selS[:], s_wfm[:, 0:512],
                                     start=True, stop=True), [t_wfm],
                  inc=True)
        twb1 = PE(lambda h: h.matmul(pwb[1][:], s_selS[:], s_wfm[:, 512:1024],
                                     start=True, stop=True), [t_wfm],
                  inc=True)
        twe0 = DVE(lambda h: h.tensor_copy(out=s_wb[:, 0:512], in_=pwb[0][:]),
                   [twb0])
        twe1 = DVE(lambda h: h.tensor_copy(out=s_wb[:, 512:1024],
                                           in_=pwb[1][:]), [twb1])
        tok_tsc = DVE(lambda h: h.tensor_tensor(out=s_tsc[:], in0=s_tmp[:],
                                                in1=s_wb[:], op=ALU.mult),
                      [twe0, twe1, t_tmp_use])

        p6.close()
        b.barrier(("pe", "act", "dve"))
        p7 = ExitStack()
        pproj7 = [p7.enter_context(nc.psum_tensor(f"pp7{i}", [128, 512], F32))
                  for i in range(4)]
        ring7 = PbfRing(p7.enter_context(
            nc.psum_tensor("pbf7", [128, 2048], BF16))[:])
        open_stream_bufs(p7)
        s_aofm7v = p7.enter_context(nc.sbuf_tensor("s_aofm7", [128, TOK], BF16))

        F3 = regA_buf[:].rearrange("p (t f) -> p t f", f=HID)

        t_pp7 = [None, None, None, None]
        t_aofm_use = None
        t_pbf = [None] * 8
        tok_f3 = None
        for m in range(NCH):
            pr = m % 2
            pp = pproj7[2 * pr:2 * pr + 2]
            for hf in range(2):
                xd = [tok_tsc, t_spill[-1]] if m == 0 else []
                par, trdy = ws.next(wm[m, :, hf * 2048:(hf + 1) * 2048],
                                    "cast", extra_deps=xd)
                tokm = None
                for j in range(16):
                    c = hf * 16 + j
                    for tg in range(2):
                        dep = [trdy, tok_x3]
                        if c == 0:
                            dep.append(t_pp7[2 * pr + tg])
                        tokm = PE(lambda h, wb_=ws.bf(par), j=j, c=c, tg=tg,
                                  pp_=pp[tg]: h.matmul(
                            pp_[:],
                            wb_[:, j * 128:(j + 1) * 128],
                            X3[:, c, tg * 512:(tg + 1) * 512],
                            start=(c == 0), stop=False),
                            dep, inc=(j == 15 and tg == 1))
                ws.consumed(par, tokm)
            tlb_ = [None, None]
            for tg in range(2):
                tlb_[tg] = PE(lambda h, m=m, tg=tg, pp_=pp[tg]: h.matmul(
                    pp_[:], s_bbf[:, m * 128:(m + 1) * 128],
                    s_tsc[:, tg * 512:(tg + 1) * 512],
                    start=False, stop=True),
                    [tok_tsc, t_blc], inc=True)
            te0 = ACT(lambda h, pp_=pp[0]: h.activation(
                out=s_aofm7v[:, 0:512], in_=pp_[:], func=AF.Copy),
                [tlb_[0], t_aofm_use])
            te1 = ACT(lambda h, pp_=pp[1]: h.activation(
                out=s_aofm7v[:, 512:1024], in_=pp_[:], func=AF.Copy),
                [tlb_[1]])
            t_pp7[2 * pr], t_pp7[2 * pr + 1] = te0, te1
            tevl = None
            for qt in range(NT):
                sl, bk = ring7.next()
                tp = PE(lambda h, qt=qt, sl=sl: h.transpose(
                    sl, s_aofm7v[:, qt * 128:(qt + 1) * 128], s_identbf[:]),
                    [te0, te1, ring7.rd[bk]], inc=True)
                tev = DVE(lambda h, qt=qt, m=m, sl=sl: h.tensor_copy(
                    out=F3[:, qt, m * 128:(m + 1) * 128], in_=sl),
                    [tp, t_spill[-1]])
                ring7.rd[bk] = tev
                tevl = tev
            t_aofm_use = tevl
        tok_f3 = t_aofm_use

        p7.close()
        lora.close()
        b.barrier(("pe", "act", "dve", "pool"))
        p8 = ExitStack()
        s_x8v = [p8.enter_context(nc.sbuf_tensor(f"s_x8{i}", [128, HID], F32))
               for i in range(2)]
        s_y18v = p8.enter_context(nc.sbuf_tensor("s_y18", [128, HID], F32))
        s_z8v = p8.enter_context(nc.sbuf_tensor("s_z8", [128, HID], F32))
        s_jnk8 = p8.enter_context(nc.sbuf_tensor("s_jnk8", [128, HID], BF16))
        t_x_use = [None, None]
        t_store = []
        tok_dM_all = ("dM", b.cnt["dM"])
        prev_tx1 = None
        prev_tst = None
        for t in range(NT):
            par = t % 2
            xb = s_x8v[par]
            tl = b.dma("pool", "dX%d" % par, xb[:],
                       x2d.ap()[t * 128:(t + 1) * 128, :],
                       deps=[t_x_use[par], tok_dM_all])
            r5, b5, tcp = ln_stats(F3[:, t, :], HID, [tok_f3, prev_tx1],
                                   copy_dst=s_y18v[:], sq_dst=s_jnk8[:])
            tx1 = DVE(lambda h, xb=xb: h.scalar_tensor_tensor(
                out=s_z8v[:], in0=s_y18v[:], scalar=c_r, in1=xb[:],
                op0=ALU.mult, op1=ALU.add), [r5, b5, tcp, tl, prev_tst])
            tx2 = DVE(lambda h: h.tensor_scalar(out=s_z8v[:], in0=s_z8v[:],
                                                scalar1=c_b, scalar2=None,
                                                op0=ALU.add), [tx1])
            t_x_use[par] = tx1
            prev_tx1 = tx1
            tst = b.dma("pool", "dS", y_own[t * 128:(t + 1) * 128, :],
                        s_z8v[:], deps=[tx2])
            t_store.append(tst)
            prev_tst = tst
        if DEBUG_DUMPS:
            b.dma("pool", "dM", dbg_ccg.ap(), ccg_in.ap(), deps=t_stage)
            b.dma("pool", "dM", dbg_at3.ap(), at3d.ap(), deps=[tok_at3])
            b.dma("pool", "dM", dbg_x2.ap(), x2d.ap(), deps=[t_spill[-1]])
            tsc_ = b.dma("pool", "dM", dbg_sc.ap()[:, 0:12], s_scales[:],
                         deps=[t_pb_use[0]])
            b.dma("pool", "dM", dbg_sc.ap()[0:1, 12:12 + NH], s_qn2[0:1, :],
                  deps=[t_pnorm[0]])
            b.dma("pool", "dM", dbg_sc.ap()[0:1, 12 + NH:12 + NH + 1],
                  s_negM[0:1, :], deps=[tok_negM])
            t_store.append(b.dma("pool", "dS", dbg_sc.ap()[1:2, 12:20],
                                 s_kn2[0:1, :], deps=[t_kn_back]))
        b.op("sp", lambda h: h.nop(), deps=[t_store[-1]], inc=False)

        with nc.Block() as block:
            @block.tensor
            def _(h):
                for f in b.stream["pe"]:
                    f(h)

            @block.scalar
            def _(h):
                for f in b.stream["act"]:
                    f(h)

            @block.vector
            def _(h):
                for f in b.stream["dve"]:
                    f(h)

            @block.gpsimd
            def _(h):
                for f in b.stream["pool"]:
                    f(h)

            @block.sync
            def _(h):
                for f in b.stream["sp"]:
                    f(h)

        p8.close()

    return nc


def _prep_inputs(inputs):
    f32 = np.float32
    x = np.ascontiguousarray(np.asarray(inputs["L0_attn_hidden_states"], f32))
    Wq = np.asarray(inputs["L0_attn_Wq"], f32)
    Wk = np.asarray(inputs["L0_attn_Wk"], f32)
    Wv = np.asarray(inputs["L0_attn_Wv"], f32)
    Wo = np.asarray(inputs["L0_attn_Wo"], f32)
    Wm = np.asarray(inputs["L0_ffn_weight"], f32)
    lA = np.asarray(inputs["L0_ffn_lora_A"], f32)
    lB = np.asarray(inputs["L0_ffn_lora_B"], f32)
    gw1 = np.asarray(inputs["L0_ffn_gw1"], f32)
    gb1 = np.asarray(inputs["L0_ffn_gb1"], f32)
    lng = np.asarray(inputs["L0_ffn_ln_gamma"], f32)
    lnbta = np.asarray(inputs["L0_ffn_ln_beta"], f32)
    gw2 = np.asarray(inputs["L0_ffn_gw2"], f32)
    gb2 = np.asarray(inputs["L0_ffn_gb2"], f32)
    ln_w = np.asarray(inputs["L0_attn_ln_w"], f32)
    ln_b = np.asarray(inputs["L0_attn_ln_b"], f32)

    def blk(W):
        nm = W.shape[0] // 128
        return np.ascontiguousarray(
            W.reshape(nm, 128, NCH, 128).transpose(0, 3, 2, 1)
        ).reshape(nm, 128, HID)

    shared = {
        "wq": blk(Wq), "wk": blk(Wk), "wv": blk(Wv), "wo": blk(Wo),
        "wm": blk(Wm),
        "a_all": np.ascontiguousarray(lA.transpose(1, 0, 2).reshape(HID, 128)),
        "b_all": np.ascontiguousarray(lB.reshape(128, HID)),
        "gw1t": np.ascontiguousarray(gw1.T),
        "gb1c": np.ascontiguousarray(gb1.reshape(32, 1)),
        "g32": np.ascontiguousarray(np.broadcast_to(lng, (128, 32))),
        "bt32": np.ascontiguousarray(np.broadcast_to(lnbta, (128, 32))),
        "gw2t": np.ascontiguousarray(gw2.T),
        "gb2c": np.ascontiguousarray(gb2.reshape(8, 1)),
        "lnw": np.ascontiguousarray(np.broadcast_to(ln_w, (128, HID))),
        "lnb": np.ascontiguousarray(np.broadcast_to(ln_b, (128, HID))),
        "identbf": np.eye(128, dtype=ml_dtypes.bfloat16),
        "identf": np.eye(128, dtype=f32),
        "onesf": np.ones((128, 128), dtype=f32),
        "onesbf": np.ones((128, 1), dtype=ml_dtypes.bfloat16),
    }
    S = np.zeros((8, 128), dtype=ml_dtypes.bfloat16)
    for p in range(128):
        S[p // 16, p] = 1.0
    shared["selS"] = S

    inv = (1.0 / (np.float32(10000.0) **
                  (np.arange(0, 128, 2, dtype=f32) / f32(128.0)))).astype(f32)
    maps = []
    for core in range(8):
        bb, hh = core // 2, core % 2
        pos = np.arange(hh * TOK, (hh + 1) * TOK, dtype=f32)
        freqs = pos[None, :] * inv[:, None]
        emb = np.concatenate([freqs, freqs], axis=0)
        m = dict(shared)
        m["cosb"] = np.cos(emb).astype(f32)
        m["sinb"] = np.sin(emb).astype(f32)
        m["x_own"] = np.ascontiguousarray(x[bb, hh * TOK:(hh + 1) * TOK, :])
        maps.append(m)
    return maps


def kernel(**inputs):
    maps = _prep_inputs(inputs)
    nc = build_nc()
    res = run_bass_kernel_spmd(nc, maps, core_ids=list(range(8)))
    out = np.empty((BATCH, SEQ, HID), dtype=np.float32)
    for core in range(8):
        bb, hh = core // 2, core % 2
        out[bb, hh * TOK:(hh + 1) * TOK, :] = np.asarray(res.results[core]["y"])
    return out


# version 11
# speedup vs baseline: 1.2967x; 1.0257x over previous
import numpy as np
import ml_dtypes
from contextlib import ExitStack

import concourse.bass as bass
import concourse.mybir as mybir
from concourse.bass_utils import run_bass_kernel_spmd

F32 = mybir.dt.float32
BF16 = mybir.dt.bfloat16
I32 = mybir.dt.int32
AF = mybir.ActivationFunctionType
ALU = mybir.AluOpType
AX = mybir.AxisListType

HID = 4096
NH = 32
NKV = 8
BATCH = 4
SEQ = 2048
TOK = 1024
NT = 8
NCH = 32
EPS = 1e-5
QB = 127.0
CAPACITY = 3072.0
ALPHA = 1.0 / 16.0
SM_SCALE = 0.08838834764831845
SHIFT_CAP = 80.0

DEBUG_DUMPS = False
ENGS = ("pe", "act", "dve", "pool", "sp")
DMA_CH = ("dW0", "dW1", "dX0", "dX1", "dK0", "dK1", "dM", "dS", "cc0", "cc1", "cc2")


class Bld:
    def __init__(self, nc, ex):
        self.nc = nc
        self.sem = {ch: ex.enter_context(nc.semaphore("sem_" + ch))
                    for ch in ENGS + DMA_CH}
        self.cnt = {ch: 0 for ch in ENGS + DMA_CH}
        self.waited = {e: {} for e in ENGS}
        self.stream = {e: [] for e in ENGS}
        self.last_inc = {e: True for e in ENGS}

    def _wait(self, eng, tok):
        ch, v = tok
        if v <= 0 or self.waited[eng].get(ch, 0) >= v:
            return
        self.waited[eng][ch] = v
        sem = self.sem[ch]
        self.stream[eng].append(lambda h, sem=sem, v=v: h.wait_ge(sem, v))

    def op(self, eng, fn, deps=(), inc=True):
        for t in deps:
            if t is not None:
                self._wait(eng, t)
        tok = None
        if inc:
            self.cnt[eng] += 1
            tok = (eng, self.cnt[eng])
            sem = self.sem[eng]
            self.stream[eng].append(lambda h, fn=fn, sem=sem: fn(h).then_inc(sem, 1))
        else:
            self.stream[eng].append(lambda h, fn=fn: fn(h))
        self.last_inc[eng] = bool(inc)
        return tok

    def dma(self, eng, ch, out, in_, deps=()):
        for t in deps:
            if t is not None:
                self._wait(eng, t)
        self.cnt[ch] += 16
        tok = (ch, self.cnt[ch])
        sem = self.sem[ch]
        self.stream[eng].append(
            lambda h, out=out, in_=in_, sem=sem:
                h.dma_start(out=out, in_=in_).then_inc(sem, 16))
        self.last_inc[eng] = False
        return tok

    def coll(self, kind, alu, groups, ins, outs, deps=(), ch="cc0"):
        for t in deps:
            if t is not None:
                self._wait("pool", t)
        self.cnt[ch] += 1
        tok = (ch, self.cnt[ch])
        sem = self.sem[ch]
        self.stream["pool"].append(
            lambda h, sem=sem: h.collective_compute(
                kind, alu, replica_groups=groups, ins=ins, outs=outs
            ).then_inc(sem, 1))
        self.last_inc["pool"] = False
        return tok

    def barrier(self, engines=("pe", "act", "dve", "pool")):
        for e in engines:
            if not self.last_inc[e] and self.stream[e]:
                self.cnt[e] += 1
                sem = self.sem[e]
                self.stream[e].append(
                    lambda h, sem=sem: h.nop().then_inc(sem, 1))
                self.last_inc[e] = True
        toks = [(e, self.cnt[e]) for e in engines]
        for e in engines:
            for t in toks:
                if t[0] != e:
                    self._wait(e, t)


def build_nc():
    nc = bass.Bass("TRN2", target_bir_lowering=False,
                   detect_race_conditions=False)

    def din(name, shape, dt=F32):
        return nc.dram_tensor(name, shape, dt, kind="ExternalInput").ap()

    x_own = din("x_own", [TOK, HID])
    wq = din("wq", [NH, 128, HID])
    wk = din("wk", [NKV, 128, HID])
    wv = din("wv", [NKV, 128, HID])
    wo = din("wo", [NH, 128, HID])
    wm = din("wm", [NCH, 128, HID])
    a_all = din("a_all", [HID, 128])
    b_all = din("b_all", [128, HID])
    gw1t = din("gw1t", [HID, 32])
    gb1c = din("gb1c", [32, 1])
    g32 = din("g32", [128, 32])
    bt32 = din("bt32", [128, 32])
    gw2t = din("gw2t", [32, 8])
    gb2c = din("gb2c", [8, 1])
    lnw = din("lnw", [128, HID])
    lnb = din("lnb", [128, HID])
    cosb = din("cosb", [128, TOK])
    sinb = din("sinb", [128, TOK])
    identbf = din("identbf", [128, 128], BF16)
    identf = din("identf", [128, 128])
    onesf = din("onesf", [128, 128])
    onesbf = din("onesbf", [128, 1], BF16)
    selS = din("selS", [8, 128], BF16)

    y_own = nc.dram_tensor("y", [TOK, HID], F32, kind="ExternalOutput").ap()

    x2d = nc.dram_tensor("x2d", [TOK, HID], F32)
    at3d = nc.dram_tensor("at3d", [TOK, HID], BF16)
    ccg_in = nc.dram_tensor("ccg_in", [2 * NKV * 128, TOK], BF16)
    ccg_out = nc.dram_tensor("ccg_out", [2 * 2 * NKV * 128, TOK], BF16)
    cnt_in = nc.dram_tensor("cnt_in", [1, 16], F32)
    if DEBUG_DUMPS:
        dbg_qn = nc.dram_tensor("dbg_qn", [128, NCH * TOK], BF16,
                                kind="ExternalOutput")
        dbg_q3 = nc.dram_tensor("dbg_q3", [128, NCH * TOK], BF16,
                                kind="ExternalOutput")
        dbg_b3 = nc.dram_tensor("dbg_b3", [128, NCH * TOK], BF16,
                                kind="ExternalOutput")
        dbg_ccg = nc.dram_tensor("dbg_ccg", [2 * NKV * 128, TOK], BF16,
                                 kind="ExternalOutput")
        dbg_at3 = nc.dram_tensor("dbg_at3", [TOK, HID], BF16,
                                 kind="ExternalOutput")
        dbg_x2 = nc.dram_tensor("dbg_x2", [TOK, HID], F32,
                                kind="ExternalOutput")
        dbg_sc = nc.dram_tensor("dbg_sc", [128, 64], F32,
                                kind="ExternalOutput")
    cnt_out = nc.dram_tensor("cnt_out", [1, 16], F32)
    kn_in = nc.dram_tensor("kn_in", [1, NKV], F32)
    kn_out = nc.dram_tensor("kn_out", [1, NKV], F32)

    ex = ExitStack()
    with ex:
        b = Bld(nc, ex)
        E = ex.enter_context

        def ACT(fn, deps=(), inc=True):
            return b.op("act", fn, deps, inc)

        def DVE(fn, deps=(), inc=True):
            return b.op("dve", fn, deps, inc)

        def PE(fn, deps=(), inc=False):
            return b.op("pe", fn, deps, inc)

        s_identbf = E(nc.sbuf_tensor("s_identbf", [128, 128], BF16))
        s_identf = E(nc.sbuf_tensor("s_identf", [128, 128], F32))
        s_onesf = E(nc.sbuf_tensor("s_onesf", [128, 128], F32))
        s_onesbf = E(nc.sbuf_tensor("s_onesbf", [128, 1], BF16))
        s_selS = E(nc.sbuf_tensor("s_selS", [8, 128], BF16))
        s_g32 = E(nc.sbuf_tensor("s_g32", [128, 32], F32))
        s_bt32 = E(nc.sbuf_tensor("s_bt32", [128, 32], F32))
        s_gb1 = E(nc.sbuf_tensor("s_gb1", [32, 1], F32))
        s_gb2 = E(nc.sbuf_tensor("s_gb2", [8, 1], F32))
        s_gw2bf = E(nc.sbuf_tensor("s_gw2bf", [32, 8], BF16))
        s_scales = E(nc.sbuf_tensor("s_scales", [128, 12], F32))
        s_negM = E(nc.sbuf_tensor("s_negM", [128, 1], F32))
        s_stat = E(nc.sbuf_tensor("s_stat", [128, 48], F32))
        s_row = E(nc.sbuf_tensor("s_row", [1, 64], F32))
        s_qn2 = E(nc.sbuf_tensor("s_qn2", [1, NH], F32))
        s_kn2 = E(nc.sbuf_tensor("s_kn2", [1, NKV], F32))
        s_wcol = E(nc.sbuf_tensor("s_wcol", [128, 160], F32))
        s_cnt = E(nc.sbuf_tensor("s_cnt", [128, 16], F32))
        s_cntrow = E(nc.sbuf_tensor("s_cntrow", [1, 16], F32))
        s_alb = E(nc.sbuf_tensor("s_alb", [128, 16], F32))
        s_c1 = E(nc.sbuf_tensor("s_c1", [128, NT, 8], F32))
        s_c2 = E(nc.sbuf_tensor("s_c2", [128, NT, 8], F32))
        s_tsc = E(nc.sbuf_tensor("s_tsc", [128, TOK], BF16))
        s_cstg = E(nc.sbuf_tensor("s_cstg", [128, 128], F32))

        regA_buf = E(nc.sbuf_tensor("regA", [128, NCH * TOK], BF16))
        regC_buf = E(nc.sbuf_tensor("regC", [128, NCH * TOK], BF16))

        strm = ExitStack()
        SB = {}

        _sbn = [0]

        def open_stream_base(st):
            _sbn[0] += 1
            n = _sbn[0]
            SB["blk"] = [st.enter_context(
                nc.sbuf_tensor(f"s_wblk{i}_{n}", [128, 2048], F32))
                for i in range(2)]
            SB["bf"] = [st.enter_context(
                nc.sbuf_tensor(f"s_wbf{i}_{n}", [128, 2048], BF16))
                for i in range(2)]
            SB["junk"] = st.enter_context(
                nc.sbuf_tensor(f"s_wjunk_{n}", [128, 2048], F32))

        def open_stream_qi(st):
            n = _sbn[0]
            SB["qi"] = [st.enter_context(
                nc.sbuf_tensor(f"s_wqi{i}_{n}", [128, 2048], I32))
                for i in range(2)]

        def open_stream_bufs(st):
            open_stream_base(st)
            open_stream_qi(st)

        open_stream_base(strm)

        pscl = ExitStack()
        p_s1 = pscl.enter_context(nc.psum_tensor("p_s1", [1, 512], F32))
        pb128 = pscl.enter_context(nc.psum_tensor("pb128", [128, 512], F32))

        class PbfRing:

            def __init__(self, ap):
                self.ap = ap
                self.rd = [None, None]
                self.i = 0

            def next(self):
                s = self.i % 8
                self.i += 1
                bk = s % 2
                c0 = bk * 1024 + (s // 2) * 128
                return self.ap[:, c0:c0 + 128], bk

        def _mkset(base):
            names = ("sum", "sq", "r", "b", "am", "s", "inv", "t0", "t1",
                     "den", "m1", "m2", "d", "e2", "rg", "g1", "g2")
            d = {nm: s_stat[:, base + i:base + i + 1]
                 for i, nm in enumerate(names)}
            d["last"] = None
            return d

        SCP = [_mkset(0), _mkset(17)]
        sc0 = SCP[0]
        c_sum = sc0["sum"]
        c_sq = sc0["sq"]
        c_r = sc0["r"]
        c_b = sc0["b"]
        c_am = sc0["am"]
        c_s = sc0["s"]
        c_inv = sc0["inv"]
        c_t0 = sc0["t0"]
        c_t1 = sc0["t1"]
        c_den = sc0["den"]
        c_m1 = sc0["m1"]
        c_m2 = sc0["m2"]
        c_d = sc0["d"]
        c_e2 = sc0["e2"]
        c_rg = sc0["rg"]
        c_g1 = sc0["g1"]
        c_g2 = sc0["g2"]

        tc = []
        for dst, src in ((s_identbf, identbf), (s_identf, identf),
                         (s_onesf, onesf), (s_onesbf, onesbf), (s_selS, selS),
                         (s_g32, g32), (s_bt32, bt32), (s_gb1, gb1c),
                         (s_gb2, gb2c)):
            tc.append(b.dma("pool", "dM", dst[:], src))
        t_const = tc[-1]
        t_cnt0 = DVE(lambda h: h.memset(s_cnt[:], 0.0))
        c_eps = s_stat[:, 40:41]
        t_eps = DVE(lambda h: h.memset(c_eps, EPS))
        r_tiny = s_row[0:1, 62:63]
        t_tiny = DVE(lambda h: h.memset(r_tiny, 1e-30))

        def ln_stats(src_ap, n, deps, copy_dst, sq_dst, sc=None):
            if sc is None:
                sc = SCP[0]
            csum, csq, ct0, ct1, cr, cb = (sc["sum"], sc["sq"], sc["t0"],
                                           sc["t1"], sc["r"], sc["b"])
            deps = list(deps) + [sc["last"]]
            dst = copy_dst
            t2 = ACT(lambda h, dst=dst: h.activation(out=dst, in_=src_ap,
                                                     func=AF.Copy,
                                                     accum_out=csum), deps)
            t1 = ACT(lambda h, sq_dst=sq_dst: h.activation(
                out=sq_dst, in_=src_ap, func=AF.Square, accum_out=csq),
                     deps)
            t3 = DVE(lambda h: h.tensor_scalar(out=ct0, in0=csum,
                                               scalar1=1.0 / n, scalar2=None,
                                               op0=ALU.mult), [t2])
            t4 = DVE(lambda h: h.tensor_tensor(out=ct1, in0=ct0, in1=ct0,
                                               op=ALU.mult), [t3])
            t5 = DVE(lambda h: h.scalar_tensor_tensor(out=ct1, in0=csq,
                                                      scalar=1.0 / n, in1=ct1,
                                                      op0=ALU.mult,
                                                      op1=ALU.subtract),
                     [t1, t4])
            t6 = ACT(lambda h: h.activation(out=ct1, in_=ct1, func=AF.Ln,
                                            bias=c_eps, scale=1.0),
                     [t5, t_eps])
            t7 = ACT(lambda h: h.activation(out=cr, in_=ct1, func=AF.Exp,
                                            bias=0.0, scale=-0.5), [t6])
            t8 = DVE(lambda h: h.tensor_tensor(out=cb, in0=ct0, in1=cr,
                                               op=ALU.mult), [t7, t3])
            t9 = DVE(lambda h: h.tensor_scalar(out=cb, in0=cb, scalar1=-1.0,
                                               scalar2=None, op0=ALU.mult),
                     [t8])
            return t7, t9, t2

        def emit_act_quant(src_col, out_col, y1ap, deps_in, n_chunks=4,
                           sc=None):
            if sc is None:
                sc = SCP[0]
            cs_, cinv_ = sc["s"], sc["inv"]
            A = y1ap[:, 0:1024]
            Bi = y1ap[:, 1024:2048].bitcast(I32)
            C = y1ap[:, 2048:3072]
            Es = y1ap[:, 3072:4096]
            tA = tQ = None
            for k in range(n_chunks):
                tA = DVE(lambda h, k=k: h.tensor_scalar(
                    out=A, in0=src_col(k), scalar1=cs_, scalar2=None,
                    op0=ALU.mult), list(deps_in) + [tQ])
                tB = DVE(lambda h: h.tensor_copy(out=Bi, in_=A), [tA])
                tC = DVE(lambda h: h.tensor_tensor(out=C, in0=A, in1=Bi,
                                                   op=ALU.subtract), [tB])
                tD = DVE(lambda h: h.tensor_scalar(out=A, in0=C, scalar1=0.5,
                                                   scalar2=None,
                                                   op0=ALU.is_ge), [tC])
                tE = DVE(lambda h: h.scalar_tensor_tensor(
                    out=Es, in0=C, scalar=-0.5, in1=A, op0=ALU.is_le,
                    op1=ALU.subtract), [tD])
                tF = DVE(lambda h: h.tensor_tensor(out=C, in0=Bi, in1=Es,
                                                   op=ALU.subtract), [tE])
                tQ = DVE(lambda h, k=k: h.tensor_scalar(
                    out=out_col(k), in0=C, scalar1=cinv_, scalar2=None,
                    op0=ALU.mult), [tF])
            sc["last"] = tQ
            return tA, tQ

        class HalfStream:
            def __init__(self):
                self.i = 0
                self.t_q1 = [None, None]
                self.t_q2 = [None, None]
                self.t_mm = [None, None]
                self.t_abs = [None, None]

            def next(self, src_slice, mode, scol=0, wcol=0, extra_deps=(),
                     scol_c2=(8,), par=None):
                if par is None:
                    par = self.i % 2
                    self.i += 1
                blk, bfb, jnk = SB["blk"][par], SB["bf"][par], SB["junk"]
                qib = SB["qi"][par] if "qi" in SB else None
                tl = b.dma("sp", "dW%d" % par, blk[:], src_slice,
                           deps=[self.t_q1[par], self.t_abs[par]] +
                                list(extra_deps))
                if mode == "abs":
                    ta = ACT(lambda h, blk=blk, jnk=jnk, wcol=wcol:
                             h.activation(
                        out=jnk[:], in_=blk[:], func=AF.Abs,
                        accum_out=s_wcol[:, wcol:wcol + 1]), [tl])
                    self.t_abs[par] = ta
                    return par, ta
                if mode == "quant":
                    c2col = scol_c2[0]
                    qif = qib[:].bitcast(F32)
                    tq1 = DVE(lambda h, blk=blk, qif=qif, scol=scol:
                              h.tensor_scalar(
                        out=qif, in0=blk[:],
                        scalar1=s_scales[:, scol:scol + 1], scalar2=0.5,
                        op0=ALU.mult, op1=ALU.is_ge), [tl, self.t_q2[par]])
                    tq2 = DVE(lambda h, bfb=bfb, qif=qif, blk=blk,
                              c2col=c2col: h.scalar_tensor_tensor(
                        out=bfb[:], in0=blk[:],
                        scalar=s_scales[:, c2col:c2col + 1], in1=qif,
                        op0=ALU.is_le, op1=ALU.subtract),
                        [tq1, self.t_mm[par]])
                    self.t_q1[par] = tq2
                    self.t_q2[par] = tq2
                    return par, tq2
                tq = DVE(lambda h, blk=blk, bfb=bfb: h.tensor_copy(
                    out=bfb[:], in_=blk[:]),
                    [tl, self.t_mm[par]])
                self.t_q1[par] = tq
                self.t_q2[par] = tq
                return par, tq

            def bf(self, par):
                return SB["bf"][par]

            def consumed(self, par, tok):
                self.t_mm[par] = tok

        ws = HalfStream()

        scale_queue = []
        wc_base = {"wk": 0, "wv": 16, "wq": 32, "wo": 96}
        for nm_, src_, name_ in ((NKV, wk, "wk"), (NKV, wv, "wv"),
                                 (NH, wq, "wq"), (NH, wo, "wo")):
            for m in range(nm_):
                for hf in range(2):
                    scale_queue.append(
                        (name_, src_, m, hf, wc_base[name_] + 2 * m + hf))
        sq_pos = [0]
        sq_abs = {"wk": [], "wv": [], "wq": [], "wo": []}
        t_pb_use = [None]

        def emit_scale_halves(k):
            for _ in range(k):
                if sq_pos[0] >= len(scale_queue):
                    return
                name, src, m, hf, wcol = scale_queue[sq_pos[0]]
                sq_pos[0] += 1
                par, ta = ws.next(src[m, :, hf * 2048:(hf + 1) * 2048], "abs",
                                  wcol=wcol)
                sq_abs[name].append(ta)

        def finish_scale(name, nm, scol_s, scol_d, extra, slot):
            nelem = nm * 128 * HID
            cb = wc_base[name]
            tr = DVE(lambda h, nm=nm, cb=cb: h.tensor_reduce(
                out=c_t0, in_=s_wcol[:, cb:cb + 2 * nm], axis=AX.X,
                op=ALU.add), [sq_abs[name][-1]])
            tm1 = PE(lambda h, slot=slot: h.matmul(
                p_s1[0:1, slot:slot + 1], s_onesf[:, 0:1], c_t0,
                start=True, stop=True), [tr, t_pb_use[0], t_const], inc=True)
            r0 = 32 + slot * 4
            tm2 = DVE(lambda h, nelem=nelem, r0=r0, slot=slot: h.tensor_scalar(
                out=s_row[0:1, r0:r0 + 1], in0=p_s1[0:1, slot:slot + 1],
                scalar1=1.0 / nelem, scalar2=1e-5, op0=ALU.mult, op1=ALU.max),
                [tm1])
            tm3 = DVE(lambda h, r0=r0: h.reciprocal(
                out=s_row[0:1, r0 + 1:r0 + 2], in_=s_row[0:1, r0:r0 + 1]),
                [tm2])
            tm4 = DVE(lambda h, r0=r0, extra=extra: h.tensor_scalar(
                out=s_row[0:1, r0 + 2:r0 + 3], in0=s_row[0:1, r0:r0 + 1],
                scalar1=-extra, scalar2=None, op0=ALU.mult), [tm2])
            tm5 = DVE(lambda h, r0=r0: h.tensor_scalar(
                out=s_row[0:1, r0 + 3:r0 + 4], in0=s_row[0:1, r0:r0 + 1],
                scalar1=-0.5, scalar2=None, op0=ALU.mult), [tm2])
            tb1 = PE(lambda h, scol_d=scol_d, r0=r0: h.matmul(
                pb128[:, scol_d:scol_d + 1], s_onesf[0:1, :],
                s_row[0:1, r0 + 2:r0 + 3], start=True, stop=True),
                [tm4], inc=True)
            tb2 = PE(lambda h, scol_s=scol_s, r0=r0: h.matmul(
                pb128[:, scol_s:scol_s + 1], s_onesf[0:1, :],
                s_row[0:1, r0 + 1:r0 + 2], start=True, stop=True),
                [tm3], inc=True)
            tb3 = PE(lambda h, slot=slot, r0=r0: h.matmul(
                pb128[:, 8 + slot:9 + slot], s_onesf[0:1, :],
                s_row[0:1, r0 + 3:r0 + 4], start=True, stop=True),
                [tm5], inc=True)
            lo, hi = min(scol_s, scol_d), max(scol_s, scol_d)
            te_ = DVE(lambda h, lo=lo, hi=hi: h.tensor_copy(
                out=s_scales[:, lo:hi + 1], in_=pb128[:, lo:hi + 1]),
                [tb1, tb2, tb3])
            te = DVE(lambda h, slot=slot: h.tensor_copy(
                out=s_scales[:, 8 + slot:9 + slot],
                in_=pb128[:, 8 + slot:9 + slot]), [tb3, te_])
            t_pb_use[0] = te
            return te

        qn3 = regA_buf[:].rearrange("p (c t) -> p c t", t=TOK)

        scope1 = ExitStack()
        pbf1s = ExitStack()
        ring1 = PbfRing(pbf1s.enter_context(
            nc.psum_tensor("pbf1", [128, 2048], BF16))[:])
        s_x = [scope1.enter_context(nc.sbuf_tensor(f"s_x{i}", [128, HID], F32))
               for i in range(2)]
        s_y1 = regC_buf[:, 0:8192].bitcast(F32)
        s_lnw = regC_buf[:, 8192:16384].bitcast(F32)
        s_lnb = regC_buf[:, 16384:24576].bitcast(F32)
        s_qtm = regC_buf[:, 24576:28672]

        t_lnw = b.dma("pool", "dM", s_lnw, lnw)
        t_lnb = b.dma("pool", "dM", s_lnb, lnb)

        t_x_use = [None, None]
        t_pbf = [None] * 8
        tok_qn = None
        prev_tq1 = None
        prev_tq2 = None
        prev_tz = None
        for t in range(NT):
            par = t % 2
            xb = s_x[par]
            tl = b.dma("pool", "dX%d" % par, xb[:],
                       x_own[t * 128:(t + 1) * 128, :], deps=[t_x_use[par]])
            sc = SCP[par]
            scr, scb, scam, scs, scinv = (sc["r"], sc["b"], sc["am"],
                                          sc["s"], sc["inv"])
            r1, b1, _ = ln_stats(xb[:], HID, [tl, prev_tq2],
                                 copy_dst=s_y1, sq_dst=s_y1, sc=sc)
            ty = ACT(lambda h, xb=xb, scr=scr, scb=scb: h.activation(
                out=s_y1, in_=xb[:], func=AF.Identity, bias=scb, scale=scr),
                [r1, b1])
            r2, b2, _ = ln_stats(s_y1, HID, [ty],
                                 copy_dst=xb[:], sq_dst=xb[:], sc=sc)
            tz = ACT(lambda h, xb=xb, scr=scr, scb=scb: h.activation(
                out=xb[:], in_=s_y1, func=AF.Identity, bias=scb, scale=scr),
                [r2, b2, prev_tq1])
            tz2 = DVE(lambda h, xb=xb: h.tensor_tensor(
                out=s_y1, in0=xb[:], in1=s_lnw, op=ALU.mult),
                [tz, t_lnw])
            tz3 = DVE(lambda h, xb=xb: h.tensor_tensor(
                out=xb[:], in0=s_y1, in1=s_lnb, op=ALU.add),
                [tz2, t_lnb])
            tam = DVE(lambda h, xb=xb, scam=scam: h.tensor_reduce(
                out=scam, in_=xb[:], axis=AX.X, op=ALU.max,
                apply_absolute_value=True), [tz3])
            tmc = DVE(lambda h, scam=scam: h.tensor_scalar(
                out=scam, in0=scam, scalar1=1e-5, scalar2=None,
                op0=ALU.max), [tam])
            trc = DVE(lambda h, scs=scs, scam=scam: h.reciprocal(
                out=scs, in_=scam), [tmc])
            ts_ = DVE(lambda h, scs=scs: h.tensor_scalar(
                out=scs, in0=scs, scalar1=QB,
                scalar2=None, op0=ALU.mult), [trc])
            tin = DVE(lambda h, scinv=scinv, scam=scam: h.tensor_scalar(
                out=scinv, in0=scam, scalar1=1.0 / QB, scalar2=None,
                op0=ALU.mult), [tmc])
            tq1, tq2 = emit_act_quant(
                lambda k, xb=xb: xb[:, k * 1024:(k + 1) * 1024],
                lambda k: s_qtm[:, k * 1024:(k + 1) * 1024],
                s_y1, [ts_, tin, tz3, tam, tok_qn], sc=sc)
            prev_tq1 = tq1
            t_x_use[par] = tq1
            prev_tq2 = tq2
            prev_tz = tz
            for c in range(NCH):
                sl, bk = ring1.next()
                tp = PE(lambda h, c=c, sl=sl: h.transpose(
                    sl, s_qtm[:, c * 128:(c + 1) * 128], s_identbf[:]),
                    [tq2, ring1.rd[bk], t_const], inc=True)
                tev = DVE(lambda h, c=c, t=t, sl=sl: h.tensor_copy(
                    out=qn3[:, c, t * 128:(t + 1) * 128], in_=sl), [tp])
                ring1.rd[bk] = tev
                tok_qn = tev
            emit_scale_halves(6)
        scope1.close()
        pbf1s.close()

        if DEBUG_DUMPS:
            td_ = b.dma("pool", "dM", dbg_qn.ap(), regA_buf[:],
                        deps=[tok_qn, prev_tq1])
            b._wait("dve", td_)
            b._wait("pe", td_)
        tok_swk = finish_scale("wk", NKV, 2, 3, 1.0, 0)
        tok_swv = finish_scale("wv", NKV, 4, 5, 1.0, 1)

        Q3 = regC_buf[:].rearrange("p (m t) -> p m t", t=TOK)
        strmqi = ExitStack()
        open_stream_qi(strmqi)
        scope2 = ExitStack()
        s_cos = scope2.enter_context(nc.sbuf_tensor("s_cos", [128, TOK], F32))
        s_sin = scope2.enter_context(nc.sbuf_tensor("s_sin", [128, TOK], F32))
        s_kpre = scope2.enter_context(nc.sbuf_tensor("s_kpre", [128, TOK], F32))
        s_krot = scope2.enter_context(nc.sbuf_tensor("s_krot", [128, TOK], F32))
        s_kt1 = scope2.enter_context(nc.sbuf_tensor("s_kt1", [128, TOK], F32))
        s_stg = [scope2.enter_context(
            nc.sbuf_tensor(f"s_stg{i}", [128, TOK], BF16)) for i in range(2)]
        s_sq = s_kt1[:].bitcast(BF16)[:, 0:TOK]
        t_cos = b.dma("pool", "dM", s_cos[:], cosb,
                      deps=[prev_tq1, prev_tq2, prev_tz])
        t_sin = b.dma("pool", "dM", s_sin[:], sinb)
        pj = ExitStack()
        pproj4 = [pj.enter_context(nc.psum_tensor(f"pproj{i}", [128, 512], F32))
                  for i in range(4)]
        pnorm = pj.enter_context(nc.psum_tensor("pnorm", [1, 1024], F32))

        t_pp = [None, None, None, None]
        t_stg_use = [None, None]
        t_rope_rd = [None]
        t_stage = []
        t_kn = []
        t_pnorm = [None]
        t_sq_use = [None]

        blk_ctr = [0]
        cur_pr = [0]

        def q_half(src, m, hf, scol_q, scale_tok, c2col_q):
            return ws.next(src[m, :, hf * 2048:(hf + 1) * 2048], "quant",
                           scol=scol_q, extra_deps=[scale_tok],
                           scol_c2=(c2col_q,), par=hf)

        def mm_half(hf, qinfo, scale_tok, new_block):
            if new_block:
                cur_pr[0] = blk_ctr[0] % 2
                blk_ctr[0] += 1
            pr = cur_pr[0]
            pproj = pproj4[2 * pr:2 * pr + 2]
            par, trdy = qinfo
            tokm = None
            for j in range(16):
                c = hf * 16 + j
                for tg in range(2):
                    dep = [trdy, tok_qn, scale_tok]
                    if c == 0:
                        dep.append(t_pp[2 * pr + tg])
                    tokm = PE(lambda h, wb_=ws.bf(par), j=j, c=c, tg=tg,
                              pp_=pproj[tg]: h.matmul(
                        pp_[:],
                        wb_[:, j * 128:(j + 1) * 128],
                        qn3[:, c, tg * 512:(tg + 1) * 512],
                        start=(c == 0), stop=(c == 31)),
                        dep, inc=(tg == 1 and (c == 15 or c == 31)))
            ws.consumed(par, tokm)
            return tokm

        pend_norm = [None]

        def flush_norm():
            if pend_norm[0] is None:
                return
            tsq, kind, m = pend_norm[0]
            pend_norm[0] = None
            tn1 = PE(lambda h: h.matmul(pnorm[0:1, 0:512],
                                        s_onesbf[:, 0:1], s_sq[:, 0:512],
                                        start=True, stop=True),
                     [tsq, t_pnorm[0], t_const], inc=True)
            tn2 = PE(lambda h: h.matmul(pnorm[0:1, 512:1024],
                                        s_onesbf[:, 0:1],
                                        s_sq[:, 512:1024],
                                        start=True, stop=True),
                     [tsq], inc=True)
            t_sq_use[0] = tn2
            dst = s_qn2 if kind == "q" else s_kn2
            tnr = DVE(lambda h, dst=dst, m=m: h.tensor_reduce(
                out=dst[0:1, m:m + 1], in_=pnorm[0:1, :], axis=AX.X,
                op=ALU.max), [tn1, tn2])
            t_pnorm[0] = tnr
            if kind == "k":
                t_kn.append(tnr)

        def proj_evac(m, dcol, rope, kind, mm_last):
            flush_norm()
            pr = cur_pr[0]
            pproj = pproj4[2 * pr:2 * pr + 2]
            stg_par = m % 2
            deps_e = [mm_last[0], mm_last[1]]
            if rope:
                te0 = ACT(lambda h, dcol=dcol, pp_=pproj[0]: h.activation(
                    out=s_kpre[:, 0:512], in_=pp_[:], func=AF.Copy,
                    scale=s_scales[:, dcol:dcol + 1]),
                    deps_e + [t_rope_rd[0]])
                te1 = ACT(lambda h, dcol=dcol, pp_=pproj[1]: h.activation(
                    out=s_kpre[:, 512:1024], in_=pp_[:], func=AF.Copy,
                    scale=s_scales[:, dcol:dcol + 1]), deps_e)
                GPS = DVE
                tn = GPS(lambda h: h.tensor_scalar(
                    out=s_krot[0:64, :], in0=s_kpre[64:128, :], scalar1=-1.0,
                    scalar2=None, op0=ALU.mult), [te0, te1])
                tcp = GPS(lambda h: h.tensor_copy(out=s_krot[64:128, :],
                                                  in_=s_kpre[0:64, :]),
                          [te0, te1])
                t1 = GPS(lambda h: h.tensor_tensor(out=s_kt1[:], in0=s_kpre[:],
                                                   in1=s_cos[:], op=ALU.mult),
                         [te0, te1, t_cos, t_sq_use[0]])
                t2 = GPS(lambda h: h.tensor_tensor(out=s_kpre[:],
                                                   in0=s_krot[:], in1=s_sin[:],
                                                   op=ALU.mult),
                         [tn, tcp, t_sin, t1])
                if kind == "q":
                    t3 = GPS(lambda h, m=m: h.tensor_tensor(
                        out=Q3[:, m, :], in0=s_kt1[:], in1=s_kpre[:],
                        op=ALU.add), [t1, t2])
                else:
                    t3 = GPS(lambda h, stg_par=stg_par: h.tensor_tensor(
                        out=s_stg[stg_par][:], in0=s_kt1[:], in1=s_kpre[:],
                        op=ALU.add), [t1, t2, t_stg_use[stg_par]])
                t_rope_rd[0] = t3
                t_pp[2 * pr], t_pp[2 * pr + 1] = te0, te1
            else:
                te0 = ACT(lambda h, stg_par=stg_par, dcol=dcol, pp_=pproj[0]:
                          h.activation(
                    out=s_stg[stg_par][:, 0:512], in_=pp_[:],
                    func=AF.Copy, scale=s_scales[:, dcol:dcol + 1]),
                    deps_e + [t_stg_use[stg_par]])
                te1 = ACT(lambda h, stg_par=stg_par, dcol=dcol, pp_=pproj[1]:
                          h.activation(
                    out=s_stg[stg_par][:, 512:1024], in_=pp_[:],
                    func=AF.Copy, scale=s_scales[:, dcol:dcol + 1]), deps_e)
                t3 = te1
                t_pp[2 * pr], t_pp[2 * pr + 1] = te0, te1
            tsq = None
            if kind == "q":
                tsq = ACT(lambda h, m=m: h.activation(
                    out=s_sq, in_=Q3[:, m, :], func=AF.Square),
                    [t3, t_sq_use[0]])
            elif kind == "k":
                tsq = ACT(lambda h, stg_par=stg_par: h.activation(
                    out=s_sq, in_=s_stg[stg_par][:], func=AF.Square),
                    [t3, t_sq_use[0]])
            if tsq is not None:
                pend_norm[0] = (tsq, kind, m)
            if kind in ("k", "v"):
                row0 = (0 if kind == "k" else NKV * 128) + m * 128
                td = b.dma("pool", "dK0", ccg_in.ap()[row0:row0 + 128, :],
                           s_stg[stg_par][:],
                           deps=[t3] + ([tsq] if kind == "k" else []))
                t_stg_use[stg_par] = td
                t_stage.append(td)
            return t3

        def run_pipeline(blocks, nsh):
            n = len(blocks)
            (bsrc, m, scol, dcol, rope, kind, stok, c2c) = blocks[0]
            qA = q_half(bsrc, m, 0, scol, stok, c2c)
            qB = q_half(bsrc, m, 1, scol, stok, c2c)
            for i in range(n):
                (bsrc, m, scol, dcol, rope, kind, stok, c2c) = blocks[i]
                mm0 = mm_half(0, qA, stok, True)
                if i + 1 < n:
                    (b2, m2, s2, d2, r2, k2, st2, cc2) = blocks[i + 1]
                    qA = q_half(b2, m2, 0, s2, st2, cc2)
                mm1 = mm_half(1, qB, stok, False)
                if i + 1 < n:
                    qB = q_half(b2, m2, 1, s2, st2, cc2)
                proj_evac(m, dcol, rope, kind, [mm0, mm1])
                emit_scale_halves(nsh)

        blocks = ([(wk, m, 2, 3, True, "k", tok_swk, 8)
                   for m in range(NKV)] +
                  [(wv, m, 4, 5, False, "v", tok_swv, 9)
                   for m in range(NKV)])
        run_pipeline(blocks, 3)
        flush_norm()
        tok_swq = finish_scale("wq", NH, 0, 1, SM_SCALE, 2)

        t_kn_dma = b.dma("pool", "dK0", kn_in.ap(), s_kn2[0:1, :], deps=t_kn)
        tok_ag = b.coll("AllGather", ALU.bypass,
                        [[0, 1], [2, 3], [4, 5], [6, 7]],
                        [ccg_in.ap().opt()], [ccg_out.ap().opt()],
                        deps=t_stage, ch="cc0")
        tok_knr = b.coll("AllReduce", ALU.max,
                         [[0, 1], [2, 3], [4, 5], [6, 7]],
                         [kn_in.ap().opt()], [kn_out.ap().opt()],
                         deps=[t_kn_dma], ch="cc1")
        t_kn_back = b.dma("pool", "dK0", s_kn2[0:1, :], kn_out.ap(),
                          deps=[tok_knr])

        run_pipeline([(wq, m, 0, 1, True, "q", tok_swq, 10)
                      for m in range(NH)], 2)
        flush_norm()
        tok_swo = finish_scale("wo", NH, 6, 7, 1.0, 3)

        tq2m = DVE(lambda h: h.tensor_reduce(out=s_row[0:1, 0:1],
                                             in_=s_qn2[0:1, :], axis=AX.X,
                                             op=ALU.max), [t_pnorm[0]])
        tk2m = DVE(lambda h: h.tensor_reduce(out=s_row[0:1, 1:2],
                                             in_=s_kn2[0:1, :], axis=AX.X,
                                             op=ALU.max), [t_kn_back])
        tl1 = ACT(lambda h: h.activation(out=s_row[0:1, 2:3],
                                         in_=s_row[0:1, 0:1], func=AF.Ln,
                                         bias=r_tiny, scale=1.0),
                  [tq2m, t_tiny])
        tl2 = ACT(lambda h: h.activation(out=s_row[0:1, 3:4],
                                         in_=s_row[0:1, 1:2], func=AF.Ln,
                                         bias=r_tiny, scale=1.0),
                  [tk2m, t_tiny])
        tl3 = DVE(lambda h: h.tensor_tensor(out=s_row[0:1, 4:5],
                                            in0=s_row[0:1, 2:3],
                                            in1=s_row[0:1, 3:4], op=ALU.add),
                  [tl1, tl2])
        tl4 = ACT(lambda h: h.activation(out=s_row[0:1, 5:6],
                                         in_=s_row[0:1, 4:5], func=AF.Exp,
                                         bias=0.0, scale=0.5), [tl3])
        tl5 = DVE(lambda h: h.tensor_scalar(out=s_row[0:1, 6:7],
                                            in0=s_row[0:1, 5:6],
                                            scalar1=SHIFT_CAP, scalar2=-1.0,
                                            op0=ALU.min, op1=ALU.mult), [tl4])
        tlb = PE(lambda h: h.matmul(pb128[:, 2:3], s_onesf[0:1, :],
                                    s_row[0:1, 6:7], start=True, stop=True),
                 [tl5, t_pb_use[0]], inc=True)
        tok_negM = DVE(lambda h: h.tensor_copy(out=s_negM[:],
                                               in_=pb128[:, 2:3]), [tlb])
        t_pb_use[0] = tok_negM

        if DEBUG_DUMPS:
            td_ = b.dma("pool", "dM", dbg_q3.ap(), regC_buf[:],
                        deps=[tok_negM, t_rope_rd[0]])
            b._wait("dve", td_)
        pj.close()
        pscl.close()
        scope2.close()
        strmqi.close()
        strm.close()
        b.barrier(("pe", "act", "dve"))
        sd = ExitStack()
        psc = [sd.enter_context(nc.psum_tensor(f"psc{i}", [128, 1024], F32))
               for i in range(2)]
        pout = [sd.enter_context(nc.psum_tensor(f"pout{i}", [128, 512], F32))
                for i in range(3)]
        s_kst = [sd.enter_context(nc.sbuf_tensor(f"s_kst{i}", [128, 2048],
                                                 BF16)) for i in range(2)]
        s_vstg = [sd.enter_context(nc.sbuf_tensor(f"s_vstg{i}", [128, 2048],
                                                  BF16)) for i in range(2)]
        s_vtm = [sd.enter_context(nc.sbuf_tensor(f"s_vtm{i}", [128, 16 * 129],
                                                 BF16)) for i in range(2)]
        s_atn = [sd.enter_context(nc.sbuf_tensor(f"s_atn{i}", [128, 1024],
                                                 BF16)) for i in range(4)]

        t_vtm_use = [DVE(lambda h: h.memset(s_vtm[0][:], 1.0)),
                     DVE(lambda h: h.memset(s_vtm[1][:], 1.0))]
        t_kst_use = [None, None]
        t_vstg_use = [None, None]
        t_psc_use = [None, None]
        t_atn_use = [None, None, None, None]
        t_pout_use = [None, None, None]
        t_pbf = [None] * 8

        B3 = regA_buf[:].rearrange("p (m t) -> p m t", t=TOK)

        for g in range(NKV):
            par = g % 2
            tk0 = b.dma("pool", "dK%d" % par, s_kst[par][:, 0:1024],
                        ccg_out.ap()[g * 128:(g + 1) * 128, :],
                        deps=[tok_ag, t_kst_use[par]] +
                             ([tok_swo] if g < 2 else []))
            tk1 = b.dma("pool", "dK%d" % par, s_kst[par][:, 1024:2048],
                        ccg_out.ap()[2048 + g * 128:2048 + (g + 1) * 128, :],
                        deps=[tok_ag])
            tv0_ = b.dma("pool", "dK%d" % par, s_vstg[par][:, 0:1024],
                         ccg_out.ap()[1024 + g * 128:1024 + (g + 1) * 128, :],
                         deps=[tok_ag, t_vstg_use[par]])
            tv1_ = b.dma("pool", "dK%d" % par, s_vstg[par][:, 1024:2048],
                         ccg_out.ap()[3072 + g * 128:3072 + (g + 1) * 128, :],
                         deps=[tok_ag])
            t_ld_all = tv1_
            tvl = None
            for kt in range(16):
                bk = kt % 2
                sl = pout[bk][:].bitcast(BF16)[:, 0:128]
                tp = PE(lambda h, par=par, kt=kt, sl=sl: h.transpose(
                    sl, s_vstg[par][:, kt * 128:(kt + 1) * 128], s_identbf[:]),
                    [t_ld_all, t_pout_use[bk], t_vtm_use[par]], inc=True)
                tev = DVE(lambda h, par=par, kt=kt, sl=sl: h.tensor_copy(
                    out=s_vtm[par][:, kt * 129:kt * 129 + 128], in_=sl), [tp])
                t_pout_use[bk] = tev
                tvl = tev
            t_vstg_use[par] = tvl
            tok_vtm = tvl

            for hq in range(4):
                hh = 4 * g + hq
                av_toks = [None] * NT
                exp_toks = [None] * 16

                def emit_av(j, hh=hh, par=par, av_toks=av_toks,
                            exp_toks=exp_toks):
                    ab = j % 4
                    tav = None
                    for qt in range(NT):
                        bk, slot = qt // 3, qt % 3
                        dep = [exp_toks[j], tok_vtm]
                        if j == 0:
                            dep.append(t_pout_use[bk])
                        tav = PE(lambda h, ab=ab, par=par, j=j, bk=bk,
                                 slot=slot, qt=qt: h.matmul(
                            pout[bk][:, slot * 129:slot * 129 + 129],
                            s_atn[ab][:, qt * 128:(qt + 1) * 128],
                            s_vtm[par][:, j * 129:j * 129 + 129],
                            start=(j == 0 and slot == 0), stop=(j == 15),
                            skip_group_check=True),
                            dep, inc=(j == 15 or qt == NT - 1))
                        if j == 15:
                            av_toks[qt] = tav
                    t_atn_use[ab] = tav

                for kt in range(16):
                    pp = kt % 2
                    ab = kt % 4
                    ts0 = PE(lambda h, par=par, kt=kt, hh=hh, pp=pp: h.matmul(
                        psc[pp][:, 0:512],
                        s_kst[par][:, kt * 128:(kt + 1) * 128],
                        Q3[:, hh, 0:512], start=True, stop=True),
                        [t_ld_all, t_psc_use[pp], tok_negM], inc=True)
                    ts1 = PE(lambda h, par=par, kt=kt, hh=hh, pp=pp: h.matmul(
                        psc[pp][:, 512:1024],
                        s_kst[par][:, kt * 128:(kt + 1) * 128],
                        Q3[:, hh, 512:1024], start=True, stop=True),
                        [], inc=True)
                    tex = ACT(lambda h, pp=pp, ab=ab: h.activation(
                        out=s_atn[ab][:], in_=psc[pp][:], func=AF.Exp,
                        bias=s_negM[:], scale=1.0),
                        [ts0, ts1, t_atn_use[ab]])
                    exp_toks[kt] = tex
                    t_psc_use[pp] = tex
                    if kt >= 2:
                        emit_av(kt - 2)
                emit_av(14)
                emit_av(15)
                t_kst_use[par] = av_toks[-1]
                t_vtm_use[par] = av_toks[-1]
                for qt in range(NT):
                    bk, slot = qt // 3, qt % 3
                    tcd_ = DVE(lambda h, bk=bk, slot=slot: h.tensor_copy(
                        out=c_t0,
                        in_=pout[bk][:, slot * 129 + 128:slot * 129 + 129]),
                        [av_toks[qt]])
                    trd = DVE(lambda h: h.reciprocal(out=c_den, in_=c_t0),
                              [tcd_])
                    tnm = DVE(lambda h, bk=bk, slot=slot, hh=hh, qt=qt:
                              h.tensor_scalar(
                        out=B3[:, hh, qt * 128:(qt + 1) * 128],
                        in0=pout[bk][:, slot * 129:slot * 129 + 128],
                        scalar1=c_den, scalar2=None, op0=ALU.mult), [trd])
                    t_pout_use[bk] = tnm

        tok_sdpa = [t for t in t_pout_use if t is not None]
        if DEBUG_DUMPS:
            td_ = b.dma("pool", "dM", dbg_b3.ap(), regA_buf[:],
                        deps=tok_sdpa)
            b._wait("dve", td_)
            b._wait("pool", td_)
        sd.close()
        b.barrier(("pe", "act", "dve"))
        p5 = ExitStack()
        pproj5 = [p5.enter_context(nc.psum_tensor(f"pp5{i}", [128, 512], F32))
                  for i in range(4)]
        ring5 = PbfRing(p5.enter_context(
            nc.psum_tensor("pbf5", [128, 2048], BF16))[:])
        open_stream_bufs(p5)
        s_y1p5 = p5.enter_context(nc.sbuf_tensor("s_y1b", [128, HID], F32))
        s_sh5v = p5.enter_context(nc.sbuf_tensor("s_sh5v", [128, HID], BF16))
        s_qtm5 = s_sh5v[:]
        s_aofm5 = s_sh5v[:, 0:TOK]
        s_at3s = s_sh5v[:, 2048:3072].rearrange("p (t f) -> p t f", f=128)

        Qo3 = regC_buf[:].rearrange("p (c t) -> p c t", t=TOK)

        tok_qo = None
        t_prev_chain = [None]
        p5_sc_toks = {}

        def p5_scale(qt):
            sc = SCP[qt % 2]
            tam = DVE(lambda h, qt=qt, sc=sc: h.tensor_reduce(
                out=sc["am"], in_=B3[:, :, qt * 128:(qt + 1) * 128],
                axis=AX.XY, op=ALU.max, apply_absolute_value=True),
                [sc["last"]])
            tmc = DVE(lambda h, sc=sc: h.tensor_scalar(
                out=sc["am"], in0=sc["am"], scalar1=1e-5, scalar2=None,
                op0=ALU.max), [tam])
            trc = DVE(lambda h, sc=sc: h.reciprocal(
                out=sc["s"], in_=sc["am"]), [tmc])
            ts_ = DVE(lambda h, sc=sc: h.tensor_scalar(
                out=sc["s"], in0=sc["s"], scalar1=QB, scalar2=None,
                op0=ALU.mult), [trc])
            tin = DVE(lambda h, sc=sc: h.tensor_scalar(
                out=sc["inv"], in0=sc["am"], scalar1=1.0 / QB, scalar2=None,
                op0=ALU.mult), [tmc])
            p5_sc_toks[qt] = [ts_, tin]

        p5_scale(0)
        for qt in range(NT):
            sc = SCP[qt % 2]
            prev_evac = tok_qo
            if qt + 1 < NT:
                p5_scale(qt + 1)
            for ck in range(4):
                tq1, tq2 = emit_act_quant(
                    lambda k, qt=qt, ck=ck: B3[:, 8 * ck:8 * (ck + 1),
                                               qt * 128:(qt + 1) * 128],
                    lambda k, ck=ck: s_qtm5[:, ck * 1024:
                                            (ck + 1) * 1024].rearrange(
                        "p (c f) -> p c f", c=8),
                    s_y1p5[:],
                    p5_sc_toks[qt] + [prev_evac, t_prev_chain[0]],
                    n_chunks=1, sc=sc)
                t_prev_chain[0] = tq2
                for c in range(8 * ck, 8 * ck + 8):
                    sl, bk = ring5.next()
                    tp = PE(lambda h, c=c, sl=sl: h.transpose(
                        sl, s_qtm5[:, c * 128:(c + 1) * 128], s_identbf[:]),
                        [tq2, ring5.rd[bk]], inc=True)
                    tev = DVE(lambda h, c=c, qt=qt, sl=sl: h.tensor_copy(
                        out=Qo3[:, c, qt * 128:(qt + 1) * 128], in_=sl), [tp])
                    ring5.rd[bk] = tev
                    tok_qo = tev

        t_pp5 = [None, None, None, None]
        t_aofm_use = None
        t_at3s_use = None
        t_at3_dma = []

        def wo_q(m, hf, first=False):
            xd = [tok_swo] + (tok_sdpa if first else [])
            return ws.next(wo[m, :, hf * 2048:(hf + 1) * 2048], "quant",
                           scol=6, extra_deps=xd, scol_c2=(11,), par=hf)

        def wo_mm(m, hf, qinfo):
            pr = m % 2
            pp = pproj5[2 * pr:2 * pr + 2]
            par, trdy = qinfo
            tokm = None
            for j in range(16):
                c = hf * 16 + j
                for tg in range(2):
                    dep = [trdy, tok_qo, tok_swo]
                    if c == 0:
                        dep.append(t_pp5[2 * pr + tg])
                    tokm = PE(lambda h, wb_=ws.bf(par), j=j, c=c, tg=tg,
                              pp_=pp[tg]: h.matmul(
                        pp_[:],
                        wb_[:, j * 128:(j + 1) * 128],
                        Qo3[:, c, tg * 512:(tg + 1) * 512],
                        start=(c == 0), stop=(c == 31)),
                        dep, inc=(tg == 1 and (c == 15 or c == 31)))
            ws.consumed(par, tokm)
            return tokm

        woq0 = wo_q(0, 0, True)
        woq1 = wo_q(0, 1, True)
        for m in range(NH):
            pr = m % 2
            pp = pproj5[2 * pr:2 * pr + 2]
            mm0 = wo_mm(m, 0, woq0)
            if m + 1 < NH:
                woq0 = wo_q(m + 1, 0)
            mm1 = wo_mm(m, 1, woq1)
            if m + 1 < NH:
                woq1 = wo_q(m + 1, 1)
            mm_last = [mm0, mm1]
            te0 = ACT(lambda h, pp_=pp[0]: h.activation(
                out=s_aofm5[:, 0:512], in_=pp_[:], func=AF.Copy,
                scale=s_scales[:, 7:8]),
                [mm_last[0], mm_last[1], t_aofm_use])
            te1 = ACT(lambda h, pp_=pp[1]: h.activation(
                out=s_aofm5[:, 512:1024], in_=pp_[:], func=AF.Copy,
                scale=s_scales[:, 7:8]), [mm_last[0], mm_last[1]])
            t_pp5[2 * pr], t_pp5[2 * pr + 1] = te0, te1
            tevl = None
            for qt in range(NT):
                sl, bk = ring5.next()
                tp = PE(lambda h, qt=qt, sl=sl: h.transpose(
                    sl, s_aofm5[:, qt * 128:(qt + 1) * 128], s_identbf[:]),
                    [te0, te1, ring5.rd[bk]], inc=True)
                tev = DVE(lambda h, qt=qt, m=m, sl=sl: h.tensor_copy(
                    out=s_at3s[:, qt, :], in_=sl), [tp, t_at3s_use])
                ring5.rd[bk] = tev
                tevl = tev
            t_aofm_use = tevl
            tdm = b.dma("pool", "dK1",
                        at3d.ap().rearrange("(t p) (m f) -> p t m f", p=128,
                                            f=128)[:, :, m, :],
                        s_at3s,
                        deps=[tevl])
            t_at3s_use = tdm
            t_at3_dma.append(tdm)
        tok_at3 = t_at3_dma[-1]

        p5.close()
        b.barrier(("pe", "act", "dve", "pool"))
        p6 = ExitStack()
        ph1 = p6.enter_context(nc.psum_tensor("ph1", [32, 512], F32))
        pg1 = p6.enter_context(nc.psum_tensor("pg1", [8, 512], F32))
        pA = p6.enter_context(nc.psum_tensor("pA", [128, 512], F32))
        pf32 = p6.enter_context(nc.psum_tensor("pf32", [128, 512], F32))
        pg2 = pf32[:, 384:392]
        pwb = [p6.enter_context(nc.psum_tensor(f"pwb{i}", [128, 512], F32))
               for i in range(2)]
        ring6 = PbfRing(p6.enter_context(
            nc.psum_tensor("pbf6", [128, 2048], BF16))[:])
        p_s16 = pwb[0]

        X3 = regC_buf[:].rearrange("p (c t) -> p c t", t=TOK)

        lora = ExitStack()
        s_gw1bf = lora.enter_context(
            nc.sbuf_tensor("s_gw1bf", [128, NCH, 32], BF16))
        s_abf = lora.enter_context(
            nc.sbuf_tensor("s_abf", [128, NCH, 128], BF16))
        s_bbf = lora.enter_context(nc.sbuf_tensor("s_bbf", [128, HID], BF16))
        s_x6v = [regA_buf[:, 0:8192].bitcast(F32),
               regA_buf[:, 8192:16384].bitcast(F32)]
        s_y16v = regA_buf[:, 16384:24576].bitcast(F32)
        s_z6v = regA_buf[:, 24576:32768].bitcast(F32)
        s_qtm6v = p6.enter_context(nc.sbuf_tensor("s_qtm6", [128, HID], BF16))
        s_at3l = [p6.enter_context(
            nc.sbuf_tensor(f"s_at3l{i}", [128, HID], BF16)) for i in range(2)]
        s_hfm = p6.enter_context(nc.sbuf_tensor("s_hfm", [32, 128], F32))
        s_ht = p6.enter_context(nc.sbuf_tensor("s_ht", [128, 32], F32))
        s_hsq = p6.enter_context(nc.sbuf_tensor("s_hsq", [128, 32], F32))
        s_ht2 = p6.enter_context(nc.sbuf_tensor("s_ht2", [128, 32], F32))
        s_h2fm = p6.enter_context(nc.sbuf_tensor("s_h2fm", [32, 128], BF16))
        s_gfm = p6.enter_context(nc.sbuf_tensor("s_gfm", [8, 128], F32))
        s_gt = p6.enter_context(nc.sbuf_tensor("s_gt", [128, 8], F32))
        s_eq1 = p6.enter_context(nc.sbuf_tensor("s_eq1", [128, 8], F32))
        s_eq2 = p6.enter_context(nc.sbuf_tensor("s_eq2", [128, 8], F32))
        s_msk = p6.enter_context(nc.sbuf_tensor("s_msk", [128, 8], F32))
        s_wt = p6.enter_context(nc.sbuf_tensor("s_wt", [128, 8], F32))
        s_wt2 = p6.enter_context(nc.sbuf_tensor("s_wt2", [128, 8], F32))
        s_wfm = p6.enter_context(nc.sbuf_tensor("s_wfm", [8, TOK], BF16))
        s_wb = p6.enter_context(nc.sbuf_tensor("s_wb", [128, TOK], F32))
        s_tmp = p6.enter_context(nc.sbuf_tensor("s_tmp", [128, TOK], F32))

        t_gw = []
        for c in range(NCH):
            tg_ = b.dma("pool", "dM", s_cstg[:, 0:32],
                        gw1t[c * 128:(c + 1) * 128, :],
                        deps=[t_gw[-1][1] if t_gw else tok_at3])
            tcst = DVE(lambda h, c=c: h.tensor_copy(out=s_gw1bf[:, c, :],
                                                    in_=s_cstg[:, 0:32]),
                       [tg_])
            t_gw.append((tg_, tcst))
        t_g2 = b.dma("pool", "dM", s_hfm[0:32, 0:8], gw2t)
        t_g2c = DVE(lambda h: h.tensor_copy(out=s_gw2bf[:],
                                            in_=s_hfm[0:32, 0:8]), [t_g2])
        t_ab = []
        for c in range(NCH):
            ta_ = b.dma("pool", "dM", s_cstg[:],
                        a_all[c * 128:(c + 1) * 128, :],
                        deps=[t_ab[-1][1] if t_ab else t_gw[-1][1]])
            tac = DVE(lambda h, c=c: h.tensor_copy(out=s_abf[:, c, :],
                                                   in_=s_cstg[:]), [ta_])
            t_ab.append((ta_, tac))
        t_bl = b.dma("pool", "dM", s_x6v[0], b_all)
        t_blc = DVE(lambda h: h.tensor_copy(out=s_bbf[:], in_=s_x6v[0]),
                    [t_bl])
        t_x_use = [t_blc, None]

        tok_gstash = []
        t_ph1 = None
        t_pg1 = None
        t_pg2 = None
        t_pA = None
        t_pf32 = None
        t_cnt_prev = t_cnt0
        t_pbf = [None] * 8
        tok_x3 = None
        t_tmp_use = None
        t_spill = []

        prev_tx1 = None
        prev_tsp = None
        prev_tx3 = None
        t_a3_use = [None, None]
        for t in range(NT):
            par = t % 2
            xb = s_x6v[par]
            a3 = s_at3l[par]
            tl = b.dma("pool", "dX%d" % par, xb,
                       x_own[t * 128:(t + 1) * 128, :], deps=[t_x_use[par]])
            tla = b.dma("pool", "dK%d" % par, a3[:],
                        at3d.ap()[t * 128:(t + 1) * 128, :],
                        deps=[t_a3_use[par], tok_at3])
            r3, b3, tcp = ln_stats(a3[:], HID, [tla, prev_tx1, tok_x3],
                                   copy_dst=s_y16v, sq_dst=s_qtm6v[:])
            t_a3_use[par] = r3
            tx1 = DVE(lambda h, xb=xb: h.scalar_tensor_tensor(
                out=s_z6v, in0=s_y16v, scalar=c_r, in1=xb,
                op0=ALU.mult, op1=ALU.add),
                [r3, b3, tcp, tl, prev_tsp, prev_tx3])
            tx2 = DVE(lambda h: h.tensor_scalar(out=s_z6v, in0=s_z6v,
                                                scalar1=c_b, scalar2=None,
                                                op0=ALU.add), [tx1])
            t_x_use[par] = tx1
            prev_tx1 = tx1
            tsp = b.dma("pool", "dM", x2d.ap()[t * 128:(t + 1) * 128, :],
                        s_z6v, deps=[tx2])
            t_spill.append(tsp)
            prev_tsp = tsp
            r4, b4, _ = ln_stats(s_z6v, HID, [tx2],
                                 copy_dst=s_y16v, sq_dst=s_y16v)
            tx3 = ACT(lambda h: h.activation(out=s_qtm6v[:], in_=s_z6v,
                                             func=AF.Identity, bias=c_b,
                                             scale=c_r), [r4, b4, tok_x3])
            prev_tx3 = tx3
            for c in range(NCH):
                sl, bk = ring6.next()
                tp = PE(lambda h, c=c, sl=sl: h.transpose(
                    sl, s_qtm6v[:, c * 128:(c + 1) * 128], s_identbf[:]),
                    [tx3, ring6.rd[bk]], inc=True)
                tev = DVE(lambda h, c=c, t=t, sl=sl: h.tensor_copy(
                    out=X3[:, c, t * 128:(t + 1) * 128], in_=sl), [tp])
                ring6.rd[bk] = tev
                tok_x3 = tev

            tokm = None
            for c in range(NCH):
                dep = [t_gw[c][1]]
                if c == 0:
                    dep += [tok_x3, t_ph1]
                tokm = PE(lambda h, c=c, t=t: h.matmul(
                    ph1[:, 0:128], s_gw1bf[:, c, :],
                    X3[:, c, t * 128:(t + 1) * 128],
                    start=(c == 0), stop=(c == NCH - 1)),
                    dep, inc=(c == NCH - 1))
            th0 = ACT(lambda h: h.activation(out=s_hfm[:], in_=ph1[:, 0:128],
                                             func=AF.Identity, bias=s_gb1[:],
                                             scale=1.0), [tokm, t_ph1])
            t_ph1 = th0
            tht = PE(lambda h: h.transpose(pf32[0:128, 0:32], s_hfm[:],
                                           s_identf[0:32, 0:32]),
                     [th0, t_pf32, t_pg2], inc=True)
            the = DVE(lambda h: h.tensor_copy(out=s_ht[:],
                                              in_=pf32[0:128, 0:32]), [tht])
            t_pf32 = the
            ts1 = DVE(lambda h: h.tensor_reduce(out=c_t0, in_=s_ht[:],
                                                axis=AX.X, op=ALU.add), [the])
            ts2 = DVE(lambda h: h.tensor_tensor(out=s_hsq[:], in0=s_ht[:],
                                                in1=s_ht[:], op=ALU.mult),
                      [the])
            ts3 = DVE(lambda h: h.tensor_reduce(out=c_t1, in_=s_hsq[:],
                                                axis=AX.X, op=ALU.add), [ts2])
            ts4 = DVE(lambda h: h.tensor_scalar(out=c_t0, in0=c_t0,
                                                scalar1=1.0 / 32, scalar2=None,
                                                op0=ALU.mult), [ts1])
            ts5 = DVE(lambda h: h.tensor_tensor(out=c_sq, in0=c_t0, in1=c_t0,
                                                op=ALU.mult), [ts4])
            ts6 = DVE(lambda h: h.scalar_tensor_tensor(out=c_t1, in0=c_t1,
                                                       scalar=1.0 / 32,
                                                       in1=c_sq, op0=ALU.mult,
                                                       op1=ALU.subtract),
                      [ts3, ts5])
            ts7 = ACT(lambda h: h.activation(out=c_t1, in_=c_t1, func=AF.Ln,
                                             bias=c_eps, scale=1.0),
                      [ts6, t_eps])
            ts8 = ACT(lambda h: h.activation(out=c_r, in_=c_t1, func=AF.Exp,
                                             bias=0.0, scale=-0.5), [ts7])
            ts9 = DVE(lambda h: h.tensor_tensor(out=c_b, in0=c_t0, in1=c_r,
                                                op=ALU.mult), [ts8, ts4])
            tsa = DVE(lambda h: h.tensor_scalar(out=c_b, in0=c_b, scalar1=-1.0,
                                                scalar2=None, op0=ALU.mult),
                      [ts9])
            tsb = ACT(lambda h: h.activation(out=s_ht2[:], in_=s_ht[:],
                                             func=AF.Identity, bias=c_b,
                                             scale=c_r), [ts8, tsa, the])
            tsc_ = DVE(lambda h: h.tensor_tensor(out=s_ht2[:], in0=s_ht2[:],
                                                 in1=s_g32[:], op=ALU.mult),
                       [tsb, t_const])
            tsd = DVE(lambda h: h.tensor_tensor(out=s_ht2[:], in0=s_ht2[:],
                                                in1=s_bt32[:], op=ALU.add),
                      [tsc_])
            tse = DVE(lambda h: h.tensor_relu(out=s_ht2[:], in_=s_ht2[:]),
                      [tsd])
            tsf = PE(lambda h: h.transpose(pf32[0:32, 0:128], s_ht2[:],
                                           s_identf[:]),
                     [tse, t_pf32], inc=True)
            tsg = DVE(lambda h: h.tensor_copy(out=s_h2fm[:],
                                              in_=pf32[0:32, 0:128]), [tsf])
            t_pf32 = tsg
            tsh = PE(lambda h: h.matmul(pg1[:, 0:128], s_gw2bf[:], s_h2fm[:],
                                        start=True, stop=True),
                     [tsg, t_g2c, t_pg1], inc=True)
            tsi = ACT(lambda h: h.activation(out=s_gfm[:], in_=pg1[:, 0:128],
                                             func=AF.Identity, bias=s_gb2[:],
                                             scale=1.0), [tsh])
            t_pg1 = tsi
            tsj = PE(lambda h: h.transpose(pg2, s_gfm[:],
                                           s_identf[0:8, 0:8]),
                     [tsi, t_pg2], inc=True)
            tsk = DVE(lambda h: h.tensor_copy(out=s_gt[:], in_=pg2), [tsj])
            t_pg2 = tsk
            tt1 = DVE(lambda h: h.tensor_reduce(out=c_m1, in_=s_gt[:],
                                                axis=AX.X, op=ALU.max), [tsk])
            tt2 = DVE(lambda h: h.tensor_scalar(out=s_eq1[:], in0=s_gt[:],
                                                scalar1=c_m1, scalar2=None,
                                                op0=ALU.is_equal), [tt1])
            tt3 = DVE(lambda h: h.scalar_tensor_tensor(out=s_msk[:],
                                                       in0=s_eq1[:],
                                                       scalar=-1e30,
                                                       in1=s_gt[:],
                                                       op0=ALU.mult,
                                                       op1=ALU.add), [tt2])
            tt4 = DVE(lambda h: h.tensor_reduce(out=c_m2, in_=s_msk[:],
                                                axis=AX.X, op=ALU.max), [tt3])
            tt5 = DVE(lambda h: h.tensor_scalar(out=s_eq2[:], in0=s_msk[:],
                                                scalar1=c_m2, scalar2=None,
                                                op0=ALU.is_equal), [tt4])
            tt6 = DVE(lambda h: h.tensor_tensor(out=c_d, in0=c_m2, in1=c_m1,
                                                op=ALU.subtract), [tt4, tt1])
            tt7 = ACT(lambda h: h.activation(out=c_e2, in_=c_d, func=AF.Exp,
                                             bias=0.0, scale=1.0), [tt6])
            tt8 = DVE(lambda h: h.tensor_scalar(out=c_rg, in0=c_e2,
                                                scalar1=1.0, scalar2=None,
                                                op0=ALU.add), [tt7])
            tt9 = DVE(lambda h: h.reciprocal(out=c_rg, in_=c_rg), [tt8])
            tta = DVE(lambda h: h.tensor_scalar(out=c_g1, in0=c_rg,
                                                scalar1=ALPHA, scalar2=None,
                                                op0=ALU.mult), [tt9])
            ttb = DVE(lambda h: h.tensor_tensor(out=c_g2, in0=c_e2, in1=c_rg,
                                                op=ALU.mult), [tt7, tt9])
            ttc = DVE(lambda h: h.tensor_scalar(out=c_g2, in0=c_g2,
                                                scalar1=ALPHA, scalar2=None,
                                                op0=ALU.mult), [ttb])
            ttd = DVE(lambda h, t=t: h.tensor_scalar(out=s_c1[:, t, :],
                                                     in0=s_eq1[:],
                                                     scalar1=c_g1,
                                                     scalar2=None,
                                                     op0=ALU.mult),
                      [tta, tt2])
            tte = DVE(lambda h, t=t: h.tensor_scalar(out=s_c2[:, t, :],
                                                     in0=s_eq2[:],
                                                     scalar1=c_g2,
                                                     scalar2=None,
                                                     op0=ALU.mult),
                      [ttc, tt5])
            ttf = DVE(lambda h: h.tensor_tensor(out=s_cnt[:, 0:8],
                                                in0=s_cnt[:, 0:8],
                                                in1=s_eq1[:], op=ALU.add),
                      [tt2, t_cnt_prev])
            ttg = DVE(lambda h: h.tensor_tensor(out=s_cnt[:, 8:16],
                                                in0=s_cnt[:, 8:16],
                                                in1=s_eq2[:], op=ALU.add),
                      [tt5, ttf])
            t_cnt_prev = ttg
            tok_gstash.append((ttd, tte))

            tokA = None
            for c in range(NCH):
                dep = [t_ab[c][1]]
                if c == 0:
                    dep += [tok_x3, t_pA]
                tokA = PE(lambda h, c=c, t=t: h.matmul(
                    pA[:, 0:128], s_abf[:, c, :], X3[:, c, t * 128:(t + 1) * 128],
                    start=(c == 0), stop=(c == NCH - 1)),
                    dep, inc=(c == NCH - 1))
            tAe = DVE(lambda h, t=t: h.tensor_copy(
                out=s_tmp[:, t * 128:(t + 1) * 128], in_=pA[:, 0:128]), [tokA])
            t_pA = tAe
            t_tmp_use = tAe

        tcs = PE(lambda h: h.matmul(p_s16[0:1, 0:16], s_onesf[:, 0:1],
                                    s_cnt[:], start=True, stop=True),
                 [t_cnt_prev], inc=True)
        tce = DVE(lambda h: h.tensor_copy(out=s_cntrow[:],
                                          in_=p_s16[0:1, 0:16]), [tcs])
        tcd = b.dma("pool", "dM", cnt_in.ap(), s_cntrow[:], deps=[tce])
        tok_cr = b.coll("AllReduce", ALU.add, [[0, 1, 2, 3, 4, 5, 6, 7]],
                        [cnt_in.ap().opt()], [cnt_out.ap().opt()],
                        deps=[tcd], ch="cc2")
        tcb = b.dma("pool", "dM", s_cntrow[:], cnt_out.ap(), deps=[tok_cr])
        tal = DVE(lambda h: h.tensor_scalar(out=s_cntrow[:], in0=s_cntrow[:],
                                            scalar1=CAPACITY, scalar2=None,
                                            op0=ALU.is_le), [tcb])
        tab = PE(lambda h: h.matmul(pA[:, 0:16], s_onesf[0:1, :],
                                    s_cntrow[:], start=True, stop=True),
                 [tal, t_pA], inc=True)
        tae = DVE(lambda h: h.tensor_copy(out=s_alb[:], in_=pA[:, 0:16]),
                  [tab])
        t_wfm = None
        for t in range(NT):
            tw1 = DVE(lambda h, t=t: h.tensor_tensor(
                out=s_wt[:], in0=s_c1[:, t, :], in1=s_alb[:, 0:8],
                op=ALU.mult), [tae, tok_gstash[t][0]])
            tw2 = DVE(lambda h, t=t: h.tensor_tensor(
                out=s_wt2[:], in0=s_c2[:, t, :], in1=s_alb[:, 8:16],
                op=ALU.mult), [tae, tok_gstash[t][1]])
            tw3 = DVE(lambda h: h.tensor_tensor(out=s_wt[:], in0=s_wt[:],
                                                in1=s_wt2[:], op=ALU.add),
                      [tw1, tw2])
            tw4 = PE(lambda h: h.transpose(pf32[0:8, 0:128], s_wt[:],
                                           s_identf[:]),
                     [tw3, t_pf32, t_pg2], inc=True)
            tw5 = DVE(lambda h, t=t: h.tensor_copy(
                out=s_wfm[:, t * 128:(t + 1) * 128],
                in_=pf32[0:8, 0:128]), [tw4])
            t_pf32 = tw5
            t_wfm = tw5
        twb0 = PE(lambda h: h.matmul(pwb[0][:], s_selS[:], s_wfm[:, 0:512],
                                     start=True, stop=True), [t_wfm],
                  inc=True)
        twb1 = PE(lambda h: h.matmul(pwb[1][:], s_selS[:], s_wfm[:, 512:1024],
                                     start=True, stop=True), [t_wfm],
                  inc=True)
        twe0 = DVE(lambda h: h.tensor_copy(out=s_wb[:, 0:512], in_=pwb[0][:]),
                   [twb0])
        twe1 = DVE(lambda h: h.tensor_copy(out=s_wb[:, 512:1024],
                                           in_=pwb[1][:]), [twb1])
        tok_tsc = DVE(lambda h: h.tensor_tensor(out=s_tsc[:], in0=s_tmp[:],
                                                in1=s_wb[:], op=ALU.mult),
                      [twe0, twe1, t_tmp_use])

        p6.close()
        b.barrier(("pe", "act", "dve"))
        p7 = ExitStack()
        pproj7 = [p7.enter_context(nc.psum_tensor(f"pp7{i}", [128, 512], F32))
                  for i in range(4)]
        ring7 = PbfRing(p7.enter_context(
            nc.psum_tensor("pbf7", [128, 2048], BF16))[:])
        open_stream_bufs(p7)
        s_aofm7v = p7.enter_context(nc.sbuf_tensor("s_aofm7", [128, TOK], BF16))

        F3 = regA_buf[:].rearrange("p (t f) -> p t f", f=HID)

        t_pp7 = [None, None, None, None]
        t_aofm_use = None
        t_pbf = [None] * 8
        tok_f3 = None

        def wm_q(m, hf, first=False):
            xd = [tok_tsc, t_spill[-1]] if first else []
            return ws.next(wm[m, :, hf * 2048:(hf + 1) * 2048], "cast",
                           extra_deps=xd, par=hf)

        def wm_mm(m, hf, qinfo):
            pr = m % 2
            pp = pproj7[2 * pr:2 * pr + 2]
            par, trdy = qinfo
            tokm = None
            for j in range(16):
                c = hf * 16 + j
                for tg in range(2):
                    dep = [trdy, tok_x3]
                    if c == 0:
                        dep.append(t_pp7[2 * pr + tg])
                    tokm = PE(lambda h, wb_=ws.bf(par), j=j, c=c, tg=tg,
                              pp_=pp[tg]: h.matmul(
                        pp_[:],
                        wb_[:, j * 128:(j + 1) * 128],
                        X3[:, c, tg * 512:(tg + 1) * 512],
                        start=(c == 0), stop=False),
                        dep, inc=(j == 15 and tg == 1))
            ws.consumed(par, tokm)
            return tokm

        wmq0 = wm_q(0, 0, True)
        wmq1 = wm_q(0, 1, True)
        for m in range(NCH):
            pr = m % 2
            pp = pproj7[2 * pr:2 * pr + 2]
            wm_mm(m, 0, wmq0)
            if m + 1 < NCH:
                wmq0 = wm_q(m + 1, 0)
            wm_mm(m, 1, wmq1)
            if m + 1 < NCH:
                wmq1 = wm_q(m + 1, 1)
            tlb_ = [None, None]
            for tg in range(2):
                tlb_[tg] = PE(lambda h, m=m, tg=tg, pp_=pp[tg]: h.matmul(
                    pp_[:], s_bbf[:, m * 128:(m + 1) * 128],
                    s_tsc[:, tg * 512:(tg + 1) * 512],
                    start=False, stop=True),
                    [tok_tsc, t_blc], inc=True)
            te0 = ACT(lambda h, pp_=pp[0]: h.activation(
                out=s_aofm7v[:, 0:512], in_=pp_[:], func=AF.Copy),
                [tlb_[0], t_aofm_use])
            te1 = ACT(lambda h, pp_=pp[1]: h.activation(
                out=s_aofm7v[:, 512:1024], in_=pp_[:], func=AF.Copy),
                [tlb_[1]])
            t_pp7[2 * pr], t_pp7[2 * pr + 1] = te0, te1
            tevl = None
            for qt in range(NT):
                sl, bk = ring7.next()
                tp = PE(lambda h, qt=qt, sl=sl: h.transpose(
                    sl, s_aofm7v[:, qt * 128:(qt + 1) * 128], s_identbf[:]),
                    [te0, te1, ring7.rd[bk]], inc=True)
                tev = DVE(lambda h, qt=qt, m=m, sl=sl: h.tensor_copy(
                    out=F3[:, qt, m * 128:(m + 1) * 128], in_=sl),
                    [tp, t_spill[-1]])
                ring7.rd[bk] = tev
                tevl = tev
            t_aofm_use = tevl
        tok_f3 = t_aofm_use

        p7.close()
        lora.close()
        b.barrier(("pe", "act", "dve", "pool"))
        p8 = ExitStack()
        s_x8v = [p8.enter_context(nc.sbuf_tensor(f"s_x8{i}", [128, HID], F32))
               for i in range(2)]
        s_y18v = p8.enter_context(nc.sbuf_tensor("s_y18", [128, HID], F32))
        s_z8v = p8.enter_context(nc.sbuf_tensor("s_z8", [128, HID], F32))
        s_jnk8 = p8.enter_context(nc.sbuf_tensor("s_jnk8", [128, HID], BF16))
        t_x_use = [None, None]
        t_store = []
        tok_dM_all = ("dM", b.cnt["dM"])
        prev_tx1 = None
        prev_tst = None
        for t in range(NT):
            par = t % 2
            xb = s_x8v[par]
            tl = b.dma("pool", "dX%d" % par, xb[:],
                       x2d.ap()[t * 128:(t + 1) * 128, :],
                       deps=[t_x_use[par], tok_dM_all])
            r5, b5, tcp = ln_stats(F3[:, t, :], HID, [tok_f3, prev_tx1],
                                   copy_dst=s_y18v[:], sq_dst=s_jnk8[:])
            tx1 = DVE(lambda h, xb=xb: h.scalar_tensor_tensor(
                out=s_z8v[:], in0=s_y18v[:], scalar=c_r, in1=xb[:],
                op0=ALU.mult, op1=ALU.add), [r5, b5, tcp, tl, prev_tst])
            tx2 = DVE(lambda h: h.tensor_scalar(out=s_z8v[:], in0=s_z8v[:],
                                                scalar1=c_b, scalar2=None,
                                                op0=ALU.add), [tx1])
            t_x_use[par] = tx1
            prev_tx1 = tx1
            tst = b.dma("pool", "dS", y_own[t * 128:(t + 1) * 128, :],
                        s_z8v[:], deps=[tx2])
            t_store.append(tst)
            prev_tst = tst
        if DEBUG_DUMPS:
            b.dma("pool", "dM", dbg_ccg.ap(), ccg_in.ap(), deps=t_stage)
            b.dma("pool", "dM", dbg_at3.ap(), at3d.ap(), deps=[tok_at3])
            b.dma("pool", "dM", dbg_x2.ap(), x2d.ap(), deps=[t_spill[-1]])
            tsc_ = b.dma("pool", "dM", dbg_sc.ap()[:, 0:12], s_scales[:],
                         deps=[t_pb_use[0]])
            b.dma("pool", "dM", dbg_sc.ap()[0:1, 12:12 + NH], s_qn2[0:1, :],
                  deps=[t_pnorm[0]])
            b.dma("pool", "dM", dbg_sc.ap()[0:1, 12 + NH:12 + NH + 1],
                  s_negM[0:1, :], deps=[tok_negM])
            t_store.append(b.dma("pool", "dS", dbg_sc.ap()[1:2, 12:20],
                                 s_kn2[0:1, :], deps=[t_kn_back]))
        b.op("sp", lambda h: h.nop(), deps=[t_store[-1]], inc=False)

        with nc.Block() as block:
            @block.tensor
            def _(h):
                for f in b.stream["pe"]:
                    f(h)

            @block.scalar
            def _(h):
                for f in b.stream["act"]:
                    f(h)

            @block.vector
            def _(h):
                for f in b.stream["dve"]:
                    f(h)

            @block.gpsimd
            def _(h):
                for f in b.stream["pool"]:
                    f(h)

            @block.sync
            def _(h):
                for f in b.stream["sp"]:
                    f(h)

        p8.close()

    return nc


def _prep_inputs(inputs):
    f32 = np.float32
    x = np.ascontiguousarray(np.asarray(inputs["L0_attn_hidden_states"], f32))
    Wq = np.asarray(inputs["L0_attn_Wq"], f32)
    Wk = np.asarray(inputs["L0_attn_Wk"], f32)
    Wv = np.asarray(inputs["L0_attn_Wv"], f32)
    Wo = np.asarray(inputs["L0_attn_Wo"], f32)
    Wm = np.asarray(inputs["L0_ffn_weight"], f32)
    lA = np.asarray(inputs["L0_ffn_lora_A"], f32)
    lB = np.asarray(inputs["L0_ffn_lora_B"], f32)
    gw1 = np.asarray(inputs["L0_ffn_gw1"], f32)
    gb1 = np.asarray(inputs["L0_ffn_gb1"], f32)
    lng = np.asarray(inputs["L0_ffn_ln_gamma"], f32)
    lnbta = np.asarray(inputs["L0_ffn_ln_beta"], f32)
    gw2 = np.asarray(inputs["L0_ffn_gw2"], f32)
    gb2 = np.asarray(inputs["L0_ffn_gb2"], f32)
    ln_w = np.asarray(inputs["L0_attn_ln_w"], f32)
    ln_b = np.asarray(inputs["L0_attn_ln_b"], f32)

    def blk(W):
        nm = W.shape[0] // 128
        return np.ascontiguousarray(
            W.reshape(nm, 128, NCH, 128).transpose(0, 3, 2, 1)
        ).reshape(nm, 128, HID)

    shared = {
        "wq": blk(Wq), "wk": blk(Wk), "wv": blk(Wv), "wo": blk(Wo),
        "wm": blk(Wm),
        "a_all": np.ascontiguousarray(lA.transpose(1, 0, 2).reshape(HID, 128)),
        "b_all": np.ascontiguousarray(lB.reshape(128, HID)),
        "gw1t": np.ascontiguousarray(gw1.T),
        "gb1c": np.ascontiguousarray(gb1.reshape(32, 1)),
        "g32": np.ascontiguousarray(np.broadcast_to(lng, (128, 32))),
        "bt32": np.ascontiguousarray(np.broadcast_to(lnbta, (128, 32))),
        "gw2t": np.ascontiguousarray(gw2.T),
        "gb2c": np.ascontiguousarray(gb2.reshape(8, 1)),
        "lnw": np.ascontiguousarray(np.broadcast_to(ln_w, (128, HID))),
        "lnb": np.ascontiguousarray(np.broadcast_to(ln_b, (128, HID))),
        "identbf": np.eye(128, dtype=ml_dtypes.bfloat16),
        "identf": np.eye(128, dtype=f32),
        "onesf": np.ones((128, 128), dtype=f32),
        "onesbf": np.ones((128, 1), dtype=ml_dtypes.bfloat16),
    }
    S = np.zeros((8, 128), dtype=ml_dtypes.bfloat16)
    for p in range(128):
        S[p // 16, p] = 1.0
    shared["selS"] = S

    inv = (1.0 / (np.float32(10000.0) **
                  (np.arange(0, 128, 2, dtype=f32) / f32(128.0)))).astype(f32)
    maps = []
    for core in range(8):
        bb, hh = core // 2, core % 2
        pos = np.arange(hh * TOK, (hh + 1) * TOK, dtype=f32)
        freqs = pos[None, :] * inv[:, None]
        emb = np.concatenate([freqs, freqs], axis=0)
        m = dict(shared)
        m["cosb"] = np.cos(emb).astype(f32)
        m["sinb"] = np.sin(emb).astype(f32)
        m["x_own"] = np.ascontiguousarray(x[bb, hh * TOK:(hh + 1) * TOK, :])
        maps.append(m)
    return maps


def kernel(**inputs):
    maps = _prep_inputs(inputs)
    nc = build_nc()
    res = run_bass_kernel_spmd(nc, maps, core_ids=list(range(8)))
    out = np.empty((BATCH, SEQ, HID), dtype=np.float32)
    for core in range(8):
        bb, hh = core // 2, core % 2
        out[bb, hh * TOK:(hh + 1) * TOK, :] = np.asarray(res.results[core]["y"])
    return out


# version 12
# speedup vs baseline: 1.3428x; 1.0356x over previous
import numpy as np
import ml_dtypes
from contextlib import ExitStack

import concourse.bass as bass
import concourse.mybir as mybir
from concourse.bass_utils import run_bass_kernel_spmd

F32 = mybir.dt.float32
BF16 = mybir.dt.bfloat16
I32 = mybir.dt.int32
AF = mybir.ActivationFunctionType
ALU = mybir.AluOpType
AX = mybir.AxisListType

HID = 4096
NH = 32
NKV = 8
BATCH = 4
SEQ = 2048
TOK = 1024
NT = 8
NCH = 32
EPS = 1e-5
QB = 127.0
CAPACITY = 3072.0
ALPHA = 1.0 / 16.0
SM_SCALE = 0.08838834764831845
SHIFT_CAP = 80.0

DEBUG_DUMPS = False
ENGS = ("pe", "act", "dve", "pool", "sp")
DMA_CH = ("dW0", "dW1", "dX0", "dX1", "dK0", "dK1", "dM", "dS", "cc0", "cc1", "cc2")


class Bld:
    def __init__(self, nc, ex):
        self.nc = nc
        self.sem = {ch: ex.enter_context(nc.semaphore("sem_" + ch))
                    for ch in ENGS + DMA_CH}
        self.cnt = {ch: 0 for ch in ENGS + DMA_CH}
        self.waited = {e: {} for e in ENGS}
        self.stream = {e: [] for e in ENGS}
        self.last_inc = {e: True for e in ENGS}

    def _wait(self, eng, tok):
        ch, v = tok
        if v <= 0 or self.waited[eng].get(ch, 0) >= v:
            return
        self.waited[eng][ch] = v
        sem = self.sem[ch]
        self.stream[eng].append(lambda h, sem=sem, v=v: h.wait_ge(sem, v))

    def op(self, eng, fn, deps=(), inc=True):
        for t in deps:
            if t is not None:
                self._wait(eng, t)
        tok = None
        if inc:
            self.cnt[eng] += 1
            tok = (eng, self.cnt[eng])
            sem = self.sem[eng]
            self.stream[eng].append(lambda h, fn=fn, sem=sem: fn(h).then_inc(sem, 1))
        else:
            self.stream[eng].append(lambda h, fn=fn: fn(h))
        self.last_inc[eng] = bool(inc)
        return tok

    def dma(self, eng, ch, out, in_, deps=()):
        for t in deps:
            if t is not None:
                self._wait(eng, t)
        self.cnt[ch] += 16
        tok = (ch, self.cnt[ch])
        sem = self.sem[ch]
        self.stream[eng].append(
            lambda h, out=out, in_=in_, sem=sem:
                h.dma_start(out=out, in_=in_).then_inc(sem, 16))
        self.last_inc[eng] = False
        return tok

    def coll(self, kind, alu, groups, ins, outs, deps=(), ch="cc0"):
        for t in deps:
            if t is not None:
                self._wait("pool", t)
        self.cnt[ch] += 1
        tok = (ch, self.cnt[ch])
        sem = self.sem[ch]
        self.stream["pool"].append(
            lambda h, sem=sem: h.collective_compute(
                kind, alu, replica_groups=groups, ins=ins, outs=outs
            ).then_inc(sem, 1))
        self.last_inc["pool"] = False
        return tok

    def barrier(self, engines=("pe", "act", "dve", "pool")):
        for e in engines:
            if not self.last_inc[e] and self.stream[e]:
                self.cnt[e] += 1
                sem = self.sem[e]
                self.stream[e].append(
                    lambda h, sem=sem: h.nop().then_inc(sem, 1))
                self.last_inc[e] = True
        toks = [(e, self.cnt[e]) for e in engines]
        for e in engines:
            for t in toks:
                if t[0] != e:
                    self._wait(e, t)


def build_nc():
    nc = bass.Bass("TRN2", target_bir_lowering=False,
                   detect_race_conditions=False)

    def din(name, shape, dt=F32):
        return nc.dram_tensor(name, shape, dt, kind="ExternalInput").ap()

    x_own = din("x_own", [TOK, HID])
    wq = din("wq", [NH, 128, HID])
    wk = din("wk", [NKV, 128, HID])
    wv = din("wv", [NKV, 128, HID])
    wo = din("wo", [NH, 128, HID])
    wm = din("wm", [NCH, 128, HID])
    a_all = din("a_all", [128, HID])
    b_all = din("b_all", [128, HID])
    gw1t = din("gw1t", [128, NCH * 32])
    gb1c = din("gb1c", [32, 1])
    g32 = din("g32", [128, 32])
    bt32 = din("bt32", [128, 32])
    gw2t = din("gw2t", [32, 8])
    gb2c = din("gb2c", [8, 1])
    lnw = din("lnw", [128, HID])
    lnb = din("lnb", [128, HID])
    cosb = din("cosb", [128, TOK])
    sinb = din("sinb", [128, TOK])
    identbf = din("identbf", [128, 128], BF16)
    identf = din("identf", [128, 128])
    onesf = din("onesf", [128, 128])
    onesbf = din("onesbf", [128, 1], BF16)
    selS = din("selS", [8, 128], BF16)

    y_own = nc.dram_tensor("y", [TOK, HID], F32, kind="ExternalOutput").ap()

    x2d = nc.dram_tensor("x2d", [TOK, HID], F32)
    at3d = nc.dram_tensor("at3d", [TOK, HID], BF16)
    ccg_in = nc.dram_tensor("ccg_in", [2 * NKV * 128, TOK], BF16)
    ccg_out = nc.dram_tensor("ccg_out", [2 * 2 * NKV * 128, TOK], BF16)
    cnt_in = nc.dram_tensor("cnt_in", [1, 16], F32)
    if DEBUG_DUMPS:
        dbg_qn = nc.dram_tensor("dbg_qn", [128, NCH * TOK], BF16,
                                kind="ExternalOutput")
        dbg_q3 = nc.dram_tensor("dbg_q3", [128, NCH * TOK], BF16,
                                kind="ExternalOutput")
        dbg_b3 = nc.dram_tensor("dbg_b3", [128, NCH * TOK], BF16,
                                kind="ExternalOutput")
        dbg_ccg = nc.dram_tensor("dbg_ccg", [2 * NKV * 128, TOK], BF16,
                                 kind="ExternalOutput")
        dbg_at3 = nc.dram_tensor("dbg_at3", [TOK, HID], BF16,
                                 kind="ExternalOutput")
        dbg_x2 = nc.dram_tensor("dbg_x2", [TOK, HID], F32,
                                kind="ExternalOutput")
        dbg_sc = nc.dram_tensor("dbg_sc", [128, 64], F32,
                                kind="ExternalOutput")
    cnt_out = nc.dram_tensor("cnt_out", [1, 16], F32)
    kn_in = nc.dram_tensor("kn_in", [1, NKV], F32)
    kn_out = nc.dram_tensor("kn_out", [1, NKV], F32)

    ex = ExitStack()
    with ex:
        b = Bld(nc, ex)
        E = ex.enter_context

        def ACT(fn, deps=(), inc=True):
            return b.op("act", fn, deps, inc)

        def DVE(fn, deps=(), inc=True):
            return b.op("dve", fn, deps, inc)

        def PE(fn, deps=(), inc=False):
            return b.op("pe", fn, deps, inc)

        s_identbf = E(nc.sbuf_tensor("s_identbf", [128, 128], BF16))
        s_identf = E(nc.sbuf_tensor("s_identf", [128, 128], F32))
        s_onesf = E(nc.sbuf_tensor("s_onesf", [128, 128], F32))
        s_onesbf = E(nc.sbuf_tensor("s_onesbf", [128, 1], BF16))
        s_selS = E(nc.sbuf_tensor("s_selS", [8, 128], BF16))
        s_g32 = E(nc.sbuf_tensor("s_g32", [128, 32], F32))
        s_bt32 = E(nc.sbuf_tensor("s_bt32", [128, 32], F32))
        s_gb1 = E(nc.sbuf_tensor("s_gb1", [32, 1], F32))
        s_gb2 = E(nc.sbuf_tensor("s_gb2", [8, 1], F32))
        s_gw2bf = E(nc.sbuf_tensor("s_gw2bf", [32, 8], BF16))
        s_scales = E(nc.sbuf_tensor("s_scales", [128, 12], F32))
        s_negM = E(nc.sbuf_tensor("s_negM", [128, 1], F32))
        s_stat = E(nc.sbuf_tensor("s_stat", [128, 48], F32))
        s_row = E(nc.sbuf_tensor("s_row", [1, 64], F32))
        s_qn2 = E(nc.sbuf_tensor("s_qn2", [1, NH], F32))
        s_kn2 = E(nc.sbuf_tensor("s_kn2", [1, NKV], F32))
        s_wcol = E(nc.sbuf_tensor("s_wcol", [128, 160], F32))
        s_cnt = E(nc.sbuf_tensor("s_cnt", [128, 16], F32))
        s_cntrow = E(nc.sbuf_tensor("s_cntrow", [1, 16], F32))
        s_alb = E(nc.sbuf_tensor("s_alb", [128, 16], F32))
        s_c1 = E(nc.sbuf_tensor("s_c1", [128, NT, 8], F32))
        s_c2 = E(nc.sbuf_tensor("s_c2", [128, NT, 8], F32))
        s_tsc = E(nc.sbuf_tensor("s_tsc", [128, TOK], BF16))
        s_cstg = E(nc.sbuf_tensor("s_cstg", [128, 128], F32))

        regA_buf = E(nc.sbuf_tensor("regA", [128, NCH * TOK], BF16))
        regC_buf = E(nc.sbuf_tensor("regC", [128, NCH * TOK], BF16))

        strm = ExitStack()
        SB = {}

        _sbn = [0]

        def open_stream_base(st):
            _sbn[0] += 1
            n = _sbn[0]
            SB["blk"] = [st.enter_context(
                nc.sbuf_tensor(f"s_wblk{i}_{n}", [128, 2048], F32))
                for i in range(2)]
            SB["bf"] = [st.enter_context(
                nc.sbuf_tensor(f"s_wbf{i}_{n}", [128, 2048], BF16))
                for i in range(2)]
            SB["junk"] = st.enter_context(
                nc.sbuf_tensor(f"s_wjunk_{n}", [128, 2048], F32))

        def open_stream_qi(st):
            n = _sbn[0]
            SB["qi"] = [st.enter_context(
                nc.sbuf_tensor(f"s_wqi{i}_{n}", [128, 2048], I32))
                for i in range(2)]

        def open_stream_bufs(st):
            open_stream_base(st)
            open_stream_qi(st)

        open_stream_base(strm)

        pscl = ExitStack()
        p_s1 = pscl.enter_context(nc.psum_tensor("p_s1", [1, 512], F32))
        pb128 = pscl.enter_context(nc.psum_tensor("pb128", [128, 512], F32))

        class PbfRing:

            def __init__(self, ap):
                self.ap = ap
                self.rd = [None, None]
                self.i = 0

            def next(self):
                s = self.i % 8
                self.i += 1
                bk = s % 2
                c0 = bk * 1024 + (s // 2) * 128
                return self.ap[:, c0:c0 + 128], bk

        def _mkset(base):
            names = ("sum", "sq", "r", "b", "am", "s", "inv", "t0", "t1",
                     "den", "m1", "m2", "d", "e2", "rg", "g1", "g2")
            d = {nm: s_stat[:, base + i:base + i + 1]
                 for i, nm in enumerate(names)}
            d["last"] = None
            return d

        SCP = [_mkset(0), _mkset(17)]
        sc0 = SCP[0]
        c_sum = sc0["sum"]
        c_sq = sc0["sq"]
        c_r = sc0["r"]
        c_b = sc0["b"]
        c_am = sc0["am"]
        c_s = sc0["s"]
        c_inv = sc0["inv"]
        c_t0 = sc0["t0"]
        c_t1 = sc0["t1"]
        c_den = sc0["den"]
        c_m1 = sc0["m1"]
        c_m2 = sc0["m2"]
        c_d = sc0["d"]
        c_e2 = sc0["e2"]
        c_rg = sc0["rg"]
        c_g1 = sc0["g1"]
        c_g2 = sc0["g2"]

        tc = []
        for dst, src in ((s_identbf, identbf), (s_identf, identf),
                         (s_onesf, onesf), (s_onesbf, onesbf), (s_selS, selS),
                         (s_g32, g32), (s_bt32, bt32), (s_gb1, gb1c),
                         (s_gb2, gb2c)):
            tc.append(b.dma("pool", "dM", dst[:], src))
        t_const = tc[-1]
        t_cnt0 = DVE(lambda h: h.memset(s_cnt[:], 0.0))
        c_eps = s_stat[:, 40:41]
        t_eps = DVE(lambda h: h.memset(c_eps, EPS))
        r_tiny = s_row[0:1, 62:63]
        t_tiny = DVE(lambda h: h.memset(r_tiny, 1e-30))

        def ln_stats(src_ap, n, deps, copy_dst, sq_dst, sc=None):
            if sc is None:
                sc = SCP[0]
            csum, csq, ct0, ct1, cr, cb = (sc["sum"], sc["sq"], sc["t0"],
                                           sc["t1"], sc["r"], sc["b"])
            deps = list(deps) + [sc["last"]]
            dst = copy_dst
            t2 = ACT(lambda h, dst=dst: h.activation(out=dst, in_=src_ap,
                                                     func=AF.Copy,
                                                     accum_out=csum), deps)
            t1 = ACT(lambda h, sq_dst=sq_dst: h.activation(
                out=sq_dst, in_=src_ap, func=AF.Square, accum_out=csq),
                     deps)
            t3 = DVE(lambda h: h.tensor_scalar(out=ct0, in0=csum,
                                               scalar1=1.0 / n, scalar2=None,
                                               op0=ALU.mult), [t2])
            t4 = DVE(lambda h: h.tensor_tensor(out=ct1, in0=ct0, in1=ct0,
                                               op=ALU.mult), [t3])
            t5 = DVE(lambda h: h.scalar_tensor_tensor(out=ct1, in0=csq,
                                                      scalar=1.0 / n, in1=ct1,
                                                      op0=ALU.mult,
                                                      op1=ALU.subtract),
                     [t1, t4])
            t6 = ACT(lambda h: h.activation(out=ct1, in_=ct1, func=AF.Ln,
                                            bias=c_eps, scale=1.0),
                     [t5, t_eps])
            t7 = ACT(lambda h: h.activation(out=cr, in_=ct1, func=AF.Exp,
                                            bias=0.0, scale=-0.5), [t6])
            t8 = DVE(lambda h: h.tensor_tensor(out=cb, in0=ct0, in1=cr,
                                               op=ALU.mult), [t7, t3])
            t9 = DVE(lambda h: h.tensor_scalar(out=cb, in0=cb, scalar1=-1.0,
                                               scalar2=None, op0=ALU.mult),
                     [t8])
            return t7, t9, t2

        def emit_act_quant(src_col, out_col, y1ap, deps_in, n_chunks=4,
                           sc=None):
            if sc is None:
                sc = SCP[0]
            cs_, cinv_ = sc["s"], sc["inv"]
            A = y1ap[:, 0:1024]
            Bi = y1ap[:, 1024:2048].bitcast(I32)
            C = y1ap[:, 2048:3072]
            Es = y1ap[:, 3072:4096]
            tA = tQ = None
            for k in range(n_chunks):
                tA = DVE(lambda h, k=k: h.tensor_scalar(
                    out=A, in0=src_col(k), scalar1=cs_, scalar2=None,
                    op0=ALU.mult), list(deps_in) + [tQ])
                tB = DVE(lambda h: h.tensor_copy(out=Bi, in_=A), [tA])
                tC = DVE(lambda h: h.tensor_tensor(out=C, in0=A, in1=Bi,
                                                   op=ALU.subtract), [tB])
                tD = DVE(lambda h: h.tensor_scalar(out=A, in0=C, scalar1=0.5,
                                                   scalar2=None,
                                                   op0=ALU.is_ge), [tC])
                tE = DVE(lambda h: h.scalar_tensor_tensor(
                    out=Es, in0=C, scalar=-0.5, in1=A, op0=ALU.is_le,
                    op1=ALU.subtract), [tD])
                tF = DVE(lambda h: h.tensor_tensor(out=C, in0=Bi, in1=Es,
                                                   op=ALU.subtract), [tE])
                tQ = DVE(lambda h, k=k: h.tensor_scalar(
                    out=out_col(k), in0=C, scalar1=cinv_, scalar2=None,
                    op0=ALU.mult), [tF])
            sc["last"] = tQ
            return tA, tQ

        class HalfStream:
            def __init__(self):
                self.i = 0
                self.t_q1 = [None, None]
                self.t_q2 = [None, None]
                self.t_mm = [None, None]
                self.t_abs = [None, None]

            def next(self, src_slice, mode, scol=0, wcol=0, extra_deps=(),
                     scol_c2=(8,), par=None):
                if par is None:
                    par = self.i % 2
                    self.i += 1
                blk, bfb, jnk = SB["blk"][par], SB["bf"][par], SB["junk"]
                qib = SB["qi"][par] if "qi" in SB else None
                tl = b.dma("sp", "dW%d" % par, blk[:], src_slice,
                           deps=[self.t_q1[par], self.t_abs[par]] +
                                list(extra_deps))
                if mode == "abs":
                    ta = ACT(lambda h, blk=blk, jnk=jnk, wcol=wcol:
                             h.activation(
                        out=jnk[:], in_=blk[:], func=AF.Abs,
                        accum_out=s_wcol[:, wcol:wcol + 1]), [tl])
                    self.t_abs[par] = ta
                    return par, ta
                if mode == "quant":
                    c2col = scol_c2[0]
                    qif = qib[:].bitcast(F32)
                    tq1 = DVE(lambda h, blk=blk, qif=qif, scol=scol:
                              h.tensor_scalar(
                        out=qif, in0=blk[:],
                        scalar1=s_scales[:, scol:scol + 1], scalar2=0.5,
                        op0=ALU.mult, op1=ALU.is_ge), [tl, self.t_q2[par]])
                    tq2 = DVE(lambda h, bfb=bfb, qif=qif, blk=blk,
                              c2col=c2col: h.scalar_tensor_tensor(
                        out=bfb[:], in0=blk[:],
                        scalar=s_scales[:, c2col:c2col + 1], in1=qif,
                        op0=ALU.is_le, op1=ALU.subtract),
                        [tq1, self.t_mm[par]])
                    self.t_q1[par] = tq2
                    self.t_q2[par] = tq2
                    return par, tq2
                tq = DVE(lambda h, blk=blk, bfb=bfb: h.tensor_copy(
                    out=bfb[:], in_=blk[:]),
                    [tl, self.t_mm[par]])
                self.t_q1[par] = tq
                self.t_q2[par] = tq
                return par, tq

            def bf(self, par):
                return SB["bf"][par]

            def consumed(self, par, tok):
                self.t_mm[par] = tok

        ws = HalfStream()

        scale_queue = []
        wc_base = {"wk": 0, "wv": 16, "wq": 32, "wo": 96}
        for nm_, src_, name_ in ((NKV, wk, "wk"), (NKV, wv, "wv"),
                                 (NH, wq, "wq"), (NH, wo, "wo")):
            for m in range(nm_):
                for hf in range(2):
                    scale_queue.append(
                        (name_, src_, m, hf, wc_base[name_] + 2 * m + hf))
        sq_pos = [0]
        sq_abs = {"wk": [], "wv": [], "wq": [], "wo": []}
        t_pb_use = [None]

        def emit_scale_halves(k):
            for _ in range(k):
                if sq_pos[0] >= len(scale_queue):
                    return
                name, src, m, hf, wcol = scale_queue[sq_pos[0]]
                sq_pos[0] += 1
                par, ta = ws.next(src[m, :, hf * 2048:(hf + 1) * 2048], "abs",
                                  wcol=wcol)
                sq_abs[name].append(ta)

        def finish_scale(name, nm, scol_s, scol_d, extra, slot):
            nelem = nm * 128 * HID
            cb = wc_base[name]
            tr = DVE(lambda h, nm=nm, cb=cb: h.tensor_reduce(
                out=c_t0, in_=s_wcol[:, cb:cb + 2 * nm], axis=AX.X,
                op=ALU.add), [sq_abs[name][-1]])
            tm1 = PE(lambda h, slot=slot: h.matmul(
                p_s1[0:1, slot:slot + 1], s_onesf[:, 0:1], c_t0,
                start=True, stop=True), [tr, t_pb_use[0], t_const], inc=True)
            r0 = 32 + slot * 4
            tm2 = DVE(lambda h, nelem=nelem, r0=r0, slot=slot: h.tensor_scalar(
                out=s_row[0:1, r0:r0 + 1], in0=p_s1[0:1, slot:slot + 1],
                scalar1=1.0 / nelem, scalar2=1e-5, op0=ALU.mult, op1=ALU.max),
                [tm1])
            tm3 = DVE(lambda h, r0=r0: h.reciprocal(
                out=s_row[0:1, r0 + 1:r0 + 2], in_=s_row[0:1, r0:r0 + 1]),
                [tm2])
            tm4 = DVE(lambda h, r0=r0, extra=extra: h.tensor_scalar(
                out=s_row[0:1, r0 + 2:r0 + 3], in0=s_row[0:1, r0:r0 + 1],
                scalar1=-extra, scalar2=None, op0=ALU.mult), [tm2])
            tm5 = DVE(lambda h, r0=r0: h.tensor_scalar(
                out=s_row[0:1, r0 + 3:r0 + 4], in0=s_row[0:1, r0:r0 + 1],
                scalar1=-0.5, scalar2=None, op0=ALU.mult), [tm2])
            tb1 = PE(lambda h, scol_d=scol_d, r0=r0: h.matmul(
                pb128[:, scol_d:scol_d + 1], s_onesf[0:1, :],
                s_row[0:1, r0 + 2:r0 + 3], start=True, stop=True),
                [tm4], inc=True)
            tb2 = PE(lambda h, scol_s=scol_s, r0=r0: h.matmul(
                pb128[:, scol_s:scol_s + 1], s_onesf[0:1, :],
                s_row[0:1, r0 + 1:r0 + 2], start=True, stop=True),
                [tm3], inc=True)
            tb3 = PE(lambda h, slot=slot, r0=r0: h.matmul(
                pb128[:, 8 + slot:9 + slot], s_onesf[0:1, :],
                s_row[0:1, r0 + 3:r0 + 4], start=True, stop=True),
                [tm5], inc=True)
            lo, hi = min(scol_s, scol_d), max(scol_s, scol_d)
            te_ = DVE(lambda h, lo=lo, hi=hi: h.tensor_copy(
                out=s_scales[:, lo:hi + 1], in_=pb128[:, lo:hi + 1]),
                [tb1, tb2, tb3])
            te = DVE(lambda h, slot=slot: h.tensor_copy(
                out=s_scales[:, 8 + slot:9 + slot],
                in_=pb128[:, 8 + slot:9 + slot]), [tb3, te_])
            t_pb_use[0] = te
            return te

        qn3 = regA_buf[:].rearrange("p (c t) -> p c t", t=TOK)

        scope1 = ExitStack()
        pbf1s = ExitStack()
        ring1 = PbfRing(pbf1s.enter_context(
            nc.psum_tensor("pbf1", [128, 2048], BF16))[:])
        s_x = [scope1.enter_context(nc.sbuf_tensor(f"s_x{i}", [128, HID], F32))
               for i in range(2)]
        s_y1 = regC_buf[:, 0:8192].bitcast(F32)
        s_lnw = regC_buf[:, 8192:16384].bitcast(F32)
        s_lnb = regC_buf[:, 16384:24576].bitcast(F32)
        s_qtm = regC_buf[:, 24576:28672]

        t_lnw = b.dma("pool", "dM", s_lnw, lnw)
        t_lnb = b.dma("pool", "dM", s_lnb, lnb)

        t_x_use = [None, None]
        t_pbf = [None] * 8
        tok_qn = None
        prev_tq1 = None
        prev_tq2 = None
        prev_tz = None
        for t in range(NT):
            par = t % 2
            xb = s_x[par]
            tl = b.dma("pool", "dX%d" % par, xb[:],
                       x_own[t * 128:(t + 1) * 128, :], deps=[t_x_use[par]])
            sc = SCP[par]
            scr, scb, scam, scs, scinv = (sc["r"], sc["b"], sc["am"],
                                          sc["s"], sc["inv"])
            r1, b1, _ = ln_stats(xb[:], HID, [tl, prev_tq2],
                                 copy_dst=s_y1, sq_dst=s_y1, sc=sc)
            ty = ACT(lambda h, xb=xb, scr=scr, scb=scb: h.activation(
                out=s_y1, in_=xb[:], func=AF.Identity, bias=scb, scale=scr),
                [r1, b1])
            r2, b2, _ = ln_stats(s_y1, HID, [ty],
                                 copy_dst=xb[:], sq_dst=xb[:], sc=sc)
            tz = ACT(lambda h, xb=xb, scr=scr, scb=scb: h.activation(
                out=xb[:], in_=s_y1, func=AF.Identity, bias=scb, scale=scr),
                [r2, b2, prev_tq1])
            tz2 = DVE(lambda h, xb=xb: h.tensor_tensor(
                out=s_y1, in0=xb[:], in1=s_lnw, op=ALU.mult),
                [tz, t_lnw])
            tz3 = DVE(lambda h, xb=xb: h.tensor_tensor(
                out=xb[:], in0=s_y1, in1=s_lnb, op=ALU.add),
                [tz2, t_lnb])
            tam = DVE(lambda h, xb=xb, scam=scam: h.tensor_reduce(
                out=scam, in_=xb[:], axis=AX.X, op=ALU.max,
                apply_absolute_value=True), [tz3])
            tmc = DVE(lambda h, scam=scam: h.tensor_scalar(
                out=scam, in0=scam, scalar1=1e-5, scalar2=None,
                op0=ALU.max), [tam])
            trc = DVE(lambda h, scs=scs, scam=scam: h.reciprocal(
                out=scs, in_=scam), [tmc])
            ts_ = DVE(lambda h, scs=scs: h.tensor_scalar(
                out=scs, in0=scs, scalar1=QB,
                scalar2=None, op0=ALU.mult), [trc])
            tin = DVE(lambda h, scinv=scinv, scam=scam: h.tensor_scalar(
                out=scinv, in0=scam, scalar1=1.0 / QB, scalar2=None,
                op0=ALU.mult), [tmc])
            tq1, tq2 = emit_act_quant(
                lambda k, xb=xb: xb[:, k * 1024:(k + 1) * 1024],
                lambda k: s_qtm[:, k * 1024:(k + 1) * 1024],
                s_y1, [ts_, tin, tz3, tam, tok_qn], sc=sc)
            prev_tq1 = tq1
            t_x_use[par] = tq1
            prev_tq2 = tq2
            prev_tz = tz
            for c in range(NCH):
                sl, bk = ring1.next()
                tp = PE(lambda h, c=c, sl=sl: h.transpose(
                    sl, s_qtm[:, c * 128:(c + 1) * 128], s_identbf[:]),
                    [tq2, ring1.rd[bk], t_const], inc=True)
                tev = DVE(lambda h, c=c, t=t, sl=sl: h.tensor_copy(
                    out=qn3[:, c, t * 128:(t + 1) * 128], in_=sl), [tp])
                ring1.rd[bk] = tev
                tok_qn = tev
            emit_scale_halves(6)
        scope1.close()
        pbf1s.close()

        if DEBUG_DUMPS:
            td_ = b.dma("pool", "dM", dbg_qn.ap(), regA_buf[:],
                        deps=[tok_qn, prev_tq1])
            b._wait("dve", td_)
            b._wait("pe", td_)
        tok_swk = finish_scale("wk", NKV, 2, 3, 1.0, 0)
        tok_swv = finish_scale("wv", NKV, 4, 5, 1.0, 1)

        Q3 = regC_buf[:].rearrange("p (m t) -> p m t", t=TOK)
        strmqi = ExitStack()
        open_stream_qi(strmqi)
        scope2 = ExitStack()
        s_cos = scope2.enter_context(nc.sbuf_tensor("s_cos", [128, TOK], F32))
        s_sin = scope2.enter_context(nc.sbuf_tensor("s_sin", [128, TOK], F32))
        s_kpre = scope2.enter_context(nc.sbuf_tensor("s_kpre", [128, TOK], F32))
        s_krot = scope2.enter_context(nc.sbuf_tensor("s_krot", [128, TOK], F32))
        s_kt1 = scope2.enter_context(nc.sbuf_tensor("s_kt1", [128, TOK], F32))
        s_stg = [scope2.enter_context(
            nc.sbuf_tensor(f"s_stg{i}", [128, TOK], BF16)) for i in range(2)]
        s_sq = s_kt1[:].bitcast(BF16)[:, 0:TOK]
        t_cos = b.dma("pool", "dM", s_cos[:], cosb,
                      deps=[prev_tq1, prev_tq2, prev_tz])
        t_sin = b.dma("pool", "dM", s_sin[:], sinb)
        pj = ExitStack()
        pproj4 = [pj.enter_context(nc.psum_tensor(f"pproj{i}", [128, 512], F32))
                  for i in range(4)]
        pnorm = pj.enter_context(nc.psum_tensor("pnorm", [1, 1024], F32))

        t_pp = [None, None, None, None]
        t_stg_use = [None, None]
        t_rope_rd = [None]
        t_stage = []
        t_kn = []
        t_pnorm = [None]
        t_sq_use = [None]

        blk_ctr = [0]
        cur_pr = [0]

        def q_half(src, m, hf, scol_q, scale_tok, c2col_q):
            return ws.next(src[m, :, hf * 2048:(hf + 1) * 2048], "quant",
                           scol=scol_q, extra_deps=[scale_tok],
                           scol_c2=(c2col_q,), par=hf)

        def mm_half(hf, qinfo, scale_tok, new_block):
            if new_block:
                cur_pr[0] = blk_ctr[0] % 2
                blk_ctr[0] += 1
            pr = cur_pr[0]
            pproj = pproj4[2 * pr:2 * pr + 2]
            par, trdy = qinfo
            tokm = None
            for j in range(16):
                c = hf * 16 + j
                for tg in range(2):
                    dep = [trdy, tok_qn, scale_tok]
                    if c == 0:
                        dep.append(t_pp[2 * pr + tg])
                    tokm = PE(lambda h, wb_=ws.bf(par), j=j, c=c, tg=tg,
                              pp_=pproj[tg]: h.matmul(
                        pp_[:],
                        wb_[:, j * 128:(j + 1) * 128],
                        qn3[:, c, tg * 512:(tg + 1) * 512],
                        start=(c == 0), stop=(c == 31)),
                        dep, inc=(tg == 1 and (c == 15 or c == 31)))
            ws.consumed(par, tokm)
            return tokm

        pend_norm = [None]

        def flush_norm():
            if pend_norm[0] is None:
                return
            tsq, kind, m = pend_norm[0]
            pend_norm[0] = None
            tn1 = PE(lambda h: h.matmul(pnorm[0:1, 0:512],
                                        s_onesbf[:, 0:1], s_sq[:, 0:512],
                                        start=True, stop=True),
                     [tsq, t_pnorm[0], t_const], inc=True)
            tn2 = PE(lambda h: h.matmul(pnorm[0:1, 512:1024],
                                        s_onesbf[:, 0:1],
                                        s_sq[:, 512:1024],
                                        start=True, stop=True),
                     [tsq], inc=True)
            t_sq_use[0] = tn2
            dst = s_qn2 if kind == "q" else s_kn2
            tnr = DVE(lambda h, dst=dst, m=m: h.tensor_reduce(
                out=dst[0:1, m:m + 1], in_=pnorm[0:1, :], axis=AX.X,
                op=ALU.max), [tn1, tn2])
            t_pnorm[0] = tnr
            if kind == "k":
                t_kn.append(tnr)

        def proj_evac(m, dcol, rope, kind, mm_last):
            flush_norm()
            pr = cur_pr[0]
            pproj = pproj4[2 * pr:2 * pr + 2]
            stg_par = m % 2
            deps_e = [mm_last[0], mm_last[1]]
            if rope:
                te0 = ACT(lambda h, dcol=dcol, pp_=pproj[0]: h.activation(
                    out=s_kpre[:, 0:512], in_=pp_[:], func=AF.Copy,
                    scale=s_scales[:, dcol:dcol + 1]),
                    deps_e + [t_rope_rd[0]])
                te1 = ACT(lambda h, dcol=dcol, pp_=pproj[1]: h.activation(
                    out=s_kpre[:, 512:1024], in_=pp_[:], func=AF.Copy,
                    scale=s_scales[:, dcol:dcol + 1]), deps_e)
                GPS = DVE
                tn = GPS(lambda h: h.tensor_scalar(
                    out=s_krot[0:64, :], in0=s_kpre[64:128, :], scalar1=-1.0,
                    scalar2=None, op0=ALU.mult), [te0, te1])
                tcp = GPS(lambda h: h.tensor_copy(out=s_krot[64:128, :],
                                                  in_=s_kpre[0:64, :]),
                          [te0, te1])
                t1 = GPS(lambda h: h.tensor_tensor(out=s_kt1[:], in0=s_kpre[:],
                                                   in1=s_cos[:], op=ALU.mult),
                         [te0, te1, t_cos, t_sq_use[0]])
                t2 = GPS(lambda h: h.tensor_tensor(out=s_kpre[:],
                                                   in0=s_krot[:], in1=s_sin[:],
                                                   op=ALU.mult),
                         [tn, tcp, t_sin, t1])
                if kind == "q":
                    t3 = GPS(lambda h, m=m: h.tensor_tensor(
                        out=Q3[:, m, :], in0=s_kt1[:], in1=s_kpre[:],
                        op=ALU.add), [t1, t2])
                else:
                    t3 = GPS(lambda h, stg_par=stg_par: h.tensor_tensor(
                        out=s_stg[stg_par][:], in0=s_kt1[:], in1=s_kpre[:],
                        op=ALU.add), [t1, t2, t_stg_use[stg_par]])
                t_rope_rd[0] = t3
                t_pp[2 * pr], t_pp[2 * pr + 1] = te0, te1
            else:
                te0 = ACT(lambda h, stg_par=stg_par, dcol=dcol, pp_=pproj[0]:
                          h.activation(
                    out=s_stg[stg_par][:, 0:512], in_=pp_[:],
                    func=AF.Copy, scale=s_scales[:, dcol:dcol + 1]),
                    deps_e + [t_stg_use[stg_par]])
                te1 = ACT(lambda h, stg_par=stg_par, dcol=dcol, pp_=pproj[1]:
                          h.activation(
                    out=s_stg[stg_par][:, 512:1024], in_=pp_[:],
                    func=AF.Copy, scale=s_scales[:, dcol:dcol + 1]), deps_e)
                t3 = te1
                t_pp[2 * pr], t_pp[2 * pr + 1] = te0, te1
            tsq = None
            if kind == "q":
                tsq = ACT(lambda h, m=m: h.activation(
                    out=s_sq, in_=Q3[:, m, :], func=AF.Square),
                    [t3, t_sq_use[0]])
            elif kind == "k":
                tsq = ACT(lambda h, stg_par=stg_par: h.activation(
                    out=s_sq, in_=s_stg[stg_par][:], func=AF.Square),
                    [t3, t_sq_use[0]])
            if tsq is not None:
                pend_norm[0] = (tsq, kind, m)
            if kind in ("k", "v"):
                row0 = (0 if kind == "k" else NKV * 128) + m * 128
                td = b.dma("pool", "dK0", ccg_in.ap()[row0:row0 + 128, :],
                           s_stg[stg_par][:],
                           deps=[t3] + ([tsq] if kind == "k" else []))
                t_stg_use[stg_par] = td
                t_stage.append(td)
            return t3

        def run_pipeline(blocks, nsh):
            n = len(blocks)
            (bsrc, m, scol, dcol, rope, kind, stok, c2c) = blocks[0]
            qA = q_half(bsrc, m, 0, scol, stok, c2c)
            qB = q_half(bsrc, m, 1, scol, stok, c2c)
            for i in range(n):
                (bsrc, m, scol, dcol, rope, kind, stok, c2c) = blocks[i]
                mm0 = mm_half(0, qA, stok, True)
                if i + 1 < n:
                    (b2, m2, s2, d2, r2, k2, st2, cc2) = blocks[i + 1]
                    qA = q_half(b2, m2, 0, s2, st2, cc2)
                mm1 = mm_half(1, qB, stok, False)
                if i + 1 < n:
                    qB = q_half(b2, m2, 1, s2, st2, cc2)
                proj_evac(m, dcol, rope, kind, [mm0, mm1])
                emit_scale_halves(nsh)

        blocks = ([(wk, m, 2, 3, True, "k", tok_swk, 8)
                   for m in range(NKV)] +
                  [(wv, m, 4, 5, False, "v", tok_swv, 9)
                   for m in range(NKV)])
        run_pipeline(blocks, 3)
        flush_norm()
        tok_swq = finish_scale("wq", NH, 0, 1, SM_SCALE, 2)

        t_kn_dma = b.dma("pool", "dK0", kn_in.ap(), s_kn2[0:1, :], deps=t_kn)
        tok_ag = b.coll("AllGather", ALU.bypass,
                        [[0, 1], [2, 3], [4, 5], [6, 7]],
                        [ccg_in.ap().opt()], [ccg_out.ap().opt()],
                        deps=t_stage, ch="cc0")
        tok_knr = b.coll("AllReduce", ALU.max,
                         [[0, 1], [2, 3], [4, 5], [6, 7]],
                         [kn_in.ap().opt()], [kn_out.ap().opt()],
                         deps=[t_kn_dma], ch="cc1")
        t_kn_back = b.dma("pool", "dK0", s_kn2[0:1, :], kn_out.ap(),
                          deps=[tok_knr])

        run_pipeline([(wq, m, 0, 1, True, "q", tok_swq, 10)
                      for m in range(NH)], 2)
        flush_norm()
        tok_swo = finish_scale("wo", NH, 6, 7, 1.0, 3)

        tq2m = DVE(lambda h: h.tensor_reduce(out=s_row[0:1, 0:1],
                                             in_=s_qn2[0:1, :], axis=AX.X,
                                             op=ALU.max), [t_pnorm[0]])
        tk2m = DVE(lambda h: h.tensor_reduce(out=s_row[0:1, 1:2],
                                             in_=s_kn2[0:1, :], axis=AX.X,
                                             op=ALU.max), [t_kn_back])
        tl1 = ACT(lambda h: h.activation(out=s_row[0:1, 2:3],
                                         in_=s_row[0:1, 0:1], func=AF.Ln,
                                         bias=r_tiny, scale=1.0),
                  [tq2m, t_tiny])
        tl2 = ACT(lambda h: h.activation(out=s_row[0:1, 3:4],
                                         in_=s_row[0:1, 1:2], func=AF.Ln,
                                         bias=r_tiny, scale=1.0),
                  [tk2m, t_tiny])
        tl3 = DVE(lambda h: h.tensor_tensor(out=s_row[0:1, 4:5],
                                            in0=s_row[0:1, 2:3],
                                            in1=s_row[0:1, 3:4], op=ALU.add),
                  [tl1, tl2])
        tl4 = ACT(lambda h: h.activation(out=s_row[0:1, 5:6],
                                         in_=s_row[0:1, 4:5], func=AF.Exp,
                                         bias=0.0, scale=0.5), [tl3])
        tl5 = DVE(lambda h: h.tensor_scalar(out=s_row[0:1, 6:7],
                                            in0=s_row[0:1, 5:6],
                                            scalar1=SHIFT_CAP, scalar2=-1.0,
                                            op0=ALU.min, op1=ALU.mult), [tl4])
        tlb = PE(lambda h: h.matmul(pb128[:, 2:3], s_onesf[0:1, :],
                                    s_row[0:1, 6:7], start=True, stop=True),
                 [tl5, t_pb_use[0]], inc=True)
        tok_negM = DVE(lambda h: h.tensor_copy(out=s_negM[:],
                                               in_=pb128[:, 2:3]), [tlb])
        t_pb_use[0] = tok_negM

        if DEBUG_DUMPS:
            td_ = b.dma("pool", "dM", dbg_q3.ap(), regC_buf[:],
                        deps=[tok_negM, t_rope_rd[0]])
            b._wait("dve", td_)
        pj.close()
        pscl.close()
        scope2.close()
        strmqi.close()
        strm.close()
        b.barrier(("pe", "act", "dve"))
        sd = ExitStack()
        psc = [sd.enter_context(nc.psum_tensor(f"psc{i}", [128, 1024], F32))
               for i in range(2)]
        pout = [sd.enter_context(nc.psum_tensor(f"pout{i}", [128, 512], F32))
                for i in range(3)]
        s_kst = [sd.enter_context(nc.sbuf_tensor(f"s_kst{i}", [128, 2048],
                                                 BF16)) for i in range(2)]
        s_vstg = [sd.enter_context(nc.sbuf_tensor(f"s_vstg{i}", [128, 2048],
                                                  BF16)) for i in range(2)]
        s_vtm = [sd.enter_context(nc.sbuf_tensor(f"s_vtm{i}", [128, 16 * 129],
                                                 BF16)) for i in range(2)]
        s_atn = [sd.enter_context(nc.sbuf_tensor(f"s_atn{i}", [128, 1024],
                                                 BF16)) for i in range(4)]

        t_vtm_use = [DVE(lambda h: h.memset(s_vtm[0][:], 1.0)),
                     DVE(lambda h: h.memset(s_vtm[1][:], 1.0))]
        t_kst_use = [None, None]
        t_vstg_use = [None, None]
        t_psc_use = [None, None]
        t_atn_use = [None, None, None, None]
        t_pout_use = [None, None, None]
        t_pbf = [None] * 8

        B3 = regA_buf[:].rearrange("p (m t) -> p m t", t=TOK)

        for g in range(NKV):
            par = g % 2
            tk0 = b.dma("pool", "dK%d" % par, s_kst[par][:, 0:1024],
                        ccg_out.ap()[g * 128:(g + 1) * 128, :],
                        deps=[tok_ag, t_kst_use[par]] +
                             ([tok_swo] if g < 2 else []))
            tk1 = b.dma("pool", "dK%d" % par, s_kst[par][:, 1024:2048],
                        ccg_out.ap()[2048 + g * 128:2048 + (g + 1) * 128, :],
                        deps=[tok_ag])
            tv0_ = b.dma("pool", "dK%d" % par, s_vstg[par][:, 0:1024],
                         ccg_out.ap()[1024 + g * 128:1024 + (g + 1) * 128, :],
                         deps=[tok_ag, t_vstg_use[par]])
            tv1_ = b.dma("pool", "dK%d" % par, s_vstg[par][:, 1024:2048],
                         ccg_out.ap()[3072 + g * 128:3072 + (g + 1) * 128, :],
                         deps=[tok_ag])
            t_ld_all = tv1_
            tvl = None
            for kt in range(16):
                bk = kt % 2
                sl = pout[bk][:].bitcast(BF16)[:, 0:128]
                tp = PE(lambda h, par=par, kt=kt, sl=sl: h.transpose(
                    sl, s_vstg[par][:, kt * 128:(kt + 1) * 128], s_identbf[:]),
                    [t_ld_all, t_pout_use[bk], t_vtm_use[par]], inc=True)
                tev = DVE(lambda h, par=par, kt=kt, sl=sl: h.tensor_copy(
                    out=s_vtm[par][:, kt * 129:kt * 129 + 128], in_=sl), [tp])
                t_pout_use[bk] = tev
                tvl = tev
            t_vstg_use[par] = tvl
            tok_vtm = tvl

            for hq in range(4):
                hh = 4 * g + hq
                av_toks = [None] * NT
                exp_toks = [None] * 16

                def emit_av(j, hh=hh, par=par, av_toks=av_toks,
                            exp_toks=exp_toks):
                    ab = j % 4
                    tav = None
                    for qt in range(NT):
                        bk, slot = qt // 3, qt % 3
                        dep = [exp_toks[j], tok_vtm]
                        if j == 0:
                            dep.append(t_pout_use[bk])
                        tav = PE(lambda h, ab=ab, par=par, j=j, bk=bk,
                                 slot=slot, qt=qt: h.matmul(
                            pout[bk][:, slot * 129:slot * 129 + 129],
                            s_atn[ab][:, qt * 128:(qt + 1) * 128],
                            s_vtm[par][:, j * 129:j * 129 + 129],
                            start=(j == 0 and slot == 0), stop=(j == 15),
                            skip_group_check=True),
                            dep, inc=(j == 15 or qt == NT - 1))
                        if j == 15:
                            av_toks[qt] = tav
                    t_atn_use[ab] = tav

                for kt in range(16):
                    pp = kt % 2
                    ab = kt % 4
                    ts0 = PE(lambda h, par=par, kt=kt, hh=hh, pp=pp: h.matmul(
                        psc[pp][:, 0:512],
                        s_kst[par][:, kt * 128:(kt + 1) * 128],
                        Q3[:, hh, 0:512], start=True, stop=True),
                        [t_ld_all, t_psc_use[pp], tok_negM], inc=True)
                    ts1 = PE(lambda h, par=par, kt=kt, hh=hh, pp=pp: h.matmul(
                        psc[pp][:, 512:1024],
                        s_kst[par][:, kt * 128:(kt + 1) * 128],
                        Q3[:, hh, 512:1024], start=True, stop=True),
                        [], inc=True)
                    tex = ACT(lambda h, pp=pp, ab=ab: h.activation(
                        out=s_atn[ab][:], in_=psc[pp][:], func=AF.Exp,
                        bias=s_negM[:], scale=1.0),
                        [ts0, ts1, t_atn_use[ab]])
                    exp_toks[kt] = tex
                    t_psc_use[pp] = tex
                    if kt >= 2:
                        emit_av(kt - 2)
                emit_av(14)
                emit_av(15)
                t_kst_use[par] = av_toks[-1]
                t_vtm_use[par] = av_toks[-1]
                for qt in range(NT):
                    bk, slot = qt // 3, qt % 3
                    tcd_ = DVE(lambda h, bk=bk, slot=slot: h.tensor_copy(
                        out=c_t0,
                        in_=pout[bk][:, slot * 129 + 128:slot * 129 + 129]),
                        [av_toks[qt]])
                    trd = DVE(lambda h: h.reciprocal(out=c_den, in_=c_t0),
                              [tcd_])
                    tnm = DVE(lambda h, bk=bk, slot=slot, hh=hh, qt=qt:
                              h.tensor_scalar(
                        out=B3[:, hh, qt * 128:(qt + 1) * 128],
                        in0=pout[bk][:, slot * 129:slot * 129 + 128],
                        scalar1=c_den, scalar2=None, op0=ALU.mult), [trd])
                    t_pout_use[bk] = tnm

        tok_sdpa = [t for t in t_pout_use if t is not None]
        if DEBUG_DUMPS:
            td_ = b.dma("pool", "dM", dbg_b3.ap(), regA_buf[:],
                        deps=tok_sdpa)
            b._wait("dve", td_)
            b._wait("pool", td_)
        sd.close()
        b.barrier(("pe", "act", "dve"))
        p5 = ExitStack()
        pproj5 = [p5.enter_context(nc.psum_tensor(f"pp5{i}", [128, 512], F32))
                  for i in range(4)]
        ring5 = PbfRing(p5.enter_context(
            nc.psum_tensor("pbf5", [128, 2048], BF16))[:])
        open_stream_bufs(p5)
        s_y1p5 = p5.enter_context(nc.sbuf_tensor("s_y1b", [128, HID], F32))
        s_sh5v = p5.enter_context(nc.sbuf_tensor("s_sh5v", [128, HID], BF16))
        s_qtm5 = s_sh5v[:]
        s_aofm5 = s_sh5v[:, 0:TOK]
        s_at3s = s_sh5v[:, 2048:3072].rearrange("p (t f) -> p t f", f=128)

        Qo3 = regC_buf[:].rearrange("p (c t) -> p c t", t=TOK)

        tok_qo = None
        t_prev_chain = [None]
        p5_sc_toks = {}

        def p5_scale(qt):
            sc = SCP[qt % 2]
            tam = DVE(lambda h, qt=qt, sc=sc: h.tensor_reduce(
                out=sc["am"], in_=B3[:, :, qt * 128:(qt + 1) * 128],
                axis=AX.XY, op=ALU.max, apply_absolute_value=True),
                [sc["last"]])
            tmc = DVE(lambda h, sc=sc: h.tensor_scalar(
                out=sc["am"], in0=sc["am"], scalar1=1e-5, scalar2=None,
                op0=ALU.max), [tam])
            trc = DVE(lambda h, sc=sc: h.reciprocal(
                out=sc["s"], in_=sc["am"]), [tmc])
            ts_ = DVE(lambda h, sc=sc: h.tensor_scalar(
                out=sc["s"], in0=sc["s"], scalar1=QB, scalar2=None,
                op0=ALU.mult), [trc])
            tin = DVE(lambda h, sc=sc: h.tensor_scalar(
                out=sc["inv"], in0=sc["am"], scalar1=1.0 / QB, scalar2=None,
                op0=ALU.mult), [tmc])
            p5_sc_toks[qt] = [ts_, tin]

        p5_scale(0)
        for qt in range(NT):
            sc = SCP[qt % 2]
            prev_evac = tok_qo
            if qt + 1 < NT:
                p5_scale(qt + 1)
            for ck in range(4):
                tq1, tq2 = emit_act_quant(
                    lambda k, qt=qt, ck=ck: B3[:, 8 * ck:8 * (ck + 1),
                                               qt * 128:(qt + 1) * 128],
                    lambda k, ck=ck: s_qtm5[:, ck * 1024:
                                            (ck + 1) * 1024].rearrange(
                        "p (c f) -> p c f", c=8),
                    s_y1p5[:],
                    p5_sc_toks[qt] + [prev_evac, t_prev_chain[0]],
                    n_chunks=1, sc=sc)
                t_prev_chain[0] = tq2
                for c in range(8 * ck, 8 * ck + 8):
                    sl, bk = ring5.next()
                    tp = PE(lambda h, c=c, sl=sl: h.transpose(
                        sl, s_qtm5[:, c * 128:(c + 1) * 128], s_identbf[:]),
                        [tq2, ring5.rd[bk]], inc=True)
                    tev = DVE(lambda h, c=c, qt=qt, sl=sl: h.tensor_copy(
                        out=Qo3[:, c, qt * 128:(qt + 1) * 128], in_=sl), [tp])
                    ring5.rd[bk] = tev
                    tok_qo = tev

        t_pp5 = [None, None, None, None]
        t_aofm_use = None
        t_at3s_use = None
        t_at3_dma = []

        def wo_q(m, hf, first=False):
            xd = [tok_swo] + (tok_sdpa if first else [])
            return ws.next(wo[m, :, hf * 2048:(hf + 1) * 2048], "quant",
                           scol=6, extra_deps=xd, scol_c2=(11,), par=hf)

        def wo_mm(m, hf, qinfo):
            pr = m % 2
            pp = pproj5[2 * pr:2 * pr + 2]
            par, trdy = qinfo
            tokm = None
            for j in range(16):
                c = hf * 16 + j
                for tg in range(2):
                    dep = [trdy, tok_qo, tok_swo]
                    if c == 0:
                        dep.append(t_pp5[2 * pr + tg])
                    tokm = PE(lambda h, wb_=ws.bf(par), j=j, c=c, tg=tg,
                              pp_=pp[tg]: h.matmul(
                        pp_[:],
                        wb_[:, j * 128:(j + 1) * 128],
                        Qo3[:, c, tg * 512:(tg + 1) * 512],
                        start=(c == 0), stop=(c == 31)),
                        dep, inc=(tg == 1 and (c == 15 or c == 31)))
            ws.consumed(par, tokm)
            return tokm

        woq0 = wo_q(0, 0, True)
        woq1 = wo_q(0, 1, True)
        for m in range(NH):
            pr = m % 2
            pp = pproj5[2 * pr:2 * pr + 2]
            mm0 = wo_mm(m, 0, woq0)
            if m + 1 < NH:
                woq0 = wo_q(m + 1, 0)
            mm1 = wo_mm(m, 1, woq1)
            if m + 1 < NH:
                woq1 = wo_q(m + 1, 1)
            mm_last = [mm0, mm1]
            te0 = ACT(lambda h, pp_=pp[0]: h.activation(
                out=s_aofm5[:, 0:512], in_=pp_[:], func=AF.Copy,
                scale=s_scales[:, 7:8]),
                [mm_last[0], mm_last[1], t_aofm_use])
            te1 = ACT(lambda h, pp_=pp[1]: h.activation(
                out=s_aofm5[:, 512:1024], in_=pp_[:], func=AF.Copy,
                scale=s_scales[:, 7:8]), [mm_last[0], mm_last[1]])
            t_pp5[2 * pr], t_pp5[2 * pr + 1] = te0, te1
            tevl = None
            for qt in range(NT):
                sl, bk = ring5.next()
                tp = PE(lambda h, qt=qt, sl=sl: h.transpose(
                    sl, s_aofm5[:, qt * 128:(qt + 1) * 128], s_identbf[:]),
                    [te0, te1, ring5.rd[bk]], inc=True)
                tev = DVE(lambda h, qt=qt, m=m, sl=sl: h.tensor_copy(
                    out=s_at3s[:, qt, :], in_=sl), [tp, t_at3s_use])
                ring5.rd[bk] = tev
                tevl = tev
            t_aofm_use = tevl
            tdm = b.dma("pool", "dK1",
                        at3d.ap().rearrange("(t p) (m f) -> p t m f", p=128,
                                            f=128)[:, :, m, :],
                        s_at3s,
                        deps=[tevl])
            t_at3s_use = tdm
            t_at3_dma.append(tdm)
        tok_at3 = t_at3_dma[-1]

        p5.close()
        b.barrier(("pe", "act", "dve", "pool"))
        p6 = ExitStack()
        ph1 = p6.enter_context(nc.psum_tensor("ph1", [32, 512], F32))
        pg1 = p6.enter_context(nc.psum_tensor("pg1", [8, 512], F32))
        pA = p6.enter_context(nc.psum_tensor("pA", [128, 512], F32))
        pf32 = p6.enter_context(nc.psum_tensor("pf32", [128, 512], F32))
        pg2 = pf32[:, 384:392]
        pwb = [p6.enter_context(nc.psum_tensor(f"pwb{i}", [128, 512], F32))
               for i in range(2)]
        ring6 = PbfRing(p6.enter_context(
            nc.psum_tensor("pbf6", [128, 2048], BF16))[:])
        p_s16 = pwb[0]

        X3 = regC_buf[:].rearrange("p (c t) -> p c t", t=TOK)

        lora = ExitStack()
        s_gw1bf = lora.enter_context(
            nc.sbuf_tensor("s_gw1bf", [128, NCH, 32], BF16))
        s_abf = lora.enter_context(
            nc.sbuf_tensor("s_abf", [128, NCH, 128], BF16))
        s_bbf = lora.enter_context(nc.sbuf_tensor("s_bbf", [128, HID], BF16))
        s_x6v = [regA_buf[:, 0:8192].bitcast(F32),
               regA_buf[:, 8192:16384].bitcast(F32)]
        s_y16v = regA_buf[:, 16384:24576].bitcast(F32)
        s_z6v = regA_buf[:, 24576:32768].bitcast(F32)
        s_qtm6v = p6.enter_context(nc.sbuf_tensor("s_qtm6", [128, HID], BF16))
        s_at3l = [p6.enter_context(
            nc.sbuf_tensor(f"s_at3l{i}", [128, HID], BF16)) for i in range(2)]
        s_hfm = p6.enter_context(nc.sbuf_tensor("s_hfm", [32, 128], F32))
        s_ht = p6.enter_context(nc.sbuf_tensor("s_ht", [128, 32], F32))
        s_hsq = p6.enter_context(nc.sbuf_tensor("s_hsq", [128, 32], F32))
        s_ht2 = p6.enter_context(nc.sbuf_tensor("s_ht2", [128, 32], F32))
        s_h2fm = p6.enter_context(nc.sbuf_tensor("s_h2fm", [32, 128], BF16))
        s_gfm = p6.enter_context(nc.sbuf_tensor("s_gfm", [8, 128], F32))
        s_gt = p6.enter_context(nc.sbuf_tensor("s_gt", [128, 8], F32))
        s_eq1 = p6.enter_context(nc.sbuf_tensor("s_eq1", [128, 8], F32))
        s_eq2 = p6.enter_context(nc.sbuf_tensor("s_eq2", [128, 8], F32))
        s_msk = p6.enter_context(nc.sbuf_tensor("s_msk", [128, 8], F32))
        s_wt = p6.enter_context(nc.sbuf_tensor("s_wt", [128, 8], F32))
        s_wt2 = p6.enter_context(nc.sbuf_tensor("s_wt2", [128, 8], F32))
        s_wfm = p6.enter_context(nc.sbuf_tensor("s_wfm", [8, TOK], BF16))
        s_wb = p6.enter_context(nc.sbuf_tensor("s_wb", [128, TOK], F32))
        s_tmp = p6.enter_context(nc.sbuf_tensor("s_tmp", [128, TOK], F32))

        t_g1l = b.dma("pool", "dM", s_x6v[0][:, 0:NCH * 32], gw1t,
                      deps=[tok_at3])
        t_g1c = DVE(lambda h: h.tensor_copy(
            out=s_gw1bf[:].rearrange("p a b -> p (a b)"),
            in_=s_x6v[0][:, 0:NCH * 32]), [t_g1l])
        t_g2 = b.dma("pool", "dM", s_hfm[0:32, 0:8], gw2t)
        t_g2c = DVE(lambda h: h.tensor_copy(out=s_gw2bf[:],
                                            in_=s_hfm[0:32, 0:8]), [t_g2])
        t_al = b.dma("pool", "dM", s_x6v[1], a_all, deps=[tok_at3])
        t_ac = DVE(lambda h: h.tensor_copy(
            out=s_abf[:].rearrange("p a b -> p (a b)"),
            in_=s_x6v[1]), [t_al])
        t_bl = b.dma("pool", "dM", s_x6v[0], b_all, deps=[t_g1c])
        t_blc = DVE(lambda h: h.tensor_copy(out=s_bbf[:], in_=s_x6v[0]),
                    [t_bl])
        t_x_use = [t_blc, t_ac]

        tok_gstash = []
        t_ph1 = None
        t_pg1 = None
        t_pg2 = None
        t_pA = None
        t_pf32 = None
        t_cnt_prev = t_cnt0
        t_pbf = [None] * 8
        tok_x3 = None
        t_tmp_use = None
        t_spill = []

        prev_tx1 = None
        prev_tsp = None
        prev_tx3 = None
        t_a3_use = [None, None]
        for t in range(NT):
            par = t % 2
            xb = s_x6v[par]
            a3 = s_at3l[par]
            tl = b.dma("pool", "dX%d" % par, xb,
                       x_own[t * 128:(t + 1) * 128, :], deps=[t_x_use[par]])
            tla = b.dma("pool", "dK%d" % par, a3[:],
                        at3d.ap()[t * 128:(t + 1) * 128, :],
                        deps=[t_a3_use[par], tok_at3])
            r3, b3, tcp = ln_stats(a3[:], HID, [tla, prev_tx1, tok_x3],
                                   copy_dst=s_y16v, sq_dst=s_qtm6v[:])
            t_a3_use[par] = r3
            tx1 = DVE(lambda h, xb=xb: h.scalar_tensor_tensor(
                out=s_z6v, in0=s_y16v, scalar=c_r, in1=xb,
                op0=ALU.mult, op1=ALU.add),
                [r3, b3, tcp, tl, prev_tsp, prev_tx3])
            tx2 = DVE(lambda h: h.tensor_scalar(out=s_z6v, in0=s_z6v,
                                                scalar1=c_b, scalar2=None,
                                                op0=ALU.add), [tx1])
            t_x_use[par] = tx1
            prev_tx1 = tx1
            tsp = b.dma("pool", "dM", x2d.ap()[t * 128:(t + 1) * 128, :],
                        s_z6v, deps=[tx2])
            t_spill.append(tsp)
            prev_tsp = tsp
            r4, b4, _ = ln_stats(s_z6v, HID, [tx2],
                                 copy_dst=s_y16v, sq_dst=s_y16v)
            tx3 = ACT(lambda h: h.activation(out=s_qtm6v[:], in_=s_z6v,
                                             func=AF.Identity, bias=c_b,
                                             scale=c_r), [r4, b4, tok_x3])
            prev_tx3 = tx3
            for c in range(NCH):
                sl, bk = ring6.next()
                tp = PE(lambda h, c=c, sl=sl: h.transpose(
                    sl, s_qtm6v[:, c * 128:(c + 1) * 128], s_identbf[:]),
                    [tx3, ring6.rd[bk]], inc=True)
                tev = DVE(lambda h, c=c, t=t, sl=sl: h.tensor_copy(
                    out=X3[:, c, t * 128:(t + 1) * 128], in_=sl), [tp])
                ring6.rd[bk] = tev
                tok_x3 = tev

            tokm = None
            for c in range(NCH):
                dep = [t_g1c]
                if c == 0:
                    dep += [tok_x3, t_ph1]
                tokm = PE(lambda h, c=c, t=t: h.matmul(
                    ph1[:, 0:128], s_gw1bf[:, c, :],
                    X3[:, c, t * 128:(t + 1) * 128],
                    start=(c == 0), stop=(c == NCH - 1)),
                    dep, inc=(c == NCH - 1))
            th0 = ACT(lambda h: h.activation(out=s_hfm[:], in_=ph1[:, 0:128],
                                             func=AF.Identity, bias=s_gb1[:],
                                             scale=1.0), [tokm, t_ph1])
            t_ph1 = th0
            tht = PE(lambda h: h.transpose(pf32[0:128, 0:32], s_hfm[:],
                                           s_identf[0:32, 0:32]),
                     [th0, t_pf32, t_pg2], inc=True)
            the = DVE(lambda h: h.tensor_copy(out=s_ht[:],
                                              in_=pf32[0:128, 0:32]), [tht])
            t_pf32 = the
            ts1 = DVE(lambda h: h.tensor_reduce(out=c_t0, in_=s_ht[:],
                                                axis=AX.X, op=ALU.add), [the])
            ts2 = DVE(lambda h: h.tensor_tensor(out=s_hsq[:], in0=s_ht[:],
                                                in1=s_ht[:], op=ALU.mult),
                      [the])
            ts3 = DVE(lambda h: h.tensor_reduce(out=c_t1, in_=s_hsq[:],
                                                axis=AX.X, op=ALU.add), [ts2])
            ts4 = DVE(lambda h: h.tensor_scalar(out=c_t0, in0=c_t0,
                                                scalar1=1.0 / 32, scalar2=None,
                                                op0=ALU.mult), [ts1])
            ts5 = DVE(lambda h: h.tensor_tensor(out=c_sq, in0=c_t0, in1=c_t0,
                                                op=ALU.mult), [ts4])
            ts6 = DVE(lambda h: h.scalar_tensor_tensor(out=c_t1, in0=c_t1,
                                                       scalar=1.0 / 32,
                                                       in1=c_sq, op0=ALU.mult,
                                                       op1=ALU.subtract),
                      [ts3, ts5])
            ts7 = ACT(lambda h: h.activation(out=c_t1, in_=c_t1, func=AF.Ln,
                                             bias=c_eps, scale=1.0),
                      [ts6, t_eps])
            ts8 = ACT(lambda h: h.activation(out=c_r, in_=c_t1, func=AF.Exp,
                                             bias=0.0, scale=-0.5), [ts7])
            ts9 = DVE(lambda h: h.tensor_tensor(out=c_b, in0=c_t0, in1=c_r,
                                                op=ALU.mult), [ts8, ts4])
            tsa = DVE(lambda h: h.tensor_scalar(out=c_b, in0=c_b, scalar1=-1.0,
                                                scalar2=None, op0=ALU.mult),
                      [ts9])
            tsb = ACT(lambda h: h.activation(out=s_ht2[:], in_=s_ht[:],
                                             func=AF.Identity, bias=c_b,
                                             scale=c_r), [ts8, tsa, the])
            tsc_ = DVE(lambda h: h.tensor_tensor(out=s_ht2[:], in0=s_ht2[:],
                                                 in1=s_g32[:], op=ALU.mult),
                       [tsb, t_const])
            tsd = DVE(lambda h: h.tensor_tensor(out=s_ht2[:], in0=s_ht2[:],
                                                in1=s_bt32[:], op=ALU.add),
                      [tsc_])
            tse = DVE(lambda h: h.tensor_relu(out=s_ht2[:], in_=s_ht2[:]),
                      [tsd])
            tsf = PE(lambda h: h.transpose(pf32[0:32, 0:128], s_ht2[:],
                                           s_identf[:]),
                     [tse, t_pf32], inc=True)
            tsg = DVE(lambda h: h.tensor_copy(out=s_h2fm[:],
                                              in_=pf32[0:32, 0:128]), [tsf])
            t_pf32 = tsg
            tsh = PE(lambda h: h.matmul(pg1[:, 0:128], s_gw2bf[:], s_h2fm[:],
                                        start=True, stop=True),
                     [tsg, t_g2c, t_pg1], inc=True)
            tsi = ACT(lambda h: h.activation(out=s_gfm[:], in_=pg1[:, 0:128],
                                             func=AF.Identity, bias=s_gb2[:],
                                             scale=1.0), [tsh])
            t_pg1 = tsi
            tsj = PE(lambda h: h.transpose(pg2, s_gfm[:],
                                           s_identf[0:8, 0:8]),
                     [tsi, t_pg2], inc=True)
            tsk = DVE(lambda h: h.tensor_copy(out=s_gt[:], in_=pg2), [tsj])
            t_pg2 = tsk
            tt1 = DVE(lambda h: h.tensor_reduce(out=c_m1, in_=s_gt[:],
                                                axis=AX.X, op=ALU.max), [tsk])
            tt2 = DVE(lambda h: h.tensor_scalar(out=s_eq1[:], in0=s_gt[:],
                                                scalar1=c_m1, scalar2=None,
                                                op0=ALU.is_equal), [tt1])
            tt3 = DVE(lambda h: h.scalar_tensor_tensor(out=s_msk[:],
                                                       in0=s_eq1[:],
                                                       scalar=-1e30,
                                                       in1=s_gt[:],
                                                       op0=ALU.mult,
                                                       op1=ALU.add), [tt2])
            tt4 = DVE(lambda h: h.tensor_reduce(out=c_m2, in_=s_msk[:],
                                                axis=AX.X, op=ALU.max), [tt3])
            tt5 = DVE(lambda h: h.tensor_scalar(out=s_eq2[:], in0=s_msk[:],
                                                scalar1=c_m2, scalar2=None,
                                                op0=ALU.is_equal), [tt4])
            tt6 = DVE(lambda h: h.tensor_tensor(out=c_d, in0=c_m2, in1=c_m1,
                                                op=ALU.subtract), [tt4, tt1])
            tt7 = ACT(lambda h: h.activation(out=c_e2, in_=c_d, func=AF.Exp,
                                             bias=0.0, scale=1.0), [tt6])
            tt8 = DVE(lambda h: h.tensor_scalar(out=c_rg, in0=c_e2,
                                                scalar1=1.0, scalar2=None,
                                                op0=ALU.add), [tt7])
            tt9 = DVE(lambda h: h.reciprocal(out=c_rg, in_=c_rg), [tt8])
            tta = DVE(lambda h: h.tensor_scalar(out=c_g1, in0=c_rg,
                                                scalar1=ALPHA, scalar2=None,
                                                op0=ALU.mult), [tt9])
            ttb = DVE(lambda h: h.tensor_tensor(out=c_g2, in0=c_e2, in1=c_rg,
                                                op=ALU.mult), [tt7, tt9])
            ttc = DVE(lambda h: h.tensor_scalar(out=c_g2, in0=c_g2,
                                                scalar1=ALPHA, scalar2=None,
                                                op0=ALU.mult), [ttb])
            ttd = DVE(lambda h, t=t: h.tensor_scalar(out=s_c1[:, t, :],
                                                     in0=s_eq1[:],
                                                     scalar1=c_g1,
                                                     scalar2=None,
                                                     op0=ALU.mult),
                      [tta, tt2])
            tte = DVE(lambda h, t=t: h.tensor_scalar(out=s_c2[:, t, :],
                                                     in0=s_eq2[:],
                                                     scalar1=c_g2,
                                                     scalar2=None,
                                                     op0=ALU.mult),
                      [ttc, tt5])
            ttf = DVE(lambda h: h.tensor_tensor(out=s_cnt[:, 0:8],
                                                in0=s_cnt[:, 0:8],
                                                in1=s_eq1[:], op=ALU.add),
                      [tt2, t_cnt_prev])
            ttg = DVE(lambda h: h.tensor_tensor(out=s_cnt[:, 8:16],
                                                in0=s_cnt[:, 8:16],
                                                in1=s_eq2[:], op=ALU.add),
                      [tt5, ttf])
            t_cnt_prev = ttg
            tok_gstash.append((ttd, tte))

            tokA = None
            for c in range(NCH):
                dep = [t_ac]
                if c == 0:
                    dep += [tok_x3, t_pA]
                tokA = PE(lambda h, c=c, t=t: h.matmul(
                    pA[:, 0:128], s_abf[:, c, :], X3[:, c, t * 128:(t + 1) * 128],
                    start=(c == 0), stop=(c == NCH - 1)),
                    dep, inc=(c == NCH - 1))
            tAe = DVE(lambda h, t=t: h.tensor_copy(
                out=s_tmp[:, t * 128:(t + 1) * 128], in_=pA[:, 0:128]), [tokA])
            t_pA = tAe
            t_tmp_use = tAe

        tcs = PE(lambda h: h.matmul(p_s16[0:1, 0:16], s_onesf[:, 0:1],
                                    s_cnt[:], start=True, stop=True),
                 [t_cnt_prev], inc=True)
        tce = DVE(lambda h: h.tensor_copy(out=s_cntrow[:],
                                          in_=p_s16[0:1, 0:16]), [tcs])
        tcd = b.dma("pool", "dM", cnt_in.ap(), s_cntrow[:], deps=[tce])
        tok_cr = b.coll("AllReduce", ALU.add, [[0, 1, 2, 3, 4, 5, 6, 7]],
                        [cnt_in.ap().opt()], [cnt_out.ap().opt()],
                        deps=[tcd], ch="cc2")
        tcb = b.dma("pool", "dM", s_cntrow[:], cnt_out.ap(), deps=[tok_cr])
        tal = DVE(lambda h: h.tensor_scalar(out=s_cntrow[:], in0=s_cntrow[:],
                                            scalar1=CAPACITY, scalar2=None,
                                            op0=ALU.is_le), [tcb])
        tab = PE(lambda h: h.matmul(pA[:, 0:16], s_onesf[0:1, :],
                                    s_cntrow[:], start=True, stop=True),
                 [tal, t_pA], inc=True)
        tae = DVE(lambda h: h.tensor_copy(out=s_alb[:], in_=pA[:, 0:16]),
                  [tab])
        t_wfm = None
        for t in range(NT):
            tw1 = DVE(lambda h, t=t: h.tensor_tensor(
                out=s_wt[:], in0=s_c1[:, t, :], in1=s_alb[:, 0:8],
                op=ALU.mult), [tae, tok_gstash[t][0]])
            tw2 = DVE(lambda h, t=t: h.tensor_tensor(
                out=s_wt2[:], in0=s_c2[:, t, :], in1=s_alb[:, 8:16],
                op=ALU.mult), [tae, tok_gstash[t][1]])
            tw3 = DVE(lambda h: h.tensor_tensor(out=s_wt[:], in0=s_wt[:],
                                                in1=s_wt2[:], op=ALU.add),
                      [tw1, tw2])
            tw4 = PE(lambda h: h.transpose(pf32[0:8, 0:128], s_wt[:],
                                           s_identf[:]),
                     [tw3, t_pf32, t_pg2], inc=True)
            tw5 = DVE(lambda h, t=t: h.tensor_copy(
                out=s_wfm[:, t * 128:(t + 1) * 128],
                in_=pf32[0:8, 0:128]), [tw4])
            t_pf32 = tw5
            t_wfm = tw5
        twb0 = PE(lambda h: h.matmul(pwb[0][:], s_selS[:], s_wfm[:, 0:512],
                                     start=True, stop=True), [t_wfm],
                  inc=True)
        twb1 = PE(lambda h: h.matmul(pwb[1][:], s_selS[:], s_wfm[:, 512:1024],
                                     start=True, stop=True), [t_wfm],
                  inc=True)
        twe0 = DVE(lambda h: h.tensor_copy(out=s_wb[:, 0:512], in_=pwb[0][:]),
                   [twb0])
        twe1 = DVE(lambda h: h.tensor_copy(out=s_wb[:, 512:1024],
                                           in_=pwb[1][:]), [twb1])
        tok_tsc = DVE(lambda h: h.tensor_tensor(out=s_tsc[:], in0=s_tmp[:],
                                                in1=s_wb[:], op=ALU.mult),
                      [twe0, twe1, t_tmp_use])

        p6.close()
        b.barrier(("pe", "act", "dve"))
        p7 = ExitStack()
        pproj7 = [p7.enter_context(nc.psum_tensor(f"pp7{i}", [128, 512], F32))
                  for i in range(4)]
        ring7 = PbfRing(p7.enter_context(
            nc.psum_tensor("pbf7", [128, 2048], BF16))[:])
        open_stream_bufs(p7)
        s_aofm7v = p7.enter_context(nc.sbuf_tensor("s_aofm7", [128, TOK], BF16))

        F3 = regA_buf[:].rearrange("p (t f) -> p t f", f=HID)

        t_pp7 = [None, None, None, None]
        t_aofm_use = None
        t_pbf = [None] * 8
        tok_f3 = None

        def wm_q(m, hf, first=False):
            xd = [tok_tsc, t_spill[-1]] if first else []
            return ws.next(wm[m, :, hf * 2048:(hf + 1) * 2048], "cast",
                           extra_deps=xd, par=hf)

        def wm_mm(m, hf, qinfo):
            pr = m % 2
            pp = pproj7[2 * pr:2 * pr + 2]
            par, trdy = qinfo
            tokm = None
            for j in range(16):
                c = hf * 16 + j
                for tg in range(2):
                    dep = [trdy, tok_x3]
                    if c == 0:
                        dep.append(t_pp7[2 * pr + tg])
                    tokm = PE(lambda h, wb_=ws.bf(par), j=j, c=c, tg=tg,
                              pp_=pp[tg]: h.matmul(
                        pp_[:],
                        wb_[:, j * 128:(j + 1) * 128],
                        X3[:, c, tg * 512:(tg + 1) * 512],
                        start=(c == 0), stop=False),
                        dep, inc=(j == 15 and tg == 1))
            ws.consumed(par, tokm)
            return tokm

        wmq0 = wm_q(0, 0, True)
        wmq1 = wm_q(0, 1, True)
        for m in range(NCH):
            pr = m % 2
            pp = pproj7[2 * pr:2 * pr + 2]
            wm_mm(m, 0, wmq0)
            if m + 1 < NCH:
                wmq0 = wm_q(m + 1, 0)
            wm_mm(m, 1, wmq1)
            if m + 1 < NCH:
                wmq1 = wm_q(m + 1, 1)
            tlb_ = [None, None]
            for tg in range(2):
                tlb_[tg] = PE(lambda h, m=m, tg=tg, pp_=pp[tg]: h.matmul(
                    pp_[:], s_bbf[:, m * 128:(m + 1) * 128],
                    s_tsc[:, tg * 512:(tg + 1) * 512],
                    start=False, stop=True),
                    [tok_tsc, t_blc], inc=True)
            te0 = ACT(lambda h, pp_=pp[0]: h.activation(
                out=s_aofm7v[:, 0:512], in_=pp_[:], func=AF.Copy),
                [tlb_[0], t_aofm_use])
            te1 = ACT(lambda h, pp_=pp[1]: h.activation(
                out=s_aofm7v[:, 512:1024], in_=pp_[:], func=AF.Copy),
                [tlb_[1]])
            t_pp7[2 * pr], t_pp7[2 * pr + 1] = te0, te1
            tevl = None
            for qt in range(NT):
                sl, bk = ring7.next()
                tp = PE(lambda h, qt=qt, sl=sl: h.transpose(
                    sl, s_aofm7v[:, qt * 128:(qt + 1) * 128], s_identbf[:]),
                    [te0, te1, ring7.rd[bk]], inc=True)
                tev = DVE(lambda h, qt=qt, m=m, sl=sl: h.tensor_copy(
                    out=F3[:, qt, m * 128:(m + 1) * 128], in_=sl),
                    [tp, t_spill[-1]])
                ring7.rd[bk] = tev
                tevl = tev
            t_aofm_use = tevl
        tok_f3 = t_aofm_use

        p7.close()
        lora.close()
        b.barrier(("pe", "act", "dve", "pool"))
        p8 = ExitStack()
        s_x8v = [p8.enter_context(nc.sbuf_tensor(f"s_x8{i}", [128, HID], F32))
               for i in range(2)]
        s_y18v = [regC_buf[:, 0:8192].bitcast(F32),
                  regC_buf[:, 8192:16384].bitcast(F32)]
        s_z8v = [regC_buf[:, 16384:24576].bitcast(F32),
                 regC_buf[:, 24576:32768].bitcast(F32)]
        s_jnk8 = p8.enter_context(nc.sbuf_tensor("s_jnk8", [128, HID], BF16))
        t_x_use = [None, None]
        t_y_use = [None, None]
        t_z_use = [None, None]
        t_store = []
        tok_dM_all = ("dM", b.cnt["dM"])
        for t in range(NT):
            par = t % 2
            sc8 = SCP[par]
            xb, yb, zb = s_x8v[par], s_y18v[par], s_z8v[par]
            tl = b.dma("pool", "dX%d" % par, xb[:],
                       x2d.ap()[t * 128:(t + 1) * 128, :],
                       deps=[t_x_use[par], tok_dM_all])
            r5, b5, tcp = ln_stats(F3[:, t, :], HID, [tok_f3, t_y_use[par]],
                                   copy_dst=yb[:], sq_dst=s_jnk8[:], sc=sc8)
            tx1 = DVE(lambda h, xb=xb, yb=yb, zb=zb, sc8=sc8:
                      h.scalar_tensor_tensor(
                out=zb[:], in0=yb[:], scalar=sc8["r"], in1=xb[:],
                op0=ALU.mult, op1=ALU.add), [r5, b5, tcp, tl, t_z_use[par]])
            tx2 = DVE(lambda h, zb=zb, sc8=sc8: h.tensor_scalar(
                out=zb[:], in0=zb[:], scalar1=sc8["b"], scalar2=None,
                op0=ALU.add), [tx1])
            t_x_use[par] = tx1
            t_y_use[par] = tx1
            sc8["last"] = tx2
            tst = b.dma("pool", "dS", y_own[t * 128:(t + 1) * 128, :],
                        zb[:], deps=[tx2])
            t_store.append(tst)
            t_z_use[par] = tst
        if DEBUG_DUMPS:
            b.dma("pool", "dM", dbg_ccg.ap(), ccg_in.ap(), deps=t_stage)
            b.dma("pool", "dM", dbg_at3.ap(), at3d.ap(), deps=[tok_at3])
            b.dma("pool", "dM", dbg_x2.ap(), x2d.ap(), deps=[t_spill[-1]])
            tsc_ = b.dma("pool", "dM", dbg_sc.ap()[:, 0:12], s_scales[:],
                         deps=[t_pb_use[0]])
            b.dma("pool", "dM", dbg_sc.ap()[0:1, 12:12 + NH], s_qn2[0:1, :],
                  deps=[t_pnorm[0]])
            b.dma("pool", "dM", dbg_sc.ap()[0:1, 12 + NH:12 + NH + 1],
                  s_negM[0:1, :], deps=[tok_negM])
            t_store.append(b.dma("pool", "dS", dbg_sc.ap()[1:2, 12:20],
                                 s_kn2[0:1, :], deps=[t_kn_back]))
        b.op("sp", lambda h: h.nop(), deps=[t_store[-1]], inc=False)

        with nc.Block() as block:
            @block.tensor
            def _(h):
                for f in b.stream["pe"]:
                    f(h)

            @block.scalar
            def _(h):
                for f in b.stream["act"]:
                    f(h)

            @block.vector
            def _(h):
                for f in b.stream["dve"]:
                    f(h)

            @block.gpsimd
            def _(h):
                for f in b.stream["pool"]:
                    f(h)

            @block.sync
            def _(h):
                for f in b.stream["sp"]:
                    f(h)

        p8.close()

    return nc


def _prep_inputs(inputs):
    f32 = np.float32
    x = np.ascontiguousarray(np.asarray(inputs["L0_attn_hidden_states"], f32))
    Wq = np.asarray(inputs["L0_attn_Wq"], f32)
    Wk = np.asarray(inputs["L0_attn_Wk"], f32)
    Wv = np.asarray(inputs["L0_attn_Wv"], f32)
    Wo = np.asarray(inputs["L0_attn_Wo"], f32)
    Wm = np.asarray(inputs["L0_ffn_weight"], f32)
    lA = np.asarray(inputs["L0_ffn_lora_A"], f32)
    lB = np.asarray(inputs["L0_ffn_lora_B"], f32)
    gw1 = np.asarray(inputs["L0_ffn_gw1"], f32)
    gb1 = np.asarray(inputs["L0_ffn_gb1"], f32)
    lng = np.asarray(inputs["L0_ffn_ln_gamma"], f32)
    lnbta = np.asarray(inputs["L0_ffn_ln_beta"], f32)
    gw2 = np.asarray(inputs["L0_ffn_gw2"], f32)
    gb2 = np.asarray(inputs["L0_ffn_gb2"], f32)
    ln_w = np.asarray(inputs["L0_attn_ln_w"], f32)
    ln_b = np.asarray(inputs["L0_attn_ln_b"], f32)

    def blk(W):
        nm = W.shape[0] // 128
        return np.ascontiguousarray(
            W.reshape(nm, 128, NCH, 128).transpose(0, 3, 2, 1)
        ).reshape(nm, 128, HID)

    shared = {
        "wq": blk(Wq), "wk": blk(Wk), "wv": blk(Wv), "wo": blk(Wo),
        "wm": blk(Wm),
        "a_all": np.ascontiguousarray(
            lA.transpose(1, 0, 2).reshape(HID, 128).reshape(
                NCH, 128, 128).transpose(1, 0, 2).reshape(128, HID)),
        "b_all": np.ascontiguousarray(lB.reshape(128, HID)),
        "gw1t": np.ascontiguousarray(
            gw1.T.reshape(NCH, 128, 32).transpose(1, 0, 2).reshape(
                128, NCH * 32)),
        "gb1c": np.ascontiguousarray(gb1.reshape(32, 1)),
        "g32": np.ascontiguousarray(np.broadcast_to(lng, (128, 32))),
        "bt32": np.ascontiguousarray(np.broadcast_to(lnbta, (128, 32))),
        "gw2t": np.ascontiguousarray(gw2.T),
        "gb2c": np.ascontiguousarray(gb2.reshape(8, 1)),
        "lnw": np.ascontiguousarray(np.broadcast_to(ln_w, (128, HID))),
        "lnb": np.ascontiguousarray(np.broadcast_to(ln_b, (128, HID))),
        "identbf": np.eye(128, dtype=ml_dtypes.bfloat16),
        "identf": np.eye(128, dtype=f32),
        "onesf": np.ones((128, 128), dtype=f32),
        "onesbf": np.ones((128, 1), dtype=ml_dtypes.bfloat16),
    }
    S = np.zeros((8, 128), dtype=ml_dtypes.bfloat16)
    for p in range(128):
        S[p // 16, p] = 1.0
    shared["selS"] = S

    inv = (1.0 / (np.float32(10000.0) **
                  (np.arange(0, 128, 2, dtype=f32) / f32(128.0)))).astype(f32)
    maps = []
    for core in range(8):
        bb, hh = core // 2, core % 2
        pos = np.arange(hh * TOK, (hh + 1) * TOK, dtype=f32)
        freqs = pos[None, :] * inv[:, None]
        emb = np.concatenate([freqs, freqs], axis=0)
        m = dict(shared)
        m["cosb"] = np.cos(emb).astype(f32)
        m["sinb"] = np.sin(emb).astype(f32)
        m["x_own"] = np.ascontiguousarray(x[bb, hh * TOK:(hh + 1) * TOK, :])
        maps.append(m)
    return maps


def kernel(**inputs):
    maps = _prep_inputs(inputs)
    nc = build_nc()
    res = run_bass_kernel_spmd(nc, maps, core_ids=list(range(8)))
    out = np.empty((BATCH, SEQ, HID), dtype=np.float32)
    for core in range(8):
        bb, hh = core // 2, core % 2
        out[bb, hh * TOK:(hh + 1) * TOK, :] = np.asarray(res.results[core]["y"])
    return out
